# version 02
# baseline (speedup 1.0000x reference)
import sys
from contextlib import ExitStack

import numpy as np

sys.path.insert(0, "/opt/trn_rl_repo")
import concourse.bass as bass
import concourse.mybir as mybir
from concourse.bass_utils import run_bass_kernel_spmd

F32 = mybir.dt.float32
BF16 = mybir.dt.bfloat16
U32 = mybir.dt.uint32
AF = mybir.ActivationFunctionType
ALU = mybir.AluOpType
AX = mybir.AxisListType

P = 128
TC = 2048
NT = TC // P
D = 1024
DC = D // P
H4 = 4096
HC = H4 // P
NHEAD = 8
DK = 128
SCALE = float(DK) ** -0.5
PZROWS = D + P


class SemCtr:
    def __init__(self, es, nc, name):
        self.h = es.enter_context(nc.semaphore(name))
        self.n = 0

    def inc(self, inst, k=1):
        inst.then_inc(self.h, k)
        self.n += k
        return self.n


def build_fused(upto="H"):
    nc = bass.Bass("TRN2", target_bir_lowering=False, debug=False)
    order = ["A", "P", "B", "D", "E", "F", "G"]
    assert upto in order
    stages = order[: order.index(upto) + 1]

    def want(s):
        return s in stages

    xin = nc.dram_tensor("xin", [TC, D], F32, kind="ExternalInput").ap()
    wqT = nc.dram_tensor("wqT", [D, D], F32, kind="ExternalInput").ap()
    wkT = nc.dram_tensor("wkT", [D, D], F32, kind="ExternalInput").ap()
    wvT = nc.dram_tensor("wvT", [D, D], F32, kind="ExternalInput").ap()
    wg1T = nc.dram_tensor("wg1T", [D, 128], F32, kind="ExternalInput").ap()
    wg2T = nc.dram_tensor("wg2T", [128, D], F32, kind="ExternalInput").ap()
    gn_in = nc.dram_tensor("gnorm_pc", [P, DC], F32, kind="ExternalInput").ap()
    wo_in = nc.dram_tensor("woT", [D, D], F32, kind="ExternalInput").ap()
    wfcT = nc.dram_tensor("wfcT", [D, H4], F32, kind="ExternalInput").ap()
    wpT = nc.dram_tensor("wpT", [H4, D], F32, kind="ExternalInput").ap()
    lnw_in = nc.dram_tensor("lnw_pc", [P, HC], F32, kind="ExternalInput").ap()
    lnb_in = nc.dram_tensor("lnb_pc", [P, HC], F32, kind="ExternalInput").ap()
    pmask_in = nc.dram_tensor("pmask", [P, 4], F32, kind="ExternalInput").ap()
    xprev = nc.dram_tensor("xprev", [3 * TC, D], F32, kind="ExternalInput").ap()

    sgd = nc.dram_tensor("sgd", [D, TC], BF16)
    attn0_kind = "ExternalOutput" if upto == "D" else "Internal"
    attn0_d = nc.dram_tensor("attn0_d", [TC, D], F32, kind=attn0_kind)
    h_scr = nc.dram_tensor("h_scr", [4 * P, HC * 512], BF16)
    bounce = nc.dram_tensor("bounce", [1, TC], F32)
    x1_d = nc.dram_tensor("x1_d", [TC, D], F32, kind="ExternalOutput")
    mloc_d = nc.dram_tensor("mloc_d", [D, D], F32, kind="ExternalOutput")
    xn1fm_d = nc.dram_tensor("xn1fm_d", [D, TC], F32, kind="ExternalOutput")
    dbg = None
    if upto in ("A", "P", "B", "F"):
        shp = {"A": [D, TC], "P": [D + P, DK], "B": [4 * D, TC],
               "F": [D, TC]}[upto]
        dbg = nc.dram_tensor("dbg", shp, F32, kind="ExternalOutput").ap()

    with ExitStack() as es:
        e = es.enter_context
        ldsy = SemCtr(es, nc, "ldsy")
        ldgp = SemCtr(es, nc, "ldgp")
        stsy = SemCtr(es, nc, "stsy")
        pe = SemCtr(es, nc, "pe")
        dv = SemCtr(es, nc, "dv")
        ac = SemCtr(es, nc, "ac")
        gp = SemCtr(es, nc, "gp")
        stgp = SemCtr(es, nc, "stgp")

        ones_bf = e(nc.sbuf_tensor("ones_bf", [P, DK], BF16))
        ident = e(nc.sbuf_tensor("ident", [P, DK], BF16))
        mask = e(nc.sbuf_tensor("mask", [P, DK], BF16))
        zeros_f = e(nc.sbuf_tensor("zeros_f", [P, DK], F32))
        ones_col_bf = e(nc.sbuf_tensor("ones_col_bf", [P, 1], BF16))
        ones_row_f = e(nc.sbuf_tensor("ones_row_f", [1, P], F32))
        ones_row_bf = e(nc.sbuf_tensor("ones_row_bf", [1, P], BF16))
        scale_col = e(nc.sbuf_tensor("scale_col", [P, 1], F32))
        eps_col = e(nc.sbuf_tensor("eps_col", [P, 1], F32))
        eps_row = e(nc.sbuf_tensor("eps_row", [1, 1], F32))
        onecol_f = e(nc.sbuf_tensor("onecol_f", [P, 1], F32))
        seven = e(nc.sbuf_tensor("seven", [P, 1], U32))
        zrow_bf = e(nc.sbuf_tensor("zrow_bf", [1, 512], BF16))
        pmask_sb = e(nc.sbuf_tensor("pmask_sb", [P, 4], F32))

        blk0 = nc.Block("s0"); blk0.__enter__()

        @blk0.gpsimd
        def _(g):
            gp.inc(g.memset(ones_bf[:], 1.0))
            gp.inc(g.memset(zeros_f[:], 0.0))
            gp.inc(g.memset(ones_col_bf[:], 1.0))
            gp.inc(g.memset(ones_row_f[:], 1.0))
            gp.inc(g.memset(ones_row_bf[:], 1.0))
            gp.inc(g.memset(scale_col[:], SCALE))
            gp.inc(g.memset(eps_col[:], 1e-5))
            gp.inc(g.memset(eps_row[:], 1e-5))
            gp.inc(g.memset(onecol_f[:], 1.0))
            gp.inc(g.memset(seven[:], 7))
            gp.inc(g.memset(zrow_bf[:], 0.0))
            g.wait_ge(gp.h, 1)
            gp.inc(g.affine_select(ident[:], ones_bf[:], [[-1, DK]], ALU.is_equal,
                                   0.0, base=0, channel_multiplier=1))
            gp.inc(g.affine_select(mask[:], ones_bf[:], [[1, DK]], ALU.is_ge,
                                   0.0, base=0, channel_multiplier=-1))
            ldgp.inc(g.dma_start(out=pmask_sb[:], in_=pmask_in), 16)
        blk0.__exit__(None, None, None)
        GP_CONST = gp.n
        LD_PMASK = ldgp.n

        esXn = ExitStack()
        esQ = ExitStack()
        esActs = ExitStack()
        xnfm = esXn.enter_context(nc.sbuf_tensor("xnfm", [P, DC * TC], BF16))

        def close_stacks():
            esActs.close()
            esQ.close()
            esXn.close()

        marks = {}

        with ExitStack() as esA:
            eA = esA.enter_context
            x_sb = eA(nc.sbuf_tensor("x_sb", [P, NT * D], F32))
            o_sb = eA(nc.sbuf_tensor("o_sb", [P, NT * D], F32))
            s1 = eA(nc.sbuf_tensor("s1", [P, NT], F32))
            s2 = eA(nc.sbuf_tensor("s2", [P, NT], F32))
            mu = eA(nc.sbuf_tensor("mu", [P, NT], F32))
            var = eA(nc.sbuf_tensor("var", [P, NT], F32))
            musq = eA(nc.sbuf_tensor("musq", [P, NT], F32))
            std = eA(nc.sbuf_tensor("std", [P, NT], F32))
            rstd = eA(nc.sbuf_tensor("rstd", [P, NT], F32))
            scr = eA(nc.sbuf_tensor("scr", [P, D], F32))
            xbf = eA(nc.sbuf_tensor("xbf", [P, 2 * D], BF16))
            psT = [eA(nc.psum_tensor(f"psT{i}", [P, D], F32)) for i in range(2)]

            b = {s: s.n for s in (ldsy, pe, dv, ac, gp)}

            x_t = xin.rearrange("(t p) d -> p t d", p=P)
            x_sb3 = x_sb[:].rearrange("p (t d) -> p t d", t=NT)

            blkA = nc.Block("sA"); blkA.__enter__()

            ln_norm = []
            a2_cast = []
            a2_mm = []
            a2_evac = []

            def a2_mm_pred(t):
                return b[pe] + t + 1

            def a2_cast_pred(t):
                return b[ac] + NT + 1 + t + 1

            def a2_evac_pred(t):
                return b[dv] + 2 * NT + 4 + t + 1

            @blkA.sync
            def _(sy):
                ldsy.inc(sy.dma_start(out=x_sb3, in_=x_t), 16)
            LD_X = ldsy.n

            @blkA.vector
            def _(v):
                v.wait_ge(ldsy.h, LD_X)
                for t in range(NT):
                    dv.inc(v.reduce_sum(s1[:, t:t + 1], x_sb[:, t * D:(t + 1) * D], axis=AX.X))
                v.wait_ge(dv.h, b[dv] + NT)
                dv.inc(v.tensor_scalar(mu[:], s1[:], 1.0 / D, None, ALU.mult))
                v.wait_ge(dv.h, b[dv] + NT + 1)
                dv.inc(v.tensor_mul(musq[:], mu[:], mu[:]))
                v.wait_ge(ac.h, b[ac] + NT)
                v.wait_ge(dv.h, b[dv] + NT + 2)
                dv.inc(v.scalar_tensor_tensor(var[:], s2[:], 1.0 / D, musq[:],
                                              ALU.mult, ALU.subtract))
                v.wait_ge(ac.h, b[ac] + NT + 1)
                dv.inc(v.reciprocal(rstd[:], std[:]))
                v.wait_ge(dv.h, b[dv] + NT + 4)
                for t in range(NT):
                    ln_norm.append(dv.inc(v.tensor_scalar(
                        o_sb[:, t * D:(t + 1) * D], x_sb[:, t * D:(t + 1) * D],
                        mu[:, t:t + 1], rstd[:, t:t + 1], ALU.subtract, ALU.mult)))
                for t in range(NT):
                    v.wait_ge(pe.h, a2_mm_pred(t))
                    for c in range(DC):
                        inst = v.tensor_copy(
                            xnfm[:, c * TC + t * P:c * TC + (t + 1) * P],
                            psT[t % 2][:, c * P:(c + 1) * P])
                    a2_evac.append(dv.inc(inst))

            @blkA.scalar
            def _(s):
                s.wait_ge(ldsy.h, LD_X)
                for t in range(NT):
                    ac.inc(s.activation(scr[:], x_sb[:, t * D:(t + 1) * D], AF.Square,
                                        accum_out=s2[:, t:t + 1]))
                s.wait_ge(dv.h, b[dv] + NT + 3)
                s.wait_ge(gp.h, GP_CONST)
                ac.inc(s.activation(std[:], var[:], AF.Sqrt, bias=eps_col[:]))
                for t in range(NT):
                    s.wait_ge(dv.h, ln_norm[t])
                    if t >= 2:
                        s.wait_ge(pe.h, a2_mm_pred(t - 2))
                    a2_cast.append(ac.inc(s.activation(
                        xbf[:, (t % 2) * D:(t % 2 + 1) * D],
                        o_sb[:, t * D:(t + 1) * D], AF.Copy)))

            @blkA.tensor
            def _(t_):
                t_.wait_ge(gp.h, GP_CONST)
                for t in range(NT):
                    t_.wait_ge(ac.h, a2_cast_pred(t))
                    if t >= 2:
                        t_.wait_ge(dv.h, a2_evac_pred(t - 2))
                    for c in range(DC):
                        inst = t_.matmul(psT[t % 2][:, c * P:(c + 1) * P],
                                         xbf[:, (t % 2) * D + c * P:(t % 2) * D + (c + 1) * P],
                                         ident[:], start=True, stop=True)
                    a2_mm.append(pe.inc(inst))

            blkA.__exit__(None, None, None)
            marks["xnfm_dv"] = dv.n

            if upto == "A":
                blkD = nc.Block("dbgA"); blkD.__enter__()

                @blkD.gpsimd
                def _(g):
                    g.wait_ge(dv.h, marks["xnfm_dv"])
                    stsy.inc(g.dma_start(
                        out=dbg.rearrange("(c p) t -> p c t", p=P),
                        in_=xnfm[:].rearrange("p (c t) -> p c t", c=DC)), 16)
                    g.wait_ge(stsy.h, stsy.n)
                blkD.__exit__(None, None, None)

        if upto == "A":
            close_stacks()
            return nc

        base_acc = esQ.enter_context(nc.sbuf_tensor("base_acc", [P, NHEAD * DK], F32))
        zb_acc = esQ.enter_context(nc.sbuf_tensor("zb_acc", [P, NHEAD], F32))

        with ExitStack() as esPre:
            eP = esPre.enter_context
            wkp_sb = eP(nc.sbuf_tensor("wkp_sb", [P, DC * D], BF16))
            wvp_sb = eP(nc.sbuf_tensor("wvp_sb", [P, DC * D], BF16))
            xpr = eP(nc.sbuf_tensor("xpr", [P, 2 * D], F32))
            xbfp = eP(nc.sbuf_tensor("xbfp", [P, 2 * D], BF16))
            xfmp = eP(nc.sbuf_tensor("xfmp", [P, 2 * D], BF16))
            ap_sb = eP(nc.sbuf_tensor("ap_sb", [P, 2 * D], BF16))
            vp_sb = eP(nc.sbuf_tensor("vp_sb", [P, 2 * D], BF16))
            scrp = eP(nc.sbuf_tensor("scrp", [P, D], F32))
            s1p = eP(nc.sbuf_tensor("s1p", [P, 2], F32))
            s2p = eP(nc.sbuf_tensor("s2p", [P, 2], F32))
            mup = eP(nc.sbuf_tensor("mup", [P, 2], F32))
            musqp = eP(nc.sbuf_tensor("musqp", [P, 2], F32))
            varp = eP(nc.sbuf_tensor("varp", [P, 2], F32))
            stdp = eP(nc.sbuf_tensor("stdp", [P, 2], F32))
            rstdp = eP(nc.sbuf_tensor("rstdp", [P, 2], F32))
            psTp = eP(nc.psum_tensor("psTp", [P, D], F32))
            psPrA = eP(nc.psum_tensor("psPrA", [P, 512], F32))
            psPrV = eP(nc.psum_tensor("psPrV", [P, 512], F32))
            psPacc = [eP(nc.psum_tensor(f"psPacc{i}", [P, 512], F32)) for i in range(2)]
            psZacc = eP(nc.psum_tensor("psZacc", [P, NHEAD], F32))

            NTP = 48
            bP = {s: s.n for s in (ldsy, ldgp, pe, dv, ac, gp)}

            def adv(i):
                return bP[dv] + 9 * i + 3 * (i // 16)

            def aac(i):
                return bP[ac] + 4 * i

            def ape(i):
                return bP[pe] + 6 * i

            def ald(i):
                return bP[ldsy] + 16 * (i + 1)

            blkP = nc.Block("sP"); blkP.__enter__()

            @blkP.gpsimd
            def _(g):
                g.wait_ge(dv.h, marks["xnfm_dv"])
                ldgp.inc(g.dma_start(out=wkp_sb[:].rearrange("p (c o) -> p c o", c=DC),
                                     in_=wkT.rearrange("(c p) o -> p c o", p=P)), 16)
                ldgp.inc(g.dma_start(out=wvp_sb[:].rearrange("p (c o) -> p c o", c=DC),
                                     in_=wvT.rearrange("(c p) o -> p c o", p=P)), 16)
                gp.inc(g.memset(base_acc[:], 0.0))
                gp.inc(g.memset(zb_acc[:], 0.0))
            LD_PREW = ldgp.n
            GP_PRE = gp.n

            @blkP.sync
            def _(sy):
                sy.wait_ge(dv.h, marks["xnfm_dv"])
                sy.wait_ge(ac.h, bP[ac])
                sy.wait_ge(pe.h, bP[pe])
                for i in range(NTP):
                    r, tt = divmod(i, 16)
                    par = i % 2
                    if i >= 2:
                        sy.wait_ge(ac.h, aac(i - 2) + 1)
                        sy.wait_ge(dv.h, adv(i - 2) + 6)
                    ldsy.inc(sy.dma_start(
                        out=xpr[:, par * D:(par + 1) * D],
                        in_=xprev[r * TC + tt * P:r * TC + (tt + 1) * P, :]), 16)

            @blkP.vector
            def _(v):
                for i in range(NTP):
                    r, tt = divmod(i, 16)
                    par = i % 2
                    pc = par
                    v.wait_ge(ldsy.h, ald(i))
                    dv.inc(v.reduce_sum(s1p[:, pc:pc + 1], xpr[:, par * D:(par + 1) * D], axis=AX.X))
                    v.wait_ge(dv.h, adv(i) + 1)
                    dv.inc(v.tensor_scalar(mup[:, pc:pc + 1], s1p[:, pc:pc + 1], 1.0 / D, None, ALU.mult))
                    v.wait_ge(dv.h, adv(i) + 2)
                    dv.inc(v.tensor_mul(musqp[:, pc:pc + 1], mup[:, pc:pc + 1], mup[:, pc:pc + 1]))
                    v.wait_ge(ac.h, aac(i) + 1)
                    v.wait_ge(dv.h, adv(i) + 3)
                    dv.inc(v.scalar_tensor_tensor(varp[:, pc:pc + 1], s2p[:, pc:pc + 1], 1.0 / D,
                                                  musqp[:, pc:pc + 1], ALU.mult, ALU.subtract))
                    v.wait_ge(ac.h, aac(i) + 2)
                    dv.inc(v.reciprocal(rstdp[:, pc:pc + 1], stdp[:, pc:pc + 1]))
                    v.wait_ge(dv.h, adv(i) + 5)
                    if i >= 2:
                        v.wait_ge(pe.h, ape(i - 2) + 1)
                    dv.inc(v.tensor_scalar(xbfp[:, par * D:(par + 1) * D],
                                           xpr[:, par * D:(par + 1) * D],
                                           mup[:, pc:pc + 1], rstdp[:, pc:pc + 1],
                                           ALU.subtract, ALU.mult))
                    v.wait_ge(pe.h, ape(i) + 1)
                    dv.inc(v.tensor_copy(xfmp[:, par * D:(par + 1) * D], psTp[:, 0:D]))
                    v.wait_ge(pe.h, ape(i) + 4)
                    dv.inc(v.tensor_copy(vp_sb[:, par * D:par * D + 512], psPrV[:, 0:512]))
                    v.wait_ge(pe.h, ape(i) + 5)
                    dv.inc(v.tensor_copy(vp_sb[:, par * D + 512:(par + 1) * D], psPrV[:, 0:512]))
                    if tt == 15:
                        v.wait_ge(pe.h, ape(i) + 6)
                        v.wait_ge(ldgp.h, LD_PMASK)
                        if r == 0:
                            v.wait_ge(gp.h, GP_PRE)
                        v.wait_ge(dv.h, adv(i) + 9)
                        dv.inc(v.scalar_tensor_tensor(base_acc[:, 0:512], psPacc[0][:, 0:512],
                                                      pmask_sb[:, r:r + 1], base_acc[:, 0:512],
                                                      ALU.mult, ALU.add))
                        dv.inc(v.scalar_tensor_tensor(base_acc[:, 512:1024], psPacc[1][:, 0:512],
                                                      pmask_sb[:, r:r + 1], base_acc[:, 512:1024],
                                                      ALU.mult, ALU.add))
                        dv.inc(v.scalar_tensor_tensor(zb_acc[:], psZacc[:, 0:NHEAD],
                                                      pmask_sb[:, r:r + 1], zb_acc[:],
                                                      ALU.mult, ALU.add))

            @blkP.scalar
            def _(s):
                s.wait_ge(gp.h, GP_CONST)
                s.wait_ge(dv.h, marks["xnfm_dv"])
                s.wait_ge(pe.h, bP[pe])
                for i in range(NTP):
                    par = i % 2
                    pc = par
                    s.wait_ge(ldsy.h, ald(i))
                    if i >= 2:
                        s.wait_ge(dv.h, adv(i - 2) + 4)
                    ac.inc(s.activation(scrp[:], xpr[:, par * D:(par + 1) * D], AF.Square,
                                        accum_out=s2p[:, pc:pc + 1]))
                    s.wait_ge(dv.h, adv(i) + 4)
                    ac.inc(s.activation(stdp[:, pc:pc + 1], varp[:, pc:pc + 1], AF.Sqrt,
                                        bias=eps_col[:]))
                    s.wait_ge(pe.h, ape(i) + 2)
                    ac.inc(s.activation(ap_sb[:, par * D:par * D + 512], psPrA[:, 0:512], AF.Exp))
                    s.wait_ge(pe.h, ape(i) + 3)
                    ac.inc(s.activation(ap_sb[:, par * D + 512:(par + 1) * D], psPrA[:, 0:512], AF.Exp))

            @blkP.tensor
            def _(t_):
                t_.wait_ge(gp.h, GP_CONST)
                t_.wait_ge(ldgp.h, LD_PREW)
                t_.wait_ge(dv.h, marks["xnfm_dv"])
                for i in range(NTP):
                    r, tt = divmod(i, 16)
                    par = i % 2
                    t_.wait_ge(dv.h, adv(i) + 6)
                    if i >= 1:
                        t_.wait_ge(dv.h, adv(i - 1) + 7)
                    for c in range(DC):
                        inst = t_.matmul(psTp[:, c * P:(c + 1) * P],
                                         xbfp[:, par * D + c * P:par * D + (c + 1) * P],
                                         ident[:], start=True, stop=True)
                    pe.inc(inst)
                    t_.wait_ge(dv.h, adv(i) + 7)
                    if i >= 1:
                        t_.wait_ge(ac.h, aac(i - 1) + 4)
                    for c in range(DC):
                        inst = t_.matmul(psPrA[:, 0:512],
                                         xfmp[:, par * D + c * P:par * D + (c + 1) * P],
                                         wkp_sb[:, c * D:c * D + 512],
                                         start=(c == 0), stop=(c == DC - 1))
                    pe.inc(inst)
                    t_.wait_ge(ac.h, aac(i) + 3)
                    for c in range(DC):
                        inst = t_.matmul(psPrA[:, 0:512],
                                         xfmp[:, par * D + c * P:par * D + (c + 1) * P],
                                         wkp_sb[:, c * D + 512:(c + 1) * D],
                                         start=(c == 0), stop=(c == DC - 1))
                    pe.inc(inst)
                    if i >= 1:
                        t_.wait_ge(dv.h, adv(i - 1) + 9)
                    for c in range(DC):
                        inst = t_.matmul(psPrV[:, 0:512],
                                         xfmp[:, par * D + c * P:par * D + (c + 1) * P],
                                         wvp_sb[:, c * D:c * D + 512],
                                         start=(c == 0), stop=(c == DC - 1))
                    pe.inc(inst)
                    t_.wait_ge(dv.h, adv(i) + 8)
                    for c in range(DC):
                        inst = t_.matmul(psPrV[:, 0:512],
                                         xfmp[:, par * D + c * P:par * D + (c + 1) * P],
                                         wvp_sb[:, c * D + 512:(c + 1) * D],
                                         start=(c == 0), stop=(c == DC - 1))
                    pe.inc(inst)
                    t_.wait_ge(ac.h, aac(i) + 4)
                    t_.wait_ge(dv.h, adv(i) + 9)
                    if tt == 0:
                        if r >= 1:
                            t_.wait_ge(dv.h, adv(i))
                        t_.matmul(psPacc[0][:, 0:512], zrow_bf[0:1, 0:P],
                                  zrow_bf[0:1, 0:512], start=True, stop=False)
                        t_.matmul(psPacc[1][:, 0:512], zrow_bf[0:1, 0:P],
                                  zrow_bf[0:1, 0:512], start=True, stop=False)
                        t_.matmul(psZacc[:, 0:NHEAD], zrow_bf[0:1, 0:P],
                                  zrow_bf[0:1, 0:NHEAD], start=True, stop=False)
                    for h in range(NHEAD):
                        t_.matmul(psPacc[h // 4][:, (h % 4) * DK:(h % 4 + 1) * DK],
                                  ap_sb[:, par * D + h * DK:par * D + (h + 1) * DK],
                                  vp_sb[:, par * D + h * DK:par * D + (h + 1) * DK],
                                  start=False, stop=(tt == 15 and h % 4 == 3))
                    for h in range(NHEAD):
                        inst = t_.matmul(psZacc[:, h:h + 1],
                                         ap_sb[:, par * D + h * DK:par * D + (h + 1) * DK],
                                         ones_col_bf[:],
                                         start=False, stop=(tt == 15 and h == 7))
                    pe.inc(inst)
            blkP.__exit__(None, None, None)
            marks["pre_dv"] = dv.n
            marks["pre_ac"] = ac.n
            marks["pre_pe"] = pe.n

            if upto == "P":
                blkD = nc.Block("dbgP"); blkD.__enter__()

                @blkD.sync
                def _(sy):
                    sy.wait_ge(dv.h, marks["pre_dv"])
                    stsy.inc(sy.dma_start(out=dbg[0:D, :].rearrange("(h k) v -> k h v", k=P),
                                          in_=base_acc[:].rearrange("k (h v) -> k h v", h=NHEAD)), 16)
                    stsy.inc(sy.dma_start(out=dbg[D:D + P, 0:NHEAD], in_=zb_acc[:]), 16)
                    sy.wait_ge(stsy.h, stsy.n)
                blkD.__exit__(None, None, None)

        if upto == "P":
            close_stacks()
            return nc

        q_sb = esQ.enter_context(nc.sbuf_tensor("q_sb", [P, DC * TC], BF16))
        afm_sb = esActs.enter_context(nc.sbuf_tensor("afm_sb", [P, DC * TC], BF16))
        atm_sb = esActs.enter_context(nc.sbuf_tensor("atm_sb", [P, NT * D], BF16))
        vtm_sb = esActs.enter_context(nc.sbuf_tensor("vtm_sb", [P, NT * D], BF16))

        with ExitStack() as esB:
            eB = esB.enter_context
            wq_sb = eB(nc.sbuf_tensor("wq_sb", [P, DC * D], BF16))
            wk_sb = eB(nc.sbuf_tensor("wk_sb", [P, DC * D], BF16))
            psB = [eB(nc.psum_tensor(f"psB{i}", [P, 512], F32)) for i in range(4)]

            b = {s: s.n for s in (ldgp, pe, dv, ac)}
            blkB1 = nc.Block("sB1"); blkB1.__enter__()

            @blkB1.gpsimd
            def _(g):
                g.wait_ge(dv.h, marks["pre_dv"])
                g.wait_ge(ac.h, marks["pre_ac"])
                g.wait_ge(pe.h, marks["pre_pe"])
                ldgp.inc(g.dma_start(out=wq_sb[:].rearrange("p (c o) -> p c o", c=DC),
                                     in_=wqT.rearrange("(c p) o -> p c o", p=P)), 16)
                ldgp.inc(g.dma_start(out=wk_sb[:].rearrange("p (c o) -> p c o", c=DC),
                                     in_=wkT.rearrange("(c p) o -> p c o", p=P)), 16)
            LD_B1 = ldgp.n

            tasksB1 = ([("q", o, t) for o in range(DC) for t in range(4)]
                       + [("a", o, t) for o in range(DC) for t in range(4)]
                       + [("at", tt, hf) for tt in range(NT) for hf in range(2)])

            @blkB1.tensor
            def _(t_):
                t_.wait_ge(ldgp.h, LD_B1)
                t_.wait_ge(dv.h, marks["xnfm_dv"])
                for i, task in enumerate(tasksB1):
                    pst = psB[i % 4]
                    if i >= 4:
                        t_.wait_ge(ac.h, b[ac] + i - 3)
                    kind = task[0]
                    if kind in ("q", "a"):
                        _, o, tt = task
                        w = wq_sb if kind == "q" else wk_sb
                        for ccx in range(DC):
                            inst = t_.matmul(pst[:, 0:512],
                                             w[:, ccx * D + o * P:ccx * D + (o + 1) * P],
                                             xnfm[:, ccx * TC + tt * 512:ccx * TC + (tt + 1) * 512],
                                             start=(ccx == 0), stop=(ccx == DC - 1))
                    else:
                        _, tt, hf = task
                        for ccx in range(DC):
                            inst = t_.matmul(pst[:, 0:512],
                                             xnfm[:, ccx * TC + tt * P:ccx * TC + (tt + 1) * P],
                                             wk_sb[:, ccx * D + hf * 512:ccx * D + (hf + 1) * 512],
                                             start=(ccx == 0), stop=(ccx == DC - 1))
                    pe.inc(inst)

            @blkB1.scalar
            def _(s):
                for i, task in enumerate(tasksB1):
                    pst = psB[i % 4]
                    s.wait_ge(pe.h, b[pe] + i + 1)
                    kind = task[0]
                    if kind in ("q", "a"):
                        _, o, tt = task
                        buf = q_sb if kind == "q" else afm_sb
                        fn = AF.Silu if kind == "q" else AF.Exp
                        ac.inc(s.activation(buf[:, o * TC + tt * 512:o * TC + (tt + 1) * 512],
                                            pst[:, 0:512], fn))
                    else:
                        _, tt, hf = task
                        ac.inc(s.activation(atm_sb[:, tt * D + hf * 512:tt * D + (hf + 1) * 512],
                                            pst[:, 0:512], AF.Exp))
            blkB1.__exit__(None, None, None)
            marks["b1_ac"] = ac.n

        with ExitStack() as esB2:
            eB = esB2.enter_context
            wv_sb = eB(nc.sbuf_tensor("wv_sb", [P, DC * D], BF16))
            wg1_sb = eB(nc.sbuf_tensor("wg1_sb", [P, DC * 128], BF16))
            wg2_sb = eB(nc.sbuf_tensor("wg2_sb", [P, D], BF16))
            g1_sb = eB(nc.sbuf_tensor("g1_sb", [P, TC], BF16))
            sgr_sb = eB(nc.sbuf_tensor("sgr_sb", [P, 2 * TC], BF16))
            psB = [eB(nc.psum_tensor(f"psB2{i}", [P, 512], F32)) for i in range(4)]

            b = {s: s.n for s in (ldgp, pe, dv, ac, stsy)}
            blkB2 = nc.Block("sB2"); blkB2.__enter__()

            @blkB2.gpsimd
            def _(g):
                g.wait_ge(ac.h, marks["b1_ac"])
                ldgp.inc(g.dma_start(out=wv_sb[:].rearrange("p (c o) -> p c o", c=DC),
                                     in_=wvT.rearrange("(c p) o -> p c o", p=P)), 16)
                ldgp.inc(g.dma_start(out=wg1_sb[:].rearrange("p (c o) -> p c o", c=DC),
                                     in_=wg1T.rearrange("(c p) o -> p c o", p=P)), 16)
                ldgp.inc(g.dma_start(out=wg2_sb[:], in_=wg2T), 16)
            LD_B2 = ldgp.n

            tasksB2 = ([("v", tt, hf) for tt in range(NT) for hf in range(2)]
                       + [("g1", t) for t in range(4)]
                       + [("sg", o, t) for o in range(DC) for t in range(4)])
            evac_no = []
            ndv = nac = 0
            g1_done_dv = None
            for k in tasksB2:
                if k[0] == "sg":
                    nac += 1
                    evac_no.append(("ac", b[ac] + nac))
                else:
                    ndv += 1
                    evac_no.append(("dv", b[dv] + ndv))
                    if k[0] == "g1" and k[1] == 3:
                        g1_done_dv = b[dv] + ndv

            def sg_dma_no(o):
                return b[stsy] + 16 * (o + 1)

            @blkB2.tensor
            def _(t_):
                t_.wait_ge(ldgp.h, LD_B2)
                for i, task in enumerate(tasksB2):
                    pst = psB[i % 4]
                    if i >= 4:
                        eng, cval = evac_no[i - 4]
                        t_.wait_ge(dv.h if eng == "dv" else ac.h, cval)
                    kind = task[0]
                    if kind == "v":
                        _, tt, hf = task
                        for ccx in range(DC):
                            inst = t_.matmul(pst[:, 0:512],
                                             xnfm[:, ccx * TC + tt * P:ccx * TC + (tt + 1) * P],
                                             wv_sb[:, ccx * D + hf * 512:ccx * D + (hf + 1) * 512],
                                             start=(ccx == 0), stop=(ccx == DC - 1))
                        pe.inc(inst)
                    elif kind == "g1":
                        _, tt = task
                        for ccx in range(DC):
                            inst = t_.matmul(pst[:, 0:512],
                                             wg1_sb[:, ccx * 128:(ccx + 1) * 128],
                                             xnfm[:, ccx * TC + tt * 512:ccx * TC + (tt + 1) * 512],
                                             start=(ccx == 0), stop=(ccx == DC - 1))
                        pe.inc(inst)
                    else:
                        _, o, tt = task
                        t_.wait_ge(dv.h, g1_done_dv)
                        pe.inc(t_.matmul(pst[:, 0:512],
                                         wg2_sb[:, o * P:(o + 1) * P],
                                         g1_sb[:, tt * 512:(tt + 1) * 512],
                                         start=True, stop=True))

            @blkB2.vector
            def _(v):
                for i, task in enumerate(tasksB2):
                    if task[0] == "sg":
                        continue
                    pst = psB[i % 4]
                    v.wait_ge(pe.h, b[pe] + i + 1)
                    if task[0] == "v":
                        _, tt, hf = task
                        dv.inc(v.tensor_copy(vtm_sb[:, tt * D + hf * 512:tt * D + (hf + 1) * 512],
                                             pst[:, 0:512]))
                    else:
                        _, tt = task
                        dv.inc(v.tensor_copy(g1_sb[:, tt * 512:(tt + 1) * 512], pst[:, 0:512]))

            @blkB2.scalar
            def _(s):
                for i, task in enumerate(tasksB2):
                    if task[0] != "sg":
                        continue
                    pst = psB[i % 4]
                    _, o, tt = task
                    if tt == 0 and o >= 2:
                        s.wait_ge(stsy.h, sg_dma_no(o - 2))
                    s.wait_ge(pe.h, b[pe] + i + 1)
                    ac.inc(s.activation(sgr_sb[:, (o % 2) * TC + tt * 512:(o % 2) * TC + (tt + 1) * 512],
                                        pst[:, 0:512], AF.Silu))

            @blkB2.sync
            def _(sy):
                sg3 = sgd.ap().rearrange("(c p) t -> p c t", p=P)
                for o in range(DC):
                    sy.wait_ge(ac.h, b[ac] + 4 * (o + 1))
                    stsy.inc(sy.dma_start(out=sg3[:, o:o + 1, :],
                                          in_=sgr_sb[:, (o % 2) * TC:(o % 2 + 1) * TC].unsqueeze(1)), 16)
            blkB2.__exit__(None, None, None)
            marks["vtm_dv"] = dv.n
            marks["sg_st"] = stsy.n

            if upto == "B":
                blkD = nc.Block("dbgB"); blkD.__enter__()
                d4 = dbg.rearrange("(r c p) t -> r p c t", r=4, p=P)

                @blkD.gpsimd
                def _(g):
                    g.wait_ge(ac.h, marks["b1_ac"])
                    g.wait_ge(dv.h, marks["vtm_dv"])
                    stsy.inc(g.dma_start(out=d4[0], in_=q_sb[:].rearrange("p (c t) -> p c t", c=DC)), 16)
                    stsy.inc(g.dma_start(out=d4[1], in_=afm_sb[:].rearrange("p (c t) -> p c t", c=DC)), 16)
                    stsy.inc(g.dma_start(out=dbg.rearrange("(r t p) d -> r p t d", r=4, p=P)[2],
                                         in_=atm_sb[:].rearrange("p (t d) -> p t d", t=NT)), 16)
                    stsy.inc(g.dma_start(out=dbg.rearrange("(r t p) d -> r p t d", r=4, p=P)[3],
                                         in_=vtm_sb[:].rearrange("p (t d) -> p t d", t=NT)), 16)
                    g.wait_ge(stsy.h, stsy.n)
                blkD.__exit__(None, None, None)

        if upto == "B":
            close_stacks()
            return nc


        o_sb = q_sb
        eAct = esActs.enter_context
        P_sb = eAct(nc.sbuf_tensor("P_sb", [P, NHEAD * DK], BF16))
        zbase = zb_acc
        zc_sb = eAct(nc.sbuf_tensor("zc_sb", [P, 2 * NHEAD * DK], F32))
        rz_sb = eAct(nc.sbuf_tensor("rz_sb", [P, 2 * DK], F32))
        qt_sb = eAct(nc.sbuf_tensor("qt_sb", [P, 2 * DK], BF16))
        at_sb = eAct(nc.sbuf_tensor("at_sb", [P, 2 * DK], BF16))

        NIT = NT * NHEAD

        with ExitStack() as esD:
            eD = esD.enter_context
            psP = [eD(nc.psum_tensor(f"psP{i}", [P, 512], F32)) for i in range(2)]
            psA = [eD(nc.psum_tensor(f"psA{i}", [P, 512], F32)) for i in range(2)]
            psO = [eD(nc.psum_tensor(f"psO{i}", [P, 512], F32)) for i in range(2)]

            b = {s: s.n for s in (ldsy, pe, dv, ac)}
            bD = dict(b)
            blkDm = nc.Block("sD"); blkDm.__enter__()

            DV0 = 1
            PE0 = 2

            def peAT(it):
                return bD[pe] + PE0 + 3 * it + 1

            def peO(it):
                return bD[pe] + PE0 + 3 * it + 2

            def peP(it):
                return bD[pe] + PE0 + 3 * it + 3

            def dvScan(it):
                return bD[dv] + DV0 + 4 * it + 1

            def dvRecip(it):
                return bD[dv] + DV0 + 4 * it + 2

            def dvQt(it):
                return bD[dv] + DV0 + 4 * it + 3

            def dvAtm(it):
                return bD[dv] + DV0 + 4 * it + 4

            def acOEvac(it):
                ci, h = divmod(it, NHEAD)
                return bD[ac] + 16 * ci + h + 1

            def acPEvac(ci, h):
                return bD[ac] + 16 * ci + 8 + h + 1

            @blkDm.vector
            def _(v):
                v.wait_ge(dv.h, marks["pre_dv"])
                dv.inc(v.tensor_copy(P_sb[:], base_acc[:]))
                for it in range(NIT):
                    ci, h = divmod(it, NHEAD)
                    par = it % 2
                    cpar = ci % 2
                    if ci == 0:
                        init = zbase[:, h:h + 1]
                        v.wait_ge(dv.h, bD[dv] + DV0)
                    zc_out = zc_sb[:, cpar * NHEAD * DK + h * DK:cpar * NHEAD * DK + (h + 1) * DK]
                    if ci > 0:
                        pcpar = 1 - cpar
                        init = zc_sb[:, pcpar * NHEAD * DK + (h + 1) * DK - 1:
                                     pcpar * NHEAD * DK + (h + 1) * DK]
                        v.wait_ge(dv.h, dvScan(it - NHEAD))
                    dv.inc(v.tensor_tensor_scan(
                        zc_out, afm_sb[:, h * TC + ci * DK:h * TC + (ci + 1) * DK],
                        zeros_f[:], init, ALU.add, ALU.add))
                    v.wait_ge(dv.h, dvScan(it))
                    dv.inc(v.reciprocal(rz_sb[:, par * DK:(par + 1) * DK], zc_out))
                    v.wait_ge(dv.h, dvRecip(it))
                    if it >= 2:
                        v.wait_ge(pe.h, peO(it - 2))
                    dv.inc(v.tensor_mul(qt_sb[:, par * DK:(par + 1) * DK],
                                        q_sb[:, h * TC + ci * DK:h * TC + (ci + 1) * DK],
                                        rz_sb[:, par * DK:(par + 1) * DK]))
                    v.wait_ge(pe.h, peAT(it))
                    dv.inc(v.tensor_mul(at_sb[:, par * DK:(par + 1) * DK],
                                        psA[par][:, 0:DK], mask[:]))

            @blkDm.tensor
            def _(t_):
                t_.wait_ge(gp.h, GP_CONST)
                t_.wait_ge(dv.h, bD[dv] + 1)
                for bk in range(2):
                    pe.inc(t_.matmul(psP[bk][:, 0:512], ident[:], P_sb[:, bk * 512:(bk + 1) * 512],
                                     start=True, stop=False))
                for it in range(NIT):
                    ci, h = divmod(it, NHEAD)
                    par = it % 2
                    t_.wait_ge(dv.h, dvQt(it))
                    if it >= 2:
                        t_.wait_ge(dv.h, dvAtm(it - 2))
                    pe.inc(t_.matmul(psA[par][:, 0:DK],
                                     afm_sb[:, h * TC + ci * DK:h * TC + (ci + 1) * DK],
                                     qt_sb[:, par * DK:(par + 1) * DK],
                                     start=True, stop=True))
                    t_.wait_ge(dv.h, dvAtm(it))
                    if ci > 0:
                        t_.wait_ge(ac.h, acPEvac(ci - 1, h))
                    if it >= 2:
                        t_.wait_ge(ac.h, acOEvac(it - 2))
                    t_.matmul(psO[par][:, 0:DK],
                              vtm_sb[:, ci * D + h * DK:ci * D + (h + 1) * DK],
                              at_sb[:, par * DK:(par + 1) * DK],
                              start=True, stop=False)
                    pe.inc(t_.matmul(psO[par][:, 0:DK],
                                     P_sb[:, h * DK:(h + 1) * DK],
                                     qt_sb[:, par * DK:(par + 1) * DK],
                                     start=False, stop=True))
                    if ci > 0:
                        t_.wait_ge(ac.h, acPEvac(ci - 1, 7))
                    pe.inc(t_.matmul(psP[h // 4][:, (h % 4) * DK:(h % 4 + 1) * DK],
                                     atm_sb[:, ci * D + h * DK:ci * D + (h + 1) * DK],
                                     vtm_sb[:, ci * D + h * DK:ci * D + (h + 1) * DK],
                                     start=False,
                                     stop=(ci == NT - 1 and h % 4 == 3)))

            @blkDm.scalar
            def _(s):
                s.wait_ge(gp.h, GP_CONST)
                for ci in range(NT):
                    for h in range(NHEAD):
                        it = ci * NHEAD + h
                        par = it % 2
                        s.wait_ge(pe.h, peO(it))
                        ac.inc(s.activation(o_sb[:, h * TC + ci * DK:h * TC + (ci + 1) * DK],
                                            psO[par][:, 0:DK], AF.Copy,
                                            scale=scale_col[:]))
                    s.wait_ge(pe.h, peP(ci * NHEAD + 7))
                    for h in range(NHEAD):
                        ac.inc(s.activation(P_sb[:, h * DK:(h + 1) * DK],
                                            psP[h // 4][:, (h % 4) * DK:(h % 4 + 1) * DK],
                                            AF.Copy))
            blkDm.__exit__(None, None, None)

        esActs.close()

        PE_L = pe.n
        DV_L = dv.n
        AC_L = ac.n

        with ExitStack() as esDf:
            eD = esDf.enter_context
            sg_sb = eD(nc.sbuf_tensor("sg_sb", [P, DC * TC], BF16))
            wo_sb = eD(nc.sbuf_tensor("wo_sb", [P, DC * D], BF16))
            attn_sb = eD(nc.sbuf_tensor("attn_sb", [P, NT * D], F32))
            gn_sb = eD(nc.sbuf_tensor("gn_sb", [P, DC], F32))
            o2_sb = eD(nc.sbuf_tensor("o2_sb", [P, 2 * 512], BF16))
            sgp_sb = eD(nc.sbuf_tensor("sgp_sb", [P, 2 * 512], BF16))
            ms_row = eD(nc.sbuf_tensor("ms_row", [1, TC], F32))
            psS = [eD(nc.psum_tensor(f"psS{i}", [P, 512], F32)) for i in range(2)]
            psR = [eD(nc.psum_tensor(f"psR{i}", [P, 512], F32)) for i in range(2)]
            psW = [eD(nc.psum_tensor(f"psW{i}", [P, 512], F32)) for i in range(2)]

            bF = {s: s.n for s in (ldgp, pe, dv, ac, stsy)}
            blkDf = nc.Block("sDf"); blkDf.__enter__()

            def dvO2(j):
                t, c = divmod(j, 8)
                return bF[dv] + 9 * t + c + 1

            def dvVar(t):
                return bF[dv] + 9 * t + 9

            DV_RSTD = bF[dv] + 36 + 1

            def peStat(j):
                return bF[pe] + j + 1

            def peRep(t):
                return bF[pe] + 32 + t + 1

            def dvSgp(j):
                return DV_RSTD + 2 * j + 1

            def dvOf(j):
                return DV_RSTD + 2 * j + 2

            DV_OF = DV_RSTD + 64

            def peWo(idx):
                return bF[pe] + 36 + idx + 1

            def dvWoEvac(idx):
                return DV_OF + idx + 1

            @blkDf.gpsimd
            def _(g):
                g.wait_ge(pe.h, PE_L)
                g.wait_ge(dv.h, DV_L)
                g.wait_ge(ac.h, AC_L)
                g.wait_ge(stsy.h, marks["sg_st"])
                ldgp.inc(g.dma_start(out=sg_sb[:].rearrange("p (c t) -> p c t", c=DC),
                                     in_=sgd.ap().rearrange("(c p) t -> p c t", p=P)), 16)
                ldgp.inc(g.dma_start(out=wo_sb[:].rearrange("p (c o) -> p c o", c=DC),
                                     in_=wo_in.rearrange("(c p) o -> p c o", p=P)), 16)
                ldgp.inc(g.dma_start(out=gn_sb[:], in_=gn_in), 16)
            LD_DF = ldgp.n

            @blkDf.vector
            def _(v):
                v.wait_ge(ac.h, AC_L)
                for t in range(4):
                    for c in range(8):
                        j = t * 8 + c
                        par = j % 2
                        if j >= 2:
                            v.wait_ge(pe.h, peStat(j - 2))
                        src = o_sb[:, c * TC + t * 512:c * TC + (t + 1) * 512]
                        dv.inc(v.tensor_mul(o2_sb[:, par * 512:(par + 1) * 512], src, src))
                    v.wait_ge(pe.h, peStat(8 * t + 7))
                    dv.inc(v.tensor_scalar(ms_row[0:1, t * 512:(t + 1) * 512],
                                           psS[t % 2][0:1, 0:512], 1.0 / D, 1e-5,
                                           ALU.mult, ALU.add))
                v.wait_ge(ac.h, AC_L + 1)
                dv.inc(v.reciprocal(ms_row[:], ms_row[:]))
                v.wait_ge(ldgp.h, LD_DF)
                for j in range(32):
                    t, c = divmod(j, 8)
                    par = j % 2
                    v.wait_ge(pe.h, peRep(t))
                    dv.inc(v.tensor_mul(sgp_sb[:, par * 512:(par + 1) * 512],
                                        sg_sb[:, c * TC + t * 512:c * TC + (t + 1) * 512],
                                        psR[t % 2][:, 0:512]))
                    v.wait_ge(dv.h, dvSgp(j))
                    dv.inc(v.scalar_tensor_tensor(o_sb[:, c * TC + t * 512:c * TC + (t + 1) * 512],
                                                  o_sb[:, c * TC + t * 512:c * TC + (t + 1) * 512],
                                                  gn_sb[:, c:c + 1],
                                                  sgp_sb[:, par * 512:(par + 1) * 512],
                                                  ALU.mult, ALU.mult))
                for idx in range(32):
                    v.wait_ge(pe.h, peWo(idx))
                    t128, hf = divmod(idx, 2)
                    dv.inc(v.tensor_copy(attn_sb[:, t128 * D + hf * 512:t128 * D + (hf + 1) * 512],
                                         psW[idx % 2][:, 0:512]))

            @blkDf.tensor
            def _(t_):
                for j in range(32):
                    t, c = divmod(j, 8)
                    par = j % 2
                    t_.wait_ge(dv.h, dvO2(j))
                    if t >= 2 and c == 0:
                        t_.wait_ge(dv.h, dvVar(t - 2))
                    pe.inc(t_.matmul(psS[t % 2][0:1, 0:512], ones_col_bf[:],
                                     o2_sb[:, par * 512:(par + 1) * 512],
                                     start=(c == 0), stop=(c == 7)))
                for t in range(4):
                    t_.wait_ge(dv.h, DV_RSTD)
                    if t >= 2:
                        t_.wait_ge(dv.h, dvOf(8 * (t - 2) + 7))
                    pe.inc(t_.matmul(psR[t % 2][:, 0:512], ones_row_f[:],
                                     ms_row[0:1, t * 512:(t + 1) * 512],
                                     start=True, stop=True))
                t_.wait_ge(ldgp.h, LD_DF)
                for idx in range(32):
                    t128, hf = divmod(idx, 2)
                    t512 = t128 // 4
                    t_.wait_ge(dv.h, dvOf(8 * t512 + 7))
                    if idx >= 2:
                        t_.wait_ge(dv.h, dvWoEvac(idx - 2))
                    for c in range(DC):
                        inst = t_.matmul(psW[idx % 2][:, 0:512],
                                         o_sb[:, c * TC + t128 * P:c * TC + (t128 + 1) * P],
                                         wo_sb[:, c * D + hf * 512:c * D + (hf + 1) * 512],
                                         start=(c == 0), stop=(c == DC - 1))
                    pe.inc(inst)

            @blkDf.scalar
            def _(s):
                s.wait_ge(dv.h, dvVar(3))
                ac.inc(s.activation(ms_row[:], ms_row[:], AF.Sqrt))

            @blkDf.sync
            def _(sy):
                sy.wait_ge(dv.h, DV_OF + 32)
                stsy.inc(sy.dma_start(out=attn0_d.ap().rearrange("(t p) d -> p t d", p=P),
                                      in_=attn_sb[:].rearrange("p (t d) -> p t d", t=NT)), 16)
                if upto == "D":
                    sy.wait_ge(stsy.h, stsy.n)
            blkDf.__exit__(None, None, None)
            marks["attn0_st"] = stsy.n

        if upto == "D":
            close_stacks()
            return nc

        esQ.close()
        wfcT3 = wfcT.rearrange("(c p) f -> p c f", p=P)
        wpT3 = wpT.rearrange("(c p) f -> p c f", p=P)

        with ExitStack() as esE:
            eE = esE.enter_context
            h_sb = eE(nc.sbuf_tensor("h_sb", [P, HC * 512], BF16))
            h2_sb = eE(nc.sbuf_tensor("h2_sb", [P, 2 * 512], BF16))
            ccrep = eE(nc.sbuf_tensor("ccrep", [P, D], F32))
            ga = eE(nc.sbuf_tensor("ga", [P, DC], F32))
            gp32 = eE(nc.sbuf_tensor("gp32", [P, HC], F32))
            gb = eE(nc.sbuf_tensor("gb", [P, 1], F32))
            thr_row = eE(nc.sbuf_tensor("thr_row", [1, 1], F32))
            thr_col = eE(nc.sbuf_tensor("thr_col", [P, 1], F32))
            nthr_col = eE(nc.sbuf_tensor("nthr_col", [P, 1], F32))
            negmu_row = eE(nc.sbuf_tensor("negmu_row", [1, TC], F32))
            rstd_row = eE(nc.sbuf_tensor("rstd_rowE", [1, TC], F32))
            musq_s = eE(nc.sbuf_tensor("musq_s", [1, 512], F32))
            var_s = eE(nc.sbuf_tensor("var_s", [1, 512], F32))
            std_s = eE(nc.sbuf_tensor("std_s", [1, 512], F32))
            rstd_cols = eE(nc.sbuf_tensor("rstd_cols", [P, NT], F32))
            lnw_sb = eE(nc.sbuf_tensor("lnw_sb", [P, HC], F32))
            lnb_bf = eE(nc.sbuf_tensor("lnb_bf", [P, HC], BF16))
            c_row = eE(nc.sbuf_tensor("c_row", [1, D], F32))
            cc_row = eE(nc.sbuf_tensor("cc_row", [1, D], F32))
            psMisc = eE(nc.psum_tensor("psMisc", [P, 512], F32))
            psS1 = eE(nc.psum_tensor("psES1", [P, 512], F32))
            psS2 = eE(nc.psum_tensor("psES2", [P, 512], F32))
            psCC = eE(nc.psum_tensor("psCC", [P, 512], F32))
            psCr = eE(nc.psum_tensor("psCr", [P, 512], F32))
            psREP = eE(nc.psum_tensor("psREP", [P, 512], F32))

            bE = {s: s.n for s in (ldsy, ldgp, pe, dv, ac, gp, stsy)}
            E_dv, E_pe, E_ac, E_ld, E_ld2, E_st = (bE[dv], bE[pe], bE[ac],
                                                   bE[ldsy], bE[ldgp], bE[stsy])

            with ExitStack() as es_a:
                e1 = es_a.enter_context
                wbuf = e1(nc.sbuf_tensor("wbuf", [P, H4], F32))
                t1b = e1(nc.sbuf_tensor("t1b", [P, H4], BF16))
                t2b = e1(nc.sbuf_tensor("t2b", [P, H4], BF16))
                wfcq = e1(nc.sbuf_tensor("wfcq", [P, DC * H4], BF16))
                psF = [e1(nc.psum_tensor(f"psF{i}", [P, 512], F32)) for i in range(2)]

                blk = nc.Block("e1"); blk.__enter__()

                @blk.gpsimd
                def _(g):
                    g.wait_ge(pe.h, E_pe)
                    g.wait_ge(dv.h, E_dv)
                    g.wait_ge(ac.h, E_ac)
                    g.wait_ge(stsy.h, marks["attn0_st"])
                    for c in range(DC):
                        if c >= 1:
                            g.wait_ge(dv.h, E_dv + c)
                        ldgp.inc(g.dma_start(out=wbuf[:], in_=wfcT3[:, c:c + 1, :]), 16)
                    ldgp.inc(g.dma_start(out=lnw_sb[:], in_=lnw_in), 16)
                    ldgp.inc(g.dma_start(out=lnb_bf[:], in_=lnb_in), 16)
                    for c in range(DC):
                        if c == 0:
                            g.wait_ge(dv.h, E_dv + 8)
                        else:
                            g.wait_ge(dv.h, E_dv + 14 + 3 * (c - 1))
                        ldgp.inc(g.dma_start(out=wbuf[:], in_=wfcT3[:, c:c + 1, :]), 16)

                @blk.vector
                def _(v):
                    for c in range(DC):
                        v.wait_ge(ldgp.h, E_ld2 + 16 * (c + 1))
                        dv.inc(v.reduce_sum(ga[:, c:c + 1], wbuf[:], axis=AX.X,
                                            apply_absolute_value=True))
                    v.wait_ge(dv.h, E_dv + 8)
                    dv.inc(v.reduce_sum(gb[:], ga[:], axis=AX.X))
                    v.wait_ge(pe.h, E_pe + 1)
                    dv.inc(v.tensor_scalar(thr_row[:], psMisc[0:1, 0:1],
                                           0.5 / (H4 * D), 5e-7, ALU.mult, ALU.add))
                    v.wait_ge(pe.h, E_pe + 2)
                    dv.inc(v.tensor_copy(thr_col[:], psMisc[:, 0:1]))
                    v.wait_ge(dv.h, E_dv + 11)
                    dv.inc(v.tensor_scalar(nthr_col[:], thr_col[:], -1.0, None,
                                           ALU.mult))
                    for c in range(DC):
                        v.wait_ge(ldgp.h, E_ld2 + 16 * (11 + c))
                        dv.inc(v.tensor_scalar(t1b[:], wbuf[:], thr_col[:], None,
                                               ALU.is_gt))
                        dv.inc(v.tensor_scalar(t2b[:], wbuf[:], nthr_col[:], None,
                                               ALU.is_lt))
                        v.wait_ge(dv.h, E_dv + 14 + 3 * c)
                        dv.inc(v.tensor_sub(wfcq[:, c * H4:(c + 1) * H4], t1b[:], t2b[:]))

                @blk.tensor
                def _(t_):
                    t_.wait_ge(gp.h, GP_CONST)
                    t_.wait_ge(dv.h, E_dv + 9)
                    pe.inc(t_.matmul(psMisc[0:1, 0:1], gb[:], onecol_f[:],
                                     start=True, stop=True))
                    t_.wait_ge(dv.h, E_dv + 10)
                    pe.inc(t_.matmul(psMisc[:, 0:1], ones_row_f[:], thr_row[:],
                                     start=True, stop=True))
                blk.__exit__(None, None, None)

                E1_dv = E_dv + 12 + 3 * DC
                E1_pe = E_pe + 2
                E1_ac = E_ac
                E1_ld2 = ldgp.n

                peN = {}
                n = E1_pe
                for t in range(4):
                    for f in range(HC):
                        n += 1
                        peN[("fc", t, f)] = n
                        if f >= 1:
                            n += 1
                            peN[("s1", t, f - 1)] = n
                            n += 1
                            peN[("s2", t, f - 1)] = n
                    n += 1
                    peN[("s1", t, HC - 1)] = n
                    n += 1
                    peN[("s2", t, HC - 1)] = n
                E2_pe = n
                acN = {}
                n = E1_ac
                for t in range(4):
                    for f in range(HC):
                        n += 1
                        acN[("gelu", t, f)] = n
                    n += 1
                    acN[("std", t)] = n
                E2_ac = n
                dvN = {}
                n = E1_dv
                for t in range(4):
                    for f in range(HC):
                        n += 1
                        dvN[("h2", t, f)] = n
                    for k in ("negmu", "musq", "var", "rstd"):
                        n += 1
                        dvN[(k, t)] = n
                E2_dv = n
                E2_st = E_st + 64

                blk2 = nc.Block("e2"); blk2.__enter__()

                @blk2.tensor
                def _(t_):
                    t_.wait_ge(ldgp.h, E1_ld2)
                    t_.wait_ge(dv.h, E1_dv)
                    for t in range(4):
                        for f in range(HC):
                            pf = t * HC + f
                            if pf >= 2:
                                t2, f2 = divmod(pf - 2, HC)
                                t_.wait_ge(ac.h, acN[("gelu", t2, f2)])
                            for ccx in range(DC):
                                inst = t_.matmul(psF[pf % 2][:, 0:512],
                                                 wfcq[:, ccx * H4 + f * P:ccx * H4 + (f + 1) * P],
                                                 xnfm[:, ccx * TC + t * 512:ccx * TC + (t + 1) * 512],
                                                 start=(ccx == 0), stop=(ccx == DC - 1))
                            pe.inc(inst)
                            if f >= 1:
                                t_.wait_ge(ac.h, acN[("gelu", t, f - 1)])
                                pe.inc(t_.matmul(psS1[0:1, 0:512], ones_col_bf[:],
                                                 h_sb[:, (f - 1) * 512:f * 512],
                                                 start=(f == 1), stop=False))
                                t_.wait_ge(dv.h, dvN[("h2", t, f - 1)])
                                pe.inc(t_.matmul(psS2[0:1, 0:512], ones_col_bf[:],
                                                 h2_sb[:, ((f - 1) % 2) * 512:((f - 1) % 2 + 1) * 512],
                                                 start=(f == 1), stop=False))
                        t_.wait_ge(ac.h, acN[("gelu", t, HC - 1)])
                        pe.inc(t_.matmul(psS1[0:1, 0:512], ones_col_bf[:],
                                         h_sb[:, (HC - 1) * 512:HC * 512],
                                         start=False, stop=True))
                        t_.wait_ge(dv.h, dvN[("h2", t, HC - 1)])
                        pe.inc(t_.matmul(psS2[0:1, 0:512], ones_col_bf[:],
                                         h2_sb[:, ((HC - 1) % 2) * 512:((HC - 1) % 2 + 1) * 512],
                                         start=False, stop=True))

                @blk2.scalar
                def _(sc):
                    for t in range(4):
                        for f in range(HC):
                            if t >= 1 and f == 0:
                                sc.wait_ge(stsy.h, E_st + 16 * t)
                            sc.wait_ge(pe.h, peN[("fc", t, f)])
                            ac.inc(sc.activation(h_sb[:, f * 512:(f + 1) * 512],
                                                 psF[(t * HC + f) % 2][:, 0:512], AF.Gelu))
                        sc.wait_ge(dv.h, dvN[("var", t)])
                        ac.inc(sc.activation(std_s[:], var_s[:], AF.Sqrt,
                                             bias=eps_row[0:1, 0:1]))

                @blk2.vector
                def _(v):
                    for t in range(4):
                        for f in range(HC):
                            v.wait_ge(ac.h, acN[("gelu", t, f)])
                            pf = t * HC + f
                            if pf >= 2:
                                t2, f2 = divmod(pf - 2, HC)
                                v.wait_ge(pe.h, peN[("s2", t2, f2)])
                            src = h_sb[:, f * 512:(f + 1) * 512]
                            dv.inc(v.tensor_mul(h2_sb[:, (pf % 2) * 512:(pf % 2 + 1) * 512],
                                                src, src))
                        v.wait_ge(pe.h, peN[("s1", t, HC - 1)])
                        dv.inc(v.tensor_scalar(negmu_row[0:1, t * 512:(t + 1) * 512],
                                               psS1[0:1, 0:512], -1.0 / H4, None, ALU.mult))
                        v.wait_ge(dv.h, dvN[("negmu", t)])
                        if t >= 1:
                            v.wait_ge(ac.h, acN[("std", t - 1)])
                        dv.inc(v.tensor_mul(musq_s[:],
                                            negmu_row[0:1, t * 512:(t + 1) * 512],
                                            negmu_row[0:1, t * 512:(t + 1) * 512]))
                        v.wait_ge(pe.h, peN[("s2", t, HC - 1)])
                        v.wait_ge(dv.h, dvN[("musq", t)])
                        dv.inc(v.scalar_tensor_tensor(var_s[:], psS2[0:1, 0:512],
                                                      1.0 / H4, musq_s[:],
                                                      ALU.mult, ALU.subtract))
                        v.wait_ge(ac.h, acN[("std", t)])
                        dv.inc(v.reciprocal(rstd_row[0:1, t * 512:(t + 1) * 512],
                                            std_s[:]))

                @blk2.sync
                def _(sy):
                    for t in range(4):
                        sy.wait_ge(ac.h, acN[("gelu", t, HC - 1)])
                        sy.wait_ge(dv.h, dvN[("h2", t, HC - 1)])
                        stsy.inc(sy.dma_start(out=h_scr.ap()[t * P:(t + 1) * P, :],
                                              in_=h_sb[:]), 16)
                    sy.wait_ge(dv.h, dvN[("rstd", 3)])
                    stsy.inc(sy.dma_start(out=bounce.ap(), in_=rstd_row[:]), 16)
                    sy.wait_ge(stsy.h, stsy.n)
                    stsy.inc(sy.dma_start(out=rstd_cols[:].unsqueeze(2),
                                          in_=bounce.ap().rearrange("a (t p) -> p t a", p=P),
                                          allow_slow_non_contiguous=True), 16)
                blk2.__exit__(None, None, None)
                E2_ld = ldsy.n
                E2_st2 = stsy.n

            wpq = eE(nc.sbuf_tensor("wpq", [P, HC * D], BF16))
            esE3 = ExitStack()
            wbuf2 = esE3.enter_context(nc.sbuf_tensor("wbuf2", [P, D], F32))
            t1b2 = esE3.enter_context(nc.sbuf_tensor("t1b2", [P, D], BF16))
            t2b2 = esE3.enter_context(nc.sbuf_tensor("t2b2", [P, D], BF16))

            DV3 = E2_dv
            PE3 = E2_pe
            LD3 = E1_ld2
            DV_G = DV3 + HC
            DV_THRC = DV_G + 3
            DV_NTHR = DV_THRC + 1

            def dvRaw(c):
                return DV_NTHR + 4 * c + 3

            def dvScale(c):
                return DV_NTHR + 4 * c + 4
            DV_TERN = DV_NTHR + 4 * HC
            DV_CCROW = DV_TERN + 2
            DV_CCREP = DV_TERN + 6
            PE_G = PE3 + 2

            def peCC(c):
                return PE_G + 4 * c + 2
            PE_TERN = PE_G + 4 * HC
            PE_CCREP1 = PE_TERN + 1
            PE_CCREP2 = PE_TERN + 2

            blk3 = nc.Block("e3"); blk3.__enter__()

            @blk3.gpsimd
            def _(g):
                g.wait_ge(pe.h, PE3)
                for c in range(HC):
                    if c >= 1:
                        g.wait_ge(dv.h, DV3 + c)
                    ldgp.inc(g.dma_start(out=wbuf2[:], in_=wpT3[:, c:c + 1, :]), 16)
                for c in range(HC):
                    if c == 0:
                        g.wait_ge(dv.h, DV_G)
                    else:
                        g.wait_ge(dv.h, DV_NTHR + 4 * (c - 1) + 2)
                    ldgp.inc(g.dma_start(out=wbuf2[:], in_=wpT3[:, c:c + 1, :]), 16)

            @blk3.vector
            def _(v):
                for c in range(HC):
                    v.wait_ge(ldgp.h, LD3 + 16 * (c + 1))
                    dv.inc(v.reduce_sum(gp32[:, c:c + 1], wbuf2[:], axis=AX.X,
                                        apply_absolute_value=True))
                v.wait_ge(dv.h, DV_G)
                dv.inc(v.reduce_sum(gb[:], gp32[:], axis=AX.X))
                v.wait_ge(pe.h, PE_G - 1)
                dv.inc(v.tensor_scalar(thr_row[:], psMisc[0:1, 0:1],
                                       0.5 / (H4 * D), 5e-7, ALU.mult, ALU.add))
                v.wait_ge(pe.h, PE_G)
                dv.inc(v.tensor_copy(thr_col[:], psMisc[:, 0:1]))
                v.wait_ge(dv.h, DV_THRC)
                dv.inc(v.tensor_scalar(nthr_col[:], thr_col[:], -1.0, None, ALU.mult))
                for c in range(HC):
                    v.wait_ge(ldgp.h, LD3 + 16 * (HC + c + 1))
                    dv.inc(v.tensor_scalar(t1b2[:], wbuf2[:], thr_col[:], None,
                                           ALU.is_gt))
                    dv.inc(v.tensor_scalar(t2b2[:], wbuf2[:], nthr_col[:], None,
                                           ALU.is_lt))
                    v.wait_ge(dv.h, DV_NTHR + 4 * c + 2)
                    dv.inc(v.tensor_sub(wpq[:, c * D:(c + 1) * D], t1b2[:], t2b2[:]))
                    v.wait_ge(pe.h, peCC(c))
                    dv.inc(v.tensor_scalar(wpq[:, c * D:(c + 1) * D],
                                           wpq[:, c * D:(c + 1) * D],
                                           lnw_sb[:, c:c + 1], None, ALU.mult))
                v.wait_ge(pe.h, PE_TERN)
                dv.inc(v.tensor_copy(cc_row[0:1, 0:512], psCC[0:1, 0:512]))
                dv.inc(v.tensor_copy(cc_row[0:1, 512:1024], psS1[0:1, 0:512]))
                dv.inc(v.tensor_copy(c_row[0:1, 0:512], psCr[0:1, 0:512]))
                dv.inc(v.tensor_copy(c_row[0:1, 512:1024], psS2[0:1, 0:512]))
                v.wait_ge(pe.h, PE_CCREP1)
                dv.inc(v.tensor_copy(ccrep[:, 0:512], psREP[:, 0:512]))
                v.wait_ge(pe.h, PE_CCREP2)
                dv.inc(v.tensor_copy(ccrep[:, 512:1024], psREP[:, 0:512]))

            @blk3.tensor
            def _(t_):
                t_.wait_ge(dv.h, DV_G + 1)
                pe.inc(t_.matmul(psMisc[0:1, 0:1], gb[:], onecol_f[:],
                                 start=True, stop=True))
                t_.wait_ge(dv.h, DV_G + 2)
                pe.inc(t_.matmul(psMisc[:, 0:1], ones_row_f[:], thr_row[:],
                                 start=True, stop=True))
                t_.wait_ge(ldgp.h, E1_ld2)
                for c in range(HC):
                    t_.wait_ge(dv.h, dvRaw(c))
                    pe.inc(t_.matmul(psCC[0:1, 0:512], lnb_bf[:, c:c + 1],
                                     wpq[:, c * D:c * D + 512],
                                     start=(c == 0), stop=(c == HC - 1)))
                    pe.inc(t_.matmul(psS1[0:1, 0:512], lnb_bf[:, c:c + 1],
                                     wpq[:, c * D + 512:(c + 1) * D],
                                     start=(c == 0), stop=(c == HC - 1)))
                    t_.wait_ge(dv.h, dvScale(c))
                    pe.inc(t_.matmul(psCr[0:1, 0:512], ones_col_bf[:],
                                     wpq[:, c * D:c * D + 512],
                                     start=(c == 0), stop=(c == HC - 1)))
                    pe.inc(t_.matmul(psS2[0:1, 0:512], ones_col_bf[:],
                                     wpq[:, c * D + 512:(c + 1) * D],
                                     start=(c == 0), stop=(c == HC - 1)))
                t_.wait_ge(dv.h, DV_CCROW)
                pe.inc(t_.matmul(psREP[:, 0:512], ones_row_f[:], cc_row[0:1, 0:512],
                                 start=True, stop=True))
                t_.wait_ge(dv.h, DV_CCREP - 1)
                pe.inc(t_.matmul(psREP[:, 0:512], ones_row_f[:], cc_row[0:1, 512:1024],
                                 start=True, stop=True))
            blk3.__exit__(None, None, None)
            esE3.close()
            xbuf = eE(nc.sbuf_tensor("xbuf", [P, 4 * D], F32))
            abuf = eE(nc.sbuf_tensor("abuf", [P, 4 * D], F32))
            tmpA = eE(nc.sbuf_tensor("tmpA", [P, 2 * 512], F32))
            psJ = [eE(nc.psum_tensor(f"psJ{i}", [P, 512], F32)) for i in range(2)]

            PE4 = PE_CCREP2
            DV4 = DV_CCREP
            peJ = {}
            n = PE4
            for t in range(4):
                for idx in range(8):
                    n += 1
                    peJ[(t, idx)] = n
            dvE = {}
            n = DV4
            for t in range(4):
                for idx in range(8):
                    dvE[(t, idx)] = (n + 1, n + 2, n + 3)
                    n += 3
            ld4 = {}
            n = ldgp.n
            for t in range(4):
                ld4[("h", t)] = n + 16
                ld4[("x", t)] = n + 32
                ld4[("a", t)] = n + 48
                n += 48
            stO = {t: E2_st2 + 16 * (t + 1) for t in range(4)}

            blk4 = nc.Block("e4"); blk4.__enter__()

            @blk4.gpsimd
            def _(g):
                g.wait_ge(stsy.h, marks["attn0_st"])
                for t in range(4):
                    if t == 0:
                        g.wait_ge(stsy.h, E2_st)
                    else:
                        g.wait_ge(pe.h, peJ[(t - 1, 7)])
                    ldgp.inc(g.dma_start(out=h_sb[:],
                                         in_=h_scr.ap()[t * P:(t + 1) * P, :]), 16)
                    if t >= 1:
                        g.wait_ge(dv.h, dvE[(t - 1, 7)][2])
                        g.wait_ge(stsy.h, stO[t - 1])
                    ldgp.inc(g.dma_start(out=xbuf[:].rearrange("p (u d) -> p u d", u=4),
                                         in_=xin.rearrange("(t p) d -> p t d", p=P)[:, 4 * t:4 * t + 4, :]), 16)
                    ldgp.inc(g.dma_start(out=abuf[:].rearrange("p (u d) -> p u d", u=4),
                                         in_=attn0_d.ap().rearrange("(t p) d -> p t d", p=P)[:, 4 * t:4 * t + 4, :]), 16)

            @blk4.tensor
            def _(t_):
                t_.wait_ge(dv.h, DV4)
                for t in range(4):
                    t_.wait_ge(ldgp.h, ld4[("a", t)])
                    for idx in range(8):
                        tt, half = divmod(idx, 2)
                        tg = 4 * t + tt
                        gi = t * 8 + idx
                        if gi >= 2:
                            t2, i2 = divmod(gi - 2, 8)
                            t_.wait_ge(dv.h, dvE[(t2, i2)][0])
                        for f in range(HC):
                            t_.matmul(psJ[idx % 2][:, 0:512],
                                      h_sb[:, f * 512 + tt * P:f * 512 + (tt + 1) * P],
                                      wpq[:, f * D + half * 512:f * D + (half + 1) * 512],
                                      start=(f == 0), stop=False)
                        pe.inc(t_.matmul(psJ[idx % 2][:, 0:512],
                                         negmu_row[0:1, tg * P:(tg + 1) * P],
                                         c_row[0:1, half * 512:(half + 1) * 512],
                                         start=False, stop=True))

            @blk4.vector
            def _(v):
                v.wait_ge(stsy.h, E2_st2)
                for t in range(4):
                    v.wait_ge(ldgp.h, ld4[("a", t)])
                    for idx in range(8):
                        tt, half = divmod(idx, 2)
                        tg = 4 * t + tt
                        par = idx % 2
                        e1n, e2n, e3n = dvE[(t, idx)]
                        v.wait_ge(pe.h, peJ[(t, idx)])
                        dv.inc(v.scalar_tensor_tensor(tmpA[:, par * 512:(par + 1) * 512],
                                                      psJ[par][:, 0:512],
                                                      rstd_cols[:, tg:tg + 1],
                                                      xbuf[:, tt * D + half * 512:tt * D + (half + 1) * 512],
                                                      ALU.mult, ALU.add))
                        v.wait_ge(dv.h, e1n)
                        dv.inc(v.tensor_add(tmpA[:, par * 512:(par + 1) * 512],
                                            tmpA[:, par * 512:(par + 1) * 512],
                                            abuf[:, tt * D + half * 512:tt * D + (half + 1) * 512]))
                        v.wait_ge(dv.h, e2n)
                        dv.inc(v.tensor_add(xbuf[:, tt * D + half * 512:tt * D + (half + 1) * 512],
                                            tmpA[:, par * 512:(par + 1) * 512],
                                            ccrep[:, half * 512:(half + 1) * 512]))

            @blk4.sync
            def _(sy):
                for t in range(4):
                    sy.wait_ge(dv.h, dvE[(t, 7)][2])
                    stsy.inc(sy.dma_start(
                        out=x1_d.ap().rearrange("(t p) d -> p t d", p=P)[:, 4 * t:4 * t + 4, :],
                        in_=xbuf[:].rearrange("p (u d) -> p u d", u=4)), 16)
                if upto == "E":
                    sy.wait_ge(stsy.h, stsy.n)
            blk4.__exit__(None, None, None)
            marks["x1_st"] = stsy.n
        if upto == "E":
            close_stacks()
            return nc
        esXn.close()

        xn1fm = e(nc.sbuf_tensor("xn1fm", [P, DC * TC], BF16))
        esT = ExitStack()
        xn1tm = esT.enter_context(nc.sbuf_tensor("xn1tm", [P, NT * D], BF16))

        with ExitStack() as esF:
            eF = esF.enter_context
            x_sb = eF(nc.sbuf_tensor("x_sbF", [P, NT * D], F32))
            o_sb2 = eF(nc.sbuf_tensor("o_sbF", [P, NT * D], F32))
            s1 = eF(nc.sbuf_tensor("s1F", [P, NT], F32))
            s2 = eF(nc.sbuf_tensor("s2F", [P, NT], F32))
            mu = eF(nc.sbuf_tensor("muF", [P, NT], F32))
            var = eF(nc.sbuf_tensor("varF", [P, NT], F32))
            musq = eF(nc.sbuf_tensor("musqF", [P, NT], F32))
            std = eF(nc.sbuf_tensor("stdF", [P, NT], F32))
            rstd = eF(nc.sbuf_tensor("rstdF", [P, NT], F32))
            scr = eF(nc.sbuf_tensor("scrF", [P, D], F32))
            psT = [eF(nc.psum_tensor(f"psTF{i}", [P, D], F32)) for i in range(2)]

            b = {s: s.n for s in (ldsy, pe, dv, ac)}
            blkF = nc.Block("sF"); blkF.__enter__()

            ln_norm2 = []

            @blkF.sync
            def _(sy):
                sy.wait_ge(stsy.h, marks["x1_st"])
                sy.wait_ge(pe.h, b[pe])
                sy.wait_ge(dv.h, b[dv])
                sy.wait_ge(ac.h, b[ac])
                ldsy.inc(sy.dma_start(out=x_sb[:].rearrange("p (t d) -> p t d", t=NT),
                                      in_=x1_d.ap().rearrange("(t p) d -> p t d", p=P)), 16)
            LD_X1 = ldsy.n

            def f_cast_pred(t):
                return b[ac] + NT + 1 + t + 1

            def f_mm_pred(t):
                return b[pe] + t + 1

            def f_evac_pred(t):
                return b[dv] + 2 * NT + 4 + t + 1

            @blkF.vector
            def _(v):
                v.wait_ge(ldsy.h, LD_X1)
                for t in range(NT):
                    dv.inc(v.reduce_sum(s1[:, t:t + 1], x_sb[:, t * D:(t + 1) * D], axis=AX.X))
                v.wait_ge(dv.h, b[dv] + NT)
                dv.inc(v.tensor_scalar(mu[:], s1[:], 1.0 / D, None, ALU.mult))
                v.wait_ge(dv.h, b[dv] + NT + 1)
                dv.inc(v.tensor_mul(musq[:], mu[:], mu[:]))
                v.wait_ge(ac.h, b[ac] + NT)
                v.wait_ge(dv.h, b[dv] + NT + 2)
                dv.inc(v.scalar_tensor_tensor(var[:], s2[:], 1.0 / D, musq[:],
                                              ALU.mult, ALU.subtract))
                v.wait_ge(ac.h, b[ac] + NT + 1)
                dv.inc(v.reciprocal(rstd[:], std[:]))
                v.wait_ge(dv.h, b[dv] + NT + 4)
                for t in range(NT):
                    ln_norm2.append(dv.inc(v.tensor_scalar(
                        o_sb2[:, t * D:(t + 1) * D], x_sb[:, t * D:(t + 1) * D],
                        mu[:, t:t + 1], rstd[:, t:t + 1], ALU.subtract, ALU.mult)))
                for t in range(NT):
                    v.wait_ge(pe.h, f_mm_pred(t))
                    for c in range(DC):
                        inst = v.tensor_copy(
                            xn1fm[:, c * TC + t * P:c * TC + (t + 1) * P],
                            psT[t % 2][:, c * P:(c + 1) * P])
                    dv.inc(inst)

            @blkF.scalar
            def _(s):
                s.wait_ge(ldsy.h, LD_X1)
                for t in range(NT):
                    ac.inc(s.activation(scr[:], x_sb[:, t * D:(t + 1) * D], AF.Square,
                                        accum_out=s2[:, t:t + 1]))
                s.wait_ge(dv.h, b[dv] + NT + 3)
                ac.inc(s.activation(std[:], var[:], AF.Sqrt, bias=eps_col[:]))
                for t in range(NT):
                    s.wait_ge(dv.h, ln_norm2[t])
                    a_ = ac.inc(s.activation(xn1tm[:, t * D:(t + 1) * D],
                                             o_sb2[:, t * D:(t + 1) * D], AF.Copy))

            @blkF.tensor
            def _(t_):
                t_.wait_ge(dv.h, b[dv])
                for t in range(NT):
                    t_.wait_ge(ac.h, f_cast_pred(t))
                    if t >= 2:
                        t_.wait_ge(dv.h, f_evac_pred(t - 2))
                    for c in range(DC):
                        inst = t_.matmul(psT[t % 2][:, c * P:(c + 1) * P],
                                         xn1tm[:, t * D + c * P:t * D + (c + 1) * P],
                                         ident[:], start=True, stop=True)
                    pe.inc(inst)
            blkF.__exit__(None, None, None)
            marks["xn1tm_ac"] = ac.n
            marks["xn1fm_dv"] = dv.n

            blkFs = nc.Block("sFs"); blkFs.__enter__()

            @blkFs.gpsimd
            def _(g):
                g.wait_ge(dv.h, marks["xn1fm_dv"])
                stgp.inc(g.dma_start(
                    out=xn1fm_d.ap().rearrange("(c p) t -> p c t", p=P),
                    in_=xn1fm[:].rearrange("p (c t) -> p c t", c=DC)), 16)
            blkFs.__exit__(None, None, None)
            marks["xn1fm_st"] = stgp.n

            if upto == "F":
                blkD = nc.Block("dbgF"); blkD.__enter__()

                @blkD.gpsimd
                def _(g):
                    g.wait_ge(dv.h, marks["xn1fm_dv"])
                    stsy.inc(g.dma_start(
                        out=dbg.rearrange("(c p) t -> p c t", p=P),
                        in_=xn1fm[:].rearrange("p (c t) -> p c t", c=DC)), 16)
                    g.wait_ge(stsy.h, stsy.n)
                blkD.__exit__(None, None, None)

        if upto == "F":
            esT.close()
            return nc

        with ExitStack() as esG:
            eG = esG.enter_context
            mbuf = [eG(nc.sbuf_tensor(f"mbuf{i}", [P, 512], F32)) for i in range(2)]
            psG = [eG(nc.psum_tensor(f"psG{i}", [P, 512], F32)) for i in range(2)]

            b = {s: s.n for s in (pe, dv, stsy)}
            marks["mloc_st_pred"] = b[stsy] + 16 * 16
            blkG = nc.Block("sG"); blkG.__enter__()
            m3 = mloc_d.ap().rearrange("(d p) o -> p d o", p=P)

            @blkG.tensor
            def _(t_):
                t_.wait_ge(ac.h, marks["xn1tm_ac"])
                t_.wait_ge(dv.h, marks["xn1fm_dv"])
                for k in range(16):
                    d1, half = divmod(k, 2)
                    if k >= 2:
                        t_.wait_ge(dv.h, b[dv] + k - 1)
                    for ti in range(NT):
                        inst = t_.matmul(psG[k % 2][:, 0:512],
                                         xn1tm[:, ti * D + d1 * P:ti * D + (d1 + 1) * P],
                                         xn1tm[:, ti * D + half * 512:ti * D + (half + 1) * 512],
                                         start=(ti == 0), stop=(ti == NT - 1))
                    pe.inc(inst)

            @blkG.vector
            def _(v):
                for k in range(16):
                    v.wait_ge(pe.h, b[pe] + k + 1)
                    if k >= 2:
                        v.wait_ge(stsy.h, b[stsy] + 16 * (k - 1))
                    dv.inc(v.tensor_copy(mbuf[k % 2][:], psG[k % 2][:, 0:512]))

            @blkG.sync
            def _(sy):
                for k in range(16):
                    d1, half = divmod(k, 2)
                    sy.wait_ge(dv.h, b[dv] + k + 1)
                    stsy.inc(sy.dma_start(out=m3[:, d1:d1 + 1, half * 512:(half + 1) * 512],
                                          in_=mbuf[k % 2][:].unsqueeze(1)), 16)
            @blkG.gpsimd
            def _(g):
                g.wait_ge(stsy.h, marks["mloc_st_pred"])
                g.wait_ge(stsy.h, marks["x1_st"])
                g.wait_ge(stgp.h, marks["xn1fm_st"])
            blkG.__exit__(None, None, None)
            marks["mloc_st"] = stsy.n
        esT.close()
        return nc


def build_l2():
    nc = bass.Bass("TRN2", target_bir_lowering=False, debug=False)
    x1_in = nc.dram_tensor("x1", [TC, D], F32, kind="ExternalInput").ap()
    xn1fm = nc.dram_tensor("xn1fm", [D, TC], F32, kind="ExternalInput").ap()
    mparts = nc.dram_tensor("mparts", [4 * D, D], F32, kind="ExternalInput").ap()
    wq1_in = nc.dram_tensor("wq1", [D, D], F32, kind="ExternalInput").ap()
    wg_in = nc.dram_tensor("wg", [D, D], F32, kind="ExternalInput").ap()
    bg_in = nc.dram_tensor("bg", [1, D], F32, kind="ExternalInput").ap()
    we_in = nc.dram_tensor("we", [8 * D, D], F32, kind="ExternalInput").ap()
    be_in = nc.dram_tensor("be", [1, 8 * D], F32, kind="ExternalInput").ap()
    x2_out = nc.dram_tensor("x2", [TC, D], F32, kind="ExternalOutput").ap()

    with ExitStack() as es:
        e = es.enter_context
        xn_sb = e(nc.sbuf_tensor("xn_sb", [P, DC * TC], BF16))
        m_bf = e(nc.sbuf_tensor("m_bf", [P, DC * D], BF16))
        w2_sb = e(nc.sbuf_tensor("w2_sb", [P, DC * D], BF16))
        acc = e(nc.sbuf_tensor("acc", [P, NT * D], BF16))
        msel = e(nc.sbuf_tensor("msel", [P, 8 * NT], F32))
        lg = e(nc.sbuf_tensor("lg", [P, 2 * D], F32))
        sexp = e(nc.sbuf_tensor("sexp", [P, D], BF16))
        max8 = e(nc.sbuf_tensor("max8", [P, 8], F32))
        idx8 = e(nc.sbuf_tensor("idx8", [P, 8], U32))
        eidx_u = e(nc.sbuf_tensor("eidx_u", [P, 1], U32))
        seven = e(nc.sbuf_tensor("seven", [P, 1], U32))
        nmx_all = e(nc.sbuf_tensor("nmx_all", [P, NT], F32))
        sume_all = e(nc.sbuf_tensor("sume_all", [P, NT], F32))
        p_all = e(nc.sbuf_tensor("p_all", [P, NT], F32))
        eidx_all = e(nc.sbuf_tensor("eidx_all", [P, NT], F32))
        tmp16 = e(nc.sbuf_tensor("tmp16", [P, NT], F32))
        tmpe = e(nc.sbuf_tensor("tmpe", [P, 2 * 512], BF16))
        x1b = e(nc.sbuf_tensor("x1b", [P, 2 * D], F32))
        ones_row_bf = e(nc.sbuf_tensor("ones_row_bf", [1, P], BF16))
        psU = [e(nc.psum_tensor(f"psU{i}", [P, 512], F32)) for i in range(2)]
        ld = SemCtr(es, nc, "ld")
        ldw = SemCtr(es, nc, "ldw")
        gp = SemCtr(es, nc, "gp")
        pe = SemCtr(es, nc, "pe")
        dv = SemCtr(es, nc, "dv")
        ac = SemCtr(es, nc, "ac")
        st = SemCtr(es, nc, "st")

        with ExitStack() as es1:
            e1 = es1.enter_context
            msum = e1(nc.sbuf_tensor("msum", [P, DC * D], F32))
            mtmp = e1(nc.sbuf_tensor("mtmp", [P, DC * D], F32))
            wq1_bf = e1(nc.sbuf_tensor("wq1_bf", [P, DC * D], BF16))
            blk = nc.Block("h1"); blk.__enter__()
            mp3 = mparts.rearrange("(r c p) o -> p r c o", r=4, p=P)

            @blk.gpsimd
            def _(g):
                gp.inc(g.memset(ones_row_bf[:], 1.0))
                gp.inc(g.memset(seven[:], 7))
                ld.inc(g.dma_start(out=msum[:].rearrange("p (c o) -> p c o", c=DC),
                                   in_=mp3[:, 0]), 16)
                ld.inc(g.dma_start(out=mtmp[:].rearrange("p (c o) -> p c o", c=DC),
                                   in_=mp3[:, 1]), 16)
                g.wait_ge(dv.h, 1)
                ld.inc(g.dma_start(out=mtmp[:].rearrange("p (c o) -> p c o", c=DC),
                                   in_=mp3[:, 2]), 16)
                g.wait_ge(dv.h, 2)
                ld.inc(g.dma_start(out=mtmp[:].rearrange("p (c o) -> p c o", c=DC),
                                   in_=mp3[:, 3]), 16)
                ldw.inc(g.dma_start(out=wq1_bf[:].rearrange("p (c o) -> p c o", c=DC),
                                    in_=wq1_in.rearrange("(c p) o -> p c o", p=P)), 16)
                ldw.inc(g.dma_start(out=xn_sb[:].rearrange("p (c t) -> p c t", c=DC),
                                    in_=xn1fm.rearrange("(c p) t -> p c t", p=P)), 16)

            @blk.vector
            def _(v):
                v.wait_ge(ld.h, 32)
                dv.inc(v.tensor_add(msum[:], msum[:], mtmp[:]))
                v.wait_ge(ld.h, 48)
                v.wait_ge(dv.h, 1)
                dv.inc(v.tensor_add(msum[:], msum[:], mtmp[:]))
                v.wait_ge(ld.h, 64)
                v.wait_ge(dv.h, 2)
                dv.inc(v.tensor_add(msum[:], msum[:], mtmp[:]))
                v.wait_ge(dv.h, 3)
                dv.inc(v.tensor_copy(m_bf[:], msum[:]))

            @blk.tensor
            def _(t_):
                t_.wait_ge(ldw.h, 16)
                t_.wait_ge(dv.h, 4)
                for k in range(16):
                    hc, half = divmod(k, 2)
                    if k >= 2:
                        t_.wait_ge(dv.h, 4 + k - 1)
                    for oc in range(DC):
                        inst = t_.matmul(psU[k % 2][:, 0:512],
                                         wq1_bf[:, oc * D + hc * P:oc * D + (hc + 1) * P],
                                         m_bf[:, oc * D + half * 512:oc * D + (half + 1) * 512],
                                         start=(oc == 0), stop=(oc == DC - 1))
                    pe.inc(inst)

            @blk.vector
            def _(v):
                for k in range(16):
                    hc, half = divmod(k, 2)
                    v.wait_ge(pe.h, k + 1)
                    dv.inc(v.tensor_copy(w2_sb[:, hc * D + half * 512:hc * D + (half + 1) * 512],
                                         psU[k % 2][:, 0:512]))
            blk.__exit__(None, None, None)
        H2_pe = 16
        H2_dv = 20
        H2_gp = 2

        wg_bf = e(nc.sbuf_tensor("wg_bf", [P, DC * D], BF16))
        webf = [e(nc.sbuf_tensor(f"webf{i}", [P, DC * D], BF16)) for i in range(2)]
        bg_bf = e(nc.sbuf_tensor("bg_bf", [1, D], BF16))
        be_bf = e(nc.sbuf_tensor("be_bf", [1, 8 * D], BF16))
        dvN = {}
        n = H2_dv
        for t in range(NT):
            for k in ("ev0", "ev1", "max8", "nmx", "idx", "and", "eidx"):
                n += 1
                dvN[(k, t)] = n
        dvN["recip"] = n + 1
        n += 1
        for ee in range(8):
            dvN[("mseleq", ee)] = n + 1
            dvN[("mselmul", ee)] = n + 2
            n += 2
        H3_dv = n
        peN = {}
        n = H2_pe
        for t in range(NT):
            for half in range(2):
                n += 1
                peN[("lg", t, half)] = n
        H3_pe = n
        acN = {}
        n = 0
        for t in range(NT):
            n += 1
            acN[("exp", t)] = n
        H3_ac = n

        blk3 = nc.Block("h3"); blk3.__enter__()

        @blk3.gpsimd
        def _(g):
            ldw.inc(g.dma_start(out=wg_bf[:].rearrange("p (c o) -> p c o", c=DC),
                                in_=wg_in.rearrange("(c p) o -> p c o", p=P)), 16)
            ldw.inc(g.dma_start(out=bg_bf[:], in_=bg_in), 16)
            ldw.inc(g.dma_start(out=be_bf[:], in_=be_in), 16)

        @blk3.tensor
        def _(t_):
            t_.wait_ge(ldw.h, 80)
            t_.wait_ge(gp.h, 1)
            for t in range(NT):
                for half in range(2):
                    k = t * 2 + half
                    if k >= 2:
                        t2, h2 = divmod(k - 2, 2)
                        t_.wait_ge(dv.h, dvN[("ev" + str(h2), t2)])
                    for cc in range(DC):
                        t_.matmul(psU[k % 2][:, 0:512],
                                  xn_sb[:, cc * TC + t * P:cc * TC + (t + 1) * P],
                                  wg_bf[:, cc * D + half * 512:cc * D + (half + 1) * 512],
                                  start=(cc == 0), stop=False)
                    pe.inc(t_.matmul(psU[k % 2][:, 0:512], ones_row_bf[:],
                                     bg_bf[0:1, half * 512:(half + 1) * 512],
                                     start=False, stop=True))

        @blk3.vector
        def _(v):
            for t in range(NT):
                for half in range(2):
                    v.wait_ge(pe.h, peN[("lg", t, half)])
                    if t >= 2:
                        v.wait_ge(ac.h, acN[("exp", t - 2)])
                    dv.inc(v.tensor_copy(lg[:, (t % 2) * D + half * 512:(t % 2) * D + (half + 1) * 512],
                                         psU[(t * 2 + half) % 2][:, 0:512]))
                v.wait_ge(dv.h, dvN[("ev1", t)])
                dv.inc(v.max(max8[:], lg[:, (t % 2) * D:(t % 2 + 1) * D]))
                v.wait_ge(dv.h, dvN[("max8", t)])
                dv.inc(v.tensor_scalar(nmx_all[:, t:t + 1], max8[:, 0:1], -1.0, None,
                                       ALU.mult))
                dv.inc(v.max_index(idx8[:], max8[:], lg[:, (t % 2) * D:(t % 2 + 1) * D]))
                v.wait_ge(dv.h, dvN[("idx", t)])
                dv.inc(v.tensor_single_scalar(eidx_u[:], idx8[:, 0:1], seven[:],
                                              ALU.bitwise_and))
                v.wait_ge(dv.h, dvN[("and", t)])
                dv.inc(v.tensor_copy(eidx_all[:, t:t + 1], eidx_u[:]))
            v.wait_ge(ac.h, acN[("exp", NT - 1)])
            dv.inc(v.reciprocal(p_all[:], sume_all[:]))
            v.wait_ge(dv.h, dvN["recip"])
            for ee in range(8):
                dv.inc(v.tensor_scalar(tmp16[:], eidx_all[:], float(ee), None,
                                       ALU.is_equal))
                v.wait_ge(dv.h, dvN[("mseleq", ee)])
                dv.inc(v.tensor_mul(msel[:, ee * NT:(ee + 1) * NT], tmp16[:], p_all[:]))

        @blk3.scalar
        def _(sc):
            for t in range(NT):
                sc.wait_ge(dv.h, dvN[("nmx", t)])
                ac.inc(sc.activation(sexp[:], lg[:, (t % 2) * D:(t % 2 + 1) * D],
                                     AF.Exp, bias=nmx_all[:, t:t + 1],
                                     accum_out=sume_all[:, t:t + 1]))
        blk3.__exit__(None, None, None)

        peE = {}
        n = H3_pe
        for ee in range(8):
            for t in range(NT):
                for half in range(2):
                    n += 1
                    peE[(ee, t, half)] = n
        H4_pe = n
        dvA = {}
        n = H3_dv
        for ee in range(8):
            for t in range(NT):
                for half in range(2):
                    dvA[(ee, t, half)] = (n + 1, n + 2)
                    n += 2
        H4_dv = n
        ldE = {ee: 80 + 16 * (ee + 1) for ee in range(8)}

        blk4 = nc.Block("h4"); blk4.__enter__()
        we3 = we_in.rearrange("(ee c p) o -> p ee c o", ee=8, p=P)

        @blk4.gpsimd
        def _(g):
            for ee in range(8):
                if ee >= 2:
                    g.wait_ge(pe.h, peE[(ee - 2, NT - 1, 1)])
                ldw.inc(g.dma_start(out=webf[ee % 2][:].rearrange("p (c o) -> p c o", c=DC),
                                    in_=we3[:, ee]), 16)

        @blk4.tensor
        def _(t_):
            for ee in range(8):
                t_.wait_ge(ldw.h, ldE[ee])
                for t in range(NT):
                    for half in range(2):
                        k = (ee * NT + t) * 2 + half
                        if k >= 2:
                            kk = k - 2
                            e2, r = divmod(kk, NT * 2)
                            t2, h2 = divmod(r, 2)
                            t_.wait_ge(dv.h, dvA[(e2, t2, h2)][0])
                        for cc in range(DC):
                            t_.matmul(psU[k % 2][:, 0:512],
                                      xn_sb[:, cc * TC + t * P:cc * TC + (t + 1) * P],
                                      webf[ee % 2][:, cc * D + half * 512:cc * D + (half + 1) * 512],
                                      start=(cc == 0), stop=False)
                        pe.inc(t_.matmul(psU[k % 2][:, 0:512], ones_row_bf[:],
                                         be_bf[0:1, ee * D + half * 512:ee * D + (half + 1) * 512],
                                         start=False, stop=True))

        @blk4.vector
        def _(v):
            for ee in range(8):
                for t in range(NT):
                    for half in range(2):
                        k = (ee * NT + t) * 2 + half
                        par = k % 2
                        r1, r2 = dvA[(ee, t, half)]
                        v.wait_ge(pe.h, peE[(ee, t, half)])
                        dv.inc(v.tensor_scalar(tmpe[:, par * 512:(par + 1) * 512],
                                               psU[par][:, 0:512],
                                               msel[:, ee * NT + t:ee * NT + t + 1],
                                               0.0, ALU.mult, ALU.max))
                        v.wait_ge(dv.h, r1)
                        sl = acc[:, t * D + half * 512:t * D + (half + 1) * 512]
                        if ee == 0:
                            dv.inc(v.tensor_copy(sl, tmpe[:, par * 512:(par + 1) * 512]))
                        else:
                            v.wait_ge(dv.h, dvA[(ee - 1, t, half)][1])
                            dv.inc(v.tensor_add(sl, sl, tmpe[:, par * 512:(par + 1) * 512]))
        blk4.__exit__(None, None, None)

        peA = {}
        n = H4_pe
        for t in range(NT):
            for half in range(2):
                n += 1
                peA[(t, half)] = n
        dvC = {}
        n = H4_dv
        for t in range(NT):
            for half in range(2):
                dvC[(t, half)] = (n + 1, n + 2)
                n += 2
        H5_dv = n
        ldX = {t: 16 * (t + 1) for t in range(NT)}
        LD_B = 64

        blk5 = nc.Block("h5"); blk5.__enter__()
        x1_3 = x1_in.rearrange("(t p) d -> p t d", p=P)
        x2_3 = x2_out.rearrange("(t p) d -> p t d", p=P)

        @blk5.gpsimd
        def _(g):
            for t in range(NT):
                if t >= 2:
                    g.wait_ge(st.h, 16 * (t - 1))
                ld.inc(g.dma_start(out=x1b[:, (t % 2) * D:(t % 2 + 1) * D].unsqueeze(1),
                                   in_=x1_3[:, t:t + 1, :]), 16)

        @blk5.tensor
        def _(t_):
            t_.wait_ge(dv.h, H2_dv)
            for t in range(NT):
                for half in range(2):
                    k = t * 2 + half
                    if k >= 2:
                        t2, h2 = divmod(k - 2, 2)
                        t_.wait_ge(dv.h, dvC[(t2, h2)][0])
                    for cc in range(DC):
                        inst = t_.matmul(psU[k % 2][:, 0:512],
                                         xn_sb[:, cc * TC + t * P:cc * TC + (t + 1) * P],
                                         w2_sb[:, cc * D + half * 512:cc * D + (half + 1) * 512],
                                         start=(cc == 0), stop=(cc == DC - 1))
                    pe.inc(inst)

        @blk5.vector
        def _(v):
            for t in range(NT):
                v.wait_ge(ld.h, LD_B + ldX[t])
                for half in range(2):
                    c1, c2 = dvC[(t, half)]
                    v.wait_ge(pe.h, peA[(t, half)])
                    sl = x1b[:, (t % 2) * D + half * 512:(t % 2) * D + (half + 1) * 512]
                    dv.inc(v.tensor_add(sl, sl, psU[(t * 2 + half) % 2][:, 0:512]))
                    v.wait_ge(dv.h, c1)
                    dv.inc(v.tensor_add(sl, sl,
                                        acc[:, t * D + half * 512:t * D + (half + 1) * 512]))

        @blk5.sync
        def _(sy):
            for t in range(NT):
                sy.wait_ge(dv.h, dvC[(t, 1)][1])
                st.inc(sy.dma_start(out=x2_3[:, t:t + 1, :],
                                    in_=x1b[:, (t % 2) * D:(t % 2 + 1) * D].unsqueeze(1)), 16)
            sy.wait_ge(st.h, 16 * NT)
        blk5.__exit__(None, None, None)
    return nc



def _in_maps(inputs):
    x = np.ascontiguousarray(inputs["L0_attn_hidden_states"], dtype=np.float32)

    def T(a):
        return np.ascontiguousarray(np.asarray(a, dtype=np.float32).T)

    shared = {
        "wqT": T(inputs["L0_attn_Wq"]), "wkT": T(inputs["L0_attn_Wk"]),
        "wvT": T(inputs["L0_attn_Wv"]), "wg1T": T(inputs["L0_attn_Wg1"]),
        "wg2T": T(inputs["L0_attn_Wg2"]),
        "gnorm_pc": np.ascontiguousarray(np.asarray(inputs["L0_attn_g_norm_weight"], np.float32).reshape(DC, P).T),
        "woT": T(inputs["L0_attn_Wo"]),
        "wfcT": T(inputs["L0_ffn_W_fc"]), "wpT": T(inputs["L0_ffn_W_proj"]),
        "lnw_pc": np.ascontiguousarray(np.asarray(inputs["L0_ffn_ln_w"], np.float32).reshape(HC, P).T),
        "lnb_pc": np.ascontiguousarray(np.asarray(inputs["L0_ffn_ln_b"], np.float32).reshape(HC, P).T),
    }
    maps = []
    for c in range(8):
        b, s = divmod(c, 4)
        pm = np.zeros((P, 4), np.float32)
        pm[:, :s] = 1.0
        m = dict(shared)
        m["xin"] = np.ascontiguousarray(x[b, s * TC:(s + 1) * TC, :])
        m["xprev"] = np.ascontiguousarray(x[b, 0:3 * TC, :])
        m["pmask"] = pm
        maps.append(m)
    return maps


def kernel(**inputs):
    cores = list(range(8))
    nc1 = build_fused("G")
    res1 = run_bass_kernel_spmd(nc1, _in_maps(inputs), core_ids=cores)
    R1 = [res1.results[i] for i in range(8)]

    wq1 = np.ascontiguousarray(inputs["L1_attn_W_q"], dtype=np.float32)
    wg = np.ascontiguousarray(inputs["L1_ffn_Wg"], dtype=np.float32)
    bg1 = np.ascontiguousarray(np.asarray(inputs["L1_ffn_bg"], np.float32).reshape(1, D))
    we_flat = np.ascontiguousarray(np.asarray(inputs["L1_ffn_We"], np.float32).reshape(8 * D, D))
    be1 = np.ascontiguousarray(np.asarray(inputs["L1_ffn_be"], np.float32).reshape(1, 8 * D))
    in_maps = []
    for i in range(8):
        g0 = (i // 4) * 4
        mp = np.ascontiguousarray(np.concatenate(
            [np.asarray(R1[g0 + j]["mloc_d"]) for j in range(4)], axis=0))
        in_maps.append({"x1": np.asarray(R1[i]["x1_d"]),
                        "xn1fm": np.asarray(R1[i]["xn1fm_d"]),
                        "mparts": mp, "wq1": wq1, "wg": wg, "bg": bg1,
                        "we": we_flat, "be": be1})
    nc2 = build_l2()
    res2 = run_bass_kernel_spmd(nc2, in_maps, core_ids=cores)
    out = np.empty((2, 8192, D), np.float32)
    for c in range(8):
        b, s = divmod(c, 4)
        out[b, s * TC:(s + 1) * TC, :] = res2.results[c]["x2"]
    return out


# version 06
# speedup vs baseline: 1.0763x; 1.0763x over previous
import sys
from contextlib import ExitStack

import numpy as np

sys.path.insert(0, "/opt/trn_rl_repo")
import concourse.bass as bass
import concourse.mybir as mybir
from concourse.bass_utils import run_bass_kernel_spmd

F32 = mybir.dt.float32
BF16 = mybir.dt.bfloat16
U32 = mybir.dt.uint32
AF = mybir.ActivationFunctionType
ALU = mybir.AluOpType
AX = mybir.AxisListType

P = 128
TC = 2048
NT = TC // P
D = 1024
DC = D // P
H4 = 4096
HC = H4 // P
NHEAD = 8
DK = 128
SCALE = float(DK) ** -0.5
PZROWS = D + P


class SemCtr:
    def __init__(self, es, nc, name):
        self.h = es.enter_context(nc.semaphore(name))
        self.n = 0

    def inc(self, inst, k=1):
        inst.then_inc(self.h, k)
        self.n += k
        return self.n


def build_fused(upto="H"):
    nc = bass.Bass("TRN2", target_bir_lowering=False, debug=False)
    order = ["A", "P", "B", "D", "E", "F", "G"]
    assert upto in order
    stages = order[: order.index(upto) + 1]

    def want(s):
        return s in stages

    xin = nc.dram_tensor("xin", [TC, D], F32, kind="ExternalInput").ap()
    wqT = nc.dram_tensor("wqT", [D, D], F32, kind="ExternalInput").ap()
    wkT = nc.dram_tensor("wkT", [D, D], F32, kind="ExternalInput").ap()
    wvT = nc.dram_tensor("wvT", [D, D], F32, kind="ExternalInput").ap()
    wg1T = nc.dram_tensor("wg1T", [D, 128], F32, kind="ExternalInput").ap()
    wg2T = nc.dram_tensor("wg2T", [128, D], F32, kind="ExternalInput").ap()
    gn_in = nc.dram_tensor("gnorm_pc", [P, DC], F32, kind="ExternalInput").ap()
    wo_in = nc.dram_tensor("woT", [D, D], F32, kind="ExternalInput").ap()
    wfcT = nc.dram_tensor("wfcT", [D, H4], F32, kind="ExternalInput").ap()
    wpT = nc.dram_tensor("wpT", [H4, D], F32, kind="ExternalInput").ap()
    lnw_in = nc.dram_tensor("lnw_pc", [P, HC], F32, kind="ExternalInput").ap()
    lnb_in = nc.dram_tensor("lnb_pc", [P, HC], F32, kind="ExternalInput").ap()
    pmask_in = nc.dram_tensor("pmask", [P, 4], F32, kind="ExternalInput").ap()
    xprev = nc.dram_tensor("xprev", [3 * TC, D], F32, kind="ExternalInput").ap()

    attn0_kind = "ExternalOutput" if upto == "D" else "Internal"
    attn0_d = nc.dram_tensor("attn0_d", [TC, D], F32, kind=attn0_kind)
    h_scr = nc.dram_tensor("h_scr", [4 * P, HC * 512], BF16)
    bounce = nc.dram_tensor("bounce", [1, TC], F32)
    x1_d = nc.dram_tensor("x1_d", [TC, D], F32, kind="ExternalOutput")
    mloc_d = nc.dram_tensor("mloc_d", [D, D], F32, kind="ExternalOutput")

    xn1fm_d = nc.dram_tensor("xn1fm_d", [D, TC], F32, kind="ExternalOutput")
    dbg = None
    if upto in ("A", "P", "B", "F"):
        shp = {"A": [D, TC], "P": [D + P, DK], "B": [4 * D, TC],
               "F": [D, TC]}[upto]
        dbg = nc.dram_tensor("dbg", shp, F32, kind="ExternalOutput").ap()

    with ExitStack() as es:
        e = es.enter_context
        ldsy = SemCtr(es, nc, "ldsy")
        ldgp = SemCtr(es, nc, "ldgp")
        stsy = SemCtr(es, nc, "stsy")
        pe = SemCtr(es, nc, "pe")
        dv = SemCtr(es, nc, "dv")
        ac = SemCtr(es, nc, "ac")
        gp = SemCtr(es, nc, "gp")
        stgp = SemCtr(es, nc, "stgp")
        stbg = SemCtr(es, nc, "stbg")

        ones_bf = e(nc.sbuf_tensor("ones_bf", [P, DK], BF16))
        ident = e(nc.sbuf_tensor("ident", [P, DK], BF16))
        mask = e(nc.sbuf_tensor("mask", [P, DK], BF16))
        zeros_f = e(nc.sbuf_tensor("zeros_f", [P, DK], F32))
        ones_col_bf = e(nc.sbuf_tensor("ones_col_bf", [P, 1], BF16))
        ones_row_f = e(nc.sbuf_tensor("ones_row_f", [1, P], F32))
        ones_row_bf = e(nc.sbuf_tensor("ones_row_bf", [1, P], BF16))
        scale_col = e(nc.sbuf_tensor("scale_col", [P, 1], F32))
        eps_col = e(nc.sbuf_tensor("eps_col", [P, 1], F32))
        eps_row = e(nc.sbuf_tensor("eps_row", [1, 1], F32))
        onecol_f = e(nc.sbuf_tensor("onecol_f", [P, 1], F32))
        seven = e(nc.sbuf_tensor("seven", [P, 1], U32))
        zrow_bf = e(nc.sbuf_tensor("zrow_bf", [1, 512], BF16))
        pmask_sb = e(nc.sbuf_tensor("pmask_sb", [P, 4], F32))
        GG = 1024
        wgbuf = e(nc.sbuf_tensor("wgbuf", [P, 2 * GG], F32))
        absb = e(nc.sbuf_tensor("absb", [P, GG // 2], F32))
        t1q = e(nc.sbuf_tensor("t1q", [P, GG], BF16))
        t2q = e(nc.sbuf_tensor("t2q", [P, GG], BF16))
        ga_fc = e(nc.sbuf_tensor("ga_fc", [P, 64], F32))
        ga_p = e(nc.sbuf_tensor("ga_p", [P, 64], F32))
        gb2 = e(nc.sbuf_tensor("gb2", [P, 2], F32))
        thr_rows = e(nc.sbuf_tensor("thr_rows", [1, 2], F32))
        thr_cols = e(nc.sbuf_tensor("thr_cols", [P, 2], F32))
        nthr_cols = e(nc.sbuf_tensor("nthr_cols", [P, 2], F32))
        lnw_sb = e(nc.sbuf_tensor("lnw_sb", [P, HC], F32))

        blk0 = nc.Block("s0"); blk0.__enter__()

        @blk0.gpsimd
        def _(g):
            gp.inc(g.memset(ones_bf[:], 1.0))
            gp.inc(g.memset(zeros_f[:], 0.0))
            gp.inc(g.memset(ones_col_bf[:], 1.0))
            gp.inc(g.memset(ones_row_f[:], 1.0))
            gp.inc(g.memset(ones_row_bf[:], 1.0))
            gp.inc(g.memset(scale_col[:], SCALE))
            gp.inc(g.memset(eps_col[:], 1e-5))
            gp.inc(g.memset(eps_row[:], 1e-5))
            gp.inc(g.memset(onecol_f[:], 1.0))
            gp.inc(g.memset(seven[:], 7))
            gp.inc(g.memset(zrow_bf[:], 0.0))
            g.wait_ge(gp.h, 1)
            gp.inc(g.affine_select(ident[:], ones_bf[:], [[-1, DK]], ALU.is_equal,
                                   0.0, base=0, channel_multiplier=1))
            gp.inc(g.affine_select(mask[:], ones_bf[:], [[1, DK]], ALU.is_ge,
                                   0.0, base=0, channel_multiplier=-1))
            ldgp.inc(g.dma_start(out=pmask_sb[:], in_=pmask_in), 16)
        blk0.__exit__(None, None, None)
        GP_CONST = gp.n
        LD_PMASK = ldgp.n

        esXn = ExitStack()
        esQ = ExitStack()
        esActs = ExitStack()
        xnfm = esXn.enter_context(nc.sbuf_tensor("xnfm", [P, DC * TC], BF16))

        def close_stacks():
            esActs.close()
            esQ.close()
            esXn.close()

        marks = {}

        with ExitStack() as esA:
            eA = esA.enter_context
            x_sb = eA(nc.sbuf_tensor("x_sb", [P, NT * D], F32))
            o_sb = eA(nc.sbuf_tensor("o_sb", [P, NT * D], F32))
            s1 = eA(nc.sbuf_tensor("s1", [P, NT], F32))
            s2 = eA(nc.sbuf_tensor("s2", [P, NT], F32))
            mu = eA(nc.sbuf_tensor("mu", [P, NT], F32))
            var = eA(nc.sbuf_tensor("var", [P, NT], F32))
            musq = eA(nc.sbuf_tensor("musq", [P, NT], F32))
            std = eA(nc.sbuf_tensor("std", [P, NT], F32))
            rstd = eA(nc.sbuf_tensor("rstd", [P, NT], F32))
            scr = eA(nc.sbuf_tensor("scr", [P, D], F32))
            xbf = eA(nc.sbuf_tensor("xbf", [P, 2 * D], BF16))
            psT = [eA(nc.psum_tensor(f"psT{i}", [P, D], F32)) for i in range(2)]

            b = {s: s.n for s in (ldsy, pe, dv, ac, gp)}

            x_t = xin.rearrange("(t p) d -> p t d", p=P)
            x_sb3 = x_sb[:].rearrange("p (t d) -> p t d", t=NT)

            blkA = nc.Block("sA"); blkA.__enter__()

            ln_norm = []
            a2_cast = []
            a2_mm = []
            a2_evac = []

            def a2_mm_pred(t):
                return b[pe] + t + 1

            def a2_cast_pred(t):
                return b[ac] + NT + 1 + t + 1

            def a2_evac_pred(t):
                return b[dv] + 2 * NT + 4 + t + 1

            @blkA.sync
            def _(sy):
                ldsy.inc(sy.dma_start(out=x_sb3, in_=x_t), 16)
            LD_X = ldsy.n

            @blkA.vector
            def _(v):
                v.wait_ge(ldsy.h, LD_X)
                for t in range(NT):
                    dv.inc(v.reduce_sum(s1[:, t:t + 1], x_sb[:, t * D:(t + 1) * D], axis=AX.X))
                v.wait_ge(dv.h, b[dv] + NT)
                dv.inc(v.tensor_scalar(mu[:], s1[:], 1.0 / D, None, ALU.mult))
                v.wait_ge(dv.h, b[dv] + NT + 1)
                dv.inc(v.tensor_mul(musq[:], mu[:], mu[:]))
                v.wait_ge(ac.h, b[ac] + NT)
                v.wait_ge(dv.h, b[dv] + NT + 2)
                dv.inc(v.scalar_tensor_tensor(var[:], s2[:], 1.0 / D, musq[:],
                                              ALU.mult, ALU.subtract))
                v.wait_ge(ac.h, b[ac] + NT + 1)
                dv.inc(v.reciprocal(rstd[:], std[:]))
                v.wait_ge(dv.h, b[dv] + NT + 4)
                for t in range(NT):
                    ln_norm.append(dv.inc(v.tensor_scalar(
                        o_sb[:, t * D:(t + 1) * D], x_sb[:, t * D:(t + 1) * D],
                        mu[:, t:t + 1], rstd[:, t:t + 1], ALU.subtract, ALU.mult)))
                for t in range(NT):
                    v.wait_ge(pe.h, a2_mm_pred(t))
                    for c in range(DC):
                        inst = v.tensor_copy(
                            xnfm[:, c * TC + t * P:c * TC + (t + 1) * P],
                            psT[t % 2][:, c * P:(c + 1) * P])
                    a2_evac.append(dv.inc(inst))

            @blkA.scalar
            def _(s):
                s.wait_ge(ldsy.h, LD_X)
                for t in range(NT):
                    if t >= 1:
                        s.wait_ge(ac.h, b[ac] + t)
                    ac.inc(s.activation(scr[:], x_sb[:, t * D:(t + 1) * D], AF.Square,
                                        accum_out=s2[:, t:t + 1]))
                s.wait_ge(dv.h, b[dv] + NT + 3)
                s.wait_ge(gp.h, GP_CONST)
                ac.inc(s.activation(std[:], var[:], AF.Sqrt, bias=eps_col[:]))
                for t in range(NT):
                    s.wait_ge(dv.h, ln_norm[t])
                    if t >= 2:
                        s.wait_ge(pe.h, a2_mm_pred(t - 2))
                    a2_cast.append(ac.inc(s.activation(
                        xbf[:, (t % 2) * D:(t % 2 + 1) * D],
                        o_sb[:, t * D:(t + 1) * D], AF.Copy)))

            @blkA.tensor
            def _(t_):
                t_.wait_ge(gp.h, GP_CONST)
                for t in range(NT):
                    t_.wait_ge(ac.h, a2_cast_pred(t))
                    if t >= 2:
                        t_.wait_ge(dv.h, a2_evac_pred(t - 2))
                    for c in range(DC):
                        inst = t_.matmul(psT[t % 2][:, c * P:(c + 1) * P],
                                         xbf[:, (t % 2) * D + c * P:(t % 2) * D + (c + 1) * P],
                                         ident[:], start=True, stop=True)
                    a2_mm.append(pe.inc(inst))

            blkA.__exit__(None, None, None)
            marks["xnfm_dv"] = dv.n

            if upto == "A":
                blkD = nc.Block("dbgA"); blkD.__enter__()

                @blkD.gpsimd
                def _(g):
                    g.wait_ge(dv.h, marks["xnfm_dv"])
                    stsy.inc(g.dma_start(
                        out=dbg.rearrange("(c p) t -> p c t", p=P),
                        in_=xnfm[:].rearrange("p (c t) -> p c t", c=DC)), 16)
                    g.wait_ge(stsy.h, stsy.n)
                blkD.__exit__(None, None, None)

        if upto == "A":
            close_stacks()
            return nc

        base_acc = esQ.enter_context(nc.sbuf_tensor("base_acc", [P, NHEAD * DK], F32))
        zb_acc = esQ.enter_context(nc.sbuf_tensor("zb_acc", [P, NHEAD], F32))
        wfcT3g = wfcT.rearrange("(c p) f -> p c f", p=P)
        wpT3g = wpT.rearrange("(c p) f -> p c f", p=P)

        gchunks = ([("fc", c, fsl) for c in range(DC) for fsl in range(8)]
                   + [("p", c, half) for c in range(HC) for half in range(2)])
        GH = GG // 2

        class GammaFeeder:
            def __init__(self):
                self.k = 0
                self.gmark = {}
                self.gld_base = None

            def feed(self, s, count=1):
                for _ in range(count):
                    if self.k >= len(gchunks):
                        return
                    if self.gld_base is None:
                        self.gld_base = ldgp.n
                    k = self.k
                    tag, c, fsl = gchunks[k]
                    par = k % 4
                    s.wait_ge(ldgp.h, self.gld_base + 16 * (k + 1))
                    if k >= 1:
                        s.wait_ge(ac.h, self.gmark[k - 1])
                    dst = (ga_fc[:, c * 8 + fsl:c * 8 + fsl + 1] if tag == "fc"
                           else ga_p[:, c * 2 + fsl:c * 2 + fsl + 1])
                    self.gmark[k] = ac.inc(s.activation(
                        absb[:], wgbuf[:, par * GH:(par + 1) * GH], AF.Abs,
                        accum_out=dst))
                    self.k += 1

            def emit_loads(self, g, limit=None):
                assert self.gld_base == ldgp.n
                for k, (tag, c, fsl) in enumerate(gchunks[:limit]):
                    par = k % 4
                    if k >= 4:
                        g.wait_ge(ac.h, self.gmark_lookup(k - 4))
                    if tag == "fc":
                        ap_ = wfcT3g[:, c:c + 1, fsl * GH:(fsl + 1) * GH]
                    else:
                        ap_ = wpT3g[:, c:c + 1, fsl * GH:(fsl + 1) * GH]
                    ldgp.inc(g.dma_start(
                        out=wgbuf[:, par * GH:(par + 1) * GH].unsqueeze(1), in_=ap_), 16)

            def gmark_lookup(self, k):
                return self.gmark[k]

        gfeed = GammaFeeder()


        with ExitStack() as esPre:
            eP = esPre.enter_context
            wkp_sb = eP(nc.sbuf_tensor("wkp_sb", [P, DC * D], BF16))
            wvp_sb = eP(nc.sbuf_tensor("wvp_sb", [P, DC * D], BF16))
            wcb = eP(nc.sbuf_tensor("wcb", [P, 2 * D], F32))
            xpr = eP(nc.sbuf_tensor("xpr", [P, 4 * D], F32))
            xbfp = eP(nc.sbuf_tensor("xbfp", [P, 4 * D], BF16))
            xfmp = eP(nc.sbuf_tensor("xfmp", [P, 4 * D], BF16))
            ap_sb = eP(nc.sbuf_tensor("ap_sb", [P, 4 * D], BF16))
            vp_sb = eP(nc.sbuf_tensor("vp_sb", [P, 4 * D], BF16))
            scrp = eP(nc.sbuf_tensor("scrp", [P, D], F32))
            s1p = eP(nc.sbuf_tensor("s1p", [P, 4], F32))
            s2p = eP(nc.sbuf_tensor("s2p", [P, 4], F32))
            mup = eP(nc.sbuf_tensor("mup", [P, 4], F32))
            musqp = eP(nc.sbuf_tensor("musqp", [P, 4], F32))
            varp = eP(nc.sbuf_tensor("varp", [P, 4], F32))
            stdp = eP(nc.sbuf_tensor("stdp", [P, 4], F32))
            rstdp = eP(nc.sbuf_tensor("rstdp", [P, 4], F32))
            psTp = eP(nc.psum_tensor("psTp", [P, D], F32))
            psPr = [eP(nc.psum_tensor(f"psPr{i}", [P, 512], F32)) for i in range(2)]
            psPacc = [eP(nc.psum_tensor(f"psPacc{i}", [P, 512], F32)) for i in range(2)]
            psZacc = eP(nc.psum_tensor("psZacc", [P, NHEAD], F32))

            NTP = 48
            bP = {s: s.n for s in (ldsy, ldgp, pe, dv, ac, gp)}

            def adv(i):
                return bP[dv] + 9 * i + 3 * (i // 16)

            def aac(i):
                return bP[ac] + 16 + 4 * i

            def ape(i):
                return bP[pe] + 6 * i

            def ald(i):
                return bP[ldsy] + 16 * 16 + 16 * (i + 1)

            blkP = nc.Block("sP"); blkP.__enter__()

            @blkP.gpsimd
            def _(g):
                g.wait_ge(dv.h, marks["xnfm_dv"])
                gp.inc(g.memset(base_acc[:], 0.0))
                gp.inc(g.memset(zb_acc[:], 0.0))
                ldgp.inc(g.dma_start(out=lnw_sb[:], in_=lnw_in), 16)
            GP_PRE = bP[gp] + 2

            @blkP.sync
            def _(sy):
                sy.wait_ge(dv.h, marks["xnfm_dv"])
                sy.wait_ge(ac.h, bP[ac])
                sy.wait_ge(pe.h, bP[pe])
                for k in range(16):
                    if k >= 2:
                        sy.wait_ge(ac.h, bP[ac] + k - 1)
                    if k < 8:
                        ap_ = wkT.rearrange("(c p) o -> p c o", p=P)[:, k:k + 1, :]
                    else:
                        ap_ = wvT.rearrange("(c p) o -> p c o", p=P)[:, k - 8:k - 7, :]
                    ldsy.inc(sy.dma_start(
                        out=wcb[:, (k % 2) * D:(k % 2 + 1) * D].unsqueeze(1), in_=ap_), 16)
                for i in range(NTP):
                    r, tt = divmod(i, 16)
                    par = i % 4
                    if i >= 4:
                        sy.wait_ge(ac.h, aac(i - 4) + 1)
                        sy.wait_ge(dv.h, adv(i - 4) + 6)
                    ldsy.inc(sy.dma_start(
                        out=xpr[:, par * D:(par + 1) * D],
                        in_=xprev[r * TC + tt * P:r * TC + (tt + 1) * P, :]), 16)

            @blkP.vector
            def _(v):
                for i in range(NTP):
                    r, tt = divmod(i, 16)
                    par = i % 4
                    pc = par
                    v.wait_ge(ldsy.h, ald(i))
                    dv.inc(v.reduce_sum(s1p[:, pc:pc + 1], xpr[:, par * D:(par + 1) * D], axis=AX.X))
                    v.wait_ge(dv.h, adv(i) + 1)
                    dv.inc(v.tensor_scalar(mup[:, pc:pc + 1], s1p[:, pc:pc + 1], 1.0 / D, None, ALU.mult))
                    v.wait_ge(dv.h, adv(i) + 2)
                    dv.inc(v.tensor_mul(musqp[:, pc:pc + 1], mup[:, pc:pc + 1], mup[:, pc:pc + 1]))
                    v.wait_ge(ac.h, aac(i) + 1)
                    v.wait_ge(dv.h, adv(i) + 3)
                    dv.inc(v.scalar_tensor_tensor(varp[:, pc:pc + 1], s2p[:, pc:pc + 1], 1.0 / D,
                                                  musqp[:, pc:pc + 1], ALU.mult, ALU.subtract))
                    v.wait_ge(ac.h, aac(i) + 2)
                    dv.inc(v.reciprocal(rstdp[:, pc:pc + 1], stdp[:, pc:pc + 1]))
                    v.wait_ge(dv.h, adv(i) + 5)
                    if i >= 4:
                        v.wait_ge(pe.h, ape(i - 4) + 1)
                    dv.inc(v.tensor_scalar(xbfp[:, par * D:(par + 1) * D],
                                           xpr[:, par * D:(par + 1) * D],
                                           mup[:, pc:pc + 1], rstdp[:, pc:pc + 1],
                                           ALU.subtract, ALU.mult))
                    v.wait_ge(pe.h, ape(i) + 1)
                    dv.inc(v.tensor_copy(xfmp[:, par * D:(par + 1) * D], psTp[:, 0:D]))
                    v.wait_ge(pe.h, ape(i) + 4)
                    dv.inc(v.tensor_copy(vp_sb[:, par * D:par * D + 512], psPr[0][:, 0:512]))
                    v.wait_ge(pe.h, ape(i) + 5)
                    dv.inc(v.tensor_copy(vp_sb[:, par * D + 512:(par + 1) * D], psPr[1][:, 0:512]))
                    if tt == 15:
                        v.wait_ge(pe.h, ape(i) + 6)
                        v.wait_ge(ldgp.h, LD_PMASK)
                        if r == 0:
                            v.wait_ge(gp.h, GP_PRE)
                        v.wait_ge(dv.h, adv(i) + 9)
                        dv.inc(v.scalar_tensor_tensor(base_acc[:, 0:512], psPacc[0][:, 0:512],
                                                      pmask_sb[:, r:r + 1], base_acc[:, 0:512],
                                                      ALU.mult, ALU.add))
                        dv.inc(v.scalar_tensor_tensor(base_acc[:, 512:1024], psPacc[1][:, 0:512],
                                                      pmask_sb[:, r:r + 1], base_acc[:, 512:1024],
                                                      ALU.mult, ALU.add))
                        dv.inc(v.scalar_tensor_tensor(zb_acc[:], psZacc[:, 0:NHEAD],
                                                      pmask_sb[:, r:r + 1], zb_acc[:],
                                                      ALU.mult, ALU.add))

            @blkP.scalar
            def _(s):
                s.wait_ge(gp.h, GP_CONST)
                s.wait_ge(dv.h, marks["xnfm_dv"])
                s.wait_ge(pe.h, bP[pe])
                for k in range(16):
                    s.wait_ge(ldsy.h, bP[ldsy] + 16 * (k + 1))
                    dst = (wkp_sb if k < 8 else wvp_sb)
                    c = k % 8
                    ac.inc(s.activation(dst[:, c * D:(c + 1) * D],
                                        wcb[:, (k % 2) * D:(k % 2 + 1) * D], AF.Copy))
                for i in range(NTP):
                    par = i % 4
                    pc = par
                    s.wait_ge(ldsy.h, ald(i))
                    if i >= 1:
                        s.wait_ge(ac.h, aac(i))
                    if i >= 4:
                        s.wait_ge(dv.h, adv(i - 4) + 4)
                    ac.inc(s.activation(scrp[:], xpr[:, par * D:(par + 1) * D], AF.Square,
                                        accum_out=s2p[:, pc:pc + 1]))
                    s.wait_ge(dv.h, adv(i) + 4)
                    ac.inc(s.activation(stdp[:, pc:pc + 1], varp[:, pc:pc + 1], AF.Sqrt,
                                        bias=eps_col[:]))
                    s.wait_ge(pe.h, ape(i) + 2)
                    ac.inc(s.activation(ap_sb[:, par * D:par * D + 512], psPr[0][:, 0:512], AF.Exp))
                    s.wait_ge(pe.h, ape(i) + 3)
                    ac.inc(s.activation(ap_sb[:, par * D + 512:(par + 1) * D], psPr[1][:, 0:512], AF.Exp))

            @blkP.tensor
            def _(t_):
                t_.wait_ge(gp.h, GP_CONST)
                t_.wait_ge(ac.h, bP[ac] + 16)
                t_.wait_ge(dv.h, marks["xnfm_dv"])
                for i in range(NTP):
                    r, tt = divmod(i, 16)
                    par = i % 4
                    t_.wait_ge(dv.h, adv(i) + 6)
                    if i >= 1:
                        t_.wait_ge(dv.h, adv(i - 1) + 7)
                    for c in range(DC):
                        inst = t_.matmul(psTp[:, c * P:(c + 1) * P],
                                         xbfp[:, par * D + c * P:par * D + (c + 1) * P],
                                         ident[:], start=True, stop=True)
                    pe.inc(inst)
                    t_.wait_ge(dv.h, adv(i) + 7)
                    if i >= 1:
                        t_.wait_ge(dv.h, adv(i - 1) + 8)
                    for c in range(DC):
                        inst = t_.matmul(psPr[0][:, 0:512],
                                         xfmp[:, par * D + c * P:par * D + (c + 1) * P],
                                         wkp_sb[:, c * D:c * D + 512],
                                         start=(c == 0), stop=(c == DC - 1))
                    pe.inc(inst)
                    if i >= 1:
                        t_.wait_ge(dv.h, adv(i - 1) + 9)
                    for c in range(DC):
                        inst = t_.matmul(psPr[1][:, 0:512],
                                         xfmp[:, par * D + c * P:par * D + (c + 1) * P],
                                         wkp_sb[:, c * D + 512:(c + 1) * D],
                                         start=(c == 0), stop=(c == DC - 1))
                    pe.inc(inst)
                    t_.wait_ge(ac.h, aac(i) + 3)
                    for c in range(DC):
                        inst = t_.matmul(psPr[0][:, 0:512],
                                         xfmp[:, par * D + c * P:par * D + (c + 1) * P],
                                         wvp_sb[:, c * D:c * D + 512],
                                         start=(c == 0), stop=(c == DC - 1))
                    pe.inc(inst)
                    t_.wait_ge(ac.h, aac(i) + 4)
                    for c in range(DC):
                        inst = t_.matmul(psPr[1][:, 0:512],
                                         xfmp[:, par * D + c * P:par * D + (c + 1) * P],
                                         wvp_sb[:, c * D + 512:(c + 1) * D],
                                         start=(c == 0), stop=(c == DC - 1))
                    pe.inc(inst)
                    t_.wait_ge(dv.h, adv(i) + 9)
                    if tt == 0:
                        if r >= 1:
                            t_.wait_ge(dv.h, adv(i))
                        t_.matmul(psPacc[0][:, 0:512], zrow_bf[0:1, 0:P],
                                  zrow_bf[0:1, 0:512], start=True, stop=False)
                        t_.matmul(psPacc[1][:, 0:512], zrow_bf[0:1, 0:P],
                                  zrow_bf[0:1, 0:512], start=True, stop=False)
                        t_.matmul(psZacc[:, 0:NHEAD], zrow_bf[0:1, 0:P],
                                  zrow_bf[0:1, 0:NHEAD], start=True, stop=False)
                    for h in range(NHEAD):
                        t_.matmul(psPacc[h // 4][:, (h % 4) * DK:(h % 4 + 1) * DK],
                                  ap_sb[:, par * D + h * DK:par * D + (h + 1) * DK],
                                  vp_sb[:, par * D + h * DK:par * D + (h + 1) * DK],
                                  start=False, stop=(tt == 15 and h % 4 == 3))
                    for h in range(NHEAD):
                        inst = t_.matmul(psZacc[:, h:h + 1],
                                         ap_sb[:, par * D + h * DK:par * D + (h + 1) * DK],
                                         ones_col_bf[:],
                                         start=False, stop=(tt == 15 and h == 7))
                    pe.inc(inst)
            blkP.__exit__(None, None, None)
            marks["pre_dv"] = dv.n
            marks["pre_ac"] = ac.n
            marks["pre_pe"] = pe.n

            if upto == "P":
                blkD = nc.Block("dbgP"); blkD.__enter__()

                @blkD.gpsimd
                def _(g):
                    gfeed.emit_loads(g, limit=gfeed.k)

                @blkD.sync
                def _(sy):
                    sy.wait_ge(dv.h, marks["pre_dv"])
                    stsy.inc(sy.dma_start(out=dbg[0:D, :].rearrange("(h k) v -> k h v", k=P),
                                          in_=base_acc[:].rearrange("k (h v) -> k h v", h=NHEAD)), 16)
                    stsy.inc(sy.dma_start(out=dbg[D:D + P, 0:NHEAD], in_=zb_acc[:]), 16)
                    sy.wait_ge(stsy.h, stsy.n)
                blkD.__exit__(None, None, None)

        if upto == "P":
            close_stacks()
            return nc

        q_sb = esQ.enter_context(nc.sbuf_tensor("q_sb", [P, DC * TC], BF16))
        afm_sb = esActs.enter_context(nc.sbuf_tensor("afm_sb", [P, DC * TC], BF16))
        atm_sb = esActs.enter_context(nc.sbuf_tensor("atm_sb", [P, NT * D], BF16))
        vtm_sb = esActs.enter_context(nc.sbuf_tensor("vtm_sb", [P, NT * D], BF16))

        def emit_proj_stage(name, wT, tasks, barrier, extra_scalar=None,
                            extra_gpsimd=None, feeder=None, feeds_during=0):
            with ExitStack() as esBx:
                eB = esBx.enter_context
                w_sb = eB(nc.sbuf_tensor(f"w_{name}", [P, DC * D], BF16))
                wcbB = eB(nc.sbuf_tensor(f"wcb_{name}", [P, 2 * D], F32))
                psB = [eB(nc.psum_tensor(f"ps_{name}{i}", [P, 512], F32)) for i in range(4)]
                b = {s: s.n for s in (ldsy, pe, dv, ac)}
                wT3 = wT.rearrange("(c p) o -> p c o", p=P)
                blk = nc.Block(f"s_{name}"); blk.__enter__()

                scalar_evac = tasks[0][0] != "v"

                def evac_pos(i):
                    if i < feeds_during:
                        return b[ac] + DC + 2 * i + 1
                    return b[ac] + DC + 2 * feeds_during + (i - feeds_during) + 1

                @blk.sync
                def _(sy):
                    for sem, val in barrier.get("sync", []):
                        sy.wait_ge(sem, val)
                    for k in range(DC):
                        if k >= 2:
                            sy.wait_ge(ac.h, b[ac] + k - 1)
                        ldsy.inc(sy.dma_start(
                            out=wcbB[:, (k % 2) * D:(k % 2 + 1) * D].unsqueeze(1),
                            in_=wT3[:, k:k + 1, :]), 16)

                @blk.scalar
                def _(s):
                    for sem, val in barrier.get("scalar", []):
                        s.wait_ge(sem, val)
                    for k in range(DC):
                        s.wait_ge(ldsy.h, b[ldsy] + 16 * (k + 1))
                        ac.inc(s.activation(w_sb[:, k * D:(k + 1) * D],
                                            wcbB[:, (k % 2) * D:(k % 2 + 1) * D], AF.Copy))
                    if scalar_evac:
                        for i, task in enumerate(tasks):
                            pst = psB[i % 4]
                            s.wait_ge(pe.h, b[pe] + i + 1)
                            kind = task[0]
                            if kind in ("q", "a"):
                                _, o, tt = task
                                buf = q_sb if kind == "q" else afm_sb
                                fn = AF.Silu if kind == "q" else AF.Exp
                                ac.inc(s.activation(buf[:, o * TC + tt * 512:o * TC + (tt + 1) * 512],
                                                    pst[:, 0:512], fn))
                            else:
                                _, tt, hf = task
                                ac.inc(s.activation(atm_sb[:, tt * D + hf * 512:tt * D + (hf + 1) * 512],
                                                    pst[:, 0:512], AF.Exp))
                            if feeder is not None and i < feeds_during:
                                feeder.feed(s, 1)
                    if extra_scalar is not None:
                        extra_scalar(s)

                if not scalar_evac:
                    @blk.vector
                    def _(v):
                        for sem, val in barrier.get("vector", []):
                            v.wait_ge(sem, val)
                        for i, task in enumerate(tasks):
                            pst = psB[i % 4]
                            v.wait_ge(pe.h, b[pe] + i + 1)
                            _, tt, hf = task
                            dv.inc(v.tensor_copy(vtm_sb[:, tt * D + hf * 512:tt * D + (hf + 1) * 512],
                                                 pst[:, 0:512]))

                @blk.tensor
                def _(t_):
                    t_.wait_ge(ac.h, b[ac] + DC)
                    for sem, val in barrier.get("tensor", []):
                        t_.wait_ge(sem, val)
                    for i, task in enumerate(tasks):
                        pst = psB[i % 4]
                        if i >= 4:
                            if scalar_evac:
                                t_.wait_ge(ac.h, evac_pos(i - 4))
                            else:
                                t_.wait_ge(dv.h, b[dv] + i - 3)
                        kind = task[0]
                        if kind in ("q", "a"):
                            _, o, tt = task
                            for ccx in range(DC):
                                inst = t_.matmul(pst[:, 0:512],
                                                 w_sb[:, ccx * D + o * P:ccx * D + (o + 1) * P],
                                                 xnfm[:, ccx * TC + tt * 512:ccx * TC + (tt + 1) * 512],
                                                 start=(ccx == 0), stop=(ccx == DC - 1))
                        else:
                            _, tt, hf = task
                            for ccx in range(DC):
                                inst = t_.matmul(pst[:, 0:512],
                                                 xnfm[:, ccx * TC + tt * P:ccx * TC + (tt + 1) * P],
                                                 w_sb[:, ccx * D + hf * 512:ccx * D + (hf + 1) * 512],
                                                 start=(ccx == 0), stop=(ccx == DC - 1))
                        pe.inc(inst)

                if extra_gpsimd is not None:
                    @blk.gpsimd
                    def _(g):
                        extra_gpsimd(g)
                blk.__exit__(None, None, None)

        barrier1 = {"sync": [(dv.h, marks["pre_dv"]), (ac.h, marks["pre_ac"]),
                             (pe.h, marks["pre_pe"])],
                    "scalar": [(dv.h, marks["pre_dv"])],
                    "tensor": [(dv.h, marks["pre_dv"]), (dv.h, marks["xnfm_dv"])]}
        emit_proj_stage("b1a", wqT, [("q", o, t) for o in range(DC) for t in range(4)],
                        barrier1, feeder=gfeed, feeds_during=0)

        bprev = {"sync": [(pe.h, pe.n), (ac.h, ac.n)],
                 "scalar": [], "vector": [], "tensor": []}
        emit_proj_stage("b1b", wkT,
                        [("a", o, t) for o in range(DC) for t in range(4)]
                        + [("at", tt, hf) for tt in range(NT) for hf in range(2)],
                        bprev, feeder=gfeed, feeds_during=0,
                        extra_scalar=lambda s: gfeed.feed(s, 999),
                        extra_gpsimd=lambda g: gfeed.emit_loads(g))
        assert gfeed.k == len(gchunks)
        marks["b1_ac"] = ac.n
        marks["gamma_ac"] = gfeed.gmark[len(gchunks) - 1]
        bprev2 = {"sync": [(pe.h, pe.n), (ac.h, ac.n)],
                  "scalar": [], "vector": [(ac.h, ac.n)], "tensor": []}
        emit_proj_stage("b2", wvT, [("v", tt, hf) for tt in range(NT) for hf in range(2)],
                        bprev2)
        marks["vtm_dv"] = dv.n

        if upto == "B":
            blkD = nc.Block("dbgB"); blkD.__enter__()
            d4 = dbg.rearrange("(r c p) t -> r p c t", r=4, p=P)

            @blkD.gpsimd
            def _(g):
                g.wait_ge(ac.h, marks["b1_ac"])
                g.wait_ge(dv.h, marks["vtm_dv"])
                stsy.inc(g.dma_start(out=d4[0], in_=q_sb[:].rearrange("p (c t) -> p c t", c=DC)), 16)
                stsy.inc(g.dma_start(out=d4[1], in_=afm_sb[:].rearrange("p (c t) -> p c t", c=DC)), 16)
                stsy.inc(g.dma_start(out=dbg.rearrange("(r t p) d -> r p t d", r=4, p=P)[2],
                                     in_=atm_sb[:].rearrange("p (t d) -> p t d", t=NT)), 16)
                stsy.inc(g.dma_start(out=dbg.rearrange("(r t p) d -> r p t d", r=4, p=P)[3],
                                     in_=vtm_sb[:].rearrange("p (t d) -> p t d", t=NT)), 16)
                g.wait_ge(stsy.h, stsy.n)
            blkD.__exit__(None, None, None)
            close_stacks()
            return nc

        with ExitStack() as esTh:
            psTh = esTh.enter_context(nc.psum_tensor("psTh", [P, 512], F32))
            bT = {s: s.n for s in (pe, dv, gp)}
            blkTh = nc.Block("sTh"); blkTh.__enter__()

            @blkTh.vector
            def _(v):
                v.wait_ge(ac.h, marks["gamma_ac"])
                dv.inc(v.reduce_sum(gb2[:, 0:1], ga_fc[:], axis=AX.X))
                dv.inc(v.reduce_sum(gb2[:, 1:2], ga_p[:], axis=AX.X))
                v.wait_ge(pe.h, bT[pe] + 1)
                dv.inc(v.tensor_scalar(thr_rows[:], psTh[0:1, 0:2], 0.5 / (H4 * D), 5e-7,
                                       ALU.mult, ALU.add))
                v.wait_ge(pe.h, bT[pe] + 2)
                dv.inc(v.tensor_copy(thr_cols[:], psTh[:, 4:6]))
                v.wait_ge(dv.h, bT[dv] + 4)
                dv.inc(v.tensor_scalar(nthr_cols[:], thr_cols[:], -1.0, None, ALU.mult))

            @blkTh.tensor
            def _(t_):
                t_.wait_ge(dv.h, bT[dv] + 2)
                pe.inc(t_.matmul(psTh[0:1, 0:2], onecol_f[:], gb2[:], start=True, stop=True))
                t_.wait_ge(dv.h, bT[dv] + 3)
                pe.inc(t_.matmul(psTh[:, 4:6], ones_row_f[:], thr_rows[0:1, 0:2],
                                 start=True, stop=True))
            blkTh.__exit__(None, None, None)
            marks["thr_dv"] = dv.n






        o_sb = q_sb
        eAct = esActs.enter_context
        P_sb = eAct(nc.sbuf_tensor("P_sb", [P, NHEAD * DK], BF16))
        zbase = zb_acc
        zc_sb = eAct(nc.sbuf_tensor("zc_sb", [P, 2 * NHEAD * DK], F32))
        rz_sb = eAct(nc.sbuf_tensor("rz_sb", [P, 2 * DK], F32))
        qt_sb = eAct(nc.sbuf_tensor("qt_sb", [P, 2 * DK], BF16))
        at_sb = eAct(nc.sbuf_tensor("at_sb", [P, 2 * DK], BF16))

        NIT = NT * NHEAD

        with ExitStack() as esD:
            eD = esD.enter_context
            psP = [eD(nc.psum_tensor(f"psP{i}", [P, 512], F32)) for i in range(2)]
            psA = [eD(nc.psum_tensor(f"psA{i}", [P, 512], F32)) for i in range(2)]
            psO = [eD(nc.psum_tensor(f"psO{i}", [P, 512], F32)) for i in range(2)]

            b = {s: s.n for s in (ldsy, pe, dv, ac)}
            bD = dict(b)
            blkDm = nc.Block("sD"); blkDm.__enter__()

            DV0 = 1
            PE0 = 2

            def peAT(it):
                return bD[pe] + PE0 + 3 * it + 1

            def peO(it):
                return bD[pe] + PE0 + 3 * it + 2

            def peP(it):
                return bD[pe] + PE0 + 3 * it + 3

            def dvScan(it):
                return bD[dv] + DV0 + 4 * it + 1

            def dvRecip(it):
                return bD[dv] + DV0 + 4 * it + 2

            def dvQt(it):
                return bD[dv] + DV0 + 4 * it + 3

            def dvAtm(it):
                return bD[dv] + DV0 + 4 * it + 4

            def acOEvac(it):
                ci, h = divmod(it, NHEAD)
                return bD[ac] + 16 * ci + h + 1

            def acPEvac(ci, h):
                return bD[ac] + 16 * ci + 8 + h + 1

            @blkDm.vector
            def _(v):
                v.wait_ge(dv.h, marks["pre_dv"])
                dv.inc(v.tensor_copy(P_sb[:], base_acc[:]))
                for it in range(NIT):
                    ci, h = divmod(it, NHEAD)
                    par = it % 2
                    cpar = ci % 2
                    if ci == 0:
                        init = zbase[:, h:h + 1]
                        v.wait_ge(dv.h, bD[dv] + DV0)
                    zc_out = zc_sb[:, cpar * NHEAD * DK + h * DK:cpar * NHEAD * DK + (h + 1) * DK]
                    if ci > 0:
                        pcpar = 1 - cpar
                        init = zc_sb[:, pcpar * NHEAD * DK + (h + 1) * DK - 1:
                                     pcpar * NHEAD * DK + (h + 1) * DK]
                        v.wait_ge(dv.h, dvScan(it - NHEAD))
                    dv.inc(v.tensor_tensor_scan(
                        zc_out, afm_sb[:, h * TC + ci * DK:h * TC + (ci + 1) * DK],
                        zeros_f[:], init, ALU.add, ALU.add))
                    v.wait_ge(dv.h, dvScan(it))
                    dv.inc(v.reciprocal(rz_sb[:, par * DK:(par + 1) * DK], zc_out))
                    v.wait_ge(dv.h, dvRecip(it))
                    if it >= 2:
                        v.wait_ge(pe.h, peO(it - 2))
                    dv.inc(v.tensor_mul(qt_sb[:, par * DK:(par + 1) * DK],
                                        q_sb[:, h * TC + ci * DK:h * TC + (ci + 1) * DK],
                                        rz_sb[:, par * DK:(par + 1) * DK]))
                    v.wait_ge(pe.h, peAT(it))
                    dv.inc(v.tensor_mul(at_sb[:, par * DK:(par + 1) * DK],
                                        psA[par][:, 0:DK], mask[:]))

            @blkDm.tensor
            def _(t_):
                t_.wait_ge(gp.h, GP_CONST)
                t_.wait_ge(dv.h, bD[dv] + 1)
                for bk in range(2):
                    pe.inc(t_.matmul(psP[bk][:, 0:512], ident[:], P_sb[:, bk * 512:(bk + 1) * 512],
                                     start=True, stop=False))
                for it in range(NIT):
                    ci, h = divmod(it, NHEAD)
                    par = it % 2
                    t_.wait_ge(dv.h, dvQt(it))
                    if it >= 2:
                        t_.wait_ge(dv.h, dvAtm(it - 2))
                    pe.inc(t_.matmul(psA[par][:, 0:DK],
                                     afm_sb[:, h * TC + ci * DK:h * TC + (ci + 1) * DK],
                                     qt_sb[:, par * DK:(par + 1) * DK],
                                     start=True, stop=True))
                    t_.wait_ge(dv.h, dvAtm(it))
                    if ci > 0:
                        t_.wait_ge(ac.h, acPEvac(ci - 1, h))
                    if it >= 2:
                        t_.wait_ge(ac.h, acOEvac(it - 2))
                    t_.matmul(psO[par][:, 0:DK],
                              vtm_sb[:, ci * D + h * DK:ci * D + (h + 1) * DK],
                              at_sb[:, par * DK:(par + 1) * DK],
                              start=True, stop=False)
                    pe.inc(t_.matmul(psO[par][:, 0:DK],
                                     P_sb[:, h * DK:(h + 1) * DK],
                                     qt_sb[:, par * DK:(par + 1) * DK],
                                     start=False, stop=True))
                    if ci > 0:
                        t_.wait_ge(ac.h, acPEvac(ci - 1, 7))
                    pe.inc(t_.matmul(psP[h // 4][:, (h % 4) * DK:(h % 4 + 1) * DK],
                                     atm_sb[:, ci * D + h * DK:ci * D + (h + 1) * DK],
                                     vtm_sb[:, ci * D + h * DK:ci * D + (h + 1) * DK],
                                     start=False,
                                     stop=(ci == NT - 1 and h % 4 == 3)))

            @blkDm.scalar
            def _(s):
                s.wait_ge(gp.h, GP_CONST)
                for ci in range(NT):
                    for h in range(NHEAD):
                        it = ci * NHEAD + h
                        par = it % 2
                        s.wait_ge(pe.h, peO(it))
                        ac.inc(s.activation(o_sb[:, h * TC + ci * DK:h * TC + (ci + 1) * DK],
                                            psO[par][:, 0:DK], AF.Copy,
                                            scale=scale_col[:]))
                    s.wait_ge(pe.h, peP(ci * NHEAD + 7))
                    for h in range(NHEAD):
                        ac.inc(s.activation(P_sb[:, h * DK:(h + 1) * DK],
                                            psP[h // 4][:, (h % 4) * DK:(h % 4 + 1) * DK],
                                            AF.Copy))
            blkDm.__exit__(None, None, None)

        esActs.close()

        PE_L = pe.n
        DV_L = dv.n
        AC_L = ac.n

        with ExitStack() as esDf:
            eD = esDf.enter_context
            wo_sb = eD(nc.sbuf_tensor("wo_sb", [P, DC * D], BF16))
            wg1_sb = eD(nc.sbuf_tensor("wg1_sb", [P, DC * 128], BF16))
            wg2_sb = eD(nc.sbuf_tensor("wg2_sb", [P, D], BF16))
            g1_blk = eD(nc.sbuf_tensor("g1_blk", [P, 2 * 512], BF16))
            sg_blk = eD(nc.sbuf_tensor("sg_blk", [P, 4 * DC * 512], BF16))
            gn_sb = eD(nc.sbuf_tensor("gn_sb", [P, DC], F32))
            o2_sb = eD(nc.sbuf_tensor("o2_sb", [P, 2 * 512], BF16))
            sgp_sb = eD(nc.sbuf_tensor("sgp_sb", [P, 2 * 512], BF16))
            ms_row = eD(nc.sbuf_tensor("ms_row", [1, TC], F32))
            wob = eD(nc.sbuf_tensor("wob", [P, 2 * 512], F32))
            psS = [eD(nc.psum_tensor(f"psS{i}", [P, 512], F32)) for i in range(2)]
            psR = [eD(nc.psum_tensor(f"psR{i}", [P, 512], F32)) for i in range(2)]
            psW = [eD(nc.psum_tensor(f"psW{i}", [P, 512], F32)) for i in range(2)]
            psG1 = eD(nc.psum_tensor("psG1", [P, 512], F32))
            psSG = eD(nc.psum_tensor("psSG", [P, 512], F32))

            wcbD = eD(nc.sbuf_tensor("wcbD", [P, 2 * D], F32))
            bF = {s: s.n for s in (ldsy, ldgp, pe, dv, ac, stsy)}
            blkDf = nc.Block("sDf"); blkDf.__enter__()

            nPE = bF[pe]
            peG1 = {}; peSG = {}; peStat = {}; peRep = {}; peWo = {}
            for t in range(4):
                nPE += 1; peG1[t] = nPE
                for o in range(DC):
                    nPE += 1; peSG[(t, o)] = nPE
            for j in range(32):
                nPE += 1; peStat[j] = nPE
            for t in range(4):
                nPE += 1; peRep[t] = nPE
            for idx in range(32):
                nPE += 1; peWo[idx] = nPE
            nDV = bF[dv]
            dvG1e = {}; dvO2 = {}; dvVar = {}; dvSgp = {}; dvOf = {}; dvWoE = {}
            for t in range(4):
                nDV += 1; dvG1e[t] = nDV
            for t in range(4):
                for c in range(8):
                    nDV += 1; dvO2[t * 8 + c] = nDV
                nDV += 1; dvVar[t] = nDV
            nDV += 1; DV_RSTD = nDV
            for j in range(32):
                nDV += 1; dvSgp[j] = nDV
                nDV += 1; dvOf[j] = nDV
            for idx in range(32):
                nDV += 1; dvWoE[idx] = nDV
            nAC = bF[ac] + 10
            acSilu = {}
            for t in range(4):
                for o in range(DC):
                    nAC += 1; acSilu[(t, o)] = nAC
            AC_SQRT = nAC + 1
            def stWo(idx):
                return bF[stsy] + 16 * (idx + 1)

            wo3 = wo_in.rearrange("(c p) o -> p c o", p=P)

            @blkDf.sync
            def _(sy):
                sy.wait_ge(pe.h, PE_L)
                sy.wait_ge(dv.h, DV_L)
                sy.wait_ge(ac.h, AC_L)
                for k in range(10):
                    if k >= 2:
                        sy.wait_ge(ac.h, bF[ac] + k - 1)
                    sl = wcbD[:, (k % 2) * D:(k % 2 + 1) * D]
                    if k < 8:
                        ldsy.inc(sy.dma_start(out=sl.unsqueeze(1),
                                              in_=wo3[:, k:k + 1, :]), 16)
                    elif k == 8:
                        ldsy.inc(sy.dma_start(
                            out=sl.rearrange("p (c o) -> p c o", c=DC),
                            in_=wg1T.rearrange("(c p) o -> p c o", p=P)), 16)
                    else:
                        ldsy.inc(sy.dma_start(out=sl.unsqueeze(1),
                                              in_=wg2T.unsqueeze(1)), 16)
                ldsy.inc(sy.dma_start(out=gn_sb[:], in_=gn_in), 16)
            LD_GN = ldsy.n

            @blkDf.tensor
            def _(t_):
                t_.wait_ge(ac.h, bF[ac] + 10)
                for t in range(4):
                    if t >= 1:
                        t_.wait_ge(dv.h, dvG1e[t - 1])
                    for ccx in range(DC):
                        inst = t_.matmul(psG1[:, 0:512],
                                         wg1_sb[:, ccx * 128:(ccx + 1) * 128],
                                         xnfm[:, ccx * TC + t * 512:ccx * TC + (t + 1) * 512],
                                         start=(ccx == 0), stop=(ccx == DC - 1))
                    pe.inc(inst)
                    for o in range(DC):
                        t_.wait_ge(dv.h, dvG1e[t])
                        if (t, o) != (0, 0):
                            po = (t, o - 1) if o else (t - 1, 7)
                            t_.wait_ge(ac.h, acSilu[po])
                        pe.inc(t_.matmul(psSG[:, 0:512],
                                         wg2_sb[:, o * P:(o + 1) * P],
                                         g1_blk[:, (t % 2) * 512:(t % 2 + 1) * 512],
                                         start=True, stop=True))
                for j in range(32):
                    t, c = divmod(j, 8)
                    t_.wait_ge(dv.h, dvO2[j])
                    if t >= 2 and c == 0:
                        t_.wait_ge(dv.h, dvVar[t - 2])
                    pe.inc(t_.matmul(psS[t % 2][0:1, 0:512], ones_col_bf[:],
                                     o2_sb[:, (j % 2) * 512:(j % 2 + 1) * 512],
                                     start=(c == 0), stop=(c == 7)))
                for t in range(4):
                    t_.wait_ge(dv.h, DV_RSTD)
                    if t >= 2:
                        t_.wait_ge(dv.h, dvOf[8 * (t - 2) + 7])
                    pe.inc(t_.matmul(psR[t % 2][:, 0:512], ones_row_f[:],
                                     ms_row[0:1, t * 512:(t + 1) * 512],
                                     start=True, stop=True))
                for idx in range(32):
                    t128, hf = divmod(idx, 2)
                    t512 = t128 // 4
                    t_.wait_ge(dv.h, dvOf[8 * t512 + 7])
                    if idx >= 2:
                        t_.wait_ge(dv.h, dvWoE[idx - 2])
                    for c in range(DC):
                        inst = t_.matmul(psW[idx % 2][:, 0:512],
                                         o_sb[:, c * TC + t128 * P:c * TC + (t128 + 1) * P],
                                         wo_sb[:, c * D + hf * 512:c * D + (hf + 1) * 512],
                                         start=(c == 0), stop=(c == DC - 1))
                    pe.inc(inst)

            @blkDf.vector
            def _(v):
                v.wait_ge(ac.h, AC_L)
                v.wait_ge(ldsy.h, LD_GN)
                for t in range(4):
                    v.wait_ge(pe.h, peG1[t])
                    dv.inc(v.tensor_copy(g1_blk[:, (t % 2) * 512:(t % 2 + 1) * 512],
                                         psG1[:, 0:512]))
                for t in range(4):
                    for c in range(8):
                        j = t * 8 + c
                        if j >= 2:
                            v.wait_ge(pe.h, peStat[j - 2])
                        src_ = o_sb[:, c * TC + t * 512:c * TC + (t + 1) * 512]
                        dv.inc(v.tensor_mul(o2_sb[:, (j % 2) * 512:(j % 2 + 1) * 512], src_, src_))
                    v.wait_ge(pe.h, peStat[8 * t + 7])
                    dv.inc(v.tensor_scalar(ms_row[0:1, t * 512:(t + 1) * 512],
                                           psS[t % 2][0:1, 0:512], 1.0 / D, 1e-5,
                                           ALU.mult, ALU.add))
                v.wait_ge(ac.h, AC_SQRT)
                dv.inc(v.reciprocal(ms_row[:], ms_row[:]))
                for j in range(32):
                    t, c = divmod(j, 8)
                    par = j % 2
                    v.wait_ge(pe.h, peRep[t])
                    v.wait_ge(ac.h, acSilu[(t, c)])
                    dv.inc(v.tensor_mul(sgp_sb[:, par * 512:(par + 1) * 512],
                                        sg_blk[:, t * DC * 512 + c * 512:t * DC * 512 + (c + 1) * 512],
                                        psR[t % 2][:, 0:512]))
                    v.wait_ge(dv.h, dvSgp[j])
                    dv.inc(v.scalar_tensor_tensor(o_sb[:, c * TC + t * 512:c * TC + (t + 1) * 512],
                                                  o_sb[:, c * TC + t * 512:c * TC + (t + 1) * 512],
                                                  gn_sb[:, c:c + 1],
                                                  sgp_sb[:, par * 512:(par + 1) * 512],
                                                  ALU.mult, ALU.mult))
                for idx in range(32):
                    v.wait_ge(pe.h, peWo[idx])
                    if idx >= 2:
                        v.wait_ge(stsy.h, stWo(idx - 2))
                    dv.inc(v.tensor_copy(wob[:, (idx % 2) * 512:(idx % 2 + 1) * 512],
                                         psW[idx % 2][:, 0:512]))

            @blkDf.scalar
            def _(s):
                for k in range(10):
                    s.wait_ge(ldsy.h, bF[ldsy] + 16 * (k + 1))
                    if k < 8:
                        dst = wo_sb[:, k * D:(k + 1) * D]
                    elif k == 8:
                        dst = wg1_sb[:]
                    else:
                        dst = wg2_sb[:]
                    ac.inc(s.activation(dst, wcbD[:, (k % 2) * D:(k % 2) * D + dst.shape[1]],
                                        AF.Copy))
                for t in range(4):
                    for o in range(DC):
                        s.wait_ge(pe.h, peSG[(t, o)])
                        ac.inc(s.activation(sg_blk[:, t * DC * 512 + o * 512:t * DC * 512 + (o + 1) * 512],
                                            psSG[:, 0:512], AF.Silu))
                s.wait_ge(dv.h, dvVar[3])
                ac.inc(s.activation(ms_row[:], ms_row[:], AF.Sqrt))

            @blkDf.sync
            def _(sy):
                a3 = attn0_d.ap().rearrange("(t p) d -> p t d", p=P)
                for idx in range(32):
                    t128, hf = divmod(idx, 2)
                    sy.wait_ge(dv.h, dvWoE[idx])
                    stsy.inc(sy.dma_start(
                        out=a3[:, t128:t128 + 1, hf * 512:(hf + 1) * 512],
                        in_=wob[:, (idx % 2) * 512:(idx % 2 + 1) * 512].unsqueeze(1)), 16)
            blkDf.__exit__(None, None, None)
            marks["attn0_st"] = stsy.n

        if upto == "D":
            close_stacks()
            return nc

        esQ.close()
        wfcT3 = wfcT.rearrange("(c p) f -> p c f", p=P)
        wpT3 = wpT.rearrange("(c p) f -> p c f", p=P)

        with ExitStack() as esE:
            eE = esE.enter_context
            h_sb = eE(nc.sbuf_tensor("h_sb", [P, HC * 512], BF16))
            h2_sb = eE(nc.sbuf_tensor("h2_sb", [P, 2 * 512], BF16))
            negmu_row = eE(nc.sbuf_tensor("negmu_row", [1, TC], F32))
            rstd_row = eE(nc.sbuf_tensor("rstd_rowE", [1, 512], F32))
            musq_s = eE(nc.sbuf_tensor("musq_s", [1, 512], F32))
            var_s = eE(nc.sbuf_tensor("var_s", [1, 512], F32))
            std_s = musq_s
            rstd_cols = eE(nc.sbuf_tensor("rstd_cols", [P, NT], F32))
            lnb_bf = eE(nc.sbuf_tensor("lnb_bf", [P, HC], BF16))
            lnbcb = eE(nc.sbuf_tensor("lnbcb", [P, HC], F32))
            c_row = eE(nc.sbuf_tensor("c_row", [1, D], F32))
            cc_row = eE(nc.sbuf_tensor("cc_row", [1, D], F32))
            psS1 = eE(nc.psum_tensor("psES1", [P, 512], F32))
            psS2 = eE(nc.psum_tensor("psES2", [P, 512], F32))

            bE = {s: s.n for s in (ldsy, ldgp, pe, dv, ac, gp, stsy)}
            E_dv, E_pe, E_ac, E_ld, E_ld2, E_st = (bE[dv], bE[pe], bE[ac],
                                                   bE[ldsy], bE[ldgp], bE[stsy])

            with ExitStack() as es_a:
                e1 = es_a.enter_context
                wfcq = e1(nc.sbuf_tensor("wfcq", [P, DC * H4], BF16))
                psF = [e1(nc.psum_tensor(f"psF{i}", [P, 512], F32)) for i in range(2)]

                blk = nc.Block("e1"); blk.__enter__()

                @blk.sync
                def _(sy):
                    sy.wait_ge(pe.h, E_pe)
                    sy.wait_ge(dv.h, E_dv)
                    sy.wait_ge(ac.h, E_ac)
                    sy.wait_ge(stsy.h, marks["attn0_st"])
                blk.__exit__(None, None, None)

                E1_dv = E_dv + 96
                E1_pe = E_pe
                E1_ac = E_ac
                fcchunks = [(fsl, c) for fsl in range(4) for c in range(DC)]
                E_ldg = ldgp.n

                def fc_tern_mark(k):
                    return E_dv + 3 * (k + 1)

                def fc_group_mark(fgrp8):
                    return fc_tern_mark(fgrp8 * 8 + 7)

                peN = {}
                n = E1_pe
                for t in range(4):
                    for f in range(HC):
                        n += 1
                        peN[("fc", t, f)] = n
                        if f >= 1:
                            n += 1
                            peN[("s1", t, f - 1)] = n
                            n += 1
                            peN[("s2", t, f - 1)] = n
                    n += 1
                    peN[("s1", t, HC - 1)] = n
                    n += 1
                    peN[("s2", t, HC - 1)] = n
                E2_pe = n
                acN = {}
                n = E1_ac
                for t in range(4):
                    for f in range(HC):
                        n += 1
                        acN[("gelu", t, f)] = n
                    n += 1
                    acN[("std", t)] = n
                E2_ac = n
                dvN = {}
                n = E1_dv
                for t in range(4):
                    for f in range(HC):
                        n += 1
                        dvN[("h2", t, f)] = n
                    for k in ("negmu", "musq", "var", "rstd"):
                        n += 1
                        dvN[(k, t)] = n
                E2_dv = n
                E2_st = E_st + 16 * 7

                blk2 = nc.Block("e2"); blk2.__enter__()

                @blk2.tensor
                def _(t_):
                    for t in range(4):
                        for f in range(HC):
                            t_.wait_ge(dv.h, fc_group_mark(f // 8))
                            pf = t * HC + f
                            if pf >= 2:
                                t2, f2 = divmod(pf - 2, HC)
                                t_.wait_ge(ac.h, acN[("gelu", t2, f2)])
                            for ccx in range(DC):
                                inst = t_.matmul(psF[pf % 2][:, 0:512],
                                                 wfcq[:, ccx * H4 + f * P:ccx * H4 + (f + 1) * P],
                                                 xnfm[:, ccx * TC + t * 512:ccx * TC + (t + 1) * 512],
                                                 start=(ccx == 0), stop=(ccx == DC - 1))
                            pe.inc(inst)
                            if f >= 1:
                                t_.wait_ge(ac.h, acN[("gelu", t, f - 1)])
                                pe.inc(t_.matmul(psS1[0:1, 0:512], ones_col_bf[:],
                                                 h_sb[:, (f - 1) * 512:f * 512],
                                                 start=(f == 1), stop=False))
                                t_.wait_ge(dv.h, dvN[("h2", t, f - 1)])
                                pe.inc(t_.matmul(psS2[0:1, 0:512], ones_col_bf[:],
                                                 h2_sb[:, ((f - 1) % 2) * 512:((f - 1) % 2 + 1) * 512],
                                                 start=(f == 1), stop=False))
                        t_.wait_ge(ac.h, acN[("gelu", t, HC - 1)])
                        pe.inc(t_.matmul(psS1[0:1, 0:512], ones_col_bf[:],
                                         h_sb[:, (HC - 1) * 512:HC * 512],
                                         start=False, stop=True))
                        t_.wait_ge(dv.h, dvN[("h2", t, HC - 1)])
                        pe.inc(t_.matmul(psS2[0:1, 0:512], ones_col_bf[:],
                                         h2_sb[:, ((HC - 1) % 2) * 512:((HC - 1) % 2 + 1) * 512],
                                         start=False, stop=True))

                @blk2.scalar
                def _(sc):
                    for t in range(4):
                        for f in range(HC):
                            if t >= 1 and f == 0:
                                sc.wait_ge(stsy.h, E_st + 16 * (2 * t - 1))
                            sc.wait_ge(pe.h, peN[("fc", t, f)])
                            ac.inc(sc.activation(h_sb[:, f * 512:(f + 1) * 512],
                                                 psF[(t * HC + f) % 2][:, 0:512], AF.Gelu))
                        sc.wait_ge(dv.h, dvN[("var", t)])
                        ac.inc(sc.activation(std_s[:], var_s[:], AF.Sqrt,
                                             bias=eps_row[0:1, 0:1]))

                @blk2.vector
                def _(v):
                    v.wait_ge(dv.h, marks["thr_dv"])
                    for k, (fsl, c) in enumerate(fcchunks):
                        par = k % 2
                        ch = wgbuf[:, par * GG:(par + 1) * GG]
                        v.wait_ge(ldgp.h, E_ldg + 16 * (k + 1))
                        if k >= 1:
                            v.wait_ge(dv.h, E_dv + 3 * k)
                        dv.inc(v.tensor_scalar(t1q[:], ch, thr_cols[:, 0:1], None,
                                               ALU.is_gt))
                        dv.inc(v.tensor_scalar(t2q[:], ch, nthr_cols[:, 0:1], None,
                                               ALU.is_lt))
                        v.wait_ge(dv.h, E_dv + 3 * k + 2)
                        dv.inc(v.tensor_sub(
                            wfcq[:, c * H4 + fsl * GG:c * H4 + (fsl + 1) * GG],
                            t1q[:], t2q[:]))
                    for t in range(4):
                        for f in range(HC):
                            v.wait_ge(ac.h, acN[("gelu", t, f)])
                            pf = t * HC + f
                            if pf >= 2:
                                t2, f2 = divmod(pf - 2, HC)
                                v.wait_ge(pe.h, peN[("s2", t2, f2)])
                            src = h_sb[:, f * 512:(f + 1) * 512]
                            dv.inc(v.tensor_mul(h2_sb[:, (pf % 2) * 512:(pf % 2 + 1) * 512],
                                                src, src))
                        v.wait_ge(pe.h, peN[("s1", t, HC - 1)])
                        dv.inc(v.tensor_scalar(negmu_row[0:1, t * 512:(t + 1) * 512],
                                               psS1[0:1, 0:512], -1.0 / H4, None, ALU.mult))
                        v.wait_ge(dv.h, dvN[("negmu", t)])
                        if t >= 1:
                            v.wait_ge(ac.h, acN[("std", t - 1)])
                        dv.inc(v.tensor_mul(musq_s[:],
                                            negmu_row[0:1, t * 512:(t + 1) * 512],
                                            negmu_row[0:1, t * 512:(t + 1) * 512]))
                        v.wait_ge(pe.h, peN[("s2", t, HC - 1)])
                        v.wait_ge(dv.h, dvN[("musq", t)])
                        dv.inc(v.scalar_tensor_tensor(var_s[:], psS2[0:1, 0:512],
                                                      1.0 / H4, musq_s[:],
                                                      ALU.mult, ALU.subtract))
                        v.wait_ge(ac.h, acN[("std", t)])
                        if t >= 1:
                            v.wait_ge(stsy.h, E_st + 16 * (2 * t))
                        dv.inc(v.reciprocal(rstd_row[0:1, 0:512], std_s[:]))

                @blk2.gpsimd
                def _(g):
                    assert ldgp.n == E_ldg
                    for k, (fsl, c) in enumerate(fcchunks):
                        par = k % 2
                        if k >= 2:
                            g.wait_ge(dv.h, fc_tern_mark(k - 2))
                        ldgp.inc(g.dma_start(
                            out=wgbuf[:, par * GG:(par + 1) * GG].unsqueeze(1),
                            in_=wfcT3g[:, c:c + 1, fsl * GG:(fsl + 1) * GG]), 16)

                @blk2.sync
                def _(sy):
                    for t in range(4):
                        sy.wait_ge(ac.h, acN[("gelu", t, HC - 1)])
                        sy.wait_ge(dv.h, dvN[("h2", t, HC - 1)])
                        stsy.inc(sy.dma_start(out=h_scr.ap()[t * P:(t + 1) * P, :],
                                              in_=h_sb[:]), 16)
                        sy.wait_ge(dv.h, dvN[("rstd", t)])
                        stsy.inc(sy.dma_start(out=bounce.ap()[0:1, t * 512:(t + 1) * 512],
                                              in_=rstd_row[:]), 16)
                    sy.wait_ge(stsy.h, stsy.n)
                    stsy.inc(sy.dma_start(out=rstd_cols[:].unsqueeze(2),
                                          in_=bounce.ap().rearrange("a (t p) -> p t a", p=P),
                                          allow_slow_non_contiguous=True), 16)
                blk2.__exit__(None, None, None)
                E2_ld = ldsy.n
                E2_st2 = stsy.n

            wpq = eE(nc.sbuf_tensor("wpq", [P, HC * D], BF16))
            psCC = eE(nc.psum_tensor("psCC", [P, 512], F32))
            psCr = eE(nc.psum_tensor("psCr", [P, 512], F32))
            psREP = eE(nc.psum_tensor("psREP", [P, 512], F32))
            psREPb = eE(nc.psum_tensor("psREPb", [P, 512], F32))

            b3 = {s: s.n for s in (ldsy, ldgp, pe, dv, ac)}
            blk3 = nc.Block("e3"); blk3.__enter__()

            def p_tern_mark(c):
                return b3[dv] + 3 * (c + 1)

            @blk3.sync
            def _(sy):
                ldsy.inc(sy.dma_start(out=lnbcb[:], in_=lnb_in), 16)
            LD_LNB = ldsy.n

            @blk3.gpsimd
            def _(g):
                assert ldgp.n == b3[ldgp]
                for c in range(HC):
                    par = c % 2
                    if c >= 2:
                        g.wait_ge(dv.h, p_tern_mark(c - 2))
                    ldgp.inc(g.dma_start(
                        out=wgbuf[:, par * GG:(par + 1) * GG].unsqueeze(1),
                        in_=wpT3g[:, c:c + 1, :]), 16)

            @blk3.scalar
            def _(s):
                s.wait_ge(ldsy.h, LD_LNB)
                ac.inc(s.activation(lnb_bf[:], lnbcb[:], AF.Copy))
            AC_LNB = ac.n

            PE_LNB = b3[pe] + 2 * HC
            PE_ONES = PE_LNB + 2 * HC
            PE_CCREP1 = PE_ONES + 1
            PE_CCREP2 = PE_ONES + 2
            DV_TERN3 = b3[dv] + 3 * HC

            def dvScale3(c):
                return DV_TERN3 + c + 1
            DV_SCALES = DV_TERN3 + HC
            DV_CCROW = DV_SCALES + 2
            DV_CCREP = DV_SCALES + 4

            @blk3.tensor
            def _(t_):
                t_.wait_ge(ac.h, AC_LNB)
                t_.wait_ge(dv.h, b3[dv])
                for c in range(HC):
                    t_.wait_ge(dv.h, p_tern_mark(c))
                    pe.inc(t_.matmul(psCC[0:1, 0:512], lnb_bf[:, c:c + 1],
                                     wpq[:, c * D:c * D + 512],
                                     start=(c == 0), stop=(c == HC - 1)))
                    pe.inc(t_.matmul(psS1[0:1, 0:512], lnb_bf[:, c:c + 1],
                                     wpq[:, c * D + 512:(c + 1) * D],
                                     start=(c == 0), stop=(c == HC - 1)))
                for c in range(HC):
                    t_.wait_ge(dv.h, dvScale3(c))
                    pe.inc(t_.matmul(psCr[0:1, 0:512], ones_col_bf[:],
                                     wpq[:, c * D:c * D + 512],
                                     start=(c == 0), stop=(c == HC - 1)))
                    pe.inc(t_.matmul(psS2[0:1, 0:512], ones_col_bf[:],
                                     wpq[:, c * D + 512:(c + 1) * D],
                                     start=(c == 0), stop=(c == HC - 1)))
                t_.wait_ge(dv.h, DV_CCROW)
                pe.inc(t_.matmul(psREP[:, 0:512], ones_row_f[:], cc_row[0:1, 0:512],
                                 start=True, stop=True))
                t_.wait_ge(dv.h, DV_CCROW)
                pe.inc(t_.matmul(psREPb[:, 0:512], ones_row_f[:], cc_row[0:1, 512:1024],
                                 start=True, stop=True))

            @blk3.vector
            def _(v):
                for c in range(HC):
                    par = c % 2
                    ch = wgbuf[:, par * GG:(par + 1) * GG]
                    v.wait_ge(ldgp.h, b3[ldgp] + 16 * (c + 1))
                    if c >= 1:
                        v.wait_ge(dv.h, b3[dv] + 3 * c)
                    dv.inc(v.tensor_scalar(t1q[:], ch, thr_cols[:, 1:2], None,
                                           ALU.is_gt))
                    dv.inc(v.tensor_scalar(t2q[:], ch, nthr_cols[:, 1:2], None,
                                           ALU.is_lt))
                    v.wait_ge(dv.h, b3[dv] + 3 * c + 2)
                    dv.inc(v.tensor_sub(wpq[:, c * D:(c + 1) * D], t1q[:], t2q[:]))
                for c in range(HC):
                    v.wait_ge(pe.h, b3[pe] + 2 * (c + 1))
                    dv.inc(v.tensor_scalar(wpq[:, c * D:(c + 1) * D],
                                           wpq[:, c * D:(c + 1) * D],
                                           lnw_sb[:, c:c + 1], None, ALU.mult))
                v.wait_ge(pe.h, PE_LNB)
                dv.inc(v.tensor_copy(cc_row[0:1, 0:512], psCC[0:1, 0:512]))
                dv.inc(v.tensor_copy(cc_row[0:1, 512:1024], psS1[0:1, 0:512]))
                v.wait_ge(pe.h, PE_ONES)
                dv.inc(v.tensor_copy(c_row[0:1, 0:512], psCr[0:1, 0:512]))
                dv.inc(v.tensor_copy(c_row[0:1, 512:1024], psS2[0:1, 0:512]))
            blk3.__exit__(None, None, None)
            xbuf = eE(nc.sbuf_tensor("xbuf", [P, 4 * D], F32))
            abuf = eE(nc.sbuf_tensor("abuf", [P, 4 * D], F32))
            tmpA = eE(nc.sbuf_tensor("tmpA", [P, 512], F32))
            psJ = [eE(nc.psum_tensor(f"psJ{i}", [P, 512], F32)) for i in range(2)]

            PE4 = PE_CCREP2
            DV4 = DV_CCREP
            peJ = {}
            n = PE4
            for t in range(4):
                for idx in range(8):
                    n += 1
                    peJ[(t, idx)] = n
            dvE = {}
            n = DV4
            for t in range(4):
                for idx in range(8):
                    dvE[(t, idx)] = (n + 1, n + 2, n + 3)
                    n += 3
            ld4 = {}
            n = ldgp.n
            for t in range(4):
                ld4[("h", t)] = n + 16
                ld4[("x", t)] = n + 32
                ld4[("a", t)] = n + 48
                n += 48
            stO = {t: E2_st2 + 16 * (t + 1) for t in range(4)}

            blk4 = nc.Block("e4"); blk4.__enter__()

            @blk4.gpsimd
            def _(g):
                g.wait_ge(stsy.h, marks["attn0_st"])
                for t in range(4):
                    if t == 0:
                        g.wait_ge(stsy.h, E2_st)
                    else:
                        g.wait_ge(pe.h, peJ[(t - 1, 7)])
                    ldgp.inc(g.dma_start(out=h_sb[:],
                                         in_=h_scr.ap()[t * P:(t + 1) * P, :]), 16)
                    if t >= 1:
                        g.wait_ge(dv.h, dvE[(t - 1, 7)][2])
                        g.wait_ge(stsy.h, stO[t - 1])
                    ldgp.inc(g.dma_start(out=xbuf[:].rearrange("p (u d) -> p u d", u=4),
                                         in_=xin.rearrange("(t p) d -> p t d", p=P)[:, 4 * t:4 * t + 4, :]), 16)
                    ldgp.inc(g.dma_start(out=abuf[:].rearrange("p (u d) -> p u d", u=4),
                                         in_=attn0_d.ap().rearrange("(t p) d -> p t d", p=P)[:, 4 * t:4 * t + 4, :]), 16)

            @blk4.tensor
            def _(t_):
                t_.wait_ge(dv.h, DV4)
                for t in range(4):
                    t_.wait_ge(ldgp.h, ld4[("a", t)])
                    for idx in range(8):
                        tt, half = divmod(idx, 2)
                        tg = 4 * t + tt
                        gi = t * 8 + idx
                        if gi >= 2:
                            t2, i2 = divmod(gi - 2, 8)
                            t_.wait_ge(dv.h, dvE[(t2, i2)][0])
                        for f in range(HC):
                            t_.matmul(psJ[idx % 2][:, 0:512],
                                      h_sb[:, f * 512 + tt * P:f * 512 + (tt + 1) * P],
                                      wpq[:, f * D + half * 512:f * D + (half + 1) * 512],
                                      start=(f == 0), stop=False)
                        pe.inc(t_.matmul(psJ[idx % 2][:, 0:512],
                                         negmu_row[0:1, tg * P:(tg + 1) * P],
                                         c_row[0:1, half * 512:(half + 1) * 512],
                                         start=False, stop=True))

            @blk4.vector
            def _(v):
                v.wait_ge(stsy.h, E2_st2)
                v.wait_ge(pe.h, PE_CCREP2)
                for t in range(4):
                    v.wait_ge(ldgp.h, ld4[("a", t)])
                    for idx in range(8):
                        tt, half = divmod(idx, 2)
                        tg = 4 * t + tt
                        par = idx % 2
                        e1n, e2n, e3n = dvE[(t, idx)]
                        v.wait_ge(pe.h, peJ[(t, idx)])
                        if (t, idx) != (0, 0):
                            pv = dvE[(t, idx - 1)][2] if idx else dvE[(t - 1, 7)][2]
                            v.wait_ge(dv.h, pv)
                        dv.inc(v.scalar_tensor_tensor(tmpA[:, 0:512],
                                                      psJ[par][:, 0:512],
                                                      rstd_cols[:, tg:tg + 1],
                                                      xbuf[:, tt * D + half * 512:tt * D + (half + 1) * 512],
                                                      ALU.mult, ALU.add))
                        v.wait_ge(dv.h, e1n)
                        dv.inc(v.tensor_add(tmpA[:, 0:512],
                                            tmpA[:, 0:512],
                                            abuf[:, tt * D + half * 512:tt * D + (half + 1) * 512]))
                        v.wait_ge(dv.h, e2n)
                        dv.inc(v.tensor_add(xbuf[:, tt * D + half * 512:tt * D + (half + 1) * 512],
                                            tmpA[:, 0:512],
                                            (psREP if half == 0 else psREPb)[:, 0:512]))

            @blk4.sync
            def _(sy):
                for t in range(4):
                    sy.wait_ge(dv.h, dvE[(t, 7)][2])
                    stsy.inc(sy.dma_start(
                        out=x1_d.ap().rearrange("(t p) d -> p t d", p=P)[:, 4 * t:4 * t + 4, :],
                        in_=xbuf[:].rearrange("p (u d) -> p u d", u=4)), 16)
                if upto == "E":
                    sy.wait_ge(stsy.h, stsy.n)
            blk4.__exit__(None, None, None)
            marks["x1_st"] = stsy.n
        if upto == "E":
            close_stacks()
            return nc
        esXn.close()

        xn1fm = e(nc.sbuf_tensor("xn1fm", [P, DC * TC], BF16))
        esT = ExitStack()
        xn1tm = esT.enter_context(nc.sbuf_tensor("xn1tm", [P, NT * D], BF16))

        with ExitStack() as esF:
            eF = esF.enter_context
            x_sb = eF(nc.sbuf_tensor("x_sbF", [P, NT * D], F32))
            o_sb2 = x_sb
            s1 = eF(nc.sbuf_tensor("s1F", [P, NT], F32))
            s2 = eF(nc.sbuf_tensor("s2F", [P, NT], F32))
            mu = eF(nc.sbuf_tensor("muF", [P, NT], F32))
            var = eF(nc.sbuf_tensor("varF", [P, NT], F32))
            musq = eF(nc.sbuf_tensor("musqF", [P, NT], F32))
            std = eF(nc.sbuf_tensor("stdF", [P, NT], F32))
            rstd = eF(nc.sbuf_tensor("rstdF", [P, NT], F32))
            scr = eF(nc.sbuf_tensor("scrF", [P, D], F32))
            psT = [eF(nc.psum_tensor(f"psTF{i}", [P, D], F32)) for i in range(2)]

            b = {s: s.n for s in (ldsy, pe, dv, ac)}
            blkF = nc.Block("sF"); blkF.__enter__()

            ln_norm2 = []

            @blkF.sync
            def _(sy):
                sy.wait_ge(stsy.h, marks["x1_st"])
                sy.wait_ge(pe.h, b[pe])
                sy.wait_ge(dv.h, b[dv])
                sy.wait_ge(ac.h, b[ac])
                ldsy.inc(sy.dma_start(out=x_sb[:].rearrange("p (t d) -> p t d", t=NT),
                                      in_=x1_d.ap().rearrange("(t p) d -> p t d", p=P)), 16)
            LD_X1 = ldsy.n

            def f_cast_pred(t):
                return b[ac] + NT + 1 + t + 1

            def f_mm_pred(t):
                return b[pe] + t + 1

            def f_evac_pred(t):
                return b[dv] + 2 * NT + 4 + t + 1

            @blkF.vector
            def _(v):
                v.wait_ge(ldsy.h, LD_X1)
                for t in range(NT):
                    dv.inc(v.reduce_sum(s1[:, t:t + 1], x_sb[:, t * D:(t + 1) * D], axis=AX.X))
                v.wait_ge(dv.h, b[dv] + NT)
                dv.inc(v.tensor_scalar(mu[:], s1[:], 1.0 / D, None, ALU.mult))
                v.wait_ge(dv.h, b[dv] + NT + 1)
                dv.inc(v.tensor_mul(musq[:], mu[:], mu[:]))
                v.wait_ge(ac.h, b[ac] + NT)
                v.wait_ge(dv.h, b[dv] + NT + 2)
                dv.inc(v.scalar_tensor_tensor(var[:], s2[:], 1.0 / D, musq[:],
                                              ALU.mult, ALU.subtract))
                v.wait_ge(ac.h, b[ac] + NT + 1)
                dv.inc(v.reciprocal(rstd[:], std[:]))
                v.wait_ge(dv.h, b[dv] + NT + 4)
                for t in range(NT):
                    ln_norm2.append(dv.inc(v.tensor_scalar(
                        o_sb2[:, t * D:(t + 1) * D], x_sb[:, t * D:(t + 1) * D],
                        mu[:, t:t + 1], rstd[:, t:t + 1], ALU.subtract, ALU.mult)))
                for t in range(NT):
                    v.wait_ge(pe.h, f_mm_pred(t))
                    for c in range(DC):
                        inst = v.tensor_copy(
                            xn1fm[:, c * TC + t * P:c * TC + (t + 1) * P],
                            psT[t % 2][:, c * P:(c + 1) * P])
                    dv.inc(inst)

            @blkF.scalar
            def _(s):
                s.wait_ge(ldsy.h, LD_X1)
                for t in range(NT):
                    if t >= 1:
                        s.wait_ge(ac.h, b[ac] + t)
                    ac.inc(s.activation(scr[:], x_sb[:, t * D:(t + 1) * D], AF.Square,
                                        accum_out=s2[:, t:t + 1]))
                s.wait_ge(dv.h, b[dv] + NT + 3)
                ac.inc(s.activation(std[:], var[:], AF.Sqrt, bias=eps_col[:]))
                for t in range(NT):
                    s.wait_ge(dv.h, ln_norm2[t])
                    a_ = ac.inc(s.activation(xn1tm[:, t * D:(t + 1) * D],
                                             o_sb2[:, t * D:(t + 1) * D], AF.Copy))

            @blkF.tensor
            def _(t_):
                t_.wait_ge(dv.h, b[dv])
                for t in range(NT):
                    t_.wait_ge(ac.h, f_cast_pred(t))
                    if t >= 2:
                        t_.wait_ge(dv.h, f_evac_pred(t - 2))
                    for c in range(DC):
                        inst = t_.matmul(psT[t % 2][:, c * P:(c + 1) * P],
                                         xn1tm[:, t * D + c * P:t * D + (c + 1) * P],
                                         ident[:], start=True, stop=True)
                    pe.inc(inst)
            blkF.__exit__(None, None, None)
            marks["xn1tm_ac"] = ac.n
            marks["xn1fm_dv"] = dv.n

            blkFs = nc.Block("sFs"); blkFs.__enter__()

            @blkFs.gpsimd
            def _(g):
                g.wait_ge(dv.h, marks["xn1fm_dv"])
                stgp.inc(g.dma_start(
                    out=xn1fm_d.ap().rearrange("(c p) t -> p c t", p=P),
                    in_=xn1fm[:].rearrange("p (c t) -> p c t", c=DC)), 16)
            blkFs.__exit__(None, None, None)
            marks["xn1fm_st"] = stgp.n

            if upto == "F":
                blkD = nc.Block("dbgF"); blkD.__enter__()

                @blkD.gpsimd
                def _(g):
                    g.wait_ge(dv.h, marks["xn1fm_dv"])
                    stsy.inc(g.dma_start(
                        out=dbg.rearrange("(c p) t -> p c t", p=P),
                        in_=xn1fm[:].rearrange("p (c t) -> p c t", c=DC)), 16)
                    g.wait_ge(stsy.h, stsy.n)
                blkD.__exit__(None, None, None)

        if upto == "F":
            esT.close()
            return nc

        with ExitStack() as esG:
            eG = esG.enter_context
            mbuf = [eG(nc.sbuf_tensor(f"mbuf{i}", [P, 512], F32)) for i in range(2)]
            psG = [eG(nc.psum_tensor(f"psGm{i}", [P, 512], F32)) for i in range(2)]

            b = {s: s.n for s in (pe, dv, stsy)}
            marks["mloc_st_pred"] = b[stsy] + 16 * 16
            blkG = nc.Block("sG"); blkG.__enter__()
            m3 = mloc_d.ap().rearrange("(d p) o -> p d o", p=P)

            @blkG.tensor
            def _(t_):
                t_.wait_ge(ac.h, marks["xn1tm_ac"])
                t_.wait_ge(dv.h, marks["xn1fm_dv"])
                for k in range(16):
                    d1, half = divmod(k, 2)
                    if k >= 2:
                        t_.wait_ge(dv.h, b[dv] + k - 1)
                    for ti in range(NT):
                        inst = t_.matmul(psG[k % 2][:, 0:512],
                                         xn1tm[:, ti * D + d1 * P:ti * D + (d1 + 1) * P],
                                         xn1tm[:, ti * D + half * 512:ti * D + (half + 1) * 512],
                                         start=(ti == 0), stop=(ti == NT - 1))
                    pe.inc(inst)

            @blkG.vector
            def _(v):
                for k in range(16):
                    v.wait_ge(pe.h, b[pe] + k + 1)
                    if k >= 2:
                        v.wait_ge(stsy.h, b[stsy] + 16 * (k - 1))
                    dv.inc(v.tensor_copy(mbuf[k % 2][:], psG[k % 2][:, 0:512]))

            @blkG.sync
            def _(sy):
                for k in range(16):
                    d1, half = divmod(k, 2)
                    sy.wait_ge(dv.h, b[dv] + k + 1)
                    stsy.inc(sy.dma_start(out=m3[:, d1:d1 + 1, half * 512:(half + 1) * 512],
                                          in_=mbuf[k % 2][:].unsqueeze(1)), 16)
            @blkG.gpsimd
            def _(g):
                g.wait_ge(stsy.h, marks["mloc_st_pred"])
                g.wait_ge(stsy.h, marks["x1_st"])
                g.wait_ge(stgp.h, marks["xn1fm_st"])
            blkG.__exit__(None, None, None)
            marks["mloc_st"] = stsy.n
        esT.close()
        return nc


def build_l2():
    nc = bass.Bass("TRN2", target_bir_lowering=False, debug=False)
    x1_in = nc.dram_tensor("x1", [TC, D], F32, kind="ExternalInput").ap()
    xn1fm = nc.dram_tensor("xn1fm", [D, TC], F32, kind="ExternalInput").ap()
    mparts = nc.dram_tensor("mparts", [4 * D, D], F32, kind="ExternalInput").ap()
    wq1_in = nc.dram_tensor("wq1", [D, D], F32, kind="ExternalInput").ap()
    wg_in = nc.dram_tensor("wg", [D, D], F32, kind="ExternalInput").ap()
    bg_in = nc.dram_tensor("bg", [1, D], F32, kind="ExternalInput").ap()
    we_in = nc.dram_tensor("we", [8 * D, D], F32, kind="ExternalInput").ap()
    be_in = nc.dram_tensor("be", [1, 8 * D], F32, kind="ExternalInput").ap()
    x2_out = nc.dram_tensor("x2", [TC, D], F32, kind="ExternalOutput").ap()

    with ExitStack() as es:
        e = es.enter_context
        xn_sb = e(nc.sbuf_tensor("xn_sb", [P, DC * TC], BF16))
        m_bf = e(nc.sbuf_tensor("m_bf", [P, DC * D], BF16))
        w2_sb = e(nc.sbuf_tensor("w2_sb", [P, DC * D], BF16))
        acc = e(nc.sbuf_tensor("acc", [P, NT * D], BF16))
        msel = e(nc.sbuf_tensor("msel", [P, 8 * NT], F32))
        lg = e(nc.sbuf_tensor("lg", [P, 2 * D], F32))
        sexp = e(nc.sbuf_tensor("sexp", [P, D], BF16))
        max8 = e(nc.sbuf_tensor("max8", [P, 8], F32))
        idx8 = e(nc.sbuf_tensor("idx8", [P, 8], U32))
        eidx_u = e(nc.sbuf_tensor("eidx_u", [P, 1], U32))
        seven = e(nc.sbuf_tensor("seven", [P, 1], U32))
        nmx_all = e(nc.sbuf_tensor("nmx_all", [P, NT], F32))
        sume_all = e(nc.sbuf_tensor("sume_all", [P, NT], F32))
        p_all = e(nc.sbuf_tensor("p_all", [P, NT], F32))
        eidx_all = e(nc.sbuf_tensor("eidx_all", [P, NT], F32))
        tmp16 = e(nc.sbuf_tensor("tmp16", [P, NT], F32))
        tmpe = e(nc.sbuf_tensor("tmpe", [P, 2 * 512], BF16))
        x1b = e(nc.sbuf_tensor("x1b", [P, 2 * D], F32))
        ones_row_bf = e(nc.sbuf_tensor("ones_row_bf", [1, P], BF16))
        psU = [e(nc.psum_tensor(f"psU{i}", [P, 512], F32)) for i in range(2)]
        ld = SemCtr(es, nc, "ld")
        ldw = SemCtr(es, nc, "ldw")
        gp = SemCtr(es, nc, "gp")
        pe = SemCtr(es, nc, "pe")
        dv = SemCtr(es, nc, "dv")
        ac = SemCtr(es, nc, "ac")
        st = SemCtr(es, nc, "st")

        with ExitStack() as es1:
            e1 = es1.enter_context
            msum = e1(nc.sbuf_tensor("msum", [P, DC * D], F32))
            mtmp = e1(nc.sbuf_tensor("mtmp", [P, DC * D], F32))
            wq1_bf = e1(nc.sbuf_tensor("wq1_bf", [P, DC * D], BF16))
            blk = nc.Block("h1"); blk.__enter__()
            mp3 = mparts.rearrange("(r c p) o -> p r c o", r=4, p=P)

            @blk.gpsimd
            def _(g):
                gp.inc(g.memset(ones_row_bf[:], 1.0))
                gp.inc(g.memset(seven[:], 7))
                ld.inc(g.dma_start(out=msum[:].rearrange("p (c o) -> p c o", c=DC),
                                   in_=mp3[:, 0]), 16)
                ld.inc(g.dma_start(out=mtmp[:].rearrange("p (c o) -> p c o", c=DC),
                                   in_=mp3[:, 1]), 16)
                g.wait_ge(dv.h, 1)
                ld.inc(g.dma_start(out=mtmp[:].rearrange("p (c o) -> p c o", c=DC),
                                   in_=mp3[:, 2]), 16)
                g.wait_ge(dv.h, 2)
                ld.inc(g.dma_start(out=mtmp[:].rearrange("p (c o) -> p c o", c=DC),
                                   in_=mp3[:, 3]), 16)
                ldw.inc(g.dma_start(out=wq1_bf[:].rearrange("p (c o) -> p c o", c=DC),
                                    in_=wq1_in.rearrange("(c p) o -> p c o", p=P)), 16)
                ldw.inc(g.dma_start(out=xn_sb[:].rearrange("p (c t) -> p c t", c=DC),
                                    in_=xn1fm.rearrange("(c p) t -> p c t", p=P)), 16)

            @blk.vector
            def _(v):
                v.wait_ge(ld.h, 32)
                dv.inc(v.tensor_add(msum[:], msum[:], mtmp[:]))
                v.wait_ge(ld.h, 48)
                v.wait_ge(dv.h, 1)
                dv.inc(v.tensor_add(msum[:], msum[:], mtmp[:]))
                v.wait_ge(ld.h, 64)
                v.wait_ge(dv.h, 2)
                dv.inc(v.tensor_add(msum[:], msum[:], mtmp[:]))
                v.wait_ge(dv.h, 3)
                dv.inc(v.tensor_copy(m_bf[:], msum[:]))

            @blk.tensor
            def _(t_):
                t_.wait_ge(ldw.h, 16)
                t_.wait_ge(dv.h, 4)
                for k in range(16):
                    hc, half = divmod(k, 2)
                    if k >= 2:
                        t_.wait_ge(dv.h, 4 + k - 1)
                    for oc in range(DC):
                        inst = t_.matmul(psU[k % 2][:, 0:512],
                                         wq1_bf[:, oc * D + hc * P:oc * D + (hc + 1) * P],
                                         m_bf[:, oc * D + half * 512:oc * D + (half + 1) * 512],
                                         start=(oc == 0), stop=(oc == DC - 1))
                    pe.inc(inst)

            @blk.vector
            def _(v):
                for k in range(16):
                    hc, half = divmod(k, 2)
                    v.wait_ge(pe.h, k + 1)
                    dv.inc(v.tensor_copy(w2_sb[:, hc * D + half * 512:hc * D + (half + 1) * 512],
                                         psU[k % 2][:, 0:512]))
            blk.__exit__(None, None, None)
        H2_pe = 16
        H2_dv = 20
        H2_gp = 2

        wg_bf = e(nc.sbuf_tensor("wg_bf", [P, DC * D], BF16))
        webf = [e(nc.sbuf_tensor(f"webf{i}", [P, DC * D], BF16)) for i in range(2)]
        bg_bf = e(nc.sbuf_tensor("bg_bf", [1, D], BF16))
        be_bf = e(nc.sbuf_tensor("be_bf", [1, 8 * D], BF16))
        dvN = {}
        n = H2_dv
        for t in range(NT):
            for k in ("ev0", "ev1", "max8", "nmx", "idx", "and", "eidx"):
                n += 1
                dvN[(k, t)] = n
        dvN["recip"] = n + 1
        n += 1
        for ee in range(8):
            dvN[("mseleq", ee)] = n + 1
            dvN[("mselmul", ee)] = n + 2
            n += 2
        H3_dv = n
        peN = {}
        n = H2_pe
        for t in range(NT):
            for half in range(2):
                n += 1
                peN[("lg", t, half)] = n
        H3_pe = n
        acN = {}
        n = 0
        for t in range(NT):
            n += 1
            acN[("exp", t)] = n
        H3_ac = n

        blk3 = nc.Block("h3"); blk3.__enter__()

        @blk3.gpsimd
        def _(g):
            ldw.inc(g.dma_start(out=wg_bf[:].rearrange("p (c o) -> p c o", c=DC),
                                in_=wg_in.rearrange("(c p) o -> p c o", p=P)), 16)
            ldw.inc(g.dma_start(out=bg_bf[:], in_=bg_in), 16)
            ldw.inc(g.dma_start(out=be_bf[:], in_=be_in), 16)

        @blk3.tensor
        def _(t_):
            t_.wait_ge(ldw.h, 80)
            t_.wait_ge(gp.h, 1)
            for t in range(NT):
                for half in range(2):
                    k = t * 2 + half
                    if k >= 2:
                        t2, h2 = divmod(k - 2, 2)
                        t_.wait_ge(dv.h, dvN[("ev" + str(h2), t2)])
                    for cc in range(DC):
                        t_.matmul(psU[k % 2][:, 0:512],
                                  xn_sb[:, cc * TC + t * P:cc * TC + (t + 1) * P],
                                  wg_bf[:, cc * D + half * 512:cc * D + (half + 1) * 512],
                                  start=(cc == 0), stop=False)
                    pe.inc(t_.matmul(psU[k % 2][:, 0:512], ones_row_bf[:],
                                     bg_bf[0:1, half * 512:(half + 1) * 512],
                                     start=False, stop=True))

        @blk3.vector
        def _(v):
            for t in range(NT):
                for half in range(2):
                    v.wait_ge(pe.h, peN[("lg", t, half)])
                    if t >= 2:
                        v.wait_ge(ac.h, acN[("exp", t - 2)])
                    dv.inc(v.tensor_copy(lg[:, (t % 2) * D + half * 512:(t % 2) * D + (half + 1) * 512],
                                         psU[(t * 2 + half) % 2][:, 0:512]))
                v.wait_ge(dv.h, dvN[("ev1", t)])
                dv.inc(v.max(max8[:], lg[:, (t % 2) * D:(t % 2 + 1) * D]))
                v.wait_ge(dv.h, dvN[("max8", t)])
                dv.inc(v.tensor_scalar(nmx_all[:, t:t + 1], max8[:, 0:1], -1.0, None,
                                       ALU.mult))
                dv.inc(v.max_index(idx8[:], max8[:], lg[:, (t % 2) * D:(t % 2 + 1) * D]))
                v.wait_ge(dv.h, dvN[("idx", t)])
                dv.inc(v.tensor_single_scalar(eidx_u[:], idx8[:, 0:1], seven[:],
                                              ALU.bitwise_and))
                v.wait_ge(dv.h, dvN[("and", t)])
                dv.inc(v.tensor_copy(eidx_all[:, t:t + 1], eidx_u[:]))
            v.wait_ge(ac.h, acN[("exp", NT - 1)])
            dv.inc(v.reciprocal(p_all[:], sume_all[:]))
            v.wait_ge(dv.h, dvN["recip"])
            for ee in range(8):
                dv.inc(v.tensor_scalar(tmp16[:], eidx_all[:], float(ee), None,
                                       ALU.is_equal))
                v.wait_ge(dv.h, dvN[("mseleq", ee)])
                dv.inc(v.tensor_mul(msel[:, ee * NT:(ee + 1) * NT], tmp16[:], p_all[:]))

        @blk3.scalar
        def _(sc):
            for t in range(NT):
                sc.wait_ge(dv.h, dvN[("nmx", t)])
                ac.inc(sc.activation(sexp[:], lg[:, (t % 2) * D:(t % 2 + 1) * D],
                                     AF.Exp, bias=nmx_all[:, t:t + 1],
                                     accum_out=sume_all[:, t:t + 1]))
        blk3.__exit__(None, None, None)

        peE = {}
        n = H3_pe
        for ee in range(8):
            for t in range(NT):
                for half in range(2):
                    n += 1
                    peE[(ee, t, half)] = n
        H4_pe = n
        dvA = {}
        n = H3_dv
        for ee in range(8):
            for t in range(NT):
                for half in range(2):
                    dvA[(ee, t, half)] = (n + 1, n + 2)
                    n += 2
        H4_dv = n
        ldE = {ee: 80 + 16 * (ee + 1) for ee in range(8)}

        blk4 = nc.Block("h4"); blk4.__enter__()
        we3 = we_in.rearrange("(ee c p) o -> p ee c o", ee=8, p=P)

        @blk4.gpsimd
        def _(g):
            for ee in range(8):
                if ee >= 2:
                    g.wait_ge(pe.h, peE[(ee - 2, NT - 1, 1)])
                ldw.inc(g.dma_start(out=webf[ee % 2][:].rearrange("p (c o) -> p c o", c=DC),
                                    in_=we3[:, ee]), 16)

        @blk4.tensor
        def _(t_):
            for ee in range(8):
                t_.wait_ge(ldw.h, ldE[ee])
                for t in range(NT):
                    for half in range(2):
                        k = (ee * NT + t) * 2 + half
                        if k >= 2:
                            kk = k - 2
                            e2, r = divmod(kk, NT * 2)
                            t2, h2 = divmod(r, 2)
                            t_.wait_ge(dv.h, dvA[(e2, t2, h2)][0])
                        for cc in range(DC):
                            t_.matmul(psU[k % 2][:, 0:512],
                                      xn_sb[:, cc * TC + t * P:cc * TC + (t + 1) * P],
                                      webf[ee % 2][:, cc * D + half * 512:cc * D + (half + 1) * 512],
                                      start=(cc == 0), stop=False)
                        pe.inc(t_.matmul(psU[k % 2][:, 0:512], ones_row_bf[:],
                                         be_bf[0:1, ee * D + half * 512:ee * D + (half + 1) * 512],
                                         start=False, stop=True))

        @blk4.vector
        def _(v):
            for ee in range(8):
                for t in range(NT):
                    for half in range(2):
                        k = (ee * NT + t) * 2 + half
                        par = k % 2
                        r1, r2 = dvA[(ee, t, half)]
                        v.wait_ge(pe.h, peE[(ee, t, half)])
                        dv.inc(v.tensor_scalar(tmpe[:, par * 512:(par + 1) * 512],
                                               psU[par][:, 0:512],
                                               msel[:, ee * NT + t:ee * NT + t + 1],
                                               0.0, ALU.mult, ALU.max))
                        v.wait_ge(dv.h, r1)
                        sl = acc[:, t * D + half * 512:t * D + (half + 1) * 512]
                        if ee == 0:
                            dv.inc(v.tensor_copy(sl, tmpe[:, par * 512:(par + 1) * 512]))
                        else:
                            v.wait_ge(dv.h, dvA[(ee - 1, t, half)][1])
                            dv.inc(v.tensor_add(sl, sl, tmpe[:, par * 512:(par + 1) * 512]))
        blk4.__exit__(None, None, None)

        peA = {}
        n = H4_pe
        for t in range(NT):
            for half in range(2):
                n += 1
                peA[(t, half)] = n
        dvC = {}
        n = H4_dv
        for t in range(NT):
            for half in range(2):
                dvC[(t, half)] = (n + 1, n + 2)
                n += 2
        H5_dv = n
        ldX = {t: 16 * (t + 1) for t in range(NT)}
        LD_B = 64

        blk5 = nc.Block("h5"); blk5.__enter__()
        x1_3 = x1_in.rearrange("(t p) d -> p t d", p=P)
        x2_3 = x2_out.rearrange("(t p) d -> p t d", p=P)

        @blk5.gpsimd
        def _(g):
            for t in range(NT):
                if t >= 2:
                    g.wait_ge(st.h, 16 * (t - 1))
                ld.inc(g.dma_start(out=x1b[:, (t % 2) * D:(t % 2 + 1) * D].unsqueeze(1),
                                   in_=x1_3[:, t:t + 1, :]), 16)

        @blk5.tensor
        def _(t_):
            t_.wait_ge(dv.h, H2_dv)
            for t in range(NT):
                for half in range(2):
                    k = t * 2 + half
                    if k >= 2:
                        t2, h2 = divmod(k - 2, 2)
                        t_.wait_ge(dv.h, dvC[(t2, h2)][0])
                    for cc in range(DC):
                        inst = t_.matmul(psU[k % 2][:, 0:512],
                                         xn_sb[:, cc * TC + t * P:cc * TC + (t + 1) * P],
                                         w2_sb[:, cc * D + half * 512:cc * D + (half + 1) * 512],
                                         start=(cc == 0), stop=(cc == DC - 1))
                    pe.inc(inst)

        @blk5.vector
        def _(v):
            for t in range(NT):
                v.wait_ge(ld.h, LD_B + ldX[t])
                for half in range(2):
                    c1, c2 = dvC[(t, half)]
                    v.wait_ge(pe.h, peA[(t, half)])
                    sl = x1b[:, (t % 2) * D + half * 512:(t % 2) * D + (half + 1) * 512]
                    dv.inc(v.tensor_add(sl, sl, psU[(t * 2 + half) % 2][:, 0:512]))
                    v.wait_ge(dv.h, c1)
                    dv.inc(v.tensor_add(sl, sl,
                                        acc[:, t * D + half * 512:t * D + (half + 1) * 512]))

        @blk5.sync
        def _(sy):
            for t in range(NT):
                sy.wait_ge(dv.h, dvC[(t, 1)][1])
                st.inc(sy.dma_start(out=x2_3[:, t:t + 1, :],
                                    in_=x1b[:, (t % 2) * D:(t % 2 + 1) * D].unsqueeze(1)), 16)
            sy.wait_ge(st.h, 16 * NT)
        blk5.__exit__(None, None, None)
    return nc



def _in_maps(inputs):
    x = np.ascontiguousarray(inputs["L0_attn_hidden_states"], dtype=np.float32)

    def T(a):
        return np.ascontiguousarray(np.asarray(a, dtype=np.float32).T)

    shared = {
        "wqT": T(inputs["L0_attn_Wq"]), "wkT": T(inputs["L0_attn_Wk"]),
        "wvT": T(inputs["L0_attn_Wv"]), "wg1T": T(inputs["L0_attn_Wg1"]),
        "wg2T": T(inputs["L0_attn_Wg2"]),
        "gnorm_pc": np.ascontiguousarray(np.asarray(inputs["L0_attn_g_norm_weight"], np.float32).reshape(DC, P).T),
        "woT": T(inputs["L0_attn_Wo"]),
        "wfcT": T(inputs["L0_ffn_W_fc"]), "wpT": T(inputs["L0_ffn_W_proj"]),
        "lnw_pc": np.ascontiguousarray(np.asarray(inputs["L0_ffn_ln_w"], np.float32).reshape(HC, P).T),
        "lnb_pc": np.ascontiguousarray(np.asarray(inputs["L0_ffn_ln_b"], np.float32).reshape(HC, P).T),
    }
    maps = []
    for c in range(8):
        b, s = divmod(c, 4)
        pm = np.zeros((P, 4), np.float32)
        pm[:, :s] = 1.0
        m = dict(shared)
        m["xin"] = np.ascontiguousarray(x[b, s * TC:(s + 1) * TC, :])
        m["xprev"] = np.ascontiguousarray(x[b, 0:3 * TC, :])
        m["pmask"] = pm
        maps.append(m)
    return maps


def kernel(**inputs):
    cores = list(range(8))
    nc1 = build_fused("G")
    res1 = run_bass_kernel_spmd(nc1, _in_maps(inputs), core_ids=cores)
    R1 = [res1.results[i] for i in range(8)]

    wq1 = np.ascontiguousarray(inputs["L1_attn_W_q"], dtype=np.float32)
    wg = np.ascontiguousarray(inputs["L1_ffn_Wg"], dtype=np.float32)
    bg1 = np.ascontiguousarray(np.asarray(inputs["L1_ffn_bg"], np.float32).reshape(1, D))
    we_flat = np.ascontiguousarray(np.asarray(inputs["L1_ffn_We"], np.float32).reshape(8 * D, D))
    be1 = np.ascontiguousarray(np.asarray(inputs["L1_ffn_be"], np.float32).reshape(1, 8 * D))
    in_maps = []
    for i in range(8):
        g0 = (i // 4) * 4
        mp = np.ascontiguousarray(np.concatenate(
            [np.asarray(R1[g0 + j]["mloc_d"]) for j in range(4)], axis=0))
        in_maps.append({"x1": np.asarray(R1[i]["x1_d"]),
                        "xn1fm": np.asarray(R1[i]["xn1fm_d"]),
                        "mparts": mp, "wq1": wq1, "wg": wg, "bg": bg1,
                        "we": we_flat, "be": be1})
    nc2 = build_l2()
    res2 = run_bass_kernel_spmd(nc2, in_maps, core_ids=cores)
    out = np.empty((2, 8192, D), np.float32)
    for c in range(8):
        b, s = divmod(c, 4)
        out[b, s * TC:(s + 1) * TC, :] = res2.results[c]["x2"]
    return out


# version 07
# speedup vs baseline: 1.0796x; 1.0031x over previous
import sys
from contextlib import ExitStack

import numpy as np

sys.path.insert(0, "/opt/trn_rl_repo")
import concourse.bass as bass
import concourse.mybir as mybir
from concourse.bass_utils import run_bass_kernel_spmd

F32 = mybir.dt.float32
BF16 = mybir.dt.bfloat16
U32 = mybir.dt.uint32
AF = mybir.ActivationFunctionType
ALU = mybir.AluOpType
AX = mybir.AxisListType

P = 128
TC = 2048
NT = TC // P
D = 1024
DC = D // P
H4 = 4096
HC = H4 // P
NHEAD = 8
DK = 128
SCALE = float(DK) ** -0.5
PZROWS = D + P


class SemCtr:
    def __init__(self, es, nc, name):
        self.h = es.enter_context(nc.semaphore(name))
        self.n = 0

    def inc(self, inst, k=1):
        inst.then_inc(self.h, k)
        self.n += k
        return self.n


def build_fused(upto="H"):
    nc = bass.Bass("TRN2", target_bir_lowering=False, debug=False)
    order = ["A", "P", "B", "D", "E", "F", "G"]
    assert upto in order
    stages = order[: order.index(upto) + 1]

    def want(s):
        return s in stages

    xin = nc.dram_tensor("xin", [TC, D], F32, kind="ExternalInput").ap()
    wqT = nc.dram_tensor("wqT", [D, D], F32, kind="ExternalInput").ap()
    wkT = nc.dram_tensor("wkT", [D, D], F32, kind="ExternalInput").ap()
    wvT = nc.dram_tensor("wvT", [D, D], F32, kind="ExternalInput").ap()
    wg1T = nc.dram_tensor("wg1T", [D, 128], F32, kind="ExternalInput").ap()
    wg2T = nc.dram_tensor("wg2T", [128, D], F32, kind="ExternalInput").ap()
    gn_in = nc.dram_tensor("gnorm_pc", [P, DC], F32, kind="ExternalInput").ap()
    wo_in = nc.dram_tensor("woT", [D, D], F32, kind="ExternalInput").ap()
    wfcT = nc.dram_tensor("wfcT", [D, H4], F32, kind="ExternalInput").ap()
    wpT = nc.dram_tensor("wpT", [H4, D], F32, kind="ExternalInput").ap()
    lnw_in = nc.dram_tensor("lnw_pc", [P, HC], F32, kind="ExternalInput").ap()
    lnb_in = nc.dram_tensor("lnb_pc", [P, HC], F32, kind="ExternalInput").ap()
    pmask_in = nc.dram_tensor("pmask", [P, 4], F32, kind="ExternalInput").ap()
    xprev = nc.dram_tensor("xprev", [3 * TC, D], F32, kind="ExternalInput").ap()

    attn0_kind = "ExternalOutput" if upto == "D" else "Internal"
    attn0_d = nc.dram_tensor("attn0_d", [TC, D], F32, kind=attn0_kind)
    h_scr = nc.dram_tensor("h_scr", [4 * P, HC * 512], BF16)
    bounce = nc.dram_tensor("bounce", [1, TC], F32)
    x1_d = nc.dram_tensor("x1_d", [TC, D], F32, kind="ExternalOutput")
    mloc_d = nc.dram_tensor("mloc_d", [D, D], F32, kind="ExternalOutput")

    xn1fm_d = nc.dram_tensor("xn1fm_d", [D, TC], F32, kind="ExternalOutput")
    dbg = None
    if upto in ("A", "P", "B", "F"):
        shp = {"A": [D, TC], "P": [D + P, DK], "B": [4 * D, TC],
               "F": [D, TC]}[upto]
        dbg = nc.dram_tensor("dbg", shp, F32, kind="ExternalOutput").ap()

    with ExitStack() as es:
        e = es.enter_context
        ldsy = SemCtr(es, nc, "ldsy")
        ldgp = SemCtr(es, nc, "ldgp")
        stsy = SemCtr(es, nc, "stsy")
        pe = SemCtr(es, nc, "pe")
        dv = SemCtr(es, nc, "dv")
        ac = SemCtr(es, nc, "ac")
        gp = SemCtr(es, nc, "gp")
        stgp = SemCtr(es, nc, "stgp")
        stbg = SemCtr(es, nc, "stbg")

        ones_bf = e(nc.sbuf_tensor("ones_bf", [P, DK], BF16))
        ident = e(nc.sbuf_tensor("ident", [P, DK], BF16))
        mask = e(nc.sbuf_tensor("mask", [P, DK], BF16))
        zeros_f = e(nc.sbuf_tensor("zeros_f", [P, DK], F32))
        ones_col_bf = e(nc.sbuf_tensor("ones_col_bf", [P, 1], BF16))
        ones_row_f = e(nc.sbuf_tensor("ones_row_f", [1, P], F32))
        ones_row_bf = e(nc.sbuf_tensor("ones_row_bf", [1, P], BF16))
        scale_col = e(nc.sbuf_tensor("scale_col", [P, 1], F32))
        eps_col = e(nc.sbuf_tensor("eps_col", [P, 1], F32))
        eps_row = e(nc.sbuf_tensor("eps_row", [1, 1], F32))
        onecol_f = e(nc.sbuf_tensor("onecol_f", [P, 1], F32))
        seven = e(nc.sbuf_tensor("seven", [P, 1], U32))
        zrow_bf = e(nc.sbuf_tensor("zrow_bf", [1, 512], BF16))
        pmask_sb = e(nc.sbuf_tensor("pmask_sb", [P, 4], F32))
        GG = 1024
        wgbuf = e(nc.sbuf_tensor("wgbuf", [P, 2 * GG], F32))
        absb = e(nc.sbuf_tensor("absb", [P, GG // 2], F32))
        t1q = e(nc.sbuf_tensor("t1q", [P, GG], BF16))
        t2q = e(nc.sbuf_tensor("t2q", [P, GG], BF16))
        ga_fc = e(nc.sbuf_tensor("ga_fc", [P, 64], F32))
        ga_p = e(nc.sbuf_tensor("ga_p", [P, 64], F32))
        gb2 = e(nc.sbuf_tensor("gb2", [P, 2], F32))
        thr_rows = e(nc.sbuf_tensor("thr_rows", [1, 2], F32))
        thr_cols = e(nc.sbuf_tensor("thr_cols", [P, 2], F32))
        nthr_cols = e(nc.sbuf_tensor("nthr_cols", [P, 2], F32))
        lnw_sb = e(nc.sbuf_tensor("lnw_sb", [P, HC], F32))

        blk0 = nc.Block("s0"); blk0.__enter__()

        @blk0.gpsimd
        def _(g):
            gp.inc(g.memset(ones_bf[:], 1.0))
            gp.inc(g.memset(zeros_f[:], 0.0))
            gp.inc(g.memset(ones_col_bf[:], 1.0))
            gp.inc(g.memset(ones_row_f[:], 1.0))
            gp.inc(g.memset(ones_row_bf[:], 1.0))
            gp.inc(g.memset(scale_col[:], SCALE))
            gp.inc(g.memset(eps_col[:], 1e-5))
            gp.inc(g.memset(eps_row[:], 1e-5))
            gp.inc(g.memset(onecol_f[:], 1.0))
            gp.inc(g.memset(seven[:], 7))
            gp.inc(g.memset(zrow_bf[:], 0.0))
            g.wait_ge(gp.h, 1)
            gp.inc(g.affine_select(ident[:], ones_bf[:], [[-1, DK]], ALU.is_equal,
                                   0.0, base=0, channel_multiplier=1))
            gp.inc(g.affine_select(mask[:], ones_bf[:], [[1, DK]], ALU.is_ge,
                                   0.0, base=0, channel_multiplier=-1))
            ldgp.inc(g.dma_start(out=pmask_sb[:], in_=pmask_in), 16)
        blk0.__exit__(None, None, None)
        GP_CONST = gp.n
        LD_PMASK = ldgp.n

        esXn = ExitStack()
        esQ = ExitStack()
        esActs = ExitStack()
        xnfm = esXn.enter_context(nc.sbuf_tensor("xnfm", [P, DC * TC], BF16))

        def close_stacks():
            esActs.close()
            esQ.close()
            esXn.close()

        marks = {}

        with ExitStack() as esA:
            eA = esA.enter_context
            x_sb = eA(nc.sbuf_tensor("x_sb", [P, NT * D], F32))
            o_sb = eA(nc.sbuf_tensor("o_sb", [P, NT * D], F32))
            s1 = eA(nc.sbuf_tensor("s1", [P, NT], F32))
            s2 = eA(nc.sbuf_tensor("s2", [P, NT], F32))
            mu = eA(nc.sbuf_tensor("mu", [P, NT], F32))
            var = eA(nc.sbuf_tensor("var", [P, NT], F32))
            musq = eA(nc.sbuf_tensor("musq", [P, NT], F32))
            std = eA(nc.sbuf_tensor("std", [P, NT], F32))
            rstd = eA(nc.sbuf_tensor("rstd", [P, NT], F32))
            scr = eA(nc.sbuf_tensor("scr", [P, D], F32))
            xbf = eA(nc.sbuf_tensor("xbf", [P, 2 * D], BF16))
            psT = [eA(nc.psum_tensor(f"psT{i}", [P, D], F32)) for i in range(2)]

            b = {s: s.n for s in (ldsy, pe, dv, ac, gp)}

            x_t = xin.rearrange("(t p) d -> p t d", p=P)
            x_sb3 = x_sb[:].rearrange("p (t d) -> p t d", t=NT)

            blkA = nc.Block("sA"); blkA.__enter__()

            ln_norm = []
            a2_cast = []
            a2_mm = []
            a2_evac = []

            def a2_mm_pred(t):
                return b[pe] + t + 1

            def a2_cast_pred(t):
                return b[ac] + NT + 1 + t + 1

            def a2_evac_pred(t):
                return b[dv] + 2 * NT + 4 + t + 1

            @blkA.sync
            def _(sy):
                ldsy.inc(sy.dma_start(out=x_sb3, in_=x_t), 16)
            LD_X = ldsy.n

            @blkA.vector
            def _(v):
                v.wait_ge(ldsy.h, LD_X)
                for t in range(NT):
                    dv.inc(v.reduce_sum(s1[:, t:t + 1], x_sb[:, t * D:(t + 1) * D], axis=AX.X))
                v.wait_ge(dv.h, b[dv] + NT)
                dv.inc(v.tensor_scalar(mu[:], s1[:], 1.0 / D, None, ALU.mult))
                v.wait_ge(dv.h, b[dv] + NT + 1)
                dv.inc(v.tensor_mul(musq[:], mu[:], mu[:]))
                v.wait_ge(ac.h, b[ac] + NT)
                v.wait_ge(dv.h, b[dv] + NT + 2)
                dv.inc(v.scalar_tensor_tensor(var[:], s2[:], 1.0 / D, musq[:],
                                              ALU.mult, ALU.subtract))
                v.wait_ge(ac.h, b[ac] + NT + 1)
                dv.inc(v.reciprocal(rstd[:], std[:]))
                v.wait_ge(dv.h, b[dv] + NT + 4)
                for t in range(NT):
                    ln_norm.append(dv.inc(v.tensor_scalar(
                        o_sb[:, t * D:(t + 1) * D], x_sb[:, t * D:(t + 1) * D],
                        mu[:, t:t + 1], rstd[:, t:t + 1], ALU.subtract, ALU.mult)))
                for t in range(NT):
                    v.wait_ge(pe.h, a2_mm_pred(t))
                    for c in range(DC):
                        inst = v.tensor_copy(
                            xnfm[:, c * TC + t * P:c * TC + (t + 1) * P],
                            psT[t % 2][:, c * P:(c + 1) * P])
                    a2_evac.append(dv.inc(inst))

            @blkA.scalar
            def _(s):
                s.wait_ge(ldsy.h, LD_X)
                for t in range(NT):
                    if t >= 1:
                        s.wait_ge(ac.h, b[ac] + t)
                    ac.inc(s.activation(scr[:], x_sb[:, t * D:(t + 1) * D], AF.Square,
                                        accum_out=s2[:, t:t + 1]))
                s.wait_ge(dv.h, b[dv] + NT + 3)
                s.wait_ge(gp.h, GP_CONST)
                ac.inc(s.activation(std[:], var[:], AF.Sqrt, bias=eps_col[:]))
                for t in range(NT):
                    s.wait_ge(dv.h, ln_norm[t])
                    if t >= 2:
                        s.wait_ge(pe.h, a2_mm_pred(t - 2))
                    a2_cast.append(ac.inc(s.activation(
                        xbf[:, (t % 2) * D:(t % 2 + 1) * D],
                        o_sb[:, t * D:(t + 1) * D], AF.Copy)))

            @blkA.tensor
            def _(t_):
                t_.wait_ge(gp.h, GP_CONST)
                for t in range(NT):
                    t_.wait_ge(ac.h, a2_cast_pred(t))
                    if t >= 2:
                        t_.wait_ge(dv.h, a2_evac_pred(t - 2))
                    for c in range(DC):
                        inst = t_.matmul(psT[t % 2][:, c * P:(c + 1) * P],
                                         xbf[:, (t % 2) * D + c * P:(t % 2) * D + (c + 1) * P],
                                         ident[:], start=True, stop=True)
                    a2_mm.append(pe.inc(inst))

            blkA.__exit__(None, None, None)
            marks["xnfm_dv"] = dv.n

            if upto == "A":
                blkD = nc.Block("dbgA"); blkD.__enter__()

                @blkD.gpsimd
                def _(g):
                    g.wait_ge(dv.h, marks["xnfm_dv"])
                    stsy.inc(g.dma_start(
                        out=dbg.rearrange("(c p) t -> p c t", p=P),
                        in_=xnfm[:].rearrange("p (c t) -> p c t", c=DC)), 16)
                    g.wait_ge(stsy.h, stsy.n)
                blkD.__exit__(None, None, None)

        if upto == "A":
            close_stacks()
            return nc

        base_acc = esQ.enter_context(nc.sbuf_tensor("base_acc", [P, NHEAD * DK], F32))
        zb_acc = esQ.enter_context(nc.sbuf_tensor("zb_acc", [P, NHEAD], F32))
        wfcT3g = wfcT.rearrange("(c p) f -> p c f", p=P)
        wpT3g = wpT.rearrange("(c p) f -> p c f", p=P)

        gchunks = ([("fc", c, fsl) for c in range(DC) for fsl in range(8)]
                   + [("p", c, half) for c in range(HC) for half in range(2)])
        GH = GG // 2

        class GammaFeeder:
            def __init__(self):
                self.k = 0
                self.gmark = {}
                self.gld_base = None

            def feed(self, s, count=1):
                for _ in range(count):
                    if self.k >= len(gchunks):
                        return
                    if self.gld_base is None:
                        self.gld_base = ldgp.n
                    k = self.k
                    tag, c, fsl = gchunks[k]
                    par = k % 4
                    s.wait_ge(ldgp.h, self.gld_base + 16 * (k + 1))
                    if k >= 1:
                        s.wait_ge(ac.h, self.gmark[k - 1])
                    dst = (ga_fc[:, c * 8 + fsl:c * 8 + fsl + 1] if tag == "fc"
                           else ga_p[:, c * 2 + fsl:c * 2 + fsl + 1])
                    self.gmark[k] = ac.inc(s.activation(
                        absb[:], wgbuf[:, par * GH:(par + 1) * GH], AF.Abs,
                        accum_out=dst))
                    self.k += 1

            def emit_loads(self, g, limit=None):
                assert self.gld_base == ldgp.n
                for k, (tag, c, fsl) in enumerate(gchunks[:limit]):
                    par = k % 4
                    if k >= 4:
                        g.wait_ge(ac.h, self.gmark_lookup(k - 4))
                    if tag == "fc":
                        ap_ = wfcT3g[:, c:c + 1, fsl * GH:(fsl + 1) * GH]
                    else:
                        ap_ = wpT3g[:, c:c + 1, fsl * GH:(fsl + 1) * GH]
                    ldgp.inc(g.dma_start(
                        out=wgbuf[:, par * GH:(par + 1) * GH].unsqueeze(1), in_=ap_), 16)

            def gmark_lookup(self, k):
                return self.gmark[k]

        gfeed = GammaFeeder()


        with ExitStack() as esPre:
            eP = esPre.enter_context
            wkp_sb = eP(nc.sbuf_tensor("wkp_sb", [P, DC * D], BF16))
            wvp_sb = eP(nc.sbuf_tensor("wvp_sb", [P, DC * D], BF16))
            wcb = eP(nc.sbuf_tensor("wcb", [P, 2 * D], F32))
            xpr = eP(nc.sbuf_tensor("xpr", [P, 4 * D], F32))
            xbfp = eP(nc.sbuf_tensor("xbfp", [P, 4 * D], BF16))
            xfmp = eP(nc.sbuf_tensor("xfmp", [P, 4 * D], BF16))
            ap_sb = eP(nc.sbuf_tensor("ap_sb", [P, 4 * D], BF16))
            vp_sb = eP(nc.sbuf_tensor("vp_sb", [P, 4 * D], BF16))
            scrp = eP(nc.sbuf_tensor("scrp", [P, D], F32))
            s1p = eP(nc.sbuf_tensor("s1p", [P, 4], F32))
            s2p = eP(nc.sbuf_tensor("s2p", [P, 4], F32))
            mup = eP(nc.sbuf_tensor("mup", [P, 4], F32))
            musqp = eP(nc.sbuf_tensor("musqp", [P, 4], F32))
            varp = eP(nc.sbuf_tensor("varp", [P, 4], F32))
            stdp = eP(nc.sbuf_tensor("stdp", [P, 4], F32))
            rstdp = eP(nc.sbuf_tensor("rstdp", [P, 4], F32))
            psTp = eP(nc.psum_tensor("psTp", [P, D], F32))
            psPr = [eP(nc.psum_tensor(f"psPr{i}", [P, 512], F32)) for i in range(2)]
            psPacc = [eP(nc.psum_tensor(f"psPacc{i}", [P, 512], F32)) for i in range(2)]
            psZacc = eP(nc.psum_tensor("psZacc", [P, NHEAD], F32))

            NTP = 48
            bP = {s: s.n for s in (ldsy, ldgp, pe, dv, ac, gp)}

            def adv(i):
                return bP[dv] + 9 * i + 3 * (i // 16)

            def aac(i):
                return bP[ac] + 16 + 4 * i

            def ape(i):
                return bP[pe] + 6 * i

            def ald(i):
                return bP[ldsy] + 16 * 16 + 16 * (i + 1)

            blkP = nc.Block("sP"); blkP.__enter__()

            @blkP.gpsimd
            def _(g):
                g.wait_ge(dv.h, marks["xnfm_dv"])
                gp.inc(g.memset(base_acc[:], 0.0))
                gp.inc(g.memset(zb_acc[:], 0.0))
                ldgp.inc(g.dma_start(out=lnw_sb[:], in_=lnw_in), 16)
            GP_PRE = bP[gp] + 2

            @blkP.sync
            def _(sy):
                sy.wait_ge(dv.h, marks["xnfm_dv"])
                sy.wait_ge(ac.h, bP[ac])
                sy.wait_ge(pe.h, bP[pe])
                for k in range(16):
                    if k >= 2:
                        sy.wait_ge(ac.h, bP[ac] + k - 1)
                    if k < 8:
                        ap_ = wkT.rearrange("(c p) o -> p c o", p=P)[:, k:k + 1, :]
                    else:
                        ap_ = wvT.rearrange("(c p) o -> p c o", p=P)[:, k - 8:k - 7, :]
                    ldsy.inc(sy.dma_start(
                        out=wcb[:, (k % 2) * D:(k % 2 + 1) * D].unsqueeze(1), in_=ap_), 16)
                for i in range(NTP):
                    r, tt = divmod(i, 16)
                    par = i % 4
                    if i >= 4:
                        sy.wait_ge(ac.h, aac(i - 4) + 1)
                        sy.wait_ge(dv.h, adv(i - 4) + 6)
                    ldsy.inc(sy.dma_start(
                        out=xpr[:, par * D:(par + 1) * D],
                        in_=xprev[r * TC + tt * P:r * TC + (tt + 1) * P, :]), 16)

            @blkP.vector
            def _(v):
                for i in range(NTP):
                    r, tt = divmod(i, 16)
                    par = i % 4
                    pc = par
                    v.wait_ge(ldsy.h, ald(i))
                    dv.inc(v.reduce_sum(s1p[:, pc:pc + 1], xpr[:, par * D:(par + 1) * D], axis=AX.X))
                    v.wait_ge(dv.h, adv(i) + 1)
                    dv.inc(v.tensor_scalar(mup[:, pc:pc + 1], s1p[:, pc:pc + 1], 1.0 / D, None, ALU.mult))
                    v.wait_ge(dv.h, adv(i) + 2)
                    dv.inc(v.tensor_mul(musqp[:, pc:pc + 1], mup[:, pc:pc + 1], mup[:, pc:pc + 1]))
                    v.wait_ge(ac.h, aac(i) + 1)
                    v.wait_ge(dv.h, adv(i) + 3)
                    dv.inc(v.scalar_tensor_tensor(varp[:, pc:pc + 1], s2p[:, pc:pc + 1], 1.0 / D,
                                                  musqp[:, pc:pc + 1], ALU.mult, ALU.subtract))
                    v.wait_ge(ac.h, aac(i) + 2)
                    dv.inc(v.reciprocal(rstdp[:, pc:pc + 1], stdp[:, pc:pc + 1]))
                    v.wait_ge(dv.h, adv(i) + 5)
                    if i >= 4:
                        v.wait_ge(pe.h, ape(i - 4) + 1)
                    dv.inc(v.tensor_scalar(xbfp[:, par * D:(par + 1) * D],
                                           xpr[:, par * D:(par + 1) * D],
                                           mup[:, pc:pc + 1], rstdp[:, pc:pc + 1],
                                           ALU.subtract, ALU.mult))
                    v.wait_ge(pe.h, ape(i) + 1)
                    dv.inc(v.tensor_copy(xfmp[:, par * D:(par + 1) * D], psTp[:, 0:D]))
                    v.wait_ge(pe.h, ape(i) + 4)
                    dv.inc(v.tensor_copy(vp_sb[:, par * D:par * D + 512], psPr[0][:, 0:512]))
                    v.wait_ge(pe.h, ape(i) + 5)
                    dv.inc(v.tensor_copy(vp_sb[:, par * D + 512:(par + 1) * D], psPr[1][:, 0:512]))
                    if tt == 15:
                        v.wait_ge(pe.h, ape(i) + 6)
                        v.wait_ge(ldgp.h, LD_PMASK)
                        if r == 0:
                            v.wait_ge(gp.h, GP_PRE)
                        v.wait_ge(dv.h, adv(i) + 9)
                        dv.inc(v.scalar_tensor_tensor(base_acc[:, 0:512], psPacc[0][:, 0:512],
                                                      pmask_sb[:, r:r + 1], base_acc[:, 0:512],
                                                      ALU.mult, ALU.add))
                        dv.inc(v.scalar_tensor_tensor(base_acc[:, 512:1024], psPacc[1][:, 0:512],
                                                      pmask_sb[:, r:r + 1], base_acc[:, 512:1024],
                                                      ALU.mult, ALU.add))
                        dv.inc(v.scalar_tensor_tensor(zb_acc[:], psZacc[:, 0:NHEAD],
                                                      pmask_sb[:, r:r + 1], zb_acc[:],
                                                      ALU.mult, ALU.add))

            @blkP.scalar
            def _(s):
                s.wait_ge(gp.h, GP_CONST)
                s.wait_ge(dv.h, marks["xnfm_dv"])
                s.wait_ge(pe.h, bP[pe])
                for k in range(16):
                    s.wait_ge(ldsy.h, bP[ldsy] + 16 * (k + 1))
                    dst = (wkp_sb if k < 8 else wvp_sb)
                    c = k % 8
                    ac.inc(s.activation(dst[:, c * D:(c + 1) * D],
                                        wcb[:, (k % 2) * D:(k % 2 + 1) * D], AF.Copy))
                for i in range(NTP):
                    par = i % 4
                    pc = par
                    s.wait_ge(ldsy.h, ald(i))
                    if i >= 1:
                        s.wait_ge(ac.h, aac(i))
                    if i >= 4:
                        s.wait_ge(dv.h, adv(i - 4) + 4)
                    ac.inc(s.activation(scrp[:], xpr[:, par * D:(par + 1) * D], AF.Square,
                                        accum_out=s2p[:, pc:pc + 1]))
                    s.wait_ge(dv.h, adv(i) + 4)
                    ac.inc(s.activation(stdp[:, pc:pc + 1], varp[:, pc:pc + 1], AF.Sqrt,
                                        bias=eps_col[:]))
                    s.wait_ge(pe.h, ape(i) + 2)
                    ac.inc(s.activation(ap_sb[:, par * D:par * D + 512], psPr[0][:, 0:512], AF.Exp))
                    s.wait_ge(pe.h, ape(i) + 3)
                    ac.inc(s.activation(ap_sb[:, par * D + 512:(par + 1) * D], psPr[1][:, 0:512], AF.Exp))

            @blkP.tensor
            def _(t_):
                t_.wait_ge(gp.h, GP_CONST)
                t_.wait_ge(ac.h, bP[ac] + 16)
                t_.wait_ge(dv.h, marks["xnfm_dv"])
                for i in range(NTP):
                    r, tt = divmod(i, 16)
                    par = i % 4
                    t_.wait_ge(dv.h, adv(i) + 6)
                    if i >= 1:
                        t_.wait_ge(dv.h, adv(i - 1) + 7)
                    for c in range(DC):
                        inst = t_.matmul(psTp[:, c * P:(c + 1) * P],
                                         xbfp[:, par * D + c * P:par * D + (c + 1) * P],
                                         ident[:], start=True, stop=True)
                    pe.inc(inst)
                    t_.wait_ge(dv.h, adv(i) + 7)
                    if i >= 1:
                        t_.wait_ge(dv.h, adv(i - 1) + 8)
                    for c in range(DC):
                        inst = t_.matmul(psPr[0][:, 0:512],
                                         xfmp[:, par * D + c * P:par * D + (c + 1) * P],
                                         wkp_sb[:, c * D:c * D + 512],
                                         start=(c == 0), stop=(c == DC - 1))
                    pe.inc(inst)
                    if i >= 1:
                        t_.wait_ge(dv.h, adv(i - 1) + 9)
                    for c in range(DC):
                        inst = t_.matmul(psPr[1][:, 0:512],
                                         xfmp[:, par * D + c * P:par * D + (c + 1) * P],
                                         wkp_sb[:, c * D + 512:(c + 1) * D],
                                         start=(c == 0), stop=(c == DC - 1))
                    pe.inc(inst)
                    t_.wait_ge(ac.h, aac(i) + 3)
                    for c in range(DC):
                        inst = t_.matmul(psPr[0][:, 0:512],
                                         xfmp[:, par * D + c * P:par * D + (c + 1) * P],
                                         wvp_sb[:, c * D:c * D + 512],
                                         start=(c == 0), stop=(c == DC - 1))
                    pe.inc(inst)
                    t_.wait_ge(ac.h, aac(i) + 4)
                    for c in range(DC):
                        inst = t_.matmul(psPr[1][:, 0:512],
                                         xfmp[:, par * D + c * P:par * D + (c + 1) * P],
                                         wvp_sb[:, c * D + 512:(c + 1) * D],
                                         start=(c == 0), stop=(c == DC - 1))
                    pe.inc(inst)
                    t_.wait_ge(dv.h, adv(i) + 9)
                    if tt == 0:
                        if r >= 1:
                            t_.wait_ge(dv.h, adv(i))
                        t_.matmul(psPacc[0][:, 0:512], zrow_bf[0:1, 0:P],
                                  zrow_bf[0:1, 0:512], start=True, stop=False)
                        t_.matmul(psPacc[1][:, 0:512], zrow_bf[0:1, 0:P],
                                  zrow_bf[0:1, 0:512], start=True, stop=False)
                        t_.matmul(psZacc[:, 0:NHEAD], zrow_bf[0:1, 0:P],
                                  zrow_bf[0:1, 0:NHEAD], start=True, stop=False)
                    for h in range(NHEAD):
                        t_.matmul(psPacc[h // 4][:, (h % 4) * DK:(h % 4 + 1) * DK],
                                  ap_sb[:, par * D + h * DK:par * D + (h + 1) * DK],
                                  vp_sb[:, par * D + h * DK:par * D + (h + 1) * DK],
                                  start=False, stop=(tt == 15 and h % 4 == 3))
                    for h in range(NHEAD):
                        inst = t_.matmul(psZacc[:, h:h + 1],
                                         ap_sb[:, par * D + h * DK:par * D + (h + 1) * DK],
                                         ones_col_bf[:],
                                         start=False, stop=(tt == 15 and h == 7))
                    pe.inc(inst)
            blkP.__exit__(None, None, None)
            marks["pre_dv"] = dv.n
            marks["pre_ac"] = ac.n
            marks["pre_pe"] = pe.n

            if upto == "P":
                blkD = nc.Block("dbgP"); blkD.__enter__()

                @blkD.gpsimd
                def _(g):
                    gfeed.emit_loads(g, limit=gfeed.k)

                @blkD.sync
                def _(sy):
                    sy.wait_ge(dv.h, marks["pre_dv"])
                    stsy.inc(sy.dma_start(out=dbg[0:D, :].rearrange("(h k) v -> k h v", k=P),
                                          in_=base_acc[:].rearrange("k (h v) -> k h v", h=NHEAD)), 16)
                    stsy.inc(sy.dma_start(out=dbg[D:D + P, 0:NHEAD], in_=zb_acc[:]), 16)
                    sy.wait_ge(stsy.h, stsy.n)
                blkD.__exit__(None, None, None)

        if upto == "P":
            close_stacks()
            return nc

        q_sb = esQ.enter_context(nc.sbuf_tensor("q_sb", [P, DC * TC], BF16))
        afm_sb = esActs.enter_context(nc.sbuf_tensor("afm_sb", [P, DC * TC], BF16))
        atm_sb = esActs.enter_context(nc.sbuf_tensor("atm_sb", [P, NT * D], BF16))
        vtm_sb = esActs.enter_context(nc.sbuf_tensor("vtm_sb", [P, NT * D], BF16))

        def emit_proj_stage(name, wT, tasks, barrier, extra_scalar=None,
                            extra_gpsimd=None, feeder=None, feeds_during=0):
            with ExitStack() as esBx:
                eB = esBx.enter_context
                w_sb = eB(nc.sbuf_tensor(f"w_{name}", [P, DC * D], BF16))
                wcbB = eB(nc.sbuf_tensor(f"wcb_{name}", [P, 2 * D], F32))
                psB = [eB(nc.psum_tensor(f"ps_{name}{i}", [P, 512], F32)) for i in range(4)]
                b = {s: s.n for s in (ldsy, pe, dv, ac)}
                wT3 = wT.rearrange("(c p) o -> p c o", p=P)
                blk = nc.Block(f"s_{name}"); blk.__enter__()

                scalar_evac = tasks[0][0] != "v"

                def evac_pos(i):
                    if i < feeds_during:
                        return b[ac] + DC + 2 * i + 1
                    return b[ac] + DC + 2 * feeds_during + (i - feeds_during) + 1

                @blk.sync
                def _(sy):
                    for sem, val in barrier.get("sync", []):
                        sy.wait_ge(sem, val)
                    for k in range(DC):
                        if k >= 2:
                            sy.wait_ge(ac.h, b[ac] + k - 1)
                        ldsy.inc(sy.dma_start(
                            out=wcbB[:, (k % 2) * D:(k % 2 + 1) * D].unsqueeze(1),
                            in_=wT3[:, k:k + 1, :]), 16)

                @blk.scalar
                def _(s):
                    for sem, val in barrier.get("scalar", []):
                        s.wait_ge(sem, val)
                    for k in range(DC):
                        s.wait_ge(ldsy.h, b[ldsy] + 16 * (k + 1))
                        ac.inc(s.activation(w_sb[:, k * D:(k + 1) * D],
                                            wcbB[:, (k % 2) * D:(k % 2 + 1) * D], AF.Copy))
                    if scalar_evac:
                        for i, task in enumerate(tasks):
                            pst = psB[i % 4]
                            s.wait_ge(pe.h, b[pe] + i + 1)
                            kind = task[0]
                            if kind in ("q", "a"):
                                _, o, tt = task
                                buf = q_sb if kind == "q" else afm_sb
                                fn = AF.Silu if kind == "q" else AF.Exp
                                ac.inc(s.activation(buf[:, o * TC + tt * 512:o * TC + (tt + 1) * 512],
                                                    pst[:, 0:512], fn))
                            else:
                                _, tt, hf = task
                                ac.inc(s.activation(atm_sb[:, tt * D + hf * 512:tt * D + (hf + 1) * 512],
                                                    pst[:, 0:512], AF.Exp))
                            if feeder is not None and i < feeds_during:
                                feeder.feed(s, 1)
                    if extra_scalar is not None:
                        extra_scalar(s)

                if not scalar_evac:
                    @blk.vector
                    def _(v):
                        for sem, val in barrier.get("vector", []):
                            v.wait_ge(sem, val)
                        for i, task in enumerate(tasks):
                            pst = psB[i % 4]
                            v.wait_ge(pe.h, b[pe] + i + 1)
                            _, tt, hf = task
                            dv.inc(v.tensor_copy(vtm_sb[:, tt * D + hf * 512:tt * D + (hf + 1) * 512],
                                                 pst[:, 0:512]))

                @blk.tensor
                def _(t_):
                    t_.wait_ge(ac.h, b[ac] + DC)
                    for sem, val in barrier.get("tensor", []):
                        t_.wait_ge(sem, val)
                    for i, task in enumerate(tasks):
                        pst = psB[i % 4]
                        if i >= 4:
                            if scalar_evac:
                                t_.wait_ge(ac.h, evac_pos(i - 4))
                            else:
                                t_.wait_ge(dv.h, b[dv] + i - 3)
                        kind = task[0]
                        if kind in ("q", "a"):
                            _, o, tt = task
                            for ccx in range(DC):
                                inst = t_.matmul(pst[:, 0:512],
                                                 w_sb[:, ccx * D + o * P:ccx * D + (o + 1) * P],
                                                 xnfm[:, ccx * TC + tt * 512:ccx * TC + (tt + 1) * 512],
                                                 start=(ccx == 0), stop=(ccx == DC - 1))
                        else:
                            _, tt, hf = task
                            for ccx in range(DC):
                                inst = t_.matmul(pst[:, 0:512],
                                                 xnfm[:, ccx * TC + tt * P:ccx * TC + (tt + 1) * P],
                                                 w_sb[:, ccx * D + hf * 512:ccx * D + (hf + 1) * 512],
                                                 start=(ccx == 0), stop=(ccx == DC - 1))
                        pe.inc(inst)

                if extra_gpsimd is not None:
                    @blk.gpsimd
                    def _(g):
                        extra_gpsimd(g)
                blk.__exit__(None, None, None)

        barrier1 = {"sync": [(dv.h, marks["pre_dv"]), (ac.h, marks["pre_ac"]),
                             (pe.h, marks["pre_pe"])],
                    "scalar": [(dv.h, marks["pre_dv"])],
                    "tensor": [(dv.h, marks["pre_dv"]), (dv.h, marks["xnfm_dv"])]}
        emit_proj_stage("b1a", wqT, [("q", o, t) for o in range(DC) for t in range(4)],
                        barrier1, feeder=gfeed, feeds_during=0)

        bprev = {"sync": [(pe.h, pe.n), (ac.h, ac.n)],
                 "scalar": [], "vector": [], "tensor": []}
        emit_proj_stage("b1b", wkT,
                        [("a", o, t) for o in range(DC) for t in range(4)]
                        + [("at", tt, hf) for tt in range(NT) for hf in range(2)],
                        bprev, feeder=gfeed, feeds_during=0,
                        extra_scalar=lambda s: gfeed.feed(s, 999),
                        extra_gpsimd=lambda g: gfeed.emit_loads(g))
        assert gfeed.k == len(gchunks)
        marks["b1_ac"] = ac.n
        marks["gamma_ac"] = gfeed.gmark[len(gchunks) - 1]
        bprev2 = {"sync": [(pe.h, pe.n), (ac.h, ac.n)],
                  "scalar": [], "vector": [(ac.h, ac.n)], "tensor": []}
        emit_proj_stage("b2", wvT, [("v", tt, hf) for tt in range(NT) for hf in range(2)],
                        bprev2)
        marks["vtm_dv"] = dv.n

        if upto == "B":
            blkD = nc.Block("dbgB"); blkD.__enter__()
            d4 = dbg.rearrange("(r c p) t -> r p c t", r=4, p=P)

            @blkD.gpsimd
            def _(g):
                g.wait_ge(ac.h, marks["b1_ac"])
                g.wait_ge(dv.h, marks["vtm_dv"])
                stsy.inc(g.dma_start(out=d4[0], in_=q_sb[:].rearrange("p (c t) -> p c t", c=DC)), 16)
                stsy.inc(g.dma_start(out=d4[1], in_=afm_sb[:].rearrange("p (c t) -> p c t", c=DC)), 16)
                stsy.inc(g.dma_start(out=dbg.rearrange("(r t p) d -> r p t d", r=4, p=P)[2],
                                     in_=atm_sb[:].rearrange("p (t d) -> p t d", t=NT)), 16)
                stsy.inc(g.dma_start(out=dbg.rearrange("(r t p) d -> r p t d", r=4, p=P)[3],
                                     in_=vtm_sb[:].rearrange("p (t d) -> p t d", t=NT)), 16)
                g.wait_ge(stsy.h, stsy.n)
            blkD.__exit__(None, None, None)
            close_stacks()
            return nc

        with ExitStack() as esTh:
            psTh = esTh.enter_context(nc.psum_tensor("psTh", [P, 512], F32))
            bT = {s: s.n for s in (pe, dv, gp)}
            blkTh = nc.Block("sTh"); blkTh.__enter__()

            @blkTh.vector
            def _(v):
                v.wait_ge(ac.h, marks["gamma_ac"])
                dv.inc(v.reduce_sum(gb2[:, 0:1], ga_fc[:], axis=AX.X))
                dv.inc(v.reduce_sum(gb2[:, 1:2], ga_p[:], axis=AX.X))
                v.wait_ge(pe.h, bT[pe] + 1)
                dv.inc(v.tensor_scalar(thr_rows[:], psTh[0:1, 0:2], 0.5 / (H4 * D), 5e-7,
                                       ALU.mult, ALU.add))
                v.wait_ge(pe.h, bT[pe] + 2)
                dv.inc(v.tensor_copy(thr_cols[:], psTh[:, 4:6]))
                v.wait_ge(dv.h, bT[dv] + 4)
                dv.inc(v.tensor_scalar(nthr_cols[:], thr_cols[:], -1.0, None, ALU.mult))

            @blkTh.tensor
            def _(t_):
                t_.wait_ge(dv.h, bT[dv] + 2)
                pe.inc(t_.matmul(psTh[0:1, 0:2], onecol_f[:], gb2[:], start=True, stop=True))
                t_.wait_ge(dv.h, bT[dv] + 3)
                pe.inc(t_.matmul(psTh[:, 4:6], ones_row_f[:], thr_rows[0:1, 0:2],
                                 start=True, stop=True))
            blkTh.__exit__(None, None, None)
            marks["thr_dv"] = dv.n






        o_sb = q_sb
        eAct = esActs.enter_context
        P_sb = eAct(nc.sbuf_tensor("P_sb", [P, NHEAD * DK], BF16))
        zbase = zb_acc
        zc_sb = eAct(nc.sbuf_tensor("zc_sb", [P, 2 * NHEAD * DK], F32))
        rz_sb = eAct(nc.sbuf_tensor("rz_sb", [P, 2 * DK], F32))
        qt_sb = eAct(nc.sbuf_tensor("qt_sb", [P, 2 * DK], BF16))
        at_sb = eAct(nc.sbuf_tensor("at_sb", [P, 2 * DK], BF16))

        NIT = NT * NHEAD

        with ExitStack() as esD:
            eD = esD.enter_context
            psP = [eD(nc.psum_tensor(f"psP{i}", [P, 512], F32)) for i in range(2)]
            psA = [eD(nc.psum_tensor(f"psA{i}", [P, 512], F32)) for i in range(2)]
            psO = [eD(nc.psum_tensor(f"psO{i}", [P, 512], F32)) for i in range(2)]

            b = {s: s.n for s in (ldsy, pe, dv, ac)}
            bD = dict(b)
            blkDm = nc.Block("sD"); blkDm.__enter__()

            DV0 = 1
            PE0 = 2

            def peAT(it):
                return bD[pe] + PE0 + 3 * it + 1

            def peO(it):
                return bD[pe] + PE0 + 3 * it + 2

            def peP(it):
                return bD[pe] + PE0 + 3 * it + 3

            def dvScan(it):
                return bD[dv] + DV0 + 4 * it + 1

            def dvRecip(it):
                return bD[dv] + DV0 + 4 * it + 2

            def dvQt(it):
                return bD[dv] + DV0 + 4 * it + 3

            def dvAtm(it):
                return bD[dv] + DV0 + 4 * it + 4

            def acOEvac(it):
                ci, h = divmod(it, NHEAD)
                return bD[ac] + 16 * ci + h + 1

            def acPEvac(ci, h):
                return bD[ac] + 16 * ci + 8 + h + 1

            @blkDm.vector
            def _(v):
                v.wait_ge(dv.h, marks["pre_dv"])
                dv.inc(v.tensor_copy(P_sb[:], base_acc[:]))
                for it in range(NIT):
                    ci, h = divmod(it, NHEAD)
                    par = it % 2
                    cpar = ci % 2
                    if ci == 0:
                        init = zbase[:, h:h + 1]
                        v.wait_ge(dv.h, bD[dv] + DV0)
                    zc_out = zc_sb[:, cpar * NHEAD * DK + h * DK:cpar * NHEAD * DK + (h + 1) * DK]
                    if ci > 0:
                        pcpar = 1 - cpar
                        init = zc_sb[:, pcpar * NHEAD * DK + (h + 1) * DK - 1:
                                     pcpar * NHEAD * DK + (h + 1) * DK]
                        v.wait_ge(dv.h, dvScan(it - NHEAD))
                    dv.inc(v.tensor_tensor_scan(
                        zc_out, afm_sb[:, h * TC + ci * DK:h * TC + (ci + 1) * DK],
                        zeros_f[:], init, ALU.add, ALU.add))
                    v.wait_ge(dv.h, dvScan(it))
                    dv.inc(v.reciprocal(rz_sb[:, par * DK:(par + 1) * DK], zc_out))
                    v.wait_ge(dv.h, dvRecip(it))
                    if it >= 2:
                        v.wait_ge(pe.h, peO(it - 2))
                    dv.inc(v.tensor_mul(qt_sb[:, par * DK:(par + 1) * DK],
                                        q_sb[:, h * TC + ci * DK:h * TC + (ci + 1) * DK],
                                        rz_sb[:, par * DK:(par + 1) * DK]))
                    v.wait_ge(pe.h, peAT(it))
                    dv.inc(v.tensor_mul(at_sb[:, par * DK:(par + 1) * DK],
                                        psA[par][:, 0:DK], mask[:]))

            @blkDm.tensor
            def _(t_):
                t_.wait_ge(gp.h, GP_CONST)
                t_.wait_ge(dv.h, bD[dv] + 1)
                for bk in range(2):
                    pe.inc(t_.matmul(psP[bk][:, 0:512], ident[:], P_sb[:, bk * 512:(bk + 1) * 512],
                                     start=True, stop=False))
                for it in range(NIT):
                    ci, h = divmod(it, NHEAD)
                    par = it % 2
                    t_.wait_ge(dv.h, dvQt(it))
                    if it >= 2:
                        t_.wait_ge(dv.h, dvAtm(it - 2))
                    pe.inc(t_.matmul(psA[par][:, 0:DK],
                                     afm_sb[:, h * TC + ci * DK:h * TC + (ci + 1) * DK],
                                     qt_sb[:, par * DK:(par + 1) * DK],
                                     start=True, stop=True))
                    t_.wait_ge(dv.h, dvAtm(it))
                    if ci > 0:
                        t_.wait_ge(ac.h, acPEvac(ci - 1, h))
                    if it >= 2:
                        t_.wait_ge(ac.h, acOEvac(it - 2))
                    t_.matmul(psO[par][:, 0:DK],
                              vtm_sb[:, ci * D + h * DK:ci * D + (h + 1) * DK],
                              at_sb[:, par * DK:(par + 1) * DK],
                              start=True, stop=False)
                    pe.inc(t_.matmul(psO[par][:, 0:DK],
                                     P_sb[:, h * DK:(h + 1) * DK],
                                     qt_sb[:, par * DK:(par + 1) * DK],
                                     start=False, stop=True))
                    if ci > 0:
                        t_.wait_ge(ac.h, acPEvac(ci - 1, 7))
                    pe.inc(t_.matmul(psP[h // 4][:, (h % 4) * DK:(h % 4 + 1) * DK],
                                     atm_sb[:, ci * D + h * DK:ci * D + (h + 1) * DK],
                                     vtm_sb[:, ci * D + h * DK:ci * D + (h + 1) * DK],
                                     start=False,
                                     stop=(ci == NT - 1 and h % 4 == 3)))

            @blkDm.scalar
            def _(s):
                s.wait_ge(gp.h, GP_CONST)
                for ci in range(NT):
                    for h in range(NHEAD):
                        it = ci * NHEAD + h
                        par = it % 2
                        s.wait_ge(pe.h, peO(it))
                        ac.inc(s.activation(o_sb[:, h * TC + ci * DK:h * TC + (ci + 1) * DK],
                                            psO[par][:, 0:DK], AF.Copy,
                                            scale=scale_col[:]))
                    s.wait_ge(pe.h, peP(ci * NHEAD + 7))
                    for h in range(NHEAD):
                        ac.inc(s.activation(P_sb[:, h * DK:(h + 1) * DK],
                                            psP[h // 4][:, (h % 4) * DK:(h % 4 + 1) * DK],
                                            AF.Copy))
            blkDm.__exit__(None, None, None)

        esActs.close()

        PE_L = pe.n
        DV_L = dv.n
        AC_L = ac.n

        with ExitStack() as esDf:
            eD = esDf.enter_context
            wo_sb = eD(nc.sbuf_tensor("wo_sb", [P, DC * D], BF16))
            wg1_sb = eD(nc.sbuf_tensor("wg1_sb", [P, DC * 128], BF16))
            wg2_sb = eD(nc.sbuf_tensor("wg2_sb", [P, D], BF16))
            g1_blk = eD(nc.sbuf_tensor("g1_blk", [P, 2 * 512], BF16))
            sg_blk = eD(nc.sbuf_tensor("sg_blk", [P, 4 * DC * 512], BF16))
            gn_sb = eD(nc.sbuf_tensor("gn_sb", [P, DC], F32))
            o2_sb = eD(nc.sbuf_tensor("o2_sb", [P, 2 * 512], BF16))
            sgp_sb = eD(nc.sbuf_tensor("sgp_sb", [P, 2 * 512], BF16))
            ms_row = eD(nc.sbuf_tensor("ms_row", [1, TC], F32))
            wob = eD(nc.sbuf_tensor("wob", [P, 2 * 512], F32))
            psS = [eD(nc.psum_tensor(f"psS{i}", [P, 512], F32)) for i in range(2)]
            psR = [eD(nc.psum_tensor(f"psR{i}", [P, 512], F32)) for i in range(2)]
            psW = [eD(nc.psum_tensor(f"psW{i}", [P, 512], F32)) for i in range(2)]
            psG1 = eD(nc.psum_tensor("psG1", [P, 512], F32))
            psSG = eD(nc.psum_tensor("psSG", [P, 512], F32))

            wcbD = eD(nc.sbuf_tensor("wcbD", [P, 2 * D], F32))
            bF = {s: s.n for s in (ldsy, ldgp, pe, dv, ac, stsy)}
            blkDf = nc.Block("sDf"); blkDf.__enter__()

            nPE = bF[pe]
            peG1 = {}; peSG = {}; peStat = {}; peRep = {}; peWo = {}
            for t in range(4):
                nPE += 1; peG1[t] = nPE
                for o in range(DC):
                    nPE += 1; peSG[(t, o)] = nPE
            for j in range(32):
                nPE += 1; peStat[j] = nPE
            for t in range(4):
                nPE += 1; peRep[t] = nPE
            for idx in range(32):
                nPE += 1; peWo[idx] = nPE
            nDV = bF[dv]
            dvG1e = {}; dvO2 = {}; dvVar = {}; dvSgp = {}; dvOf = {}; dvWoE = {}
            for t in range(4):
                nDV += 1; dvG1e[t] = nDV
            for t in range(4):
                for c in range(8):
                    nDV += 1; dvO2[t * 8 + c] = nDV
                nDV += 1; dvVar[t] = nDV
            nDV += 1; DV_RSTD = nDV
            for j in range(32):
                nDV += 1; dvSgp[j] = nDV
                nDV += 1; dvOf[j] = nDV
            for idx in range(32):
                nDV += 1; dvWoE[idx] = nDV
            nAC = bF[ac] + 10
            acSilu = {}
            for t in range(4):
                for o in range(DC):
                    nAC += 1; acSilu[(t, o)] = nAC
            AC_SQRT = nAC + 1
            def stWo(idx):
                return bF[stsy] + 16 * (idx + 1)

            wo3 = wo_in.rearrange("(c p) o -> p c o", p=P)

            @blkDf.sync
            def _(sy):
                sy.wait_ge(pe.h, PE_L)
                sy.wait_ge(dv.h, DV_L)
                sy.wait_ge(ac.h, AC_L)
                for k in range(10):
                    if k >= 2:
                        sy.wait_ge(ac.h, bF[ac] + k - 1)
                    sl = wcbD[:, (k % 2) * D:(k % 2 + 1) * D]
                    if k < 8:
                        ldsy.inc(sy.dma_start(out=sl.unsqueeze(1),
                                              in_=wo3[:, k:k + 1, :]), 16)
                    elif k == 8:
                        ldsy.inc(sy.dma_start(
                            out=sl.rearrange("p (c o) -> p c o", c=DC),
                            in_=wg1T.rearrange("(c p) o -> p c o", p=P)), 16)
                    else:
                        ldsy.inc(sy.dma_start(out=sl.unsqueeze(1),
                                              in_=wg2T.unsqueeze(1)), 16)
                ldsy.inc(sy.dma_start(out=gn_sb[:], in_=gn_in), 16)
            LD_GN = ldsy.n

            @blkDf.tensor
            def _(t_):
                t_.wait_ge(ac.h, bF[ac] + 10)
                for t in range(4):
                    if t >= 1:
                        t_.wait_ge(dv.h, dvG1e[t - 1])
                    for ccx in range(DC):
                        inst = t_.matmul(psG1[:, 0:512],
                                         wg1_sb[:, ccx * 128:(ccx + 1) * 128],
                                         xnfm[:, ccx * TC + t * 512:ccx * TC + (t + 1) * 512],
                                         start=(ccx == 0), stop=(ccx == DC - 1))
                    pe.inc(inst)
                    for o in range(DC):
                        t_.wait_ge(dv.h, dvG1e[t])
                        if (t, o) != (0, 0):
                            po = (t, o - 1) if o else (t - 1, 7)
                            t_.wait_ge(ac.h, acSilu[po])
                        pe.inc(t_.matmul(psSG[:, 0:512],
                                         wg2_sb[:, o * P:(o + 1) * P],
                                         g1_blk[:, (t % 2) * 512:(t % 2 + 1) * 512],
                                         start=True, stop=True))
                for j in range(32):
                    t, c = divmod(j, 8)
                    t_.wait_ge(dv.h, dvO2[j])
                    if t >= 2 and c == 0:
                        t_.wait_ge(dv.h, dvVar[t - 2])
                    pe.inc(t_.matmul(psS[t % 2][0:1, 0:512], ones_col_bf[:],
                                     o2_sb[:, (j % 2) * 512:(j % 2 + 1) * 512],
                                     start=(c == 0), stop=(c == 7)))
                for t in range(4):
                    t_.wait_ge(dv.h, DV_RSTD)
                    if t >= 2:
                        t_.wait_ge(dv.h, dvOf[8 * (t - 2) + 7])
                    pe.inc(t_.matmul(psR[t % 2][:, 0:512], ones_row_f[:],
                                     ms_row[0:1, t * 512:(t + 1) * 512],
                                     start=True, stop=True))
                for idx in range(32):
                    t128, hf = divmod(idx, 2)
                    t512 = t128 // 4
                    t_.wait_ge(dv.h, dvOf[8 * t512 + 7])
                    if idx >= 2:
                        t_.wait_ge(dv.h, dvWoE[idx - 2])
                    for c in range(DC):
                        inst = t_.matmul(psW[idx % 2][:, 0:512],
                                         o_sb[:, c * TC + t128 * P:c * TC + (t128 + 1) * P],
                                         wo_sb[:, c * D + hf * 512:c * D + (hf + 1) * 512],
                                         start=(c == 0), stop=(c == DC - 1))
                    pe.inc(inst)

            @blkDf.vector
            def _(v):
                v.wait_ge(ac.h, AC_L)
                v.wait_ge(ldsy.h, LD_GN)
                for t in range(4):
                    v.wait_ge(pe.h, peG1[t])
                    dv.inc(v.tensor_copy(g1_blk[:, (t % 2) * 512:(t % 2 + 1) * 512],
                                         psG1[:, 0:512]))
                for t in range(4):
                    for c in range(8):
                        j = t * 8 + c
                        if j >= 2:
                            v.wait_ge(pe.h, peStat[j - 2])
                        src_ = o_sb[:, c * TC + t * 512:c * TC + (t + 1) * 512]
                        dv.inc(v.tensor_mul(o2_sb[:, (j % 2) * 512:(j % 2 + 1) * 512], src_, src_))
                    v.wait_ge(pe.h, peStat[8 * t + 7])
                    dv.inc(v.tensor_scalar(ms_row[0:1, t * 512:(t + 1) * 512],
                                           psS[t % 2][0:1, 0:512], 1.0 / D, 1e-5,
                                           ALU.mult, ALU.add))
                v.wait_ge(ac.h, AC_SQRT)
                dv.inc(v.reciprocal(ms_row[:], ms_row[:]))
                for j in range(32):
                    t, c = divmod(j, 8)
                    par = j % 2
                    v.wait_ge(pe.h, peRep[t])
                    v.wait_ge(ac.h, acSilu[(t, c)])
                    dv.inc(v.tensor_mul(sgp_sb[:, par * 512:(par + 1) * 512],
                                        sg_blk[:, t * DC * 512 + c * 512:t * DC * 512 + (c + 1) * 512],
                                        psR[t % 2][:, 0:512]))
                    v.wait_ge(dv.h, dvSgp[j])
                    dv.inc(v.scalar_tensor_tensor(o_sb[:, c * TC + t * 512:c * TC + (t + 1) * 512],
                                                  o_sb[:, c * TC + t * 512:c * TC + (t + 1) * 512],
                                                  gn_sb[:, c:c + 1],
                                                  sgp_sb[:, par * 512:(par + 1) * 512],
                                                  ALU.mult, ALU.mult))
                for idx in range(32):
                    v.wait_ge(pe.h, peWo[idx])
                    if idx >= 2:
                        v.wait_ge(stsy.h, stWo(idx - 2))
                    dv.inc(v.tensor_copy(wob[:, (idx % 2) * 512:(idx % 2 + 1) * 512],
                                         psW[idx % 2][:, 0:512]))

            @blkDf.scalar
            def _(s):
                for k in range(10):
                    s.wait_ge(ldsy.h, bF[ldsy] + 16 * (k + 1))
                    if k < 8:
                        dst = wo_sb[:, k * D:(k + 1) * D]
                    elif k == 8:
                        dst = wg1_sb[:]
                    else:
                        dst = wg2_sb[:]
                    ac.inc(s.activation(dst, wcbD[:, (k % 2) * D:(k % 2) * D + dst.shape[1]],
                                        AF.Copy))
                for t in range(4):
                    for o in range(DC):
                        s.wait_ge(pe.h, peSG[(t, o)])
                        ac.inc(s.activation(sg_blk[:, t * DC * 512 + o * 512:t * DC * 512 + (o + 1) * 512],
                                            psSG[:, 0:512], AF.Silu))
                s.wait_ge(dv.h, dvVar[3])
                ac.inc(s.activation(ms_row[:], ms_row[:], AF.Sqrt))

            @blkDf.sync
            def _(sy):
                a3 = attn0_d.ap().rearrange("(t p) d -> p t d", p=P)
                for idx in range(32):
                    t128, hf = divmod(idx, 2)
                    sy.wait_ge(dv.h, dvWoE[idx])
                    stsy.inc(sy.dma_start(
                        out=a3[:, t128:t128 + 1, hf * 512:(hf + 1) * 512],
                        in_=wob[:, (idx % 2) * 512:(idx % 2 + 1) * 512].unsqueeze(1)), 16)
            blkDf.__exit__(None, None, None)
            marks["attn0_st"] = stsy.n

        if upto == "D":
            close_stacks()
            return nc

        esQ.close()
        wfcT3 = wfcT.rearrange("(c p) f -> p c f", p=P)
        wpT3 = wpT.rearrange("(c p) f -> p c f", p=P)

        with ExitStack() as esE:
            eE = esE.enter_context
            h_sb = eE(nc.sbuf_tensor("h_sb", [P, HC * 512], BF16))
            h2_sb = eE(nc.sbuf_tensor("h2_sb", [P, 2 * 512], BF16))
            negmu_row = eE(nc.sbuf_tensor("negmu_row", [1, TC], F32))
            rstd_row = eE(nc.sbuf_tensor("rstd_rowE", [1, 512], F32))
            musq_s = eE(nc.sbuf_tensor("musq_s", [1, 512], F32))
            var_s = eE(nc.sbuf_tensor("var_s", [1, 512], F32))
            std_s = musq_s
            rstd_cols = eE(nc.sbuf_tensor("rstd_cols", [P, NT], F32))
            lnb_bf = eE(nc.sbuf_tensor("lnb_bf", [P, HC], BF16))
            lnbcb = eE(nc.sbuf_tensor("lnbcb", [P, HC], F32))
            c_row = eE(nc.sbuf_tensor("c_row", [1, D], F32))
            cc_row = eE(nc.sbuf_tensor("cc_row", [1, D], F32))
            psS1 = eE(nc.psum_tensor("psES1", [P, 512], F32))
            psS2 = eE(nc.psum_tensor("psES2", [P, 512], F32))

            bE = {s: s.n for s in (ldsy, ldgp, pe, dv, ac, gp, stsy)}
            E_dv, E_pe, E_ac, E_ld, E_ld2, E_st = (bE[dv], bE[pe], bE[ac],
                                                   bE[ldsy], bE[ldgp], bE[stsy])

            with ExitStack() as es_a:
                e1 = es_a.enter_context
                wfcq = e1(nc.sbuf_tensor("wfcq", [P, DC * H4], BF16))
                psF = [e1(nc.psum_tensor(f"psF{i}", [P, 512], F32)) for i in range(2)]

                blk = nc.Block("e1"); blk.__enter__()

                @blk.sync
                def _(sy):
                    sy.wait_ge(pe.h, E_pe)
                    sy.wait_ge(dv.h, E_dv)
                    sy.wait_ge(ac.h, E_ac)
                    sy.wait_ge(stsy.h, marks["attn0_st"])
                blk.__exit__(None, None, None)

                E1_dv = E_dv + 96
                E1_pe = E_pe
                E1_ac = E_ac
                fcchunks = [(fsl, c) for fsl in range(4) for c in range(DC)]
                E_ldg = ldgp.n

                def fc_tern_mark(k):
                    return E_dv + 3 * (k + 1)

                def fc_group_mark(fgrp8):
                    return fc_tern_mark(fgrp8 * 8 + 7)
                marks["fc_tern_done"] = fc_tern_mark(31)

                peN = {}
                n = E1_pe
                for t in range(4):
                    for f in range(HC):
                        n += 1
                        peN[("fc", t, f)] = n
                        if f >= 1:
                            n += 1
                            peN[("s1", t, f - 1)] = n
                            n += 1
                            peN[("s2", t, f - 1)] = n
                    n += 1
                    peN[("s1", t, HC - 1)] = n
                    n += 1
                    peN[("s2", t, HC - 1)] = n
                E2_pe = n
                acN = {}
                n = E1_ac
                for t in range(4):
                    for f in range(HC):
                        n += 1
                        acN[("gelu", t, f)] = n
                    n += 1
                    acN[("std", t)] = n
                E2_ac = n
                dvN = {}
                n = E1_dv
                for t in range(4):
                    for f in range(HC):
                        n += 1
                        dvN[("h2", t, f)] = n
                    for k in ("negmu", "musq", "var", "rstd"):
                        n += 1
                        dvN[(k, t)] = n
                E2_dv = n
                E2_st = E_st + 16 * 7

                blk2 = nc.Block("e2"); blk2.__enter__()

                @blk2.tensor
                def _(t_):
                    for t in range(4):
                        for f in range(HC):
                            t_.wait_ge(dv.h, fc_group_mark(f // 8))
                            pf = t * HC + f
                            if pf >= 2:
                                t2, f2 = divmod(pf - 2, HC)
                                t_.wait_ge(ac.h, acN[("gelu", t2, f2)])
                            for ccx in range(DC):
                                inst = t_.matmul(psF[pf % 2][:, 0:512],
                                                 wfcq[:, ccx * H4 + f * P:ccx * H4 + (f + 1) * P],
                                                 xnfm[:, ccx * TC + t * 512:ccx * TC + (t + 1) * 512],
                                                 start=(ccx == 0), stop=(ccx == DC - 1))
                            pe.inc(inst)
                            if f >= 1:
                                t_.wait_ge(ac.h, acN[("gelu", t, f - 1)])
                                pe.inc(t_.matmul(psS1[0:1, 0:512], ones_col_bf[:],
                                                 h_sb[:, (f - 1) * 512:f * 512],
                                                 start=(f == 1), stop=False))
                                t_.wait_ge(dv.h, dvN[("h2", t, f - 1)])
                                pe.inc(t_.matmul(psS2[0:1, 0:512], ones_col_bf[:],
                                                 h2_sb[:, ((f - 1) % 2) * 512:((f - 1) % 2 + 1) * 512],
                                                 start=(f == 1), stop=False))
                        t_.wait_ge(ac.h, acN[("gelu", t, HC - 1)])
                        pe.inc(t_.matmul(psS1[0:1, 0:512], ones_col_bf[:],
                                         h_sb[:, (HC - 1) * 512:HC * 512],
                                         start=False, stop=True))
                        t_.wait_ge(dv.h, dvN[("h2", t, HC - 1)])
                        pe.inc(t_.matmul(psS2[0:1, 0:512], ones_col_bf[:],
                                         h2_sb[:, ((HC - 1) % 2) * 512:((HC - 1) % 2 + 1) * 512],
                                         start=False, stop=True))

                @blk2.scalar
                def _(sc):
                    for t in range(4):
                        for f in range(HC):
                            if t >= 1 and f == 0:
                                sc.wait_ge(stsy.h, E_st + 16 * (2 * t - 1))
                            sc.wait_ge(pe.h, peN[("fc", t, f)])
                            ac.inc(sc.activation(h_sb[:, f * 512:(f + 1) * 512],
                                                 psF[(t * HC + f) % 2][:, 0:512], AF.Gelu))
                        sc.wait_ge(dv.h, dvN[("var", t)])
                        ac.inc(sc.activation(std_s[:], var_s[:], AF.Sqrt,
                                             bias=eps_row[0:1, 0:1]))

                @blk2.vector
                def _(v):
                    v.wait_ge(dv.h, marks["thr_dv"])
                    for k, (fsl, c) in enumerate(fcchunks):
                        par = k % 2
                        ch = wgbuf[:, par * GG:(par + 1) * GG]
                        v.wait_ge(ldgp.h, E_ldg + 16 * (k + 1))
                        if k >= 1:
                            v.wait_ge(dv.h, E_dv + 3 * k)
                        dv.inc(v.tensor_scalar(t1q[:], ch, thr_cols[:, 0:1], None,
                                               ALU.is_gt))
                        dv.inc(v.tensor_scalar(t2q[:], ch, nthr_cols[:, 0:1], None,
                                               ALU.is_lt))
                        v.wait_ge(dv.h, E_dv + 3 * k + 2)
                        dv.inc(v.tensor_sub(
                            wfcq[:, c * H4 + fsl * GG:c * H4 + (fsl + 1) * GG],
                            t1q[:], t2q[:]))
                    for t in range(4):
                        for f in range(HC):
                            v.wait_ge(ac.h, acN[("gelu", t, f)])
                            pf = t * HC + f
                            if pf >= 2:
                                t2, f2 = divmod(pf - 2, HC)
                                v.wait_ge(pe.h, peN[("s2", t2, f2)])
                            src = h_sb[:, f * 512:(f + 1) * 512]
                            dv.inc(v.tensor_mul(h2_sb[:, (pf % 2) * 512:(pf % 2 + 1) * 512],
                                                src, src))
                        v.wait_ge(pe.h, peN[("s1", t, HC - 1)])
                        dv.inc(v.tensor_scalar(negmu_row[0:1, t * 512:(t + 1) * 512],
                                               psS1[0:1, 0:512], -1.0 / H4, None, ALU.mult))
                        v.wait_ge(dv.h, dvN[("negmu", t)])
                        if t >= 1:
                            v.wait_ge(ac.h, acN[("std", t - 1)])
                        dv.inc(v.tensor_mul(musq_s[:],
                                            negmu_row[0:1, t * 512:(t + 1) * 512],
                                            negmu_row[0:1, t * 512:(t + 1) * 512]))
                        v.wait_ge(pe.h, peN[("s2", t, HC - 1)])
                        v.wait_ge(dv.h, dvN[("musq", t)])
                        dv.inc(v.scalar_tensor_tensor(var_s[:], psS2[0:1, 0:512],
                                                      1.0 / H4, musq_s[:],
                                                      ALU.mult, ALU.subtract))
                        v.wait_ge(ac.h, acN[("std", t)])
                        if t >= 1:
                            v.wait_ge(stsy.h, E_st + 16 * (2 * t))
                        dv.inc(v.reciprocal(rstd_row[0:1, 0:512], std_s[:]))

                @blk2.gpsimd
                def _(g):
                    assert ldgp.n == E_ldg
                    for k, (fsl, c) in enumerate(fcchunks):
                        par = k % 2
                        if k < 2:
                            g.wait_ge(ac.h, marks["gamma_ac"])
                        if k >= 2:
                            g.wait_ge(dv.h, fc_tern_mark(k - 2))
                        ldgp.inc(g.dma_start(
                            out=wgbuf[:, par * GG:(par + 1) * GG].unsqueeze(1),
                            in_=wfcT3g[:, c:c + 1, fsl * GG:(fsl + 1) * GG]), 16)

                @blk2.sync
                def _(sy):
                    for t in range(4):
                        sy.wait_ge(ac.h, acN[("gelu", t, HC - 1)])
                        sy.wait_ge(dv.h, dvN[("h2", t, HC - 1)])
                        stsy.inc(sy.dma_start(out=h_scr.ap()[t * P:(t + 1) * P, :],
                                              in_=h_sb[:]), 16)
                        sy.wait_ge(dv.h, dvN[("rstd", t)])
                        stsy.inc(sy.dma_start(out=bounce.ap()[0:1, t * 512:(t + 1) * 512],
                                              in_=rstd_row[:]), 16)
                    sy.wait_ge(stsy.h, stsy.n)
                    stsy.inc(sy.dma_start(out=rstd_cols[:].unsqueeze(2),
                                          in_=bounce.ap().rearrange("a (t p) -> p t a", p=P),
                                          allow_slow_non_contiguous=True), 16)
                blk2.__exit__(None, None, None)
                E2_ld = ldsy.n
                E2_st2 = stsy.n

            wpq = eE(nc.sbuf_tensor("wpq", [P, HC * D], BF16))
            psCC = eE(nc.psum_tensor("psCC", [P, 512], F32))
            psCr = eE(nc.psum_tensor("psCr", [P, 512], F32))
            psREP = eE(nc.psum_tensor("psREP", [P, 512], F32))
            psREPb = eE(nc.psum_tensor("psREPb", [P, 512], F32))

            b3 = {s: s.n for s in (ldsy, ldgp, pe, dv, ac)}
            blk3 = nc.Block("e3"); blk3.__enter__()

            def p_tern_mark(c):
                return b3[dv] + 3 * (c + 1)

            @blk3.sync
            def _(sy):
                ldsy.inc(sy.dma_start(out=lnbcb[:], in_=lnb_in), 16)
            LD_LNB = ldsy.n

            @blk3.gpsimd
            def _(g):
                assert ldgp.n == b3[ldgp]
                for c in range(HC):
                    par = c % 2
                    if c < 2:
                        g.wait_ge(dv.h, marks["fc_tern_done"])
                    if c >= 2:
                        g.wait_ge(dv.h, p_tern_mark(c - 2))
                    ldgp.inc(g.dma_start(
                        out=wgbuf[:, par * GG:(par + 1) * GG].unsqueeze(1),
                        in_=wpT3g[:, c:c + 1, :]), 16)

            @blk3.scalar
            def _(s):
                s.wait_ge(ldsy.h, LD_LNB)
                ac.inc(s.activation(lnb_bf[:], lnbcb[:], AF.Copy))
            AC_LNB = ac.n

            PE_LNB = b3[pe] + 2 * HC
            PE_ONES = PE_LNB + 2 * HC
            PE_CCREP1 = PE_ONES + 1
            PE_CCREP2 = PE_ONES + 2
            DV_TERN3 = b3[dv] + 3 * HC

            def dvScale3(c):
                return DV_TERN3 + c + 1
            DV_SCALES = DV_TERN3 + HC
            DV_CCROW = DV_SCALES + 2
            DV_CCREP = DV_SCALES + 4

            @blk3.tensor
            def _(t_):
                t_.wait_ge(ac.h, AC_LNB)
                t_.wait_ge(dv.h, b3[dv])
                for c in range(HC):
                    t_.wait_ge(dv.h, p_tern_mark(c))
                    pe.inc(t_.matmul(psCC[0:1, 0:512], lnb_bf[:, c:c + 1],
                                     wpq[:, c * D:c * D + 512],
                                     start=(c == 0), stop=(c == HC - 1)))
                    pe.inc(t_.matmul(psS1[0:1, 0:512], lnb_bf[:, c:c + 1],
                                     wpq[:, c * D + 512:(c + 1) * D],
                                     start=(c == 0), stop=(c == HC - 1)))
                for c in range(HC):
                    t_.wait_ge(dv.h, dvScale3(c))
                    pe.inc(t_.matmul(psCr[0:1, 0:512], ones_col_bf[:],
                                     wpq[:, c * D:c * D + 512],
                                     start=(c == 0), stop=(c == HC - 1)))
                    pe.inc(t_.matmul(psS2[0:1, 0:512], ones_col_bf[:],
                                     wpq[:, c * D + 512:(c + 1) * D],
                                     start=(c == 0), stop=(c == HC - 1)))
                t_.wait_ge(dv.h, DV_CCROW)
                pe.inc(t_.matmul(psREP[:, 0:512], ones_row_f[:], cc_row[0:1, 0:512],
                                 start=True, stop=True))
                t_.wait_ge(dv.h, DV_CCROW)
                pe.inc(t_.matmul(psREPb[:, 0:512], ones_row_f[:], cc_row[0:1, 512:1024],
                                 start=True, stop=True))

            @blk3.vector
            def _(v):
                for c in range(HC):
                    par = c % 2
                    ch = wgbuf[:, par * GG:(par + 1) * GG]
                    v.wait_ge(ldgp.h, b3[ldgp] + 16 * (c + 1))
                    if c >= 1:
                        v.wait_ge(dv.h, b3[dv] + 3 * c)
                    dv.inc(v.tensor_scalar(t1q[:], ch, thr_cols[:, 1:2], None,
                                           ALU.is_gt))
                    dv.inc(v.tensor_scalar(t2q[:], ch, nthr_cols[:, 1:2], None,
                                           ALU.is_lt))
                    v.wait_ge(dv.h, b3[dv] + 3 * c + 2)
                    dv.inc(v.tensor_sub(wpq[:, c * D:(c + 1) * D], t1q[:], t2q[:]))
                for c in range(HC):
                    v.wait_ge(pe.h, b3[pe] + 2 * (c + 1))
                    dv.inc(v.tensor_scalar(wpq[:, c * D:(c + 1) * D],
                                           wpq[:, c * D:(c + 1) * D],
                                           lnw_sb[:, c:c + 1], None, ALU.mult))
                v.wait_ge(pe.h, PE_LNB)
                dv.inc(v.tensor_copy(cc_row[0:1, 0:512], psCC[0:1, 0:512]))
                dv.inc(v.tensor_copy(cc_row[0:1, 512:1024], psS1[0:1, 0:512]))
                v.wait_ge(pe.h, PE_ONES)
                dv.inc(v.tensor_copy(c_row[0:1, 0:512], psCr[0:1, 0:512]))
                dv.inc(v.tensor_copy(c_row[0:1, 512:1024], psS2[0:1, 0:512]))
            blk3.__exit__(None, None, None)
            xbuf = eE(nc.sbuf_tensor("xbuf", [P, 4 * D], F32))
            abuf = eE(nc.sbuf_tensor("abuf", [P, 4 * D], F32))
            tmpA = eE(nc.sbuf_tensor("tmpA", [P, 512], F32))
            psJ = [eE(nc.psum_tensor(f"psJ{i}", [P, 512], F32)) for i in range(2)]

            PE4 = PE_CCREP2
            DV4 = DV_CCREP
            peJ = {}
            n = PE4
            for t in range(4):
                for idx in range(8):
                    n += 1
                    peJ[(t, idx)] = n
            dvE = {}
            n = DV4
            for t in range(4):
                for idx in range(8):
                    dvE[(t, idx)] = (n + 1, n + 2, n + 3)
                    n += 3
            ld4 = {}
            n = ldgp.n
            for t in range(4):
                ld4[("h", t)] = n + 16
                ld4[("x", t)] = n + 32
                ld4[("a", t)] = n + 48
                n += 48
            stO = {t: E2_st2 + 16 * (t + 1) for t in range(4)}

            blk4 = nc.Block("e4"); blk4.__enter__()

            @blk4.gpsimd
            def _(g):
                g.wait_ge(stsy.h, marks["attn0_st"])
                for t in range(4):
                    if t == 0:
                        g.wait_ge(stsy.h, E2_st)
                    else:
                        g.wait_ge(pe.h, peJ[(t - 1, 7)])
                    ldgp.inc(g.dma_start(out=h_sb[:],
                                         in_=h_scr.ap()[t * P:(t + 1) * P, :]), 16)
                    if t >= 1:
                        g.wait_ge(dv.h, dvE[(t - 1, 7)][2])
                        g.wait_ge(stsy.h, stO[t - 1])
                    ldgp.inc(g.dma_start(out=xbuf[:].rearrange("p (u d) -> p u d", u=4),
                                         in_=xin.rearrange("(t p) d -> p t d", p=P)[:, 4 * t:4 * t + 4, :]), 16)
                    ldgp.inc(g.dma_start(out=abuf[:].rearrange("p (u d) -> p u d", u=4),
                                         in_=attn0_d.ap().rearrange("(t p) d -> p t d", p=P)[:, 4 * t:4 * t + 4, :]), 16)

            @blk4.tensor
            def _(t_):
                t_.wait_ge(dv.h, DV4)
                for t in range(4):
                    t_.wait_ge(ldgp.h, ld4[("a", t)])
                    for idx in range(8):
                        tt, half = divmod(idx, 2)
                        tg = 4 * t + tt
                        gi = t * 8 + idx
                        if gi >= 2:
                            t2, i2 = divmod(gi - 2, 8)
                            t_.wait_ge(dv.h, dvE[(t2, i2)][0])
                        for f in range(HC):
                            t_.matmul(psJ[idx % 2][:, 0:512],
                                      h_sb[:, f * 512 + tt * P:f * 512 + (tt + 1) * P],
                                      wpq[:, f * D + half * 512:f * D + (half + 1) * 512],
                                      start=(f == 0), stop=False)
                        pe.inc(t_.matmul(psJ[idx % 2][:, 0:512],
                                         negmu_row[0:1, tg * P:(tg + 1) * P],
                                         c_row[0:1, half * 512:(half + 1) * 512],
                                         start=False, stop=True))

            @blk4.vector
            def _(v):
                v.wait_ge(stsy.h, E2_st2)
                v.wait_ge(pe.h, PE_CCREP2)
                for t in range(4):
                    v.wait_ge(ldgp.h, ld4[("a", t)])
                    for idx in range(8):
                        tt, half = divmod(idx, 2)
                        tg = 4 * t + tt
                        par = idx % 2
                        e1n, e2n, e3n = dvE[(t, idx)]
                        v.wait_ge(pe.h, peJ[(t, idx)])
                        if (t, idx) != (0, 0):
                            pv = dvE[(t, idx - 1)][2] if idx else dvE[(t - 1, 7)][2]
                            v.wait_ge(dv.h, pv)
                        dv.inc(v.scalar_tensor_tensor(tmpA[:, 0:512],
                                                      psJ[par][:, 0:512],
                                                      rstd_cols[:, tg:tg + 1],
                                                      xbuf[:, tt * D + half * 512:tt * D + (half + 1) * 512],
                                                      ALU.mult, ALU.add))
                        v.wait_ge(dv.h, e1n)
                        dv.inc(v.tensor_add(tmpA[:, 0:512],
                                            tmpA[:, 0:512],
                                            abuf[:, tt * D + half * 512:tt * D + (half + 1) * 512]))
                        v.wait_ge(dv.h, e2n)
                        dv.inc(v.tensor_add(xbuf[:, tt * D + half * 512:tt * D + (half + 1) * 512],
                                            tmpA[:, 0:512],
                                            (psREP if half == 0 else psREPb)[:, 0:512]))

            @blk4.sync
            def _(sy):
                for t in range(4):
                    sy.wait_ge(dv.h, dvE[(t, 7)][2])
                    stsy.inc(sy.dma_start(
                        out=x1_d.ap().rearrange("(t p) d -> p t d", p=P)[:, 4 * t:4 * t + 4, :],
                        in_=xbuf[:].rearrange("p (u d) -> p u d", u=4)), 16)
                if upto == "E":
                    sy.wait_ge(stsy.h, stsy.n)
            blk4.__exit__(None, None, None)
            marks["x1_st"] = stsy.n
        if upto == "E":
            close_stacks()
            return nc
        esXn.close()

        xn1fm = e(nc.sbuf_tensor("xn1fm", [P, DC * TC], BF16))
        esT = ExitStack()
        xn1tm = esT.enter_context(nc.sbuf_tensor("xn1tm", [P, NT * D], BF16))

        with ExitStack() as esF:
            eF = esF.enter_context
            x_sb = eF(nc.sbuf_tensor("x_sbF", [P, NT * D], F32))
            o_sb2 = x_sb
            s1 = eF(nc.sbuf_tensor("s1F", [P, NT], F32))
            s2 = eF(nc.sbuf_tensor("s2F", [P, NT], F32))
            mu = eF(nc.sbuf_tensor("muF", [P, NT], F32))
            var = eF(nc.sbuf_tensor("varF", [P, NT], F32))
            musq = eF(nc.sbuf_tensor("musqF", [P, NT], F32))
            std = eF(nc.sbuf_tensor("stdF", [P, NT], F32))
            rstd = eF(nc.sbuf_tensor("rstdF", [P, NT], F32))
            scr = eF(nc.sbuf_tensor("scrF", [P, D], F32))
            psT = [eF(nc.psum_tensor(f"psTF{i}", [P, D], F32)) for i in range(2)]

            b = {s: s.n for s in (ldsy, pe, dv, ac)}
            blkF = nc.Block("sF"); blkF.__enter__()

            ln_norm2 = []

            @blkF.sync
            def _(sy):
                sy.wait_ge(stsy.h, marks["x1_st"])
                sy.wait_ge(pe.h, b[pe])
                sy.wait_ge(dv.h, b[dv])
                sy.wait_ge(ac.h, b[ac])
                ldsy.inc(sy.dma_start(out=x_sb[:].rearrange("p (t d) -> p t d", t=NT),
                                      in_=x1_d.ap().rearrange("(t p) d -> p t d", p=P)), 16)
            LD_X1 = ldsy.n

            def f_cast_pred(t):
                return b[ac] + NT + 1 + t + 1

            def f_mm_pred(t):
                return b[pe] + t + 1

            def f_evac_pred(t):
                return b[dv] + 2 * NT + 4 + t + 1

            @blkF.vector
            def _(v):
                v.wait_ge(ldsy.h, LD_X1)
                for t in range(NT):
                    dv.inc(v.reduce_sum(s1[:, t:t + 1], x_sb[:, t * D:(t + 1) * D], axis=AX.X))
                v.wait_ge(dv.h, b[dv] + NT)
                dv.inc(v.tensor_scalar(mu[:], s1[:], 1.0 / D, None, ALU.mult))
                v.wait_ge(dv.h, b[dv] + NT + 1)
                dv.inc(v.tensor_mul(musq[:], mu[:], mu[:]))
                v.wait_ge(ac.h, b[ac] + NT)
                v.wait_ge(dv.h, b[dv] + NT + 2)
                dv.inc(v.scalar_tensor_tensor(var[:], s2[:], 1.0 / D, musq[:],
                                              ALU.mult, ALU.subtract))
                v.wait_ge(ac.h, b[ac] + NT + 1)
                dv.inc(v.reciprocal(rstd[:], std[:]))
                v.wait_ge(dv.h, b[dv] + NT + 4)
                for t in range(NT):
                    ln_norm2.append(dv.inc(v.tensor_scalar(
                        o_sb2[:, t * D:(t + 1) * D], x_sb[:, t * D:(t + 1) * D],
                        mu[:, t:t + 1], rstd[:, t:t + 1], ALU.subtract, ALU.mult)))
                for t in range(NT):
                    v.wait_ge(pe.h, f_mm_pred(t))
                    for c in range(DC):
                        inst = v.tensor_copy(
                            xn1fm[:, c * TC + t * P:c * TC + (t + 1) * P],
                            psT[t % 2][:, c * P:(c + 1) * P])
                    dv.inc(inst)

            @blkF.scalar
            def _(s):
                s.wait_ge(ldsy.h, LD_X1)
                for t in range(NT):
                    if t >= 1:
                        s.wait_ge(ac.h, b[ac] + t)
                    ac.inc(s.activation(scr[:], x_sb[:, t * D:(t + 1) * D], AF.Square,
                                        accum_out=s2[:, t:t + 1]))
                s.wait_ge(dv.h, b[dv] + NT + 3)
                ac.inc(s.activation(std[:], var[:], AF.Sqrt, bias=eps_col[:]))
                for t in range(NT):
                    s.wait_ge(dv.h, ln_norm2[t])
                    a_ = ac.inc(s.activation(xn1tm[:, t * D:(t + 1) * D],
                                             o_sb2[:, t * D:(t + 1) * D], AF.Copy))

            @blkF.tensor
            def _(t_):
                t_.wait_ge(dv.h, b[dv])
                for t in range(NT):
                    t_.wait_ge(ac.h, f_cast_pred(t))
                    if t >= 2:
                        t_.wait_ge(dv.h, f_evac_pred(t - 2))
                    for c in range(DC):
                        inst = t_.matmul(psT[t % 2][:, c * P:(c + 1) * P],
                                         xn1tm[:, t * D + c * P:t * D + (c + 1) * P],
                                         ident[:], start=True, stop=True)
                    pe.inc(inst)
            blkF.__exit__(None, None, None)
            marks["xn1tm_ac"] = ac.n
            marks["xn1fm_dv"] = dv.n

            blkFs = nc.Block("sFs"); blkFs.__enter__()

            @blkFs.gpsimd
            def _(g):
                g.wait_ge(dv.h, marks["xn1fm_dv"])
                stgp.inc(g.dma_start(
                    out=xn1fm_d.ap().rearrange("(c p) t -> p c t", p=P),
                    in_=xn1fm[:].rearrange("p (c t) -> p c t", c=DC)), 16)
            blkFs.__exit__(None, None, None)
            marks["xn1fm_st"] = stgp.n

            if upto == "F":
                blkD = nc.Block("dbgF"); blkD.__enter__()

                @blkD.gpsimd
                def _(g):
                    g.wait_ge(dv.h, marks["xn1fm_dv"])
                    stsy.inc(g.dma_start(
                        out=dbg.rearrange("(c p) t -> p c t", p=P),
                        in_=xn1fm[:].rearrange("p (c t) -> p c t", c=DC)), 16)
                    g.wait_ge(stsy.h, stsy.n)
                blkD.__exit__(None, None, None)

        if upto == "F":
            esT.close()
            return nc

        with ExitStack() as esG:
            eG = esG.enter_context
            mbuf = [eG(nc.sbuf_tensor(f"mbuf{i}", [P, 512], F32)) for i in range(2)]
            psG = [eG(nc.psum_tensor(f"psGm{i}", [P, 512], F32)) for i in range(2)]

            b = {s: s.n for s in (pe, dv, stsy)}
            marks["mloc_st_pred"] = b[stsy] + 16 * 16
            blkG = nc.Block("sG"); blkG.__enter__()
            m3 = mloc_d.ap().rearrange("(d p) o -> p d o", p=P)

            @blkG.tensor
            def _(t_):
                t_.wait_ge(ac.h, marks["xn1tm_ac"])
                t_.wait_ge(dv.h, marks["xn1fm_dv"])
                for k in range(16):
                    d1, half = divmod(k, 2)
                    if k >= 2:
                        t_.wait_ge(dv.h, b[dv] + k - 1)
                    for ti in range(NT):
                        inst = t_.matmul(psG[k % 2][:, 0:512],
                                         xn1tm[:, ti * D + d1 * P:ti * D + (d1 + 1) * P],
                                         xn1tm[:, ti * D + half * 512:ti * D + (half + 1) * 512],
                                         start=(ti == 0), stop=(ti == NT - 1))
                    pe.inc(inst)

            @blkG.vector
            def _(v):
                for k in range(16):
                    v.wait_ge(pe.h, b[pe] + k + 1)
                    if k >= 2:
                        v.wait_ge(stsy.h, b[stsy] + 16 * (k - 1))
                    dv.inc(v.tensor_copy(mbuf[k % 2][:], psG[k % 2][:, 0:512]))

            @blkG.sync
            def _(sy):
                for k in range(16):
                    d1, half = divmod(k, 2)
                    sy.wait_ge(dv.h, b[dv] + k + 1)
                    stsy.inc(sy.dma_start(out=m3[:, d1:d1 + 1, half * 512:(half + 1) * 512],
                                          in_=mbuf[k % 2][:].unsqueeze(1)), 16)
            @blkG.gpsimd
            def _(g):
                g.wait_ge(stsy.h, marks["mloc_st_pred"])
                g.wait_ge(stsy.h, marks["x1_st"])
                g.wait_ge(stgp.h, marks["xn1fm_st"])
            blkG.__exit__(None, None, None)
            marks["mloc_st"] = stsy.n
        esT.close()
        return nc


def build_l2():
    nc = bass.Bass("TRN2", target_bir_lowering=False, debug=False)
    x1_in = nc.dram_tensor("x1", [TC, D], F32, kind="ExternalInput").ap()
    xn1fm = nc.dram_tensor("xn1fm", [D, TC], F32, kind="ExternalInput").ap()
    mparts = nc.dram_tensor("mparts", [4 * D, D], F32, kind="ExternalInput").ap()
    wq1_in = nc.dram_tensor("wq1", [D, D], F32, kind="ExternalInput").ap()
    wg_in = nc.dram_tensor("wg", [D, D], F32, kind="ExternalInput").ap()
    bg_in = nc.dram_tensor("bg", [1, D], F32, kind="ExternalInput").ap()
    we_in = nc.dram_tensor("we", [8 * D, D], F32, kind="ExternalInput").ap()
    be_in = nc.dram_tensor("be", [1, 8 * D], F32, kind="ExternalInput").ap()
    x2_out = nc.dram_tensor("x2", [TC, D], F32, kind="ExternalOutput").ap()

    with ExitStack() as es:
        e = es.enter_context
        xn_sb = e(nc.sbuf_tensor("xn_sb", [P, DC * TC], BF16))
        m_bf = e(nc.sbuf_tensor("m_bf", [P, DC * D], BF16))
        w2_sb = e(nc.sbuf_tensor("w2_sb", [P, DC * D], BF16))
        acc = e(nc.sbuf_tensor("acc", [P, NT * D], BF16))
        msel = e(nc.sbuf_tensor("msel", [P, 8 * NT], F32))
        lg = e(nc.sbuf_tensor("lg", [P, 2 * D], F32))
        sexp = e(nc.sbuf_tensor("sexp", [P, D], BF16))
        max8 = e(nc.sbuf_tensor("max8", [P, 8], F32))
        idx8 = e(nc.sbuf_tensor("idx8", [P, 8], U32))
        eidx_u = e(nc.sbuf_tensor("eidx_u", [P, 1], U32))
        seven = e(nc.sbuf_tensor("seven", [P, 1], U32))
        nmx_all = e(nc.sbuf_tensor("nmx_all", [P, NT], F32))
        sume_all = e(nc.sbuf_tensor("sume_all", [P, NT], F32))
        p_all = e(nc.sbuf_tensor("p_all", [P, NT], F32))
        eidx_all = e(nc.sbuf_tensor("eidx_all", [P, NT], F32))
        tmp16 = e(nc.sbuf_tensor("tmp16", [P, NT], F32))
        tmpe = e(nc.sbuf_tensor("tmpe", [P, 2 * 512], BF16))
        x1b = e(nc.sbuf_tensor("x1b", [P, 2 * D], F32))
        ones_row_bf = e(nc.sbuf_tensor("ones_row_bf", [1, P], BF16))
        psU = [e(nc.psum_tensor(f"psU{i}", [P, 512], F32)) for i in range(2)]
        ld = SemCtr(es, nc, "ld")
        ldw = SemCtr(es, nc, "ldw")
        gp = SemCtr(es, nc, "gp")
        pe = SemCtr(es, nc, "pe")
        dv = SemCtr(es, nc, "dv")
        ac = SemCtr(es, nc, "ac")
        st = SemCtr(es, nc, "st")

        with ExitStack() as es1:
            e1 = es1.enter_context
            msum = e1(nc.sbuf_tensor("msum", [P, DC * D], F32))
            mtmp = e1(nc.sbuf_tensor("mtmp", [P, DC * D], F32))
            wq1_bf = e1(nc.sbuf_tensor("wq1_bf", [P, DC * D], BF16))
            blk = nc.Block("h1"); blk.__enter__()
            mp3 = mparts.rearrange("(r c p) o -> p r c o", r=4, p=P)

            @blk.gpsimd
            def _(g):
                gp.inc(g.memset(ones_row_bf[:], 1.0))
                gp.inc(g.memset(seven[:], 7))
                ld.inc(g.dma_start(out=msum[:].rearrange("p (c o) -> p c o", c=DC),
                                   in_=mp3[:, 0]), 16)
                ld.inc(g.dma_start(out=mtmp[:].rearrange("p (c o) -> p c o", c=DC),
                                   in_=mp3[:, 1]), 16)
                g.wait_ge(dv.h, 1)
                ld.inc(g.dma_start(out=mtmp[:].rearrange("p (c o) -> p c o", c=DC),
                                   in_=mp3[:, 2]), 16)
                g.wait_ge(dv.h, 2)
                ld.inc(g.dma_start(out=mtmp[:].rearrange("p (c o) -> p c o", c=DC),
                                   in_=mp3[:, 3]), 16)
                ldw.inc(g.dma_start(out=wq1_bf[:].rearrange("p (c o) -> p c o", c=DC),
                                    in_=wq1_in.rearrange("(c p) o -> p c o", p=P)), 16)
                ldw.inc(g.dma_start(out=xn_sb[:].rearrange("p (c t) -> p c t", c=DC),
                                    in_=xn1fm.rearrange("(c p) t -> p c t", p=P)), 16)

            @blk.vector
            def _(v):
                v.wait_ge(ld.h, 32)
                dv.inc(v.tensor_add(msum[:], msum[:], mtmp[:]))
                v.wait_ge(ld.h, 48)
                v.wait_ge(dv.h, 1)
                dv.inc(v.tensor_add(msum[:], msum[:], mtmp[:]))
                v.wait_ge(ld.h, 64)
                v.wait_ge(dv.h, 2)
                dv.inc(v.tensor_add(msum[:], msum[:], mtmp[:]))
                v.wait_ge(dv.h, 3)
                dv.inc(v.tensor_copy(m_bf[:], msum[:]))

            @blk.tensor
            def _(t_):
                t_.wait_ge(ldw.h, 16)
                t_.wait_ge(dv.h, 4)
                for k in range(16):
                    hc, half = divmod(k, 2)
                    if k >= 2:
                        t_.wait_ge(dv.h, 4 + k - 1)
                    for oc in range(DC):
                        inst = t_.matmul(psU[k % 2][:, 0:512],
                                         wq1_bf[:, oc * D + hc * P:oc * D + (hc + 1) * P],
                                         m_bf[:, oc * D + half * 512:oc * D + (half + 1) * 512],
                                         start=(oc == 0), stop=(oc == DC - 1))
                    pe.inc(inst)

            @blk.vector
            def _(v):
                for k in range(16):
                    hc, half = divmod(k, 2)
                    v.wait_ge(pe.h, k + 1)
                    dv.inc(v.tensor_copy(w2_sb[:, hc * D + half * 512:hc * D + (half + 1) * 512],
                                         psU[k % 2][:, 0:512]))
            blk.__exit__(None, None, None)
        H2_pe = 16
        H2_dv = 20
        H2_gp = 2

        wg_bf = e(nc.sbuf_tensor("wg_bf", [P, DC * D], BF16))
        webf = [e(nc.sbuf_tensor(f"webf{i}", [P, DC * D], BF16)) for i in range(2)]
        bg_bf = e(nc.sbuf_tensor("bg_bf", [1, D], BF16))
        be_bf = e(nc.sbuf_tensor("be_bf", [1, 8 * D], BF16))
        dvN = {}
        n = H2_dv
        for t in range(NT):
            for k in ("ev0", "ev1", "max8", "nmx", "idx", "and", "eidx"):
                n += 1
                dvN[(k, t)] = n
        dvN["recip"] = n + 1
        n += 1
        for ee in range(8):
            dvN[("mseleq", ee)] = n + 1
            dvN[("mselmul", ee)] = n + 2
            n += 2
        H3_dv = n
        peN = {}
        n = H2_pe
        for t in range(NT):
            for half in range(2):
                n += 1
                peN[("lg", t, half)] = n
        H3_pe = n
        acN = {}
        n = 0
        for t in range(NT):
            n += 1
            acN[("exp", t)] = n
        H3_ac = n

        blk3 = nc.Block("h3"); blk3.__enter__()

        @blk3.gpsimd
        def _(g):
            ldw.inc(g.dma_start(out=wg_bf[:].rearrange("p (c o) -> p c o", c=DC),
                                in_=wg_in.rearrange("(c p) o -> p c o", p=P)), 16)
            ldw.inc(g.dma_start(out=bg_bf[:], in_=bg_in), 16)
            ldw.inc(g.dma_start(out=be_bf[:], in_=be_in), 16)

        @blk3.tensor
        def _(t_):
            t_.wait_ge(ldw.h, 80)
            t_.wait_ge(gp.h, 1)
            for t in range(NT):
                for half in range(2):
                    k = t * 2 + half
                    if k >= 2:
                        t2, h2 = divmod(k - 2, 2)
                        t_.wait_ge(dv.h, dvN[("ev" + str(h2), t2)])
                    for cc in range(DC):
                        t_.matmul(psU[k % 2][:, 0:512],
                                  xn_sb[:, cc * TC + t * P:cc * TC + (t + 1) * P],
                                  wg_bf[:, cc * D + half * 512:cc * D + (half + 1) * 512],
                                  start=(cc == 0), stop=False)
                    pe.inc(t_.matmul(psU[k % 2][:, 0:512], ones_row_bf[:],
                                     bg_bf[0:1, half * 512:(half + 1) * 512],
                                     start=False, stop=True))

        @blk3.vector
        def _(v):
            for t in range(NT):
                for half in range(2):
                    v.wait_ge(pe.h, peN[("lg", t, half)])
                    if t >= 2:
                        v.wait_ge(ac.h, acN[("exp", t - 2)])
                    dv.inc(v.tensor_copy(lg[:, (t % 2) * D + half * 512:(t % 2) * D + (half + 1) * 512],
                                         psU[(t * 2 + half) % 2][:, 0:512]))
                v.wait_ge(dv.h, dvN[("ev1", t)])
                dv.inc(v.max(max8[:], lg[:, (t % 2) * D:(t % 2 + 1) * D]))
                v.wait_ge(dv.h, dvN[("max8", t)])
                dv.inc(v.tensor_scalar(nmx_all[:, t:t + 1], max8[:, 0:1], -1.0, None,
                                       ALU.mult))
                dv.inc(v.max_index(idx8[:], max8[:], lg[:, (t % 2) * D:(t % 2 + 1) * D]))
                v.wait_ge(dv.h, dvN[("idx", t)])
                dv.inc(v.tensor_single_scalar(eidx_u[:], idx8[:, 0:1], seven[:],
                                              ALU.bitwise_and))
                v.wait_ge(dv.h, dvN[("and", t)])
                dv.inc(v.tensor_copy(eidx_all[:, t:t + 1], eidx_u[:]))
            v.wait_ge(ac.h, acN[("exp", NT - 1)])
            dv.inc(v.reciprocal(p_all[:], sume_all[:]))
            v.wait_ge(dv.h, dvN["recip"])
            for ee in range(8):
                dv.inc(v.tensor_scalar(tmp16[:], eidx_all[:], float(ee), None,
                                       ALU.is_equal))
                v.wait_ge(dv.h, dvN[("mseleq", ee)])
                dv.inc(v.tensor_mul(msel[:, ee * NT:(ee + 1) * NT], tmp16[:], p_all[:]))

        @blk3.scalar
        def _(sc):
            for t in range(NT):
                sc.wait_ge(dv.h, dvN[("nmx", t)])
                ac.inc(sc.activation(sexp[:], lg[:, (t % 2) * D:(t % 2 + 1) * D],
                                     AF.Exp, bias=nmx_all[:, t:t + 1],
                                     accum_out=sume_all[:, t:t + 1]))
        blk3.__exit__(None, None, None)

        peE = {}
        n = H3_pe
        for ee in range(8):
            for t in range(NT):
                for half in range(2):
                    n += 1
                    peE[(ee, t, half)] = n
        H4_pe = n
        dvA = {}
        n = H3_dv
        for ee in range(8):
            for t in range(NT):
                for half in range(2):
                    dvA[(ee, t, half)] = (n + 1, n + 2)
                    n += 2
        H4_dv = n
        ldE = {ee: 80 + 16 * (ee + 1) for ee in range(8)}

        blk4 = nc.Block("h4"); blk4.__enter__()
        we3 = we_in.rearrange("(ee c p) o -> p ee c o", ee=8, p=P)

        @blk4.gpsimd
        def _(g):
            for ee in range(8):
                if ee >= 2:
                    g.wait_ge(pe.h, peE[(ee - 2, NT - 1, 1)])
                ldw.inc(g.dma_start(out=webf[ee % 2][:].rearrange("p (c o) -> p c o", c=DC),
                                    in_=we3[:, ee]), 16)

        @blk4.tensor
        def _(t_):
            for ee in range(8):
                t_.wait_ge(ldw.h, ldE[ee])
                for t in range(NT):
                    for half in range(2):
                        k = (ee * NT + t) * 2 + half
                        if k >= 2:
                            kk = k - 2
                            e2, r = divmod(kk, NT * 2)
                            t2, h2 = divmod(r, 2)
                            t_.wait_ge(dv.h, dvA[(e2, t2, h2)][0])
                        for cc in range(DC):
                            t_.matmul(psU[k % 2][:, 0:512],
                                      xn_sb[:, cc * TC + t * P:cc * TC + (t + 1) * P],
                                      webf[ee % 2][:, cc * D + half * 512:cc * D + (half + 1) * 512],
                                      start=(cc == 0), stop=False)
                        pe.inc(t_.matmul(psU[k % 2][:, 0:512], ones_row_bf[:],
                                         be_bf[0:1, ee * D + half * 512:ee * D + (half + 1) * 512],
                                         start=False, stop=True))

        @blk4.vector
        def _(v):
            for ee in range(8):
                for t in range(NT):
                    for half in range(2):
                        k = (ee * NT + t) * 2 + half
                        par = k % 2
                        r1, r2 = dvA[(ee, t, half)]
                        v.wait_ge(pe.h, peE[(ee, t, half)])
                        dv.inc(v.tensor_scalar(tmpe[:, par * 512:(par + 1) * 512],
                                               psU[par][:, 0:512],
                                               msel[:, ee * NT + t:ee * NT + t + 1],
                                               0.0, ALU.mult, ALU.max))
                        v.wait_ge(dv.h, r1)
                        sl = acc[:, t * D + half * 512:t * D + (half + 1) * 512]
                        if ee == 0:
                            dv.inc(v.tensor_copy(sl, tmpe[:, par * 512:(par + 1) * 512]))
                        else:
                            v.wait_ge(dv.h, dvA[(ee - 1, t, half)][1])
                            dv.inc(v.tensor_add(sl, sl, tmpe[:, par * 512:(par + 1) * 512]))
        blk4.__exit__(None, None, None)

        peA = {}
        n = H4_pe
        for t in range(NT):
            for half in range(2):
                n += 1
                peA[(t, half)] = n
        dvC = {}
        n = H4_dv
        for t in range(NT):
            for half in range(2):
                dvC[(t, half)] = (n + 1, n + 2)
                n += 2
        H5_dv = n
        ldX = {t: 16 * (t + 1) for t in range(NT)}
        LD_B = 64

        blk5 = nc.Block("h5"); blk5.__enter__()
        x1_3 = x1_in.rearrange("(t p) d -> p t d", p=P)
        x2_3 = x2_out.rearrange("(t p) d -> p t d", p=P)

        @blk5.gpsimd
        def _(g):
            for t in range(NT):
                if t >= 2:
                    g.wait_ge(st.h, 16 * (t - 1))
                ld.inc(g.dma_start(out=x1b[:, (t % 2) * D:(t % 2 + 1) * D].unsqueeze(1),
                                   in_=x1_3[:, t:t + 1, :]), 16)

        @blk5.tensor
        def _(t_):
            t_.wait_ge(dv.h, H2_dv)
            for t in range(NT):
                for half in range(2):
                    k = t * 2 + half
                    if k >= 2:
                        t2, h2 = divmod(k - 2, 2)
                        t_.wait_ge(dv.h, dvC[(t2, h2)][0])
                    for cc in range(DC):
                        inst = t_.matmul(psU[k % 2][:, 0:512],
                                         xn_sb[:, cc * TC + t * P:cc * TC + (t + 1) * P],
                                         w2_sb[:, cc * D + half * 512:cc * D + (half + 1) * 512],
                                         start=(cc == 0), stop=(cc == DC - 1))
                    pe.inc(inst)

        @blk5.vector
        def _(v):
            for t in range(NT):
                v.wait_ge(ld.h, LD_B + ldX[t])
                for half in range(2):
                    c1, c2 = dvC[(t, half)]
                    v.wait_ge(pe.h, peA[(t, half)])
                    sl = x1b[:, (t % 2) * D + half * 512:(t % 2) * D + (half + 1) * 512]
                    dv.inc(v.tensor_add(sl, sl, psU[(t * 2 + half) % 2][:, 0:512]))
                    v.wait_ge(dv.h, c1)
                    dv.inc(v.tensor_add(sl, sl,
                                        acc[:, t * D + half * 512:t * D + (half + 1) * 512]))

        @blk5.sync
        def _(sy):
            for t in range(NT):
                sy.wait_ge(dv.h, dvC[(t, 1)][1])
                st.inc(sy.dma_start(out=x2_3[:, t:t + 1, :],
                                    in_=x1b[:, (t % 2) * D:(t % 2 + 1) * D].unsqueeze(1)), 16)
            sy.wait_ge(st.h, 16 * NT)
        blk5.__exit__(None, None, None)
    return nc



def _in_maps(inputs):
    x = np.ascontiguousarray(inputs["L0_attn_hidden_states"], dtype=np.float32)

    def T(a):
        return np.ascontiguousarray(np.asarray(a, dtype=np.float32).T)

    shared = {
        "wqT": T(inputs["L0_attn_Wq"]), "wkT": T(inputs["L0_attn_Wk"]),
        "wvT": T(inputs["L0_attn_Wv"]), "wg1T": T(inputs["L0_attn_Wg1"]),
        "wg2T": T(inputs["L0_attn_Wg2"]),
        "gnorm_pc": np.ascontiguousarray(np.asarray(inputs["L0_attn_g_norm_weight"], np.float32).reshape(DC, P).T),
        "woT": T(inputs["L0_attn_Wo"]),
        "wfcT": T(inputs["L0_ffn_W_fc"]), "wpT": T(inputs["L0_ffn_W_proj"]),
        "lnw_pc": np.ascontiguousarray(np.asarray(inputs["L0_ffn_ln_w"], np.float32).reshape(HC, P).T),
        "lnb_pc": np.ascontiguousarray(np.asarray(inputs["L0_ffn_ln_b"], np.float32).reshape(HC, P).T),
    }
    maps = []
    for c in range(8):
        b, s = divmod(c, 4)
        pm = np.zeros((P, 4), np.float32)
        pm[:, :s] = 1.0
        m = dict(shared)
        m["xin"] = np.ascontiguousarray(x[b, s * TC:(s + 1) * TC, :])
        m["xprev"] = np.ascontiguousarray(x[b, 0:3 * TC, :])
        m["pmask"] = pm
        maps.append(m)
    return maps


def kernel(**inputs):
    cores = list(range(8))
    nc1 = build_fused("G")
    res1 = run_bass_kernel_spmd(nc1, _in_maps(inputs), core_ids=cores)
    R1 = [res1.results[i] for i in range(8)]

    wq1 = np.ascontiguousarray(inputs["L1_attn_W_q"], dtype=np.float32)
    wg = np.ascontiguousarray(inputs["L1_ffn_Wg"], dtype=np.float32)
    bg1 = np.ascontiguousarray(np.asarray(inputs["L1_ffn_bg"], np.float32).reshape(1, D))
    we_flat = np.ascontiguousarray(np.asarray(inputs["L1_ffn_We"], np.float32).reshape(8 * D, D))
    be1 = np.ascontiguousarray(np.asarray(inputs["L1_ffn_be"], np.float32).reshape(1, 8 * D))
    in_maps = []
    for i in range(8):
        g0 = (i // 4) * 4
        mp = np.ascontiguousarray(np.concatenate(
            [np.asarray(R1[g0 + j]["mloc_d"]) for j in range(4)], axis=0))
        in_maps.append({"x1": np.asarray(R1[i]["x1_d"]),
                        "xn1fm": np.asarray(R1[i]["xn1fm_d"]),
                        "mparts": mp, "wq1": wq1, "wg": wg, "bg": bg1,
                        "we": we_flat, "be": be1})
    nc2 = build_l2()
    res2 = run_bass_kernel_spmd(nc2, in_maps, core_ids=cores)
    out = np.empty((2, 8192, D), np.float32)
    for c in range(8):
        b, s = divmod(c, 4)
        out[b, s * TC:(s + 1) * TC, :] = res2.results[c]["x2"]
    return out


# version 09
# speedup vs baseline: 1.0840x; 1.0041x over previous
import sys
from contextlib import ExitStack

import numpy as np

sys.path.insert(0, "/opt/trn_rl_repo")
import concourse.bass as bass
import concourse.mybir as mybir
from concourse.bass_utils import run_bass_kernel_spmd

F32 = mybir.dt.float32
BF16 = mybir.dt.bfloat16
U32 = mybir.dt.uint32
AF = mybir.ActivationFunctionType
ALU = mybir.AluOpType
AX = mybir.AxisListType

P = 128
TC = 2048
NT = TC // P
D = 1024
DC = D // P
H4 = 4096
HC = H4 // P
NHEAD = 8
DK = 128
SCALE = float(DK) ** -0.5
PZROWS = D + P


class SemCtr:
    def __init__(self, es, nc, name):
        self.h = es.enter_context(nc.semaphore(name))
        self.n = 0

    def inc(self, inst, k=1):
        inst.then_inc(self.h, k)
        self.n += k
        return self.n


def build_fused(upto="H"):
    nc = bass.Bass("TRN2", target_bir_lowering=False, debug=False)
    order = ["A", "P", "B", "D", "E", "F", "G"]
    assert upto in order
    stages = order[: order.index(upto) + 1]

    def want(s):
        return s in stages

    xin = nc.dram_tensor("xin", [TC, D], F32, kind="ExternalInput").ap()
    wqT = nc.dram_tensor("wqT", [D, D], F32, kind="ExternalInput").ap()
    wkT = nc.dram_tensor("wkT", [D, D], F32, kind="ExternalInput").ap()
    wvT = nc.dram_tensor("wvT", [D, D], F32, kind="ExternalInput").ap()
    wg1T = nc.dram_tensor("wg1T", [D, 128], F32, kind="ExternalInput").ap()
    wg2T = nc.dram_tensor("wg2T", [128, D], F32, kind="ExternalInput").ap()
    gn_in = nc.dram_tensor("gnorm_pc", [P, DC], F32, kind="ExternalInput").ap()
    wo_in = nc.dram_tensor("woT", [D, D], F32, kind="ExternalInput").ap()
    wfcT = nc.dram_tensor("wfcT", [D, H4], F32, kind="ExternalInput").ap()
    wpT = nc.dram_tensor("wpT", [H4, D], F32, kind="ExternalInput").ap()
    lnw_in = nc.dram_tensor("lnw_pc", [P, HC], F32, kind="ExternalInput").ap()
    lnb_in = nc.dram_tensor("lnb_pc", [P, HC], F32, kind="ExternalInput").ap()
    pmask_in = nc.dram_tensor("pmask", [P, 4], F32, kind="ExternalInput").ap()
    xprev = nc.dram_tensor("xprev", [3 * TC, D], F32, kind="ExternalInput").ap()

    attn0_kind = "ExternalOutput" if upto == "D" else "Internal"
    attn0_d = nc.dram_tensor("attn0_d", [TC, D], F32, kind=attn0_kind)
    h_scr = nc.dram_tensor("h_scr", [4 * P, HC * 512], BF16)
    bounce = nc.dram_tensor("bounce", [1, TC], F32)
    x1_d = nc.dram_tensor("x1_d", [TC, D], F32, kind="ExternalOutput")
    mloc_d = nc.dram_tensor("mloc_d", [D, D], F32, kind="ExternalOutput")

    xn1fm_d = nc.dram_tensor("xn1fm_d", [D, TC], F32, kind="ExternalOutput")
    dbg = None
    if upto in ("A", "P", "B", "F"):
        shp = {"A": [D, TC], "P": [D + P, DK], "B": [4 * D, TC],
               "F": [D, TC]}[upto]
        dbg = nc.dram_tensor("dbg", shp, F32, kind="ExternalOutput").ap()

    with ExitStack() as es:
        e = es.enter_context
        ldsy = SemCtr(es, nc, "ldsy")
        ldgp = SemCtr(es, nc, "ldgp")
        stsy = SemCtr(es, nc, "stsy")
        pe = SemCtr(es, nc, "pe")
        dv = SemCtr(es, nc, "dv")
        ac = SemCtr(es, nc, "ac")
        gp = SemCtr(es, nc, "gp")
        stgp = SemCtr(es, nc, "stgp")
        stbg = SemCtr(es, nc, "stbg")

        ones_bf = e(nc.sbuf_tensor("ones_bf", [P, DK], BF16))
        ident = e(nc.sbuf_tensor("ident", [P, DK], BF16))
        mask = e(nc.sbuf_tensor("mask", [P, DK], BF16))
        zeros_f = e(nc.sbuf_tensor("zeros_f", [P, DK], F32))
        ones_col_bf = e(nc.sbuf_tensor("ones_col_bf", [P, 1], BF16))
        ones_row_f = e(nc.sbuf_tensor("ones_row_f", [1, P], F32))
        ones_row_bf = e(nc.sbuf_tensor("ones_row_bf", [1, P], BF16))
        scale_col = e(nc.sbuf_tensor("scale_col", [P, 1], F32))
        eps_col = e(nc.sbuf_tensor("eps_col", [P, 1], F32))
        eps_row = e(nc.sbuf_tensor("eps_row", [1, 1], F32))
        onecol_f = e(nc.sbuf_tensor("onecol_f", [P, 1], F32))
        seven = e(nc.sbuf_tensor("seven", [P, 1], U32))
        zrow_bf = e(nc.sbuf_tensor("zrow_bf", [1, 512], BF16))
        pmask_sb = e(nc.sbuf_tensor("pmask_sb", [P, 4], F32))
        GG = 1024
        wgbuf = e(nc.sbuf_tensor("wgbuf", [P, 2 * GG], F32))
        absb = e(nc.sbuf_tensor("absb", [P, GG // 2], F32))
        t1q = e(nc.sbuf_tensor("t1q", [P, GG], BF16))
        t2q = e(nc.sbuf_tensor("t2q", [P, GG], BF16))
        ga_fc = e(nc.sbuf_tensor("ga_fc", [P, 64], F32))
        ga_p = e(nc.sbuf_tensor("ga_p", [P, 64], F32))
        gb2 = e(nc.sbuf_tensor("gb2", [P, 2], F32))
        thr_rows = e(nc.sbuf_tensor("thr_rows", [1, 2], F32))
        thr_cols = e(nc.sbuf_tensor("thr_cols", [P, 2], F32))
        nthr_cols = e(nc.sbuf_tensor("nthr_cols", [P, 2], F32))
        lnw_sb = e(nc.sbuf_tensor("lnw_sb", [P, HC], F32))

        blk0 = nc.Block("s0"); blk0.__enter__()

        @blk0.gpsimd
        def _(g):
            gp.inc(g.memset(ones_bf[:], 1.0))
            gp.inc(g.memset(zeros_f[:], 0.0))
            gp.inc(g.memset(ones_col_bf[:], 1.0))
            gp.inc(g.memset(ones_row_f[:], 1.0))
            gp.inc(g.memset(ones_row_bf[:], 1.0))
            gp.inc(g.memset(scale_col[:], SCALE))
            gp.inc(g.memset(eps_col[:], 1e-5))
            gp.inc(g.memset(eps_row[:], 1e-5))
            gp.inc(g.memset(onecol_f[:], 1.0))
            gp.inc(g.memset(seven[:], 7))
            gp.inc(g.memset(zrow_bf[:], 0.0))
            g.wait_ge(gp.h, 1)
            gp.inc(g.affine_select(ident[:], ones_bf[:], [[-1, DK]], ALU.is_equal,
                                   0.0, base=0, channel_multiplier=1))
            gp.inc(g.affine_select(mask[:], ones_bf[:], [[1, DK]], ALU.is_ge,
                                   0.0, base=0, channel_multiplier=-1))
            ldgp.inc(g.dma_start(out=pmask_sb[:], in_=pmask_in), 16)
        blk0.__exit__(None, None, None)
        GP_CONST = gp.n
        LD_PMASK = ldgp.n

        esXn = ExitStack()
        esQ = ExitStack()
        esActs = ExitStack()
        xnfm = esXn.enter_context(nc.sbuf_tensor("xnfm", [P, DC * TC], BF16))

        def close_stacks():
            esW.close()
            esActs.close()
            esQ.close()
            esXn.close()

        marks = {}

        base_acc = esQ.enter_context(nc.sbuf_tensor("base_acc", [P, NHEAD * DK], F32))
        zb_acc = esQ.enter_context(nc.sbuf_tensor("zb_acc", [P, NHEAD], F32))
        esW = ExitStack()
        wkbig = esW.enter_context(nc.sbuf_tensor("wkbig", [P, 8 * D], F32))
        wkp_sb = esW.enter_context(nc.sbuf_tensor("wkp_sb", [P, DC * D], BF16))
        wvp_sb = esW.enter_context(nc.sbuf_tensor("wvp_sb", [P, DC * D], BF16))
        with ExitStack() as esA:
            eA = esA.enter_context
            x_sb = eA(nc.sbuf_tensor("x_sb", [P, NT * D], F32))
            o_sb = x_sb
            s1 = eA(nc.sbuf_tensor("s1", [P, NT], F32))
            s2 = eA(nc.sbuf_tensor("s2", [P, NT], F32))
            mu = eA(nc.sbuf_tensor("mu", [P, NT], F32))
            var = eA(nc.sbuf_tensor("var", [P, NT], F32))
            musq = eA(nc.sbuf_tensor("musq", [P, NT], F32))
            std = eA(nc.sbuf_tensor("std", [P, NT], F32))
            rstd = eA(nc.sbuf_tensor("rstd", [P, NT], F32))
            scr = eA(nc.sbuf_tensor("scr", [P, D], F32))
            xbf = eA(nc.sbuf_tensor("xbf", [P, 2 * D], BF16))
            psT = [eA(nc.psum_tensor(f"psT{i}", [P, D], F32)) for i in range(2)]

            b = {s: s.n for s in (ldsy, pe, dv, ac, gp)}

            x_t = xin.rearrange("(t p) d -> p t d", p=P)
            x_sb3 = x_sb[:].rearrange("p (t d) -> p t d", t=NT)

            blkA = nc.Block("sA"); blkA.__enter__()

            ln_norm = []
            a2_cast = []
            a2_mm = []
            a2_evac = []

            def a2_mm_pred(t):
                return b[pe] + t + 1

            def a2_cast_pred(t):
                return b[ac] + NT + 1 + t + 1

            def a2_evac_pred(t):
                return b[dv] + 2 * NT + 4 + t + 1

            LDW0 = ldgp.n
            wk_cast = []

            @blkA.sync
            def _(sy):
                for g_ in range(4):
                    ldsy.inc(sy.dma_start(out=x_sb3[:, g_ * 4:(g_ + 1) * 4, :],
                                          in_=x_t[:, g_ * 4:(g_ + 1) * 4, :]), 16)
            LD_X = ldsy.n

            @blkA.vector
            def _(v):
                for t in range(NT):
                    if t % 4 == 0:
                        v.wait_ge(ldsy.h, b[ldsy] + 16 * (t // 4 + 1))
                    dv.inc(v.reduce_sum(s1[:, t:t + 1], x_sb[:, t * D:(t + 1) * D], axis=AX.X))
                v.wait_ge(dv.h, b[dv] + NT)
                dv.inc(v.tensor_scalar(mu[:], s1[:], 1.0 / D, None, ALU.mult))
                v.wait_ge(dv.h, b[dv] + NT + 1)
                dv.inc(v.tensor_mul(musq[:], mu[:], mu[:]))
                v.wait_ge(ac.h, b[ac] + NT)
                v.wait_ge(dv.h, b[dv] + NT + 2)
                dv.inc(v.scalar_tensor_tensor(var[:], s2[:], 1.0 / D, musq[:],
                                              ALU.mult, ALU.subtract))
                v.wait_ge(ac.h, b[ac] + NT + 1)
                dv.inc(v.reciprocal(rstd[:], std[:]))
                v.wait_ge(dv.h, b[dv] + NT + 4)
                for t in range(NT):
                    ln_norm.append(dv.inc(v.tensor_scalar(
                        o_sb[:, t * D:(t + 1) * D], x_sb[:, t * D:(t + 1) * D],
                        mu[:, t:t + 1], rstd[:, t:t + 1], ALU.subtract, ALU.mult)))
                for t in range(NT):
                    v.wait_ge(pe.h, a2_mm_pred(t))
                    for c in range(DC):
                        inst = v.tensor_copy(
                            xnfm[:, c * TC + t * P:c * TC + (t + 1) * P],
                            psT[t % 2][:, c * P:(c + 1) * P])
                    a2_evac.append(dv.inc(inst))

            @blkA.scalar
            def _(s):
                for t in range(NT):
                    if t % 4 == 0:
                        s.wait_ge(ldsy.h, b[ldsy] + 16 * (t // 4 + 1))
                    if t >= 1:
                        s.wait_ge(ac.h, b[ac] + t)
                    ac.inc(s.activation(scr[:], x_sb[:, t * D:(t + 1) * D], AF.Square,
                                        accum_out=s2[:, t:t + 1]))
                s.wait_ge(dv.h, b[dv] + NT + 3)
                s.wait_ge(gp.h, GP_CONST)
                ac.inc(s.activation(std[:], var[:], AF.Sqrt, bias=eps_col[:]))
                for t in range(NT):
                    s.wait_ge(dv.h, ln_norm[t])
                    if t >= 2:
                        s.wait_ge(pe.h, a2_mm_pred(t - 2))
                    a2_cast.append(ac.inc(s.activation(
                        xbf[:, (t % 2) * D:(t % 2 + 1) * D],
                        o_sb[:, t * D:(t + 1) * D], AF.Copy)))
                marks["a_cast_ac"] = ac.n
                for k in range(16):
                    s.wait_ge(ldgp.h, LDW0 + 16 * (k + 1))
                    dst = wkp_sb if k < 8 else wvp_sb
                    wk_cast.append(ac.inc(s.activation(
                        dst[:, (k % 8) * D:(k % 8 + 1) * D],
                        wkbig[:, (k % 8) * D:(k % 8 + 1) * D], AF.Copy)))

            @blkA.tensor
            def _(t_):
                t_.wait_ge(gp.h, GP_CONST)
                for t in range(NT):
                    t_.wait_ge(ac.h, a2_cast_pred(t))
                    if t >= 2:
                        t_.wait_ge(dv.h, a2_evac_pred(t - 2))
                    for c in range(DC):
                        inst = t_.matmul(psT[t % 2][:, c * P:(c + 1) * P],
                                         xbf[:, (t % 2) * D + c * P:(t % 2) * D + (c + 1) * P],
                                         ident[:], start=True, stop=True)
                    a2_mm.append(pe.inc(inst))

            @blkA.gpsimd
            def _(g):
                wkT3 = wkT.rearrange("(c p) o -> p c o", p=P)
                wvT3 = wvT.rearrange("(c p) o -> p c o", p=P)
                for k in range(16):
                    if k >= 8:
                        g.wait_ge(ac.h, wk_cast[k - 8])
                    ap_ = (wkT3 if k < 8 else wvT3)[:, k % 8:k % 8 + 1, :]
                    ldgp.inc(g.dma_start(
                        out=wkbig[:, (k % 8) * D:(k % 8 + 1) * D].unsqueeze(1),
                        in_=ap_), 16)

            blkA.__exit__(None, None, None)
            marks["xnfm_dv"] = dv.n
            marks["wcast_ac"] = ac.n

            if upto == "A":
                blkD = nc.Block("dbgA"); blkD.__enter__()

                @blkD.gpsimd
                def _(g):
                    g.wait_ge(dv.h, marks["xnfm_dv"])
                    stsy.inc(g.dma_start(
                        out=dbg.rearrange("(c p) t -> p c t", p=P),
                        in_=xnfm[:].rearrange("p (c t) -> p c t", c=DC)), 16)
                    g.wait_ge(stsy.h, stsy.n)
                blkD.__exit__(None, None, None)

        if upto == "A":
            close_stacks()
            return nc

        wfcT3g = wfcT.rearrange("(c p) f -> p c f", p=P)
        wpT3g = wpT.rearrange("(c p) f -> p c f", p=P)

        gchunks = ([("fc", c, fsl) for c in range(DC) for fsl in range(8)]
                   + [("p", c, half) for c in range(HC) for half in range(2)])
        GH = GG // 2

        class GammaFeeder:
            def __init__(self):
                self.k = 0
                self.gmark = {}
                self.gld_base = None

            def feed(self, s, count=1):
                for _ in range(count):
                    if self.k >= len(gchunks):
                        return
                    if self.gld_base is None:
                        self.gld_base = ldgp.n
                    k = self.k
                    tag, c, fsl = gchunks[k]
                    par = k % 4
                    s.wait_ge(ldgp.h, self.gld_base + 16 * (k + 1))
                    if k >= 1:
                        s.wait_ge(ac.h, self.gmark[k - 1])
                    dst = (ga_fc[:, c * 8 + fsl:c * 8 + fsl + 1] if tag == "fc"
                           else ga_p[:, c * 2 + fsl:c * 2 + fsl + 1])
                    self.gmark[k] = ac.inc(s.activation(
                        absb[:], wgbuf[:, par * GH:(par + 1) * GH], AF.Abs,
                        accum_out=dst))
                    self.k += 1

            def emit_loads(self, g, limit=None):
                if limit == 0:
                    return
                assert self.gld_base == ldgp.n
                for k, (tag, c, fsl) in enumerate(gchunks[:limit]):
                    par = k % 4
                    if k >= 4:
                        g.wait_ge(ac.h, self.gmark_lookup(k - 4))
                    if tag == "fc":
                        ap_ = wfcT3g[:, c:c + 1, fsl * GH:(fsl + 1) * GH]
                    else:
                        ap_ = wpT3g[:, c:c + 1, fsl * GH:(fsl + 1) * GH]
                    ldgp.inc(g.dma_start(
                        out=wgbuf[:, par * GH:(par + 1) * GH].unsqueeze(1), in_=ap_), 16)

            def gmark_lookup(self, k):
                return self.gmark[k]

        gfeed = GammaFeeder()


        with ExitStack() as esPre:
            eP = esPre.enter_context
            xpr = eP(nc.sbuf_tensor("xpr", [P, 4 * D], F32))
            xbfp = eP(nc.sbuf_tensor("xbfp", [P, 4 * D], BF16))
            xfmp = eP(nc.sbuf_tensor("xfmp", [P, 4 * D], BF16))
            ap_sb = eP(nc.sbuf_tensor("ap_sb", [P, 4 * D], BF16))
            vp_sb = eP(nc.sbuf_tensor("vp_sb", [P, 4 * D], BF16))
            scrp = eP(nc.sbuf_tensor("scrp", [P, D], F32))
            s1p = eP(nc.sbuf_tensor("s1p", [P, 4], F32))
            s2p = eP(nc.sbuf_tensor("s2p", [P, 4], F32))
            mup = eP(nc.sbuf_tensor("mup", [P, 4], F32))
            musqp = eP(nc.sbuf_tensor("musqp", [P, 4], F32))
            varp = eP(nc.sbuf_tensor("varp", [P, 4], F32))
            stdp = eP(nc.sbuf_tensor("stdp", [P, 4], F32))
            rstdp = eP(nc.sbuf_tensor("rstdp", [P, 4], F32))
            psTp = eP(nc.psum_tensor("psTp", [P, 512], F32))
            psPr = [eP(nc.psum_tensor(f"psPr{i}", [P, 512], F32)) for i in range(4)]
            psPacc = [eP(nc.psum_tensor(f"psPacc{i}", [P, 512], F32)) for i in range(2)]
            psZacc = eP(nc.psum_tensor("psZacc", [P, NHEAD], F32))

            NTP = 48
            bP = {s: s.n for s in (ldsy, ldgp, pe, dv, ac, gp)}

            def adv(i):
                return bP[dv] + 10 * i + 3 * (i // 16)

            def aac(i):
                return bP[ac] + 4 * i

            def ape(i):
                return bP[pe] + 7 * i

            def ald(i):
                return bP[ldsy] + 16 * (i + 1)

            blkP = nc.Block("sP"); blkP.__enter__()

            @blkP.gpsimd
            def _(g):
                g.wait_ge(dv.h, marks["xnfm_dv"])
                gp.inc(g.memset(base_acc[:], 0.0))
                gp.inc(g.memset(zb_acc[:], 0.0))
                ldgp.inc(g.dma_start(out=lnw_sb[:], in_=lnw_in), 16)
            GP_PRE = bP[gp] + 2

            @blkP.sync
            def _(sy):
                sy.wait_ge(dv.h, marks["xnfm_dv"])
                sy.wait_ge(ac.h, marks["a_cast_ac"])
                sy.wait_ge(pe.h, bP[pe])
                for i in range(NTP):
                    r, tt = divmod(i, 16)
                    par = i % 4
                    if i >= 4:
                        sy.wait_ge(ac.h, aac(i - 4) + 1)
                        sy.wait_ge(dv.h, adv(i - 4) + 6)
                    ldsy.inc(sy.dma_start(
                        out=xpr[:, par * D:(par + 1) * D],
                        in_=xprev[r * TC + tt * P:r * TC + (tt + 1) * P, :]), 16)

            @blkP.vector
            def _(v):
                for i in range(NTP):
                    r, tt = divmod(i, 16)
                    par = i % 4
                    pc = par
                    v.wait_ge(ldsy.h, ald(i))
                    dv.inc(v.reduce_sum(s1p[:, pc:pc + 1], xpr[:, par * D:(par + 1) * D], axis=AX.X))
                    v.wait_ge(dv.h, adv(i) + 1)
                    dv.inc(v.tensor_scalar(mup[:, pc:pc + 1], s1p[:, pc:pc + 1], 1.0 / D, None, ALU.mult))
                    v.wait_ge(dv.h, adv(i) + 2)
                    dv.inc(v.tensor_mul(musqp[:, pc:pc + 1], mup[:, pc:pc + 1], mup[:, pc:pc + 1]))
                    v.wait_ge(ac.h, aac(i) + 1)
                    v.wait_ge(dv.h, adv(i) + 3)
                    dv.inc(v.scalar_tensor_tensor(varp[:, pc:pc + 1], s2p[:, pc:pc + 1], 1.0 / D,
                                                  musqp[:, pc:pc + 1], ALU.mult, ALU.subtract))
                    v.wait_ge(ac.h, aac(i) + 2)
                    dv.inc(v.reciprocal(rstdp[:, pc:pc + 1], stdp[:, pc:pc + 1]))
                    v.wait_ge(dv.h, adv(i) + 5)
                    if i >= 4:
                        v.wait_ge(pe.h, ape(i - 4) + 2)
                    dv.inc(v.tensor_scalar(xbfp[:, par * D:(par + 1) * D],
                                           xpr[:, par * D:(par + 1) * D],
                                           mup[:, pc:pc + 1], rstdp[:, pc:pc + 1],
                                           ALU.subtract, ALU.mult))
                    v.wait_ge(pe.h, ape(i) + 1)
                    dv.inc(v.tensor_copy(xfmp[:, par * D:par * D + 512], psTp[:, 0:512]))
                    v.wait_ge(pe.h, ape(i) + 2)
                    dv.inc(v.tensor_copy(xfmp[:, par * D + 512:(par + 1) * D], psTp[:, 0:512]))
                    v.wait_ge(pe.h, ape(i) + 5)
                    dv.inc(v.tensor_copy(vp_sb[:, par * D:par * D + 512], psPr[2][:, 0:512]))
                    v.wait_ge(pe.h, ape(i) + 6)
                    dv.inc(v.tensor_copy(vp_sb[:, par * D + 512:(par + 1) * D], psPr[3][:, 0:512]))
                    if tt == 15:
                        v.wait_ge(pe.h, ape(i) + 7)
                        v.wait_ge(ldgp.h, LD_PMASK)
                        if r == 0:
                            v.wait_ge(gp.h, GP_PRE)
                        v.wait_ge(dv.h, adv(i) + 10)
                        dv.inc(v.scalar_tensor_tensor(base_acc[:, 0:512], psPacc[0][:, 0:512],
                                                      pmask_sb[:, r:r + 1], base_acc[:, 0:512],
                                                      ALU.mult, ALU.add))
                        dv.inc(v.scalar_tensor_tensor(base_acc[:, 512:1024], psPacc[1][:, 0:512],
                                                      pmask_sb[:, r:r + 1], base_acc[:, 512:1024],
                                                      ALU.mult, ALU.add))
                        dv.inc(v.scalar_tensor_tensor(zb_acc[:], psZacc[:, 0:NHEAD],
                                                      pmask_sb[:, r:r + 1], zb_acc[:],
                                                      ALU.mult, ALU.add))

            @blkP.scalar
            def _(s):
                s.wait_ge(gp.h, GP_CONST)
                s.wait_ge(dv.h, marks["xnfm_dv"])
                s.wait_ge(pe.h, bP[pe])
                for i in range(NTP):
                    par = i % 4
                    pc = par
                    s.wait_ge(ldsy.h, ald(i))
                    if i >= 1:
                        s.wait_ge(ac.h, aac(i))
                    if i >= 4:
                        s.wait_ge(dv.h, adv(i - 4) + 4)
                    ac.inc(s.activation(scrp[:], xpr[:, par * D:(par + 1) * D], AF.Square,
                                        accum_out=s2p[:, pc:pc + 1]))
                    s.wait_ge(dv.h, adv(i) + 4)
                    ac.inc(s.activation(stdp[:, pc:pc + 1], varp[:, pc:pc + 1], AF.Sqrt,
                                        bias=eps_col[:]))
                    s.wait_ge(pe.h, ape(i) + 3)
                    ac.inc(s.activation(ap_sb[:, par * D:par * D + 512], psPr[0][:, 0:512], AF.Exp))
                    s.wait_ge(pe.h, ape(i) + 4)
                    ac.inc(s.activation(ap_sb[:, par * D + 512:(par + 1) * D], psPr[1][:, 0:512], AF.Exp))

            @blkP.tensor
            def _(t_):
                t_.wait_ge(gp.h, GP_CONST)
                t_.wait_ge(dv.h, marks["xnfm_dv"])
                for i in range(NTP):
                    r, tt = divmod(i, 16)
                    par = i % 4
                    t_.wait_ge(dv.h, adv(i) + 6)
                    if i >= 1:
                        t_.wait_ge(dv.h, adv(i - 1) + 8)
                    for c in range(4):
                        inst = t_.matmul(psTp[:, c * P:(c + 1) * P],
                                         xbfp[:, par * D + c * P:par * D + (c + 1) * P],
                                         ident[:], start=True, stop=True)
                    pe.inc(inst)
                    t_.wait_ge(dv.h, adv(i) + 7)
                    for c in range(4, 8):
                        inst = t_.matmul(psTp[:, (c - 4) * P:(c - 3) * P],
                                         xbfp[:, par * D + c * P:par * D + (c + 1) * P],
                                         ident[:], start=True, stop=True)
                    pe.inc(inst)
                    if i == 0:
                        t_.wait_ge(ac.h, marks["wcast_ac"])
                    t_.wait_ge(dv.h, adv(i) + 8)
                    if i >= 1:
                        t_.wait_ge(ac.h, aac(i - 1) + 3)
                    for c in range(DC):
                        inst = t_.matmul(psPr[0][:, 0:512],
                                         xfmp[:, par * D + c * P:par * D + (c + 1) * P],
                                         wkp_sb[:, c * D:c * D + 512],
                                         start=(c == 0), stop=(c == DC - 1))
                    pe.inc(inst)
                    if i >= 1:
                        t_.wait_ge(ac.h, aac(i - 1) + 4)
                    for c in range(DC):
                        inst = t_.matmul(psPr[1][:, 0:512],
                                         xfmp[:, par * D + c * P:par * D + (c + 1) * P],
                                         wkp_sb[:, c * D + 512:(c + 1) * D],
                                         start=(c == 0), stop=(c == DC - 1))
                    pe.inc(inst)
                    if i >= 1:
                        t_.wait_ge(dv.h, adv(i - 1) + 9)
                    for c in range(DC):
                        inst = t_.matmul(psPr[2][:, 0:512],
                                         xfmp[:, par * D + c * P:par * D + (c + 1) * P],
                                         wvp_sb[:, c * D:c * D + 512],
                                         start=(c == 0), stop=(c == DC - 1))
                    pe.inc(inst)
                    if i >= 1:
                        t_.wait_ge(dv.h, adv(i - 1) + 10)
                    for c in range(DC):
                        inst = t_.matmul(psPr[3][:, 0:512],
                                         xfmp[:, par * D + c * P:par * D + (c + 1) * P],
                                         wvp_sb[:, c * D + 512:(c + 1) * D],
                                         start=(c == 0), stop=(c == DC - 1))
                    pe.inc(inst)
                    t_.wait_ge(ac.h, aac(i) + 4)
                    t_.wait_ge(dv.h, adv(i) + 10)
                    if tt == 0:
                        if r >= 1:
                            t_.wait_ge(dv.h, adv(i))
                        t_.matmul(psPacc[0][:, 0:512], zrow_bf[0:1, 0:P],
                                  zrow_bf[0:1, 0:512], start=True, stop=False)
                        t_.matmul(psPacc[1][:, 0:512], zrow_bf[0:1, 0:P],
                                  zrow_bf[0:1, 0:512], start=True, stop=False)
                        t_.matmul(psZacc[:, 0:NHEAD], zrow_bf[0:1, 0:P],
                                  zrow_bf[0:1, 0:NHEAD], start=True, stop=False)
                    for h in range(NHEAD):
                        t_.matmul(psPacc[h // 4][:, (h % 4) * DK:(h % 4 + 1) * DK],
                                  ap_sb[:, par * D + h * DK:par * D + (h + 1) * DK],
                                  vp_sb[:, par * D + h * DK:par * D + (h + 1) * DK],
                                  start=False, stop=(tt == 15 and h % 4 == 3))
                    for h in range(NHEAD):
                        inst = t_.matmul(psZacc[:, h:h + 1],
                                         ap_sb[:, par * D + h * DK:par * D + (h + 1) * DK],
                                         ones_col_bf[:],
                                         start=False, stop=(tt == 15 and h == 7))
                    pe.inc(inst)
            blkP.__exit__(None, None, None)
            marks["pre_dv"] = dv.n
            marks["pre_ac"] = ac.n
            marks["pre_pe"] = pe.n

            if upto == "P":
                blkD = nc.Block("dbgP"); blkD.__enter__()

                @blkD.gpsimd
                def _(g):
                    gfeed.emit_loads(g, limit=gfeed.k)

                @blkD.sync
                def _(sy):
                    sy.wait_ge(dv.h, marks["pre_dv"])
                    stsy.inc(sy.dma_start(out=dbg[0:D, :].rearrange("(h k) v -> k h v", k=P),
                                          in_=base_acc[:].rearrange("k (h v) -> k h v", h=NHEAD)), 16)
                    stsy.inc(sy.dma_start(out=dbg[D:D + P, 0:NHEAD], in_=zb_acc[:]), 16)
                    sy.wait_ge(stsy.h, stsy.n)
                blkD.__exit__(None, None, None)

        if upto == "P":
            close_stacks()
            return nc

        esW.close()

        q_sb = esQ.enter_context(nc.sbuf_tensor("q_sb", [P, DC * TC], BF16))
        afm_sb = esActs.enter_context(nc.sbuf_tensor("afm_sb", [P, DC * TC], BF16))
        atm_sb = esActs.enter_context(nc.sbuf_tensor("atm_sb", [P, NT * D], BF16))
        vtm_sb = esActs.enter_context(nc.sbuf_tensor("vtm_sb", [P, NT * D], BF16))

        def emit_proj_stage(name, wT, tasks, barrier, extra_scalar=None,
                            extra_gpsimd=None, feeder=None, feeds_during=0):
            with ExitStack() as esBx:
                eB = esBx.enter_context
                w_sb = eB(nc.sbuf_tensor(f"w_{name}", [P, DC * D], BF16))
                wcbB = eB(nc.sbuf_tensor(f"wcb_{name}", [P, 2 * D], F32))
                psB = [eB(nc.psum_tensor(f"ps_{name}{i}", [P, 512], F32)) for i in range(4)]
                b = {s: s.n for s in (ldsy, pe, dv, ac)}
                wT3 = wT.rearrange("(c p) o -> p c o", p=P)
                blk = nc.Block(f"s_{name}"); blk.__enter__()

                scalar_evac = tasks[0][0] != "v"

                def evac_pos(i):
                    if i < feeds_during:
                        return b[ac] + DC + 2 * i + 1
                    return b[ac] + DC + 2 * feeds_during + (i - feeds_during) + 1

                @blk.sync
                def _(sy):
                    for sem, val in barrier.get("sync", []):
                        sy.wait_ge(sem, val)
                    for k in range(DC):
                        if k >= 2:
                            sy.wait_ge(ac.h, b[ac] + k - 1)
                        ldsy.inc(sy.dma_start(
                            out=wcbB[:, (k % 2) * D:(k % 2 + 1) * D].unsqueeze(1),
                            in_=wT3[:, k:k + 1, :]), 16)

                @blk.scalar
                def _(s):
                    for sem, val in barrier.get("scalar", []):
                        s.wait_ge(sem, val)
                    for k in range(DC):
                        s.wait_ge(ldsy.h, b[ldsy] + 16 * (k + 1))
                        ac.inc(s.activation(w_sb[:, k * D:(k + 1) * D],
                                            wcbB[:, (k % 2) * D:(k % 2 + 1) * D], AF.Copy))
                    if scalar_evac:
                        for i, task in enumerate(tasks):
                            pst = psB[i % 4]
                            s.wait_ge(pe.h, b[pe] + i + 1)
                            kind = task[0]
                            if kind in ("q", "a"):
                                _, o, tt = task
                                buf = q_sb if kind == "q" else afm_sb
                                fn = AF.Silu if kind == "q" else AF.Exp
                                ac.inc(s.activation(buf[:, o * TC + tt * 512:o * TC + (tt + 1) * 512],
                                                    pst[:, 0:512], fn))
                            else:
                                _, tt, hf = task
                                ac.inc(s.activation(atm_sb[:, tt * D + hf * 512:tt * D + (hf + 1) * 512],
                                                    pst[:, 0:512], AF.Exp))
                            if feeder is not None and i < feeds_during:
                                feeder.feed(s, 1)
                    if extra_scalar is not None:
                        extra_scalar(s)

                if not scalar_evac:
                    @blk.vector
                    def _(v):
                        for sem, val in barrier.get("vector", []):
                            v.wait_ge(sem, val)
                        for i, task in enumerate(tasks):
                            pst = psB[i % 4]
                            v.wait_ge(pe.h, b[pe] + i + 1)
                            _, tt, hf = task
                            dv.inc(v.tensor_copy(vtm_sb[:, tt * D + hf * 512:tt * D + (hf + 1) * 512],
                                                 pst[:, 0:512]))

                @blk.tensor
                def _(t_):
                    t_.wait_ge(ac.h, b[ac] + DC)
                    for sem, val in barrier.get("tensor", []):
                        t_.wait_ge(sem, val)
                    for i, task in enumerate(tasks):
                        pst = psB[i % 4]
                        if i >= 4:
                            if scalar_evac:
                                t_.wait_ge(ac.h, evac_pos(i - 4))
                            else:
                                t_.wait_ge(dv.h, b[dv] + i - 3)
                        kind = task[0]
                        if kind in ("q", "a"):
                            _, o, tt = task
                            for ccx in range(DC):
                                inst = t_.matmul(pst[:, 0:512],
                                                 w_sb[:, ccx * D + o * P:ccx * D + (o + 1) * P],
                                                 xnfm[:, ccx * TC + tt * 512:ccx * TC + (tt + 1) * 512],
                                                 start=(ccx == 0), stop=(ccx == DC - 1))
                        else:
                            _, tt, hf = task
                            for ccx in range(DC):
                                inst = t_.matmul(pst[:, 0:512],
                                                 xnfm[:, ccx * TC + tt * P:ccx * TC + (tt + 1) * P],
                                                 w_sb[:, ccx * D + hf * 512:ccx * D + (hf + 1) * 512],
                                                 start=(ccx == 0), stop=(ccx == DC - 1))
                        pe.inc(inst)

                if extra_gpsimd is not None:
                    @blk.gpsimd
                    def _(g):
                        extra_gpsimd(g)
                blk.__exit__(None, None, None)

        barrier1 = {"sync": [(dv.h, marks["pre_dv"]), (ac.h, marks["pre_ac"]),
                             (pe.h, marks["pre_pe"])],
                    "scalar": [(dv.h, marks["pre_dv"])],
                    "tensor": [(dv.h, marks["pre_dv"]), (dv.h, marks["xnfm_dv"])]}
        emit_proj_stage("b1a", wqT, [("q", o, t) for o in range(DC) for t in range(4)],
                        barrier1, feeder=gfeed, feeds_during=0)

        bprev = {"sync": [(pe.h, pe.n), (ac.h, ac.n)],
                 "scalar": [], "vector": [], "tensor": []}
        emit_proj_stage("b1b", wkT,
                        [("a", o, t) for o in range(DC) for t in range(4)]
                        + [("at", tt, hf) for tt in range(NT) for hf in range(2)],
                        bprev, feeder=gfeed, feeds_during=0,
                        extra_scalar=lambda s: gfeed.feed(s, 999),
                        extra_gpsimd=lambda g: gfeed.emit_loads(g))
        assert gfeed.k == len(gchunks)
        marks["b1_ac"] = ac.n
        marks["gamma_ac"] = gfeed.gmark[len(gchunks) - 1]
        bprev2 = {"sync": [(pe.h, pe.n), (ac.h, ac.n)],
                  "scalar": [], "vector": [(ac.h, ac.n)], "tensor": []}
        emit_proj_stage("b2", wvT, [("v", tt, hf) for tt in range(NT) for hf in range(2)],
                        bprev2)
        marks["vtm_dv"] = dv.n

        if upto == "B":
            blkD = nc.Block("dbgB"); blkD.__enter__()
            d4 = dbg.rearrange("(r c p) t -> r p c t", r=4, p=P)

            @blkD.gpsimd
            def _(g):
                g.wait_ge(ac.h, marks["b1_ac"])
                g.wait_ge(dv.h, marks["vtm_dv"])
                stsy.inc(g.dma_start(out=d4[0], in_=q_sb[:].rearrange("p (c t) -> p c t", c=DC)), 16)
                stsy.inc(g.dma_start(out=d4[1], in_=afm_sb[:].rearrange("p (c t) -> p c t", c=DC)), 16)
                stsy.inc(g.dma_start(out=dbg.rearrange("(r t p) d -> r p t d", r=4, p=P)[2],
                                     in_=atm_sb[:].rearrange("p (t d) -> p t d", t=NT)), 16)
                stsy.inc(g.dma_start(out=dbg.rearrange("(r t p) d -> r p t d", r=4, p=P)[3],
                                     in_=vtm_sb[:].rearrange("p (t d) -> p t d", t=NT)), 16)
                g.wait_ge(stsy.h, stsy.n)
            blkD.__exit__(None, None, None)
            close_stacks()
            return nc

        with ExitStack() as esTh:
            psTh = esTh.enter_context(nc.psum_tensor("psTh", [P, 512], F32))
            bT = {s: s.n for s in (pe, dv, gp)}
            blkTh = nc.Block("sTh"); blkTh.__enter__()

            @blkTh.vector
            def _(v):
                v.wait_ge(ac.h, marks["gamma_ac"])
                dv.inc(v.reduce_sum(gb2[:, 0:1], ga_fc[:], axis=AX.X))
                dv.inc(v.reduce_sum(gb2[:, 1:2], ga_p[:], axis=AX.X))
                v.wait_ge(pe.h, bT[pe] + 1)
                dv.inc(v.tensor_scalar(thr_rows[:], psTh[0:1, 0:2], 0.5 / (H4 * D), 5e-7,
                                       ALU.mult, ALU.add))
                v.wait_ge(pe.h, bT[pe] + 2)
                dv.inc(v.tensor_copy(thr_cols[:], psTh[:, 4:6]))
                v.wait_ge(dv.h, bT[dv] + 4)
                dv.inc(v.tensor_scalar(nthr_cols[:], thr_cols[:], -1.0, None, ALU.mult))

            @blkTh.tensor
            def _(t_):
                t_.wait_ge(dv.h, bT[dv] + 2)
                pe.inc(t_.matmul(psTh[0:1, 0:2], onecol_f[:], gb2[:], start=True, stop=True))
                t_.wait_ge(dv.h, bT[dv] + 3)
                pe.inc(t_.matmul(psTh[:, 4:6], ones_row_f[:], thr_rows[0:1, 0:2],
                                 start=True, stop=True))
            blkTh.__exit__(None, None, None)
            marks["thr_dv"] = dv.n






        o_sb = q_sb
        eAct = esActs.enter_context
        P_sb = eAct(nc.sbuf_tensor("P_sb", [P, NHEAD * DK], BF16))
        zbase = zb_acc
        zc_sb = eAct(nc.sbuf_tensor("zc_sb", [P, 2 * NHEAD * DK], F32))
        rz_sb = eAct(nc.sbuf_tensor("rz_sb", [P, 2 * DK], F32))
        qt_sb = eAct(nc.sbuf_tensor("qt_sb", [P, 2 * DK], BF16))
        at_sb = eAct(nc.sbuf_tensor("at_sb", [P, 2 * DK], BF16))

        NIT = NT * NHEAD

        with ExitStack() as esD:
            eD = esD.enter_context
            psP = [eD(nc.psum_tensor(f"psP{i}", [P, 512], F32)) for i in range(2)]
            psA = [eD(nc.psum_tensor(f"psA{i}", [P, 512], F32)) for i in range(2)]
            psO = [eD(nc.psum_tensor(f"psO{i}", [P, 512], F32)) for i in range(2)]

            b = {s: s.n for s in (ldsy, pe, dv, ac)}
            bD = dict(b)
            blkDm = nc.Block("sD"); blkDm.__enter__()

            DV0 = 1
            PE0 = 2

            def peAT(it):
                return bD[pe] + PE0 + 3 * it + 1

            def peO(it):
                return bD[pe] + PE0 + 3 * it + 2

            def peP(it):
                return bD[pe] + PE0 + 3 * it + 3

            def dvScan(it):
                return bD[dv] + DV0 + 4 * it + 1

            def dvRecip(it):
                return bD[dv] + DV0 + 4 * it + 2

            def dvQt(it):
                return bD[dv] + DV0 + 4 * it + 3

            def dvAtm(it):
                return bD[dv] + DV0 + 4 * it + 4

            def acOEvac(it):
                ci, h = divmod(it, NHEAD)
                return bD[ac] + 16 * ci + h + 1

            def acPEvac(ci, h):
                return bD[ac] + 16 * ci + 8 + h + 1

            @blkDm.vector
            def _(v):
                v.wait_ge(dv.h, marks["pre_dv"])
                dv.inc(v.tensor_copy(P_sb[:], base_acc[:]))
                for it in range(NIT):
                    ci, h = divmod(it, NHEAD)
                    par = it % 2
                    cpar = ci % 2
                    if ci == 0:
                        init = zbase[:, h:h + 1]
                        v.wait_ge(dv.h, bD[dv] + DV0)
                    zc_out = zc_sb[:, cpar * NHEAD * DK + h * DK:cpar * NHEAD * DK + (h + 1) * DK]
                    if ci > 0:
                        pcpar = 1 - cpar
                        init = zc_sb[:, pcpar * NHEAD * DK + (h + 1) * DK - 1:
                                     pcpar * NHEAD * DK + (h + 1) * DK]
                        v.wait_ge(dv.h, dvScan(it - NHEAD))
                    dv.inc(v.tensor_tensor_scan(
                        zc_out, afm_sb[:, h * TC + ci * DK:h * TC + (ci + 1) * DK],
                        zeros_f[:], init, ALU.add, ALU.add))
                    v.wait_ge(dv.h, dvScan(it))
                    dv.inc(v.reciprocal(rz_sb[:, par * DK:(par + 1) * DK], zc_out))
                    v.wait_ge(dv.h, dvRecip(it))
                    if it >= 2:
                        v.wait_ge(pe.h, peO(it - 2))
                    dv.inc(v.tensor_mul(qt_sb[:, par * DK:(par + 1) * DK],
                                        q_sb[:, h * TC + ci * DK:h * TC + (ci + 1) * DK],
                                        rz_sb[:, par * DK:(par + 1) * DK]))
                    v.wait_ge(pe.h, peAT(it))
                    dv.inc(v.tensor_mul(at_sb[:, par * DK:(par + 1) * DK],
                                        psA[par][:, 0:DK], mask[:]))

            @blkDm.tensor
            def _(t_):
                t_.wait_ge(gp.h, GP_CONST)
                t_.wait_ge(dv.h, bD[dv] + 1)
                for bk in range(2):
                    pe.inc(t_.matmul(psP[bk][:, 0:512], ident[:], P_sb[:, bk * 512:(bk + 1) * 512],
                                     start=True, stop=False))
                for it in range(NIT):
                    ci, h = divmod(it, NHEAD)
                    par = it % 2
                    t_.wait_ge(dv.h, dvQt(it))
                    if it >= 2:
                        t_.wait_ge(dv.h, dvAtm(it - 2))
                    pe.inc(t_.matmul(psA[par][:, 0:DK],
                                     afm_sb[:, h * TC + ci * DK:h * TC + (ci + 1) * DK],
                                     qt_sb[:, par * DK:(par + 1) * DK],
                                     start=True, stop=True))
                    t_.wait_ge(dv.h, dvAtm(it))
                    if ci > 0:
                        t_.wait_ge(ac.h, acPEvac(ci - 1, h))
                    if it >= 2:
                        t_.wait_ge(ac.h, acOEvac(it - 2))
                    t_.matmul(psO[par][:, 0:DK],
                              vtm_sb[:, ci * D + h * DK:ci * D + (h + 1) * DK],
                              at_sb[:, par * DK:(par + 1) * DK],
                              start=True, stop=False)
                    pe.inc(t_.matmul(psO[par][:, 0:DK],
                                     P_sb[:, h * DK:(h + 1) * DK],
                                     qt_sb[:, par * DK:(par + 1) * DK],
                                     start=False, stop=True))
                    if ci > 0:
                        t_.wait_ge(ac.h, acPEvac(ci - 1, 7))
                    pe.inc(t_.matmul(psP[h // 4][:, (h % 4) * DK:(h % 4 + 1) * DK],
                                     atm_sb[:, ci * D + h * DK:ci * D + (h + 1) * DK],
                                     vtm_sb[:, ci * D + h * DK:ci * D + (h + 1) * DK],
                                     start=False,
                                     stop=(ci == NT - 1 and h % 4 == 3)))

            @blkDm.scalar
            def _(s):
                s.wait_ge(gp.h, GP_CONST)
                for ci in range(NT):
                    for h in range(NHEAD):
                        it = ci * NHEAD + h
                        par = it % 2
                        s.wait_ge(pe.h, peO(it))
                        ac.inc(s.activation(o_sb[:, h * TC + ci * DK:h * TC + (ci + 1) * DK],
                                            psO[par][:, 0:DK], AF.Copy,
                                            scale=scale_col[:]))
                    s.wait_ge(pe.h, peP(ci * NHEAD + 7))
                    for h in range(NHEAD):
                        ac.inc(s.activation(P_sb[:, h * DK:(h + 1) * DK],
                                            psP[h // 4][:, (h % 4) * DK:(h % 4 + 1) * DK],
                                            AF.Copy))
            blkDm.__exit__(None, None, None)

        esActs.close()

        PE_L = pe.n
        DV_L = dv.n
        AC_L = ac.n

        with ExitStack() as esDf:
            eD = esDf.enter_context
            wo_sb = eD(nc.sbuf_tensor("wo_sb", [P, DC * D], BF16))
            wg1_sb = eD(nc.sbuf_tensor("wg1_sb", [P, DC * 128], BF16))
            wg2_sb = eD(nc.sbuf_tensor("wg2_sb", [P, D], BF16))
            g1_blk = eD(nc.sbuf_tensor("g1_blk", [P, 2 * 512], BF16))
            sg_blk = eD(nc.sbuf_tensor("sg_blk", [P, 4 * DC * 512], BF16))
            gn_sb = eD(nc.sbuf_tensor("gn_sb", [P, DC], F32))
            o2_sb = eD(nc.sbuf_tensor("o2_sb", [P, 2 * 512], BF16))
            sgp_sb = eD(nc.sbuf_tensor("sgp_sb", [P, 2 * 512], BF16))
            ms_row = eD(nc.sbuf_tensor("ms_row", [1, TC], F32))
            wob = eD(nc.sbuf_tensor("wob", [P, 2 * 512], F32))
            psS = [eD(nc.psum_tensor(f"psS{i}", [P, 512], F32)) for i in range(2)]
            psR = [eD(nc.psum_tensor(f"psR{i}", [P, 512], F32)) for i in range(2)]
            psW = [eD(nc.psum_tensor(f"psW{i}", [P, 512], F32)) for i in range(2)]
            psG1 = eD(nc.psum_tensor("psG1", [P, 512], F32))
            psSG = eD(nc.psum_tensor("psSG", [P, 512], F32))

            wcbD = eD(nc.sbuf_tensor("wcbD", [P, 2 * D], F32))
            bF = {s: s.n for s in (ldsy, ldgp, pe, dv, ac, stsy)}
            blkDf = nc.Block("sDf"); blkDf.__enter__()

            nPE = bF[pe]
            peG1 = {}; peSG = {}; peStat = {}; peRep = {}; peWo = {}
            for t in range(4):
                nPE += 1; peG1[t] = nPE
                for o in range(DC):
                    nPE += 1; peSG[(t, o)] = nPE
            for j in range(32):
                nPE += 1; peStat[j] = nPE
            for t in range(4):
                nPE += 1; peRep[t] = nPE
            for idx in range(32):
                nPE += 1; peWo[idx] = nPE
            nDV = bF[dv]
            dvG1e = {}; dvO2 = {}; dvVar = {}; dvSgp = {}; dvOf = {}; dvWoE = {}
            for t in range(4):
                nDV += 1; dvG1e[t] = nDV
            for t in range(4):
                for c in range(8):
                    nDV += 1; dvO2[t * 8 + c] = nDV
                nDV += 1; dvVar[t] = nDV
            nDV += 1; DV_RSTD = nDV
            for j in range(32):
                nDV += 1; dvSgp[j] = nDV
                nDV += 1; dvOf[j] = nDV
            for idx in range(32):
                nDV += 1; dvWoE[idx] = nDV
            nAC = bF[ac] + 10
            acSilu = {}
            for t in range(4):
                for o in range(DC):
                    nAC += 1; acSilu[(t, o)] = nAC
            AC_SQRT = nAC + 1
            def stWo(idx):
                return bF[stsy] + 16 * (idx + 1)

            wo3 = wo_in.rearrange("(c p) o -> p c o", p=P)

            @blkDf.sync
            def _(sy):
                sy.wait_ge(pe.h, PE_L)
                sy.wait_ge(dv.h, DV_L)
                sy.wait_ge(ac.h, AC_L)
                for k in range(10):
                    if k >= 2:
                        sy.wait_ge(ac.h, bF[ac] + k - 1)
                    sl = wcbD[:, (k % 2) * D:(k % 2 + 1) * D]
                    if k < 8:
                        ldsy.inc(sy.dma_start(out=sl.unsqueeze(1),
                                              in_=wo3[:, k:k + 1, :]), 16)
                    elif k == 8:
                        ldsy.inc(sy.dma_start(
                            out=sl.rearrange("p (c o) -> p c o", c=DC),
                            in_=wg1T.rearrange("(c p) o -> p c o", p=P)), 16)
                    else:
                        ldsy.inc(sy.dma_start(out=sl.unsqueeze(1),
                                              in_=wg2T.unsqueeze(1)), 16)
                ldsy.inc(sy.dma_start(out=gn_sb[:], in_=gn_in), 16)
            LD_GN = ldsy.n

            @blkDf.tensor
            def _(t_):
                t_.wait_ge(ac.h, bF[ac] + 10)
                for t in range(4):
                    if t >= 1:
                        t_.wait_ge(dv.h, dvG1e[t - 1])
                    for ccx in range(DC):
                        inst = t_.matmul(psG1[:, 0:512],
                                         wg1_sb[:, ccx * 128:(ccx + 1) * 128],
                                         xnfm[:, ccx * TC + t * 512:ccx * TC + (t + 1) * 512],
                                         start=(ccx == 0), stop=(ccx == DC - 1))
                    pe.inc(inst)
                    for o in range(DC):
                        t_.wait_ge(dv.h, dvG1e[t])
                        if (t, o) != (0, 0):
                            po = (t, o - 1) if o else (t - 1, 7)
                            t_.wait_ge(ac.h, acSilu[po])
                        pe.inc(t_.matmul(psSG[:, 0:512],
                                         wg2_sb[:, o * P:(o + 1) * P],
                                         g1_blk[:, (t % 2) * 512:(t % 2 + 1) * 512],
                                         start=True, stop=True))
                for j in range(32):
                    t, c = divmod(j, 8)
                    t_.wait_ge(dv.h, dvO2[j])
                    if t >= 2 and c == 0:
                        t_.wait_ge(dv.h, dvVar[t - 2])
                    pe.inc(t_.matmul(psS[t % 2][0:1, 0:512], ones_col_bf[:],
                                     o2_sb[:, (j % 2) * 512:(j % 2 + 1) * 512],
                                     start=(c == 0), stop=(c == 7)))
                for t in range(4):
                    t_.wait_ge(dv.h, DV_RSTD)
                    if t >= 2:
                        t_.wait_ge(dv.h, dvOf[8 * (t - 2) + 7])
                    pe.inc(t_.matmul(psR[t % 2][:, 0:512], ones_row_f[:],
                                     ms_row[0:1, t * 512:(t + 1) * 512],
                                     start=True, stop=True))
                for idx in range(32):
                    t128, hf = divmod(idx, 2)
                    t512 = t128 // 4
                    t_.wait_ge(dv.h, dvOf[8 * t512 + 7])
                    if idx >= 2:
                        t_.wait_ge(dv.h, dvWoE[idx - 2])
                    for c in range(DC):
                        inst = t_.matmul(psW[idx % 2][:, 0:512],
                                         o_sb[:, c * TC + t128 * P:c * TC + (t128 + 1) * P],
                                         wo_sb[:, c * D + hf * 512:c * D + (hf + 1) * 512],
                                         start=(c == 0), stop=(c == DC - 1))
                    pe.inc(inst)

            @blkDf.vector
            def _(v):
                v.wait_ge(ac.h, AC_L)
                v.wait_ge(ldsy.h, LD_GN)
                for t in range(4):
                    v.wait_ge(pe.h, peG1[t])
                    dv.inc(v.tensor_copy(g1_blk[:, (t % 2) * 512:(t % 2 + 1) * 512],
                                         psG1[:, 0:512]))
                for t in range(4):
                    for c in range(8):
                        j = t * 8 + c
                        if j >= 2:
                            v.wait_ge(pe.h, peStat[j - 2])
                        src_ = o_sb[:, c * TC + t * 512:c * TC + (t + 1) * 512]
                        dv.inc(v.tensor_mul(o2_sb[:, (j % 2) * 512:(j % 2 + 1) * 512], src_, src_))
                    v.wait_ge(pe.h, peStat[8 * t + 7])
                    dv.inc(v.tensor_scalar(ms_row[0:1, t * 512:(t + 1) * 512],
                                           psS[t % 2][0:1, 0:512], 1.0 / D, 1e-5,
                                           ALU.mult, ALU.add))
                v.wait_ge(ac.h, AC_SQRT)
                dv.inc(v.reciprocal(ms_row[:], ms_row[:]))
                for j in range(32):
                    t, c = divmod(j, 8)
                    par = j % 2
                    v.wait_ge(pe.h, peRep[t])
                    v.wait_ge(ac.h, acSilu[(t, c)])
                    dv.inc(v.tensor_mul(sgp_sb[:, par * 512:(par + 1) * 512],
                                        sg_blk[:, t * DC * 512 + c * 512:t * DC * 512 + (c + 1) * 512],
                                        psR[t % 2][:, 0:512]))
                    v.wait_ge(dv.h, dvSgp[j])
                    dv.inc(v.scalar_tensor_tensor(o_sb[:, c * TC + t * 512:c * TC + (t + 1) * 512],
                                                  o_sb[:, c * TC + t * 512:c * TC + (t + 1) * 512],
                                                  gn_sb[:, c:c + 1],
                                                  sgp_sb[:, par * 512:(par + 1) * 512],
                                                  ALU.mult, ALU.mult))
                for idx in range(32):
                    v.wait_ge(pe.h, peWo[idx])
                    if idx >= 2:
                        v.wait_ge(stsy.h, stWo(idx - 2))
                    dv.inc(v.tensor_copy(wob[:, (idx % 2) * 512:(idx % 2 + 1) * 512],
                                         psW[idx % 2][:, 0:512]))

            @blkDf.scalar
            def _(s):
                for k in range(10):
                    s.wait_ge(ldsy.h, bF[ldsy] + 16 * (k + 1))
                    if k < 8:
                        dst = wo_sb[:, k * D:(k + 1) * D]
                    elif k == 8:
                        dst = wg1_sb[:]
                    else:
                        dst = wg2_sb[:]
                    ac.inc(s.activation(dst, wcbD[:, (k % 2) * D:(k % 2) * D + dst.shape[1]],
                                        AF.Copy))
                for t in range(4):
                    for o in range(DC):
                        s.wait_ge(pe.h, peSG[(t, o)])
                        ac.inc(s.activation(sg_blk[:, t * DC * 512 + o * 512:t * DC * 512 + (o + 1) * 512],
                                            psSG[:, 0:512], AF.Silu))
                s.wait_ge(dv.h, dvVar[3])
                ac.inc(s.activation(ms_row[:], ms_row[:], AF.Sqrt))

            @blkDf.sync
            def _(sy):
                a3 = attn0_d.ap().rearrange("(t p) d -> p t d", p=P)
                for idx in range(32):
                    t128, hf = divmod(idx, 2)
                    sy.wait_ge(dv.h, dvWoE[idx])
                    stsy.inc(sy.dma_start(
                        out=a3[:, t128:t128 + 1, hf * 512:(hf + 1) * 512],
                        in_=wob[:, (idx % 2) * 512:(idx % 2 + 1) * 512].unsqueeze(1)), 16)
            blkDf.__exit__(None, None, None)
            marks["attn0_st"] = stsy.n

        if upto == "D":
            close_stacks()
            return nc

        esQ.close()
        wfcT3 = wfcT.rearrange("(c p) f -> p c f", p=P)
        wpT3 = wpT.rearrange("(c p) f -> p c f", p=P)

        with ExitStack() as esE:
            eE = esE.enter_context
            h_sb = eE(nc.sbuf_tensor("h_sb", [P, HC * 512], BF16))
            h2_sb = eE(nc.sbuf_tensor("h2_sb", [P, 2 * 512], BF16))
            negmu_row = eE(nc.sbuf_tensor("negmu_row", [1, TC], F32))
            rstd_row = eE(nc.sbuf_tensor("rstd_rowE", [1, 512], F32))
            musq_s = eE(nc.sbuf_tensor("musq_s", [1, 512], F32))
            var_s = eE(nc.sbuf_tensor("var_s", [1, 512], F32))
            std_s = musq_s
            rstd_cols = eE(nc.sbuf_tensor("rstd_cols", [P, NT], F32))
            lnb_bf = eE(nc.sbuf_tensor("lnb_bf", [P, HC], BF16))
            lnbcb = eE(nc.sbuf_tensor("lnbcb", [P, HC], F32))
            c_row = eE(nc.sbuf_tensor("c_row", [1, D], F32))
            cc_row = eE(nc.sbuf_tensor("cc_row", [1, D], F32))
            psS1 = eE(nc.psum_tensor("psES1", [P, 512], F32))
            psS2 = eE(nc.psum_tensor("psES2", [P, 512], F32))

            bE = {s: s.n for s in (ldsy, ldgp, pe, dv, ac, gp, stsy)}
            E_dv, E_pe, E_ac, E_ld, E_ld2, E_st = (bE[dv], bE[pe], bE[ac],
                                                   bE[ldsy], bE[ldgp], bE[stsy])

            with ExitStack() as es_a:
                e1 = es_a.enter_context
                wfcq = e1(nc.sbuf_tensor("wfcq", [P, DC * H4], BF16))
                psF = [e1(nc.psum_tensor(f"psF{i}", [P, 512], F32)) for i in range(2)]

                blk = nc.Block("e1"); blk.__enter__()

                @blk.sync
                def _(sy):
                    sy.wait_ge(pe.h, E_pe)
                    sy.wait_ge(dv.h, E_dv)
                    sy.wait_ge(ac.h, E_ac)
                    sy.wait_ge(stsy.h, marks["attn0_st"])
                blk.__exit__(None, None, None)

                E1_dv = E_dv + 96
                E1_pe = E_pe
                E1_ac = E_ac
                fcchunks = [(fsl, c) for fsl in range(4) for c in range(DC)]
                E_ldg = ldgp.n

                def fc_tern_mark(k):
                    return E_dv + 3 * (k + 1)

                def fc_group_mark(fgrp8):
                    return fc_tern_mark(fgrp8 * 8 + 7)
                marks["fc_tern_done"] = fc_tern_mark(31)

                peN = {}
                n = E1_pe
                for t in range(4):
                    for f in range(HC):
                        n += 1
                        peN[("fc", t, f)] = n
                        if f >= 1:
                            n += 1
                            peN[("s1", t, f - 1)] = n
                            n += 1
                            peN[("s2", t, f - 1)] = n
                    n += 1
                    peN[("s1", t, HC - 1)] = n
                    n += 1
                    peN[("s2", t, HC - 1)] = n
                E2_pe = n
                acN = {}
                n = E1_ac
                for t in range(4):
                    for f in range(HC):
                        n += 1
                        acN[("gelu", t, f)] = n
                    n += 1
                    acN[("std", t)] = n
                E2_ac = n
                dvN = {}
                n = E1_dv
                for t in range(4):
                    for f in range(HC):
                        n += 1
                        dvN[("h2", t, f)] = n
                    for k in ("negmu", "musq", "var", "rstd"):
                        n += 1
                        dvN[(k, t)] = n
                E2_dv = n
                E2_st = E_st + 16 * 7

                blk2 = nc.Block("e2"); blk2.__enter__()

                @blk2.tensor
                def _(t_):
                    for t in range(4):
                        for f in range(HC):
                            t_.wait_ge(dv.h, fc_group_mark(f // 8))
                            pf = t * HC + f
                            if pf >= 2:
                                t2, f2 = divmod(pf - 2, HC)
                                t_.wait_ge(ac.h, acN[("gelu", t2, f2)])
                            for ccx in range(DC):
                                inst = t_.matmul(psF[pf % 2][:, 0:512],
                                                 wfcq[:, ccx * H4 + f * P:ccx * H4 + (f + 1) * P],
                                                 xnfm[:, ccx * TC + t * 512:ccx * TC + (t + 1) * 512],
                                                 start=(ccx == 0), stop=(ccx == DC - 1))
                            pe.inc(inst)
                            if f >= 1:
                                t_.wait_ge(ac.h, acN[("gelu", t, f - 1)])
                                pe.inc(t_.matmul(psS1[0:1, 0:512], ones_col_bf[:],
                                                 h_sb[:, (f - 1) * 512:f * 512],
                                                 start=(f == 1), stop=False))
                                t_.wait_ge(dv.h, dvN[("h2", t, f - 1)])
                                pe.inc(t_.matmul(psS2[0:1, 0:512], ones_col_bf[:],
                                                 h2_sb[:, ((f - 1) % 2) * 512:((f - 1) % 2 + 1) * 512],
                                                 start=(f == 1), stop=False))
                        t_.wait_ge(ac.h, acN[("gelu", t, HC - 1)])
                        pe.inc(t_.matmul(psS1[0:1, 0:512], ones_col_bf[:],
                                         h_sb[:, (HC - 1) * 512:HC * 512],
                                         start=False, stop=True))
                        t_.wait_ge(dv.h, dvN[("h2", t, HC - 1)])
                        pe.inc(t_.matmul(psS2[0:1, 0:512], ones_col_bf[:],
                                         h2_sb[:, ((HC - 1) % 2) * 512:((HC - 1) % 2 + 1) * 512],
                                         start=False, stop=True))

                @blk2.scalar
                def _(sc):
                    for t in range(4):
                        for f in range(HC):
                            if t >= 1 and f == 0:
                                sc.wait_ge(stsy.h, E_st + 16 * (2 * t - 1))
                            sc.wait_ge(pe.h, peN[("fc", t, f)])
                            ac.inc(sc.activation(h_sb[:, f * 512:(f + 1) * 512],
                                                 psF[(t * HC + f) % 2][:, 0:512], AF.Gelu))
                        sc.wait_ge(dv.h, dvN[("var", t)])
                        ac.inc(sc.activation(std_s[:], var_s[:], AF.Sqrt,
                                             bias=eps_row[0:1, 0:1]))

                @blk2.vector
                def _(v):
                    v.wait_ge(dv.h, marks["thr_dv"])
                    for k, (fsl, c) in enumerate(fcchunks):
                        par = k % 2
                        ch = wgbuf[:, par * GG:(par + 1) * GG]
                        v.wait_ge(ldgp.h, E_ldg + 16 * (k + 1))
                        if k >= 1:
                            v.wait_ge(dv.h, E_dv + 3 * k)
                        dv.inc(v.tensor_scalar(t1q[:], ch, thr_cols[:, 0:1], None,
                                               ALU.is_gt))
                        dv.inc(v.tensor_scalar(t2q[:], ch, nthr_cols[:, 0:1], None,
                                               ALU.is_lt))
                        v.wait_ge(dv.h, E_dv + 3 * k + 2)
                        dv.inc(v.tensor_sub(
                            wfcq[:, c * H4 + fsl * GG:c * H4 + (fsl + 1) * GG],
                            t1q[:], t2q[:]))
                    for t in range(4):
                        for f in range(HC):
                            v.wait_ge(ac.h, acN[("gelu", t, f)])
                            pf = t * HC + f
                            if pf >= 2:
                                t2, f2 = divmod(pf - 2, HC)
                                v.wait_ge(pe.h, peN[("s2", t2, f2)])
                            src = h_sb[:, f * 512:(f + 1) * 512]
                            dv.inc(v.tensor_mul(h2_sb[:, (pf % 2) * 512:(pf % 2 + 1) * 512],
                                                src, src))
                        v.wait_ge(pe.h, peN[("s1", t, HC - 1)])
                        dv.inc(v.tensor_scalar(negmu_row[0:1, t * 512:(t + 1) * 512],
                                               psS1[0:1, 0:512], -1.0 / H4, None, ALU.mult))
                        v.wait_ge(dv.h, dvN[("negmu", t)])
                        if t >= 1:
                            v.wait_ge(ac.h, acN[("std", t - 1)])
                        dv.inc(v.tensor_mul(musq_s[:],
                                            negmu_row[0:1, t * 512:(t + 1) * 512],
                                            negmu_row[0:1, t * 512:(t + 1) * 512]))
                        v.wait_ge(pe.h, peN[("s2", t, HC - 1)])
                        v.wait_ge(dv.h, dvN[("musq", t)])
                        dv.inc(v.scalar_tensor_tensor(var_s[:], psS2[0:1, 0:512],
                                                      1.0 / H4, musq_s[:],
                                                      ALU.mult, ALU.subtract))
                        v.wait_ge(ac.h, acN[("std", t)])
                        if t >= 1:
                            v.wait_ge(stsy.h, E_st + 16 * (2 * t))
                        dv.inc(v.reciprocal(rstd_row[0:1, 0:512], std_s[:]))

                @blk2.gpsimd
                def _(g):
                    assert ldgp.n == E_ldg
                    for k, (fsl, c) in enumerate(fcchunks):
                        par = k % 2
                        if k < 2:
                            g.wait_ge(ac.h, marks["gamma_ac"])
                        if k >= 2:
                            g.wait_ge(dv.h, fc_tern_mark(k - 2))
                        ldgp.inc(g.dma_start(
                            out=wgbuf[:, par * GG:(par + 1) * GG].unsqueeze(1),
                            in_=wfcT3g[:, c:c + 1, fsl * GG:(fsl + 1) * GG]), 16)

                @blk2.sync
                def _(sy):
                    for t in range(4):
                        sy.wait_ge(ac.h, acN[("gelu", t, HC - 1)])
                        sy.wait_ge(dv.h, dvN[("h2", t, HC - 1)])
                        stsy.inc(sy.dma_start(out=h_scr.ap()[t * P:(t + 1) * P, :],
                                              in_=h_sb[:]), 16)
                        sy.wait_ge(dv.h, dvN[("rstd", t)])
                        stsy.inc(sy.dma_start(out=bounce.ap()[0:1, t * 512:(t + 1) * 512],
                                              in_=rstd_row[:]), 16)
                    sy.wait_ge(stsy.h, stsy.n)
                    stsy.inc(sy.dma_start(out=rstd_cols[:].unsqueeze(2),
                                          in_=bounce.ap().rearrange("a (t p) -> p t a", p=P),
                                          allow_slow_non_contiguous=True), 16)
                blk2.__exit__(None, None, None)
                E2_ld = ldsy.n
                E2_st2 = stsy.n

            wpq = eE(nc.sbuf_tensor("wpq", [P, HC * D], BF16))
            psCC = eE(nc.psum_tensor("psCC", [P, 512], F32))
            psCr = eE(nc.psum_tensor("psCr", [P, 512], F32))
            psREP = eE(nc.psum_tensor("psREP", [P, 512], F32))
            psREPb = eE(nc.psum_tensor("psREPb", [P, 512], F32))

            b3 = {s: s.n for s in (ldsy, ldgp, pe, dv, ac)}
            blk3 = nc.Block("e3"); blk3.__enter__()

            def p_tern_mark(c):
                return b3[dv] + 3 * (c + 1)

            @blk3.sync
            def _(sy):
                ldsy.inc(sy.dma_start(out=lnbcb[:], in_=lnb_in), 16)
            LD_LNB = ldsy.n

            @blk3.gpsimd
            def _(g):
                assert ldgp.n == b3[ldgp]
                for c in range(HC):
                    par = c % 2
                    if c < 2:
                        g.wait_ge(dv.h, marks["fc_tern_done"])
                    if c >= 2:
                        g.wait_ge(dv.h, p_tern_mark(c - 2))
                    ldgp.inc(g.dma_start(
                        out=wgbuf[:, par * GG:(par + 1) * GG].unsqueeze(1),
                        in_=wpT3g[:, c:c + 1, :]), 16)

            @blk3.scalar
            def _(s):
                s.wait_ge(ldsy.h, LD_LNB)
                ac.inc(s.activation(lnb_bf[:], lnbcb[:], AF.Copy))
            AC_LNB = ac.n

            PE_LNB = b3[pe] + 2 * HC
            PE_ONES = PE_LNB + 2 * HC
            PE_CCREP1 = PE_ONES + 1
            PE_CCREP2 = PE_ONES + 2
            DV_TERN3 = b3[dv] + 3 * HC

            def dvScale3(c):
                return DV_TERN3 + c + 1
            DV_SCALES = DV_TERN3 + HC
            DV_CCROW = DV_SCALES + 2
            DV_CCREP = DV_SCALES + 4

            @blk3.tensor
            def _(t_):
                t_.wait_ge(ac.h, AC_LNB)
                t_.wait_ge(dv.h, b3[dv])
                for c in range(HC):
                    t_.wait_ge(dv.h, p_tern_mark(c))
                    pe.inc(t_.matmul(psCC[0:1, 0:512], lnb_bf[:, c:c + 1],
                                     wpq[:, c * D:c * D + 512],
                                     start=(c == 0), stop=(c == HC - 1)))
                    pe.inc(t_.matmul(psS1[0:1, 0:512], lnb_bf[:, c:c + 1],
                                     wpq[:, c * D + 512:(c + 1) * D],
                                     start=(c == 0), stop=(c == HC - 1)))
                for c in range(HC):
                    t_.wait_ge(dv.h, dvScale3(c))
                    pe.inc(t_.matmul(psCr[0:1, 0:512], ones_col_bf[:],
                                     wpq[:, c * D:c * D + 512],
                                     start=(c == 0), stop=(c == HC - 1)))
                    pe.inc(t_.matmul(psS2[0:1, 0:512], ones_col_bf[:],
                                     wpq[:, c * D + 512:(c + 1) * D],
                                     start=(c == 0), stop=(c == HC - 1)))
                t_.wait_ge(dv.h, DV_CCROW)
                pe.inc(t_.matmul(psREP[:, 0:512], ones_row_f[:], cc_row[0:1, 0:512],
                                 start=True, stop=True))
                t_.wait_ge(dv.h, DV_CCROW)
                pe.inc(t_.matmul(psREPb[:, 0:512], ones_row_f[:], cc_row[0:1, 512:1024],
                                 start=True, stop=True))

            @blk3.vector
            def _(v):
                for c in range(HC):
                    par = c % 2
                    ch = wgbuf[:, par * GG:(par + 1) * GG]
                    v.wait_ge(ldgp.h, b3[ldgp] + 16 * (c + 1))
                    if c >= 1:
                        v.wait_ge(dv.h, b3[dv] + 3 * c)
                    dv.inc(v.tensor_scalar(t1q[:], ch, thr_cols[:, 1:2], None,
                                           ALU.is_gt))
                    dv.inc(v.tensor_scalar(t2q[:], ch, nthr_cols[:, 1:2], None,
                                           ALU.is_lt))
                    v.wait_ge(dv.h, b3[dv] + 3 * c + 2)
                    dv.inc(v.tensor_sub(wpq[:, c * D:(c + 1) * D], t1q[:], t2q[:]))
                for c in range(HC):
                    v.wait_ge(pe.h, b3[pe] + 2 * (c + 1))
                    dv.inc(v.tensor_scalar(wpq[:, c * D:(c + 1) * D],
                                           wpq[:, c * D:(c + 1) * D],
                                           lnw_sb[:, c:c + 1], None, ALU.mult))
                v.wait_ge(pe.h, PE_LNB)
                dv.inc(v.tensor_copy(cc_row[0:1, 0:512], psCC[0:1, 0:512]))
                dv.inc(v.tensor_copy(cc_row[0:1, 512:1024], psS1[0:1, 0:512]))
                v.wait_ge(pe.h, PE_ONES)
                dv.inc(v.tensor_copy(c_row[0:1, 0:512], psCr[0:1, 0:512]))
                dv.inc(v.tensor_copy(c_row[0:1, 512:1024], psS2[0:1, 0:512]))
            blk3.__exit__(None, None, None)
            xbuf = eE(nc.sbuf_tensor("xbuf", [P, 4 * D], F32))
            abuf = eE(nc.sbuf_tensor("abuf", [P, 4 * D], F32))
            tmpA = eE(nc.sbuf_tensor("tmpA", [P, 512], F32))
            psJ = [eE(nc.psum_tensor(f"psJ{i}", [P, 512], F32)) for i in range(2)]

            PE4 = PE_CCREP2
            DV4 = DV_CCREP
            peJ = {}
            n = PE4
            for t in range(4):
                for idx in range(8):
                    n += 1
                    peJ[(t, idx)] = n
            dvE = {}
            n = DV4
            for t in range(4):
                for idx in range(8):
                    dvE[(t, idx)] = (n + 1, n + 2, n + 3)
                    n += 3
            ld4 = {}
            n = ldgp.n
            for t in range(4):
                ld4[("h", t)] = n + 16
                ld4[("x", t)] = n + 32
                ld4[("a", t)] = n + 48
                n += 48
            stO = {t: E2_st2 + 16 * (t + 1) for t in range(4)}

            blk4 = nc.Block("e4"); blk4.__enter__()

            @blk4.gpsimd
            def _(g):
                g.wait_ge(stsy.h, marks["attn0_st"])
                for t in range(4):
                    if t == 0:
                        g.wait_ge(stsy.h, E2_st)
                    else:
                        g.wait_ge(pe.h, peJ[(t - 1, 7)])
                    ldgp.inc(g.dma_start(out=h_sb[:],
                                         in_=h_scr.ap()[t * P:(t + 1) * P, :]), 16)
                    if t >= 1:
                        g.wait_ge(dv.h, dvE[(t - 1, 7)][2])
                        g.wait_ge(stsy.h, stO[t - 1])
                    ldgp.inc(g.dma_start(out=xbuf[:].rearrange("p (u d) -> p u d", u=4),
                                         in_=xin.rearrange("(t p) d -> p t d", p=P)[:, 4 * t:4 * t + 4, :]), 16)
                    ldgp.inc(g.dma_start(out=abuf[:].rearrange("p (u d) -> p u d", u=4),
                                         in_=attn0_d.ap().rearrange("(t p) d -> p t d", p=P)[:, 4 * t:4 * t + 4, :]), 16)

            @blk4.tensor
            def _(t_):
                t_.wait_ge(dv.h, DV4)
                for t in range(4):
                    t_.wait_ge(ldgp.h, ld4[("a", t)])
                    for idx in range(8):
                        tt, half = divmod(idx, 2)
                        tg = 4 * t + tt
                        gi = t * 8 + idx
                        if gi >= 2:
                            t2, i2 = divmod(gi - 2, 8)
                            t_.wait_ge(dv.h, dvE[(t2, i2)][0])
                        for f in range(HC):
                            t_.matmul(psJ[idx % 2][:, 0:512],
                                      h_sb[:, f * 512 + tt * P:f * 512 + (tt + 1) * P],
                                      wpq[:, f * D + half * 512:f * D + (half + 1) * 512],
                                      start=(f == 0), stop=False)
                        pe.inc(t_.matmul(psJ[idx % 2][:, 0:512],
                                         negmu_row[0:1, tg * P:(tg + 1) * P],
                                         c_row[0:1, half * 512:(half + 1) * 512],
                                         start=False, stop=True))

            @blk4.vector
            def _(v):
                v.wait_ge(stsy.h, E2_st2)
                v.wait_ge(pe.h, PE_CCREP2)
                for t in range(4):
                    v.wait_ge(ldgp.h, ld4[("a", t)])
                    for idx in range(8):
                        tt, half = divmod(idx, 2)
                        tg = 4 * t + tt
                        par = idx % 2
                        e1n, e2n, e3n = dvE[(t, idx)]
                        v.wait_ge(pe.h, peJ[(t, idx)])
                        if (t, idx) != (0, 0):
                            pv = dvE[(t, idx - 1)][2] if idx else dvE[(t - 1, 7)][2]
                            v.wait_ge(dv.h, pv)
                        dv.inc(v.scalar_tensor_tensor(tmpA[:, 0:512],
                                                      psJ[par][:, 0:512],
                                                      rstd_cols[:, tg:tg + 1],
                                                      xbuf[:, tt * D + half * 512:tt * D + (half + 1) * 512],
                                                      ALU.mult, ALU.add))
                        v.wait_ge(dv.h, e1n)
                        dv.inc(v.tensor_add(tmpA[:, 0:512],
                                            tmpA[:, 0:512],
                                            abuf[:, tt * D + half * 512:tt * D + (half + 1) * 512]))
                        v.wait_ge(dv.h, e2n)
                        dv.inc(v.tensor_add(xbuf[:, tt * D + half * 512:tt * D + (half + 1) * 512],
                                            tmpA[:, 0:512],
                                            (psREP if half == 0 else psREPb)[:, 0:512]))

            @blk4.sync
            def _(sy):
                for t in range(4):
                    sy.wait_ge(dv.h, dvE[(t, 7)][2])
                    stsy.inc(sy.dma_start(
                        out=x1_d.ap().rearrange("(t p) d -> p t d", p=P)[:, 4 * t:4 * t + 4, :],
                        in_=xbuf[:].rearrange("p (u d) -> p u d", u=4)), 16)
                if upto == "E":
                    sy.wait_ge(stsy.h, stsy.n)
            blk4.__exit__(None, None, None)
            marks["x1_st"] = stsy.n
        if upto == "E":
            close_stacks()
            return nc
        esXn.close()

        xn1fm = e(nc.sbuf_tensor("xn1fm", [P, DC * TC], BF16))
        esT = ExitStack()
        xn1tm = esT.enter_context(nc.sbuf_tensor("xn1tm", [P, NT * D], BF16))

        with ExitStack() as esF:
            eF = esF.enter_context
            x_sb = eF(nc.sbuf_tensor("x_sbF", [P, NT * D], F32))
            o_sb2 = x_sb
            s1 = eF(nc.sbuf_tensor("s1F", [P, NT], F32))
            s2 = eF(nc.sbuf_tensor("s2F", [P, NT], F32))
            mu = eF(nc.sbuf_tensor("muF", [P, NT], F32))
            var = eF(nc.sbuf_tensor("varF", [P, NT], F32))
            musq = eF(nc.sbuf_tensor("musqF", [P, NT], F32))
            std = eF(nc.sbuf_tensor("stdF", [P, NT], F32))
            rstd = eF(nc.sbuf_tensor("rstdF", [P, NT], F32))
            scr = eF(nc.sbuf_tensor("scrF", [P, D], F32))
            psT = [eF(nc.psum_tensor(f"psTF{i}", [P, D], F32)) for i in range(2)]

            b = {s: s.n for s in (ldsy, pe, dv, ac)}
            blkF = nc.Block("sF"); blkF.__enter__()

            ln_norm2 = []

            @blkF.sync
            def _(sy):
                sy.wait_ge(stsy.h, marks["x1_st"])
                sy.wait_ge(pe.h, b[pe])
                sy.wait_ge(dv.h, b[dv])
                sy.wait_ge(ac.h, b[ac])
                ldsy.inc(sy.dma_start(out=x_sb[:].rearrange("p (t d) -> p t d", t=NT),
                                      in_=x1_d.ap().rearrange("(t p) d -> p t d", p=P)), 16)
            LD_X1 = ldsy.n

            def f_cast_pred(t):
                return b[ac] + NT + 1 + t + 1

            def f_mm_pred(t):
                return b[pe] + t + 1

            def f_evac_pred(t):
                return b[dv] + 2 * NT + 4 + t + 1

            @blkF.vector
            def _(v):
                v.wait_ge(ldsy.h, LD_X1)
                for t in range(NT):
                    dv.inc(v.reduce_sum(s1[:, t:t + 1], x_sb[:, t * D:(t + 1) * D], axis=AX.X))
                v.wait_ge(dv.h, b[dv] + NT)
                dv.inc(v.tensor_scalar(mu[:], s1[:], 1.0 / D, None, ALU.mult))
                v.wait_ge(dv.h, b[dv] + NT + 1)
                dv.inc(v.tensor_mul(musq[:], mu[:], mu[:]))
                v.wait_ge(ac.h, b[ac] + NT)
                v.wait_ge(dv.h, b[dv] + NT + 2)
                dv.inc(v.scalar_tensor_tensor(var[:], s2[:], 1.0 / D, musq[:],
                                              ALU.mult, ALU.subtract))
                v.wait_ge(ac.h, b[ac] + NT + 1)
                dv.inc(v.reciprocal(rstd[:], std[:]))
                v.wait_ge(dv.h, b[dv] + NT + 4)
                for t in range(NT):
                    ln_norm2.append(dv.inc(v.tensor_scalar(
                        o_sb2[:, t * D:(t + 1) * D], x_sb[:, t * D:(t + 1) * D],
                        mu[:, t:t + 1], rstd[:, t:t + 1], ALU.subtract, ALU.mult)))
                for t in range(NT):
                    v.wait_ge(pe.h, f_mm_pred(t))
                    for c in range(DC):
                        inst = v.tensor_copy(
                            xn1fm[:, c * TC + t * P:c * TC + (t + 1) * P],
                            psT[t % 2][:, c * P:(c + 1) * P])
                    dv.inc(inst)

            @blkF.scalar
            def _(s):
                s.wait_ge(ldsy.h, LD_X1)
                for t in range(NT):
                    if t >= 1:
                        s.wait_ge(ac.h, b[ac] + t)
                    ac.inc(s.activation(scr[:], x_sb[:, t * D:(t + 1) * D], AF.Square,
                                        accum_out=s2[:, t:t + 1]))
                s.wait_ge(dv.h, b[dv] + NT + 3)
                ac.inc(s.activation(std[:], var[:], AF.Sqrt, bias=eps_col[:]))
                for t in range(NT):
                    s.wait_ge(dv.h, ln_norm2[t])
                    a_ = ac.inc(s.activation(xn1tm[:, t * D:(t + 1) * D],
                                             o_sb2[:, t * D:(t + 1) * D], AF.Copy))

            @blkF.tensor
            def _(t_):
                t_.wait_ge(dv.h, b[dv])
                for t in range(NT):
                    t_.wait_ge(ac.h, f_cast_pred(t))
                    if t >= 2:
                        t_.wait_ge(dv.h, f_evac_pred(t - 2))
                    for c in range(DC):
                        inst = t_.matmul(psT[t % 2][:, c * P:(c + 1) * P],
                                         xn1tm[:, t * D + c * P:t * D + (c + 1) * P],
                                         ident[:], start=True, stop=True)
                    pe.inc(inst)
            blkF.__exit__(None, None, None)
            marks["xn1tm_ac"] = ac.n
            marks["xn1fm_dv"] = dv.n

            blkFs = nc.Block("sFs"); blkFs.__enter__()

            @blkFs.gpsimd
            def _(g):
                g.wait_ge(dv.h, marks["xn1fm_dv"])
                stgp.inc(g.dma_start(
                    out=xn1fm_d.ap().rearrange("(c p) t -> p c t", p=P),
                    in_=xn1fm[:].rearrange("p (c t) -> p c t", c=DC)), 16)
            blkFs.__exit__(None, None, None)
            marks["xn1fm_st"] = stgp.n

            if upto == "F":
                blkD = nc.Block("dbgF"); blkD.__enter__()

                @blkD.gpsimd
                def _(g):
                    g.wait_ge(dv.h, marks["xn1fm_dv"])
                    stsy.inc(g.dma_start(
                        out=dbg.rearrange("(c p) t -> p c t", p=P),
                        in_=xn1fm[:].rearrange("p (c t) -> p c t", c=DC)), 16)
                    g.wait_ge(stsy.h, stsy.n)
                blkD.__exit__(None, None, None)

        if upto == "F":
            esT.close()
            return nc

        with ExitStack() as esG:
            eG = esG.enter_context
            mbuf = [eG(nc.sbuf_tensor(f"mbuf{i}", [P, 512], F32)) for i in range(2)]
            psG = [eG(nc.psum_tensor(f"psGm{i}", [P, 512], F32)) for i in range(2)]

            b = {s: s.n for s in (pe, dv, stsy)}
            marks["mloc_st_pred"] = b[stsy] + 16 * 16
            blkG = nc.Block("sG"); blkG.__enter__()
            m3 = mloc_d.ap().rearrange("(d p) o -> p d o", p=P)

            @blkG.tensor
            def _(t_):
                t_.wait_ge(ac.h, marks["xn1tm_ac"])
                t_.wait_ge(dv.h, marks["xn1fm_dv"])
                for k in range(16):
                    d1, half = divmod(k, 2)
                    if k >= 2:
                        t_.wait_ge(dv.h, b[dv] + k - 1)
                    for ti in range(NT):
                        inst = t_.matmul(psG[k % 2][:, 0:512],
                                         xn1tm[:, ti * D + d1 * P:ti * D + (d1 + 1) * P],
                                         xn1tm[:, ti * D + half * 512:ti * D + (half + 1) * 512],
                                         start=(ti == 0), stop=(ti == NT - 1))
                    pe.inc(inst)

            @blkG.vector
            def _(v):
                for k in range(16):
                    v.wait_ge(pe.h, b[pe] + k + 1)
                    if k >= 2:
                        v.wait_ge(stsy.h, b[stsy] + 16 * (k - 1))
                    dv.inc(v.tensor_copy(mbuf[k % 2][:], psG[k % 2][:, 0:512]))

            @blkG.sync
            def _(sy):
                for k in range(16):
                    d1, half = divmod(k, 2)
                    sy.wait_ge(dv.h, b[dv] + k + 1)
                    stsy.inc(sy.dma_start(out=m3[:, d1:d1 + 1, half * 512:(half + 1) * 512],
                                          in_=mbuf[k % 2][:].unsqueeze(1)), 16)
            @blkG.gpsimd
            def _(g):
                g.wait_ge(stsy.h, marks["mloc_st_pred"])
                g.wait_ge(stsy.h, marks["x1_st"])
                g.wait_ge(stgp.h, marks["xn1fm_st"])
            blkG.__exit__(None, None, None)
            marks["mloc_st"] = stsy.n
        esT.close()
        return nc


def build_l2():
    nc = bass.Bass("TRN2", target_bir_lowering=False, debug=False)
    x1_in = nc.dram_tensor("x1", [TC, D], F32, kind="ExternalInput").ap()
    xn1fm = nc.dram_tensor("xn1fm", [D, TC], F32, kind="ExternalInput").ap()
    mparts = nc.dram_tensor("mparts", [4 * D, D], F32, kind="ExternalInput").ap()
    wq1_in = nc.dram_tensor("wq1", [D, D], F32, kind="ExternalInput").ap()
    wg_in = nc.dram_tensor("wg", [D, D], F32, kind="ExternalInput").ap()
    bg_in = nc.dram_tensor("bg", [1, D], F32, kind="ExternalInput").ap()
    we_in = nc.dram_tensor("we", [8 * D, D], F32, kind="ExternalInput").ap()
    be_in = nc.dram_tensor("be", [1, 8 * D], F32, kind="ExternalInput").ap()
    x2_out = nc.dram_tensor("x2", [TC, D], F32, kind="ExternalOutput").ap()

    with ExitStack() as es:
        e = es.enter_context
        xn_sb = e(nc.sbuf_tensor("xn_sb", [P, DC * TC], BF16))
        m_bf = e(nc.sbuf_tensor("m_bf", [P, DC * D], BF16))
        w2_sb = e(nc.sbuf_tensor("w2_sb", [P, DC * D], BF16))
        acc = e(nc.sbuf_tensor("acc", [P, NT * D], BF16))
        msel = e(nc.sbuf_tensor("msel", [P, 8 * NT], F32))
        lg = e(nc.sbuf_tensor("lg", [P, 2 * D], F32))
        sexp = e(nc.sbuf_tensor("sexp", [P, D], BF16))
        max8 = e(nc.sbuf_tensor("max8", [P, 8], F32))
        idx8 = e(nc.sbuf_tensor("idx8", [P, 8], U32))
        eidx_u = e(nc.sbuf_tensor("eidx_u", [P, 1], U32))
        seven = e(nc.sbuf_tensor("seven", [P, 1], U32))
        nmx_all = e(nc.sbuf_tensor("nmx_all", [P, NT], F32))
        sume_all = e(nc.sbuf_tensor("sume_all", [P, NT], F32))
        p_all = e(nc.sbuf_tensor("p_all", [P, NT], F32))
        eidx_all = e(nc.sbuf_tensor("eidx_all", [P, NT], F32))
        tmp16 = e(nc.sbuf_tensor("tmp16", [P, NT], F32))
        tmpe = e(nc.sbuf_tensor("tmpe", [P, 2 * 512], BF16))
        x1b = e(nc.sbuf_tensor("x1b", [P, 2 * D], F32))
        ones_row_bf = e(nc.sbuf_tensor("ones_row_bf", [1, P], BF16))
        psU = [e(nc.psum_tensor(f"psU{i}", [P, 512], F32)) for i in range(2)]
        ld = SemCtr(es, nc, "ld")
        ldw = SemCtr(es, nc, "ldw")
        gp = SemCtr(es, nc, "gp")
        pe = SemCtr(es, nc, "pe")
        dv = SemCtr(es, nc, "dv")
        ac = SemCtr(es, nc, "ac")
        st = SemCtr(es, nc, "st")

        with ExitStack() as es1:
            e1 = es1.enter_context
            msum = e1(nc.sbuf_tensor("msum", [P, DC * D], F32))
            mtmp = e1(nc.sbuf_tensor("mtmp", [P, DC * D], F32))
            wq1_bf = e1(nc.sbuf_tensor("wq1_bf", [P, DC * D], BF16))
            blk = nc.Block("h1"); blk.__enter__()
            mp3 = mparts.rearrange("(r c p) o -> p r c o", r=4, p=P)

            @blk.gpsimd
            def _(g):
                gp.inc(g.memset(ones_row_bf[:], 1.0))
                gp.inc(g.memset(seven[:], 7))
                ld.inc(g.dma_start(out=msum[:].rearrange("p (c o) -> p c o", c=DC),
                                   in_=mp3[:, 0]), 16)
                ld.inc(g.dma_start(out=mtmp[:].rearrange("p (c o) -> p c o", c=DC),
                                   in_=mp3[:, 1]), 16)
                g.wait_ge(dv.h, 1)
                ld.inc(g.dma_start(out=mtmp[:].rearrange("p (c o) -> p c o", c=DC),
                                   in_=mp3[:, 2]), 16)
                g.wait_ge(dv.h, 2)
                ld.inc(g.dma_start(out=mtmp[:].rearrange("p (c o) -> p c o", c=DC),
                                   in_=mp3[:, 3]), 16)
                ldw.inc(g.dma_start(out=wq1_bf[:].rearrange("p (c o) -> p c o", c=DC),
                                    in_=wq1_in.rearrange("(c p) o -> p c o", p=P)), 16)
                ldw.inc(g.dma_start(out=xn_sb[:].rearrange("p (c t) -> p c t", c=DC),
                                    in_=xn1fm.rearrange("(c p) t -> p c t", p=P)), 16)

            @blk.vector
            def _(v):
                v.wait_ge(ld.h, 32)
                dv.inc(v.tensor_add(msum[:], msum[:], mtmp[:]))
                v.wait_ge(ld.h, 48)
                v.wait_ge(dv.h, 1)
                dv.inc(v.tensor_add(msum[:], msum[:], mtmp[:]))
                v.wait_ge(ld.h, 64)
                v.wait_ge(dv.h, 2)
                dv.inc(v.tensor_add(msum[:], msum[:], mtmp[:]))
                v.wait_ge(dv.h, 3)
                dv.inc(v.tensor_copy(m_bf[:], msum[:]))

            @blk.tensor
            def _(t_):
                t_.wait_ge(ldw.h, 16)
                t_.wait_ge(dv.h, 4)
                for k in range(16):
                    hc, half = divmod(k, 2)
                    if k >= 2:
                        t_.wait_ge(dv.h, 4 + k - 1)
                    for oc in range(DC):
                        inst = t_.matmul(psU[k % 2][:, 0:512],
                                         wq1_bf[:, oc * D + hc * P:oc * D + (hc + 1) * P],
                                         m_bf[:, oc * D + half * 512:oc * D + (half + 1) * 512],
                                         start=(oc == 0), stop=(oc == DC - 1))
                    pe.inc(inst)

            @blk.vector
            def _(v):
                for k in range(16):
                    hc, half = divmod(k, 2)
                    v.wait_ge(pe.h, k + 1)
                    dv.inc(v.tensor_copy(w2_sb[:, hc * D + half * 512:hc * D + (half + 1) * 512],
                                         psU[k % 2][:, 0:512]))
            blk.__exit__(None, None, None)
        H2_pe = 16
        H2_dv = 20
        H2_gp = 2

        wg_bf = e(nc.sbuf_tensor("wg_bf", [P, DC * D], BF16))
        webf = [e(nc.sbuf_tensor(f"webf{i}", [P, DC * D], BF16)) for i in range(2)]
        bg_bf = e(nc.sbuf_tensor("bg_bf", [1, D], BF16))
        be_bf = e(nc.sbuf_tensor("be_bf", [1, 8 * D], BF16))
        dvN = {}
        n = H2_dv
        for t in range(NT):
            for k in ("ev0", "ev1", "max8", "nmx", "idx", "and", "eidx"):
                n += 1
                dvN[(k, t)] = n
        dvN["recip"] = n + 1
        n += 1
        for ee in range(8):
            dvN[("mseleq", ee)] = n + 1
            dvN[("mselmul", ee)] = n + 2
            n += 2
        H3_dv = n
        peN = {}
        n = H2_pe
        for t in range(NT):
            for half in range(2):
                n += 1
                peN[("lg", t, half)] = n
        H3_pe = n
        acN = {}
        n = 0
        for t in range(NT):
            n += 1
            acN[("exp", t)] = n
        H3_ac = n

        blk3 = nc.Block("h3"); blk3.__enter__()

        @blk3.gpsimd
        def _(g):
            ldw.inc(g.dma_start(out=wg_bf[:].rearrange("p (c o) -> p c o", c=DC),
                                in_=wg_in.rearrange("(c p) o -> p c o", p=P)), 16)
            ldw.inc(g.dma_start(out=bg_bf[:], in_=bg_in), 16)
            ldw.inc(g.dma_start(out=be_bf[:], in_=be_in), 16)

        @blk3.tensor
        def _(t_):
            t_.wait_ge(ldw.h, 80)
            t_.wait_ge(gp.h, 1)
            for t in range(NT):
                for half in range(2):
                    k = t * 2 + half
                    if k >= 2:
                        t2, h2 = divmod(k - 2, 2)
                        t_.wait_ge(dv.h, dvN[("ev" + str(h2), t2)])
                    for cc in range(DC):
                        t_.matmul(psU[k % 2][:, 0:512],
                                  xn_sb[:, cc * TC + t * P:cc * TC + (t + 1) * P],
                                  wg_bf[:, cc * D + half * 512:cc * D + (half + 1) * 512],
                                  start=(cc == 0), stop=False)
                    pe.inc(t_.matmul(psU[k % 2][:, 0:512], ones_row_bf[:],
                                     bg_bf[0:1, half * 512:(half + 1) * 512],
                                     start=False, stop=True))

        @blk3.vector
        def _(v):
            for t in range(NT):
                for half in range(2):
                    v.wait_ge(pe.h, peN[("lg", t, half)])
                    if t >= 2:
                        v.wait_ge(ac.h, acN[("exp", t - 2)])
                    dv.inc(v.tensor_copy(lg[:, (t % 2) * D + half * 512:(t % 2) * D + (half + 1) * 512],
                                         psU[(t * 2 + half) % 2][:, 0:512]))
                v.wait_ge(dv.h, dvN[("ev1", t)])
                dv.inc(v.max(max8[:], lg[:, (t % 2) * D:(t % 2 + 1) * D]))
                v.wait_ge(dv.h, dvN[("max8", t)])
                dv.inc(v.tensor_scalar(nmx_all[:, t:t + 1], max8[:, 0:1], -1.0, None,
                                       ALU.mult))
                dv.inc(v.max_index(idx8[:], max8[:], lg[:, (t % 2) * D:(t % 2 + 1) * D]))
                v.wait_ge(dv.h, dvN[("idx", t)])
                dv.inc(v.tensor_single_scalar(eidx_u[:], idx8[:, 0:1], seven[:],
                                              ALU.bitwise_and))
                v.wait_ge(dv.h, dvN[("and", t)])
                dv.inc(v.tensor_copy(eidx_all[:, t:t + 1], eidx_u[:]))
            v.wait_ge(ac.h, acN[("exp", NT - 1)])
            dv.inc(v.reciprocal(p_all[:], sume_all[:]))
            v.wait_ge(dv.h, dvN["recip"])
            for ee in range(8):
                dv.inc(v.tensor_scalar(tmp16[:], eidx_all[:], float(ee), None,
                                       ALU.is_equal))
                v.wait_ge(dv.h, dvN[("mseleq", ee)])
                dv.inc(v.tensor_mul(msel[:, ee * NT:(ee + 1) * NT], tmp16[:], p_all[:]))

        @blk3.scalar
        def _(sc):
            for t in range(NT):
                sc.wait_ge(dv.h, dvN[("nmx", t)])
                ac.inc(sc.activation(sexp[:], lg[:, (t % 2) * D:(t % 2 + 1) * D],
                                     AF.Exp, bias=nmx_all[:, t:t + 1],
                                     accum_out=sume_all[:, t:t + 1]))
        blk3.__exit__(None, None, None)

        peE = {}
        n = H3_pe
        for ee in range(8):
            for t in range(NT):
                for half in range(2):
                    n += 1
                    peE[(ee, t, half)] = n
        H4_pe = n
        dvA = {}
        n = H3_dv
        for ee in range(8):
            for t in range(NT):
                for half in range(2):
                    dvA[(ee, t, half)] = (n + 1, n + 2)
                    n += 2
        H4_dv = n
        ldE = {ee: 80 + 16 * (ee + 1) for ee in range(8)}

        blk4 = nc.Block("h4"); blk4.__enter__()
        we3 = we_in.rearrange("(ee c p) o -> p ee c o", ee=8, p=P)

        @blk4.gpsimd
        def _(g):
            for ee in range(8):
                if ee >= 2:
                    g.wait_ge(pe.h, peE[(ee - 2, NT - 1, 1)])
                ldw.inc(g.dma_start(out=webf[ee % 2][:].rearrange("p (c o) -> p c o", c=DC),
                                    in_=we3[:, ee]), 16)

        @blk4.tensor
        def _(t_):
            for ee in range(8):
                t_.wait_ge(ldw.h, ldE[ee])
                for t in range(NT):
                    for half in range(2):
                        k = (ee * NT + t) * 2 + half
                        if k >= 2:
                            kk = k - 2
                            e2, r = divmod(kk, NT * 2)
                            t2, h2 = divmod(r, 2)
                            t_.wait_ge(dv.h, dvA[(e2, t2, h2)][0])
                        for cc in range(DC):
                            t_.matmul(psU[k % 2][:, 0:512],
                                      xn_sb[:, cc * TC + t * P:cc * TC + (t + 1) * P],
                                      webf[ee % 2][:, cc * D + half * 512:cc * D + (half + 1) * 512],
                                      start=(cc == 0), stop=False)
                        pe.inc(t_.matmul(psU[k % 2][:, 0:512], ones_row_bf[:],
                                         be_bf[0:1, ee * D + half * 512:ee * D + (half + 1) * 512],
                                         start=False, stop=True))

        @blk4.vector
        def _(v):
            for ee in range(8):
                for t in range(NT):
                    for half in range(2):
                        k = (ee * NT + t) * 2 + half
                        par = k % 2
                        r1, r2 = dvA[(ee, t, half)]
                        v.wait_ge(pe.h, peE[(ee, t, half)])
                        dv.inc(v.tensor_scalar(tmpe[:, par * 512:(par + 1) * 512],
                                               psU[par][:, 0:512],
                                               msel[:, ee * NT + t:ee * NT + t + 1],
                                               0.0, ALU.mult, ALU.max))
                        v.wait_ge(dv.h, r1)
                        sl = acc[:, t * D + half * 512:t * D + (half + 1) * 512]
                        if ee == 0:
                            dv.inc(v.tensor_copy(sl, tmpe[:, par * 512:(par + 1) * 512]))
                        else:
                            v.wait_ge(dv.h, dvA[(ee - 1, t, half)][1])
                            dv.inc(v.tensor_add(sl, sl, tmpe[:, par * 512:(par + 1) * 512]))
        blk4.__exit__(None, None, None)

        peA = {}
        n = H4_pe
        for t in range(NT):
            for half in range(2):
                n += 1
                peA[(t, half)] = n
        dvC = {}
        n = H4_dv
        for t in range(NT):
            for half in range(2):
                dvC[(t, half)] = (n + 1, n + 2)
                n += 2
        H5_dv = n
        ldX = {t: 16 * (t + 1) for t in range(NT)}
        LD_B = 64

        blk5 = nc.Block("h5"); blk5.__enter__()
        x1_3 = x1_in.rearrange("(t p) d -> p t d", p=P)
        x2_3 = x2_out.rearrange("(t p) d -> p t d", p=P)

        @blk5.gpsimd
        def _(g):
            for t in range(NT):
                if t >= 2:
                    g.wait_ge(st.h, 16 * (t - 1))
                ld.inc(g.dma_start(out=x1b[:, (t % 2) * D:(t % 2 + 1) * D].unsqueeze(1),
                                   in_=x1_3[:, t:t + 1, :]), 16)

        @blk5.tensor
        def _(t_):
            t_.wait_ge(dv.h, H2_dv)
            for t in range(NT):
                for half in range(2):
                    k = t * 2 + half
                    if k >= 2:
                        t2, h2 = divmod(k - 2, 2)
                        t_.wait_ge(dv.h, dvC[(t2, h2)][0])
                    for cc in range(DC):
                        inst = t_.matmul(psU[k % 2][:, 0:512],
                                         xn_sb[:, cc * TC + t * P:cc * TC + (t + 1) * P],
                                         w2_sb[:, cc * D + half * 512:cc * D + (half + 1) * 512],
                                         start=(cc == 0), stop=(cc == DC - 1))
                    pe.inc(inst)

        @blk5.vector
        def _(v):
            for t in range(NT):
                v.wait_ge(ld.h, LD_B + ldX[t])
                for half in range(2):
                    c1, c2 = dvC[(t, half)]
                    v.wait_ge(pe.h, peA[(t, half)])
                    sl = x1b[:, (t % 2) * D + half * 512:(t % 2) * D + (half + 1) * 512]
                    dv.inc(v.tensor_add(sl, sl, psU[(t * 2 + half) % 2][:, 0:512]))
                    v.wait_ge(dv.h, c1)
                    dv.inc(v.tensor_add(sl, sl,
                                        acc[:, t * D + half * 512:t * D + (half + 1) * 512]))

        @blk5.sync
        def _(sy):
            for t in range(NT):
                sy.wait_ge(dv.h, dvC[(t, 1)][1])
                st.inc(sy.dma_start(out=x2_3[:, t:t + 1, :],
                                    in_=x1b[:, (t % 2) * D:(t % 2 + 1) * D].unsqueeze(1)), 16)
            sy.wait_ge(st.h, 16 * NT)
        blk5.__exit__(None, None, None)
    return nc



def _in_maps(inputs):
    x = np.ascontiguousarray(inputs["L0_attn_hidden_states"], dtype=np.float32)

    def T(a):
        return np.ascontiguousarray(np.asarray(a, dtype=np.float32).T)

    shared = {
        "wqT": T(inputs["L0_attn_Wq"]), "wkT": T(inputs["L0_attn_Wk"]),
        "wvT": T(inputs["L0_attn_Wv"]), "wg1T": T(inputs["L0_attn_Wg1"]),
        "wg2T": T(inputs["L0_attn_Wg2"]),
        "gnorm_pc": np.ascontiguousarray(np.asarray(inputs["L0_attn_g_norm_weight"], np.float32).reshape(DC, P).T),
        "woT": T(inputs["L0_attn_Wo"]),
        "wfcT": T(inputs["L0_ffn_W_fc"]), "wpT": T(inputs["L0_ffn_W_proj"]),
        "lnw_pc": np.ascontiguousarray(np.asarray(inputs["L0_ffn_ln_w"], np.float32).reshape(HC, P).T),
        "lnb_pc": np.ascontiguousarray(np.asarray(inputs["L0_ffn_ln_b"], np.float32).reshape(HC, P).T),
    }
    maps = []
    for c in range(8):
        b, s = divmod(c, 4)
        pm = np.zeros((P, 4), np.float32)
        pm[:, :s] = 1.0
        m = dict(shared)
        m["xin"] = np.ascontiguousarray(x[b, s * TC:(s + 1) * TC, :])
        m["xprev"] = np.ascontiguousarray(x[b, 0:3 * TC, :])
        m["pmask"] = pm
        maps.append(m)
    return maps


def kernel(**inputs):
    cores = list(range(8))
    nc1 = build_fused("G")
    res1 = run_bass_kernel_spmd(nc1, _in_maps(inputs), core_ids=cores)
    R1 = [res1.results[i] for i in range(8)]

    wq1 = np.ascontiguousarray(inputs["L1_attn_W_q"], dtype=np.float32)
    wg = np.ascontiguousarray(inputs["L1_ffn_Wg"], dtype=np.float32)
    bg1 = np.ascontiguousarray(np.asarray(inputs["L1_ffn_bg"], np.float32).reshape(1, D))
    we_flat = np.ascontiguousarray(np.asarray(inputs["L1_ffn_We"], np.float32).reshape(8 * D, D))
    be1 = np.ascontiguousarray(np.asarray(inputs["L1_ffn_be"], np.float32).reshape(1, 8 * D))
    in_maps = []
    for i in range(8):
        g0 = (i // 4) * 4
        mp = np.ascontiguousarray(np.concatenate(
            [np.asarray(R1[g0 + j]["mloc_d"]) for j in range(4)], axis=0))
        in_maps.append({"x1": np.asarray(R1[i]["x1_d"]),
                        "xn1fm": np.asarray(R1[i]["xn1fm_d"]),
                        "mparts": mp, "wq1": wq1, "wg": wg, "bg": bg1,
                        "we": we_flat, "be": be1})
    nc2 = build_l2()
    res2 = run_bass_kernel_spmd(nc2, in_maps, core_ids=cores)
    out = np.empty((2, 8192, D), np.float32)
    for c in range(8):
        b, s = divmod(c, 4)
        out[b, s * TC:(s + 1) * TC, :] = res2.results[c]["x2"]
    return out
